# version 17
# baseline (speedup 1.0000x reference)
import numpy as np
import concourse.bass as bass
import concourse.mybir as mybir
from concourse import library_config
from concourse.bass_utils import run_bass_kernel_spmd

F32 = mybir.dt.float32
BF16 = mybir.dt.bfloat16
AF = mybir.ActivationFunctionType
ALU = mybir.AluOpType
AX = mybir.AxisListType

D = 1024
FF = 4096
NB = 2
KC = D // 128
FC = FF // 128
R = 16
SCL = 2.0
EPS = 1e-5
ENGS = ("sy", "gp", "ve", "sc", "te")


class EM:

    def __init__(self, nc, sems):
        self.nc = nc
        self.sems = sems
        self.cnt = {k: 0 for k in sems}
        self.lists = {e: [] for e in ENGS}

    def op(self, eng, fn, inc=None, k=1, drain=None):
        incs = []
        ret = None
        if inc is not None:
            self.cnt[inc] += k
            incs.append((self.sems[inc], k))
            ret = self.cnt[inc]
        self.lists[eng].append(("op", fn, incs))
        if drain is None:
            drain = eng == "ve"
        if drain:
            self.lists[eng].append(("drain",))
        return ret

    def drain(self, eng):
        self.lists[eng].append(("drain",))

    def inc_last(self, eng, inc, k=1):
        for entry in reversed(self.lists[eng]):
            if entry[0] == "op":
                if not entry[2]:
                    self.cnt[inc] += k
                    entry[2].append((self.sems[inc], k))
                    return self.cnt[inc]
                self.lists[eng].append(("drain",))
                return self.op(eng, lambda e: e.nop(), inc=inc, k=k)
        return self.op(eng, lambda e: e.nop(), inc=inc, k=k)

    def dma_fence(self, eng, sem_name):
        self.wait(eng, sem_name, self.cnt[sem_name])

    def wait(self, eng, sem_name, val):
        if val < 0:
            return
        self.lists[eng].append(("wait", self.sems[sem_name], val))

    def barrier(self):
        for e in ENGS:
            if e in ("sy", "gp"):
                self.op(e, lambda x: x.nop(), inc="s_ph")
            else:
                self.inc_last(e, "s_ph")
        tgt = self.cnt["s_ph"]
        for e in ENGS:
            self.wait(e, "s_ph", tgt)

    @staticmethod
    def _replay(eng, entries):
        for entry in entries:
            if entry[0] == "op":
                instr = entry[1](eng)
                for sem, k in entry[2]:
                    instr.then_inc(sem, k)
            elif entry[0] == "wait":
                eng.wait_ge(entry[1], entry[2])
            else:
                eng.drain()

    def realize(self):
        with self.nc.Block() as blk:
            for e, meth in (("sy", blk.sync), ("gp", blk.gpsimd), ("ve", blk.vector),
                            ("sc", blk.scalar), ("te", blk.tensor)):
                entries = self.lists[e]
                if entries:
                    meth(lambda eng, entries=entries: EM._replay(eng, entries))
        self.lists = {e: [] for e in ENGS}



def _close_truncated(nc, em):
    with nc.Block() as blk:
        def _w(eng):
            for name in ("s_ld", "s_gd", "s_scd", "s_uv", "s_ydma", "s_out"):
                if em.cnt[name] > 0:
                    eng.wait_ge(em.sems[name], em.cnt[name])
        blk.sync(_w)
    return nc

def build_program(L=2048, max_phase=4):
    NCH = L // 128
    NT1 = 256
    NT3 = min(512, L)
    NS3 = NT3 // 128
    NTIL1 = L // NT1
    NTIL3 = L // NT3
    nc = bass.Bass("TRN2", target_bir_lowering=False)

    x_d = nc.dram_tensor("x", [NB, L, D], F32, kind="ExternalInput")
    c_d = nc.dram_tensor("cvec", [NB, D], F32, kind="ExternalInput")
    wmod_d = nc.dram_tensor("wmod", [D, 6 * D], F32, kind="ExternalInput")
    decay_d = nc.dram_tensor("decay", [D], F32, kind="ExternalInput")
    wr_d = nc.dram_tensor("Wr", [D, D], F32, kind="ExternalInput")
    br_d = nc.dram_tensor("br", [D], F32, kind="ExternalInput")
    wg_d = nc.dram_tensor("Wg", [D, D], F32, kind="ExternalInput")
    bg_d = nc.dram_tensor("bg", [D], F32, kind="ExternalInput")
    wo_d = nc.dram_tensor("Wo", [D, D], F32, kind="ExternalInput")
    bo_d = nc.dram_tensor("bo", [D], F32, kind="ExternalInput")
    lnw_d = nc.dram_tensor("ln_w", [D], F32, kind="ExternalInput")
    lnb_d = nc.dram_tensor("ln_b", [D], F32, kind="ExternalInput")
    wfc_d = nc.dram_tensor("W_fc", [FF, D], F32, kind="ExternalInput")
    bfc_d = nc.dram_tensor("b_fc", [FF], F32, kind="ExternalInput")
    afc_d = nc.dram_tensor("A_fc", [R, D], F32, kind="ExternalInput")
    Bfc_d = nc.dram_tensor("B_fc", [FF, R], F32, kind="ExternalInput")
    wpr_d = nc.dram_tensor("W_proj", [D, FF], F32, kind="ExternalInput")
    bpr_d = nc.dram_tensor("b_proj", [D], F32, kind="ExternalInput")
    apr_d = nc.dram_tensor("A_proj", [R, FF], F32, kind="ExternalInput")
    Bpr_d = nc.dram_tensor("B_proj", [D, R], F32, kind="ExternalInput")
    out_d = nc.dram_tensor("out", [NB, L, D], F32, kind="ExternalOutput")
    ynT_h = nc.dram_tensor("ynT_h", [NB, KC, 128, L], BF16)
    wqfc_h = nc.dram_tensor("wqfc_h", [FC, 128, KC * 128], BF16)
    wqpr_h = nc.dram_tensor("wqpr_h", [KC, 128, FC * 128], BF16)

    A = nc.alloc_sbuf_tensor
    woT = A("woT", [128, KC * D], BF16)
    mod_sb = A("mod_sb", [64, 6 * D], F32)
    rep = A("rep", [128, 4 * D], BF16)
    tm_bf = A("tm_bf", [128, D], BF16)
    vecs = A("vecs", [128, 64], F32)
    dec_bc = A("dec_bc", [128, 16], F32)
    lnb_b = A("lnb_b", [128, 8], BF16)
    gfm = A("gfm", [128, 32], F32)
    bfc_fm = A("bfc_fm", [128, FC], F32)
    id_b = A("id_b", [128, 128], BF16)
    id_f = A("id_f", [128, 128], F32)
    ones1 = A("ones1", [64, 128], F32)
    afcT = A("afcT", [128, KC * R], BF16)
    aprT = A("aprT", [128, FC * R], BF16)
    bfcT = A("bfcT", [R, FF], BF16)
    bprT = A("bprT", [R, D], BF16)
    x_sb = A("x_sb", [128, 4 * D], F32)
    tm_a = A("tm_a", [128, D], F32)
    h_tmb = A("h_tmb", [128, 2 * D], BF16)
    stat = A("stat", [128, 16], F32)
    sq_dump = A("sq_dump", [128, D], F32)
    mx = A("mx", [128, 8], F32)
    siluT = A("siluT", [128, 16], F32)
    scan_tmp = A("scan_tmp", [128, 48], F32)
    s2 = A("s2", [R, NT3], BF16)

    ps_mm = nc.alloc_psum_tensor("ps_mm", [128, 1024], F32)
    ps_sc = nc.alloc_psum_tensor("ps_sc", [128, 1024], F32)
    ps_bf = nc.alloc_psum_tensor("ps_bf", [128, 2048], BF16)
    ps_lo = nc.alloc_psum_tensor("ps_lo", [128, 512], F32)
    ps_x = nc.alloc_psum_tensor("ps_x", [128, 1024], BF16)

    sems = {}
    for name in ["s_ld", "s_gd", "s_scd", "s_mm", "s_dv", "s_sc", "s_gp",
                 "s_uv", "s_p1", "s_ydma", "s_ev", "s_fl", "s_tr", "s_q", "s_qt", "s_out", "s_ph"]:
        sems[name] = nc.alloc_semaphore(name)
    em = EM(nc, sems)

    Z = vecs[:, 52:53]
    ONE = vecs[:, 53:54]

    cm_wrT = nc.sbuf_tensor("wrT", [128, KC * D], BF16)
    wrT = cm_wrT.__enter__()
    cm_wgT = nc.sbuf_tensor("wgT", [128, KC * D], BF16)
    wgT = cm_wgT.__enter__()
    cm_p0 = [nc.sbuf_tensor("wstage", [128, 2 * D], F32),
             nc.sbuf_tensor("wmodf", [128, 6 * D], F32),
             nc.sbuf_tensor("csb", [64, D], F32)]
    wstage, wmodf, csb = [c.__enter__() for c in cm_p0]
    lora_st = wstage

    em.op("ve", lambda e: e.memset(vecs[:], 0.0))
    mz = em.op("ve", lambda e: e.memset(ones1[:], 1.0), inc="s_dv")
    em.wait("gp", "s_dv", mz)
    em.op("gp", lambda e: e.iota(vecs[:, 54:55], pattern=[[0, 1]], base=0,
                                 channel_multiplier=1, allow_small_or_imprecise_dtypes=True))
    mgp = em.op("gp", lambda e: e.iota(id_f[:], pattern=[[1, 128]], base=0, channel_multiplier=0,
                                       allow_small_or_imprecise_dtypes=True), inc="s_gp")
    em.wait("ve", "s_gp", mgp)
    em.op("ve", lambda e: e.tensor_scalar(id_f[:], id_f[:], vecs[:, 54:55], None,
                                          op0=ALU.is_equal), inc="s_dv")
    em.op("ve", lambda e: e.tensor_copy(id_b[:], id_f[:]), inc="s_dv")
    mID = em.op("ve", lambda e: e.memset(vecs[:, 53:54], 1.0), inc="s_dv")
    em.wait("te", "s_dv", mID)
    em.wait("sc", "s_dv", mID)

    def vec_to_fm(rows, dsts):
        mld = 0
        for r, dr in enumerate(rows):
            mld = em.op("sy", lambda e, r=r, dr=dr: e.dma_start(
                csb[32 * r:32 * r + 1, :], dr.ap().rearrange("(o d) -> o d", o=1)),
                inc="s_ld", k=16)
        em.wait("te", "s_ld", mld)
        for r in range(len(rows)):
            for k in range(KC):
                em.op("te", lambda e, r=r, k=k: e.transpose(
                    ps_sc[:, r * 512 + k:r * 512 + k + 1],
                    csb[32 * r:32 * r + 1, k * 128:(k + 1) * 128],
                    ones1[32 * r:32 * r + 1, 0:1]))
        mtr = em.inc_last("te", "s_tr")
        em.wait("ve", "s_tr", mtr)
        for r, dcol in enumerate(dsts):
            em.op("ve", lambda e, r=r, dcol=dcol: e.tensor_copy(
                vecs[:, dcol:dcol + 8], ps_sc[:, r * 512:r * 512 + 8]))
        mdv = em.inc_last("ve", "s_dv")
        em.wait("sy", "s_dv", mdv)
        em.wait("te", "s_dv", mdv)
        return mdv

    mv1 = vec_to_fm([decay_d, br_d], [0, 16])
    vec_to_fm([bg_d, bo_d], [24, 32])
    vec_to_fm([bpr_d], [40])

    em.wait("sc", "s_dv", mv1)
    em.op("sc", lambda e: e.activation(vecs[:, 0:8], vecs[:, 0:8], AF.Sigmoid, bias=Z))
    msc = em.inc_last("sc", "s_sc")
    em.wait("ve", "s_sc", msc)
    em.op("ve", lambda e: e.tensor_scalar(vecs[:, 8:16], vecs[:, 0:8], -1.0, 1.0,
                                          op0=ALU.mult, op1=ALU.add))
    em.op("ve", lambda e: e.tensor_copy(dec_bc[:, 0:16:2], vecs[:, 0:8]))
    em.op("ve", lambda e: e.tensor_copy(dec_bc[:, 1:16:2], vecs[:, 0:8]), inc="s_dv")

    for p in range(4):
        em.dma_fence("sy", "s_ld")
        mld = em.op("sy", lambda e, p=p: e.dma_start(
            csb[0:1, :], bfc_d.ap().rearrange("(o d) -> o d", o=4)[p:p + 1, :]), inc="s_ld", k=16)
        em.wait("te", "s_ld", mld)
        for k in range(KC):
            em.op("te", lambda e, k=k: e.transpose(
                ps_sc[:, k:k + 1], csb[0:1, k * 128:(k + 1) * 128], id_f[0:1, 0:1]))
        mtr = em.inc_last("te", "s_tr")
        em.wait("ve", "s_tr", mtr)
        mdv = em.op("ve", lambda e, p=p: e.tensor_copy(
            bfc_fm[:, p * 8:(p + 1) * 8], ps_sc[:, 0:8]), inc="s_dv")
        em.wait("sy", "s_dv", mdv)
        em.wait("te", "s_dv", mdv)

    vec_to_fm([lnw_d], [55])
    mld = em.op("sy", lambda e: e.dma_start(csb[0:1, :], lnb_d.ap().rearrange("(o d) -> o d", o=1)), inc="s_ld", k=16)
    em.wait("te", "s_ld", mld)
    for k in range(KC):
        em.op("te", lambda e, k=k: e.transpose(
            ps_sc[:, k:k + 1], csb[0:1, k * 128:(k + 1) * 128], ones1[0:1, 0:1]))
    mtr = em.inc_last("te", "s_tr")
    em.wait("ve", "s_tr", mtr)
    em.op("ve", lambda e: e.tensor_copy(stat[:, 0:8], ps_sc[:, 0:8]))
    em.op("ve", lambda e: e.tensor_copy(lnb_b[:], ps_sc[:, 0:8]))
    mdv = em.inc_last("ve", "s_dv")
    em.wait("sy", "s_dv", mdv)
    em.wait("te", "s_dv", mdv)

    prev_ev = 0
    for wd, dstT in ((wg_d, wgT), (wr_d, wrT), (wo_d, woT)):
        for r in range(KC):
            par = r % 2
            em.wait("sy", "s_dv", prev_ev)
            em.dma_fence("sy", "s_ld")
            mld = em.op("sy", lambda e, wd=wd, par=par, r=r: e.dma_start(
                wstage[:, par * D:(par + 1) * D], wd.ap()[128 * r:128 * (r + 1), :]),
                inc="s_ld", k=16)
            em.wait("te", "s_ld", mld)
            em.wait("te", "s_dv", prev_ev)
            for k in range(KC):
                em.op("te", lambda e, par=par, k=k: e.transpose(
                    ps_sc[:, (k % 4) * 128 + (k // 4) * 512:(k % 4) * 128 + (k // 4) * 512 + 128],
                    wstage[:, par * D + k * 128:par * D + (k + 1) * 128], id_f[:]))
            mtr = em.inc_last("te", "s_tr")
            em.wait("ve", "s_tr", mtr)
            for half in range(2):
                dst3 = dstT[:].rearrange("p (k e) -> p k e", k=KC)[:, half * 4:(half + 1) * 4,
                                                                  r * 128:(r + 1) * 128]
                src3 = ps_sc[:, half * 512:(half + 1) * 512].rearrange("p (k i) -> p k i", k=4)
                em.op("ve", lambda e, dst3=dst3, src3=src3: e.tensor_copy(dst3, src3))
            prev_ev = em.inc_last("ve", "s_dv")

    em.wait("te", "s_dv", prev_ev)
    for hh in range(2):
        for k in range(KC):
            em.op("te", lambda e, hh=hh, k=k: e.matmul(
                ps_sc[0:1, hh * 512:(hh + 1) * 512], lnb_b[:, k:k + 1],
                woT[:, k * D + hh * 512:k * D + (hh + 1) * 512],
                start=(k == 0), stop=(k == KC - 1)))
    mq = em.inc_last("te", "s_mm")
    em.wait("ve", "s_mm", mq)
    em.op("ve", lambda e: e.tensor_copy(csb[0:1, 0:512], ps_sc[0:1, 0:512]))
    mdv = em.op("ve", lambda e: e.tensor_copy(csb[0:1, 512:1024], ps_sc[0:1, 512:1024]),
                inc="s_dv")
    em.wait("te", "s_dv", mdv)
    for k in range(KC):
        em.op("te", lambda e, k=k: e.transpose(
            ps_sc[:, k:k + 1], csb[0:1, k * 128:(k + 1) * 128], ones1[0:1, 0:1]))
    mtr = em.inc_last("te", "s_tr")
    em.wait("ve", "s_tr", mtr)
    em.op("ve", lambda e: e.tensor_tensor(vecs[:, 32:40], vecs[:, 32:40], ps_sc[:, 0:8],
                                          op=ALU.add))
    for k in range(KC):
        em.op("ve", lambda e, k=k: e.tensor_scalar(
            woT[:, k * D:(k + 1) * D], woT[:, k * D:(k + 1) * D],
            vecs[:, 55 + k:56 + k], None, op0=ALU.mult), drain=False)
    m_wofold = em.inc_last("ve", "s_dv")
    em.wait("te", "s_dv", m_wofold)
    em.wait("sy", "s_dv", m_wofold)

    em.dma_fence("sy", "s_ld")
    mld = em.op("sy", lambda e: e.dma_start(csb[0:2, :], c_d.ap()), inc="s_ld", k=16)
    em.wait("sc", "s_ld", mld)
    msc = em.op("sc", lambda e: e.activation(tm_a[0:2, :], csb[0:2, :], AF.Silu, bias=vecs[0:2, 52:53]), inc="s_sc")
    em.wait("te", "s_sc", msc)
    em.wait("te", "s_dv", prev_ev)
    for k in range(KC):
        em.op("te", lambda e, k=k: e.transpose(
            ps_sc[:, 2 * k:2 * k + 2], tm_a[0:2, k * 128:(k + 1) * 128], id_f[0:2, 0:2]))
    mtr = em.inc_last("te", "s_tr")
    em.wait("ve", "s_tr", mtr)
    msil = em.op("ve", lambda e: e.tensor_copy(siluT[:], ps_sc[:, 0:16]), inc="s_dv")

    ev_n = {}
    mq_prev = 0
    for kk in range(KC):
        em.wait("gp", "s_mm", mq_prev)
        em.dma_fence("gp", "s_gd")
        mgd = em.op("gp", lambda e, kk=kk: e.dma_start(
            wmodf[:], wmod_d.ap()[128 * kk:128 * (kk + 1), :]), inc="s_gd", k=16)
        em.wait("te", "s_gd", mgd)
        if kk == 0:
            em.wait("te", "s_dv", msil)
        for n in range(12):
            for b in range(2):
                par = (2 * n + b) % 2
                em.wait("te", "s_dv", ev_n.get((kk, 2 * n + b - 2), 0))
                em.op("te", lambda e, par=par, kk=kk, n=n, b=b: e.matmul(
                    ps_sc[0:1, par * 512:(par + 1) * 512],
                    siluT[:, 2 * kk + b:2 * kk + b + 1],
                    wmodf[:, n * 512:(n + 1) * 512],
                    start=True, stop=True))
                mmm = em.inc_last("te", "s_mm")
                mq_prev = mmm
                em.wait("ve", "s_mm", mmm)
                brow = 32 * b
                if kk == 0:
                    ev_n[(kk, 2 * n + b)] = em.op(
                        "ve", lambda e, par=par, n=n, brow=brow: e.tensor_copy(
                            mod_sb[brow:brow + 1, n * 512:(n + 1) * 512],
                            ps_sc[0:1, par * 512:(par + 1) * 512]), inc="s_dv")
                else:
                    ev_n[(kk, 2 * n + b)] = em.op(
                        "ve", lambda e, par=par, n=n, brow=brow: e.tensor_tensor(
                            mod_sb[brow:brow + 1, n * 512:(n + 1) * 512],
                            mod_sb[brow:brow + 1, n * 512:(n + 1) * 512],
                            ps_sc[0:1, par * 512:(par + 1) * 512], op=ALU.add), inc="s_dv")
        ev_n[(kk + 1, -2)] = ev_n[(kk, 22)]
        ev_n[(kk + 1, -1)] = ev_n[(kk, 23)]

    for off in (2 * D, 5 * D):
        for brow in (0, 32):
            em.op("ve", lambda e, off=off, brow=brow: e.tensor_scalar(
                tm_a[brow:brow + 1, :], mod_sb[brow:brow + 1, off:off + D], 0.0, 2.0,
                op0=ALU.is_ge, op1=ALU.mult))
            mdv = em.op("ve", lambda e, brow=brow: e.tensor_scalar(
                tm_a[brow:brow + 1, :], tm_a[brow:brow + 1, :], -1.0, None,
                op0=ALU.add), inc="s_dv")
            em.wait("sc", "s_dv", mdv)
            msc = em.op("sc", lambda e, off=off, brow=brow: e.activation(
                sq_dump[brow:brow + 1, :], mod_sb[brow:brow + 1, off:off + D], AF.Abs,
                bias=vecs[brow:brow + 1, 52:53]), inc="s_sc")
            em.wait("ve", "s_sc", msc)
            em.op("ve", lambda e, brow=brow: e.tensor_scalar(
                sq_dump[brow:brow + 1, :], sq_dump[brow:brow + 1, :], 0.25, None, op0=ALU.max))
            em.op("ve", lambda e, off=off, brow=brow: e.tensor_tensor(
                mod_sb[brow:brow + 1, off:off + D], tm_a[brow:brow + 1, :],
                sq_dump[brow:brow + 1, :], op=ALU.mult), inc="s_dv")

    em.wait("sy", "s_dv", prev_ev)
    em.dma_fence("sy", "s_ld")
    mld = em.op("sy", lambda e: e.dma_start(lora_st[0:R, 0:D], afc_d.ap()), inc="s_ld", k=16)
    em.wait("te", "s_dv", em.cnt["s_dv"])
    em.wait("te", "s_ld", mld)
    for k in range(KC):
        em.op("te", lambda e, k=k: e.transpose(
            ps_sc[:, k * 16:(k + 1) * 16], lora_st[0:R, k * 128:(k + 1) * 128], id_f[0:R, 0:R]))
    mtr = em.inc_last("te", "s_tr")
    em.wait("ve", "s_tr", mtr)
    mdv = em.op("ve", lambda e: e.tensor_copy(afcT[:], ps_sc[:, 0:KC * R]), inc="s_dv")
    em.wait("sy", "s_dv", mdv)
    em.wait("te", "s_dv", mdv)
    for hh in range(2):
        em.dma_fence("sy", "s_ld")
        mld = em.op("sy", lambda e, hh=hh: e.dma_start(
            lora_st[0:R, 0:2 * D], apr_d.ap()[:, hh * 2 * D:(hh + 1) * 2 * D]),
            inc="s_ld", k=16)
        em.wait("te", "s_ld", mld)
        for k in range(16):
            em.op("te", lambda e, k=k: e.transpose(
                ps_sc[:, k * 16:(k + 1) * 16], lora_st[0:R, k * 128:(k + 1) * 128],
                id_f[0:R, 0:R]))
        mtr = em.inc_last("te", "s_tr")
        em.wait("ve", "s_tr", mtr)
        mdv = em.op("ve", lambda e, hh=hh: e.tensor_copy(
            aprT[:, hh * 16 * R:(hh + 1) * 16 * R], ps_sc[:, 0:16 * R]), inc="s_dv")
        em.wait("te", "s_dv", mdv)
        em.wait("sy", "s_dv", mdv)
    em.wait("gp", "s_dv", mdv)
    for src_d, dstT, nt in ((Bfc_d, bfcT, FC), (Bpr_d, bprT, KC)):
        for g4 in range(nt // 4):
            em.dma_fence("gp", "s_gd")
            em.wait("gp", "s_dv", em.cnt["s_dv"])
            mgd = 0
            for jj in range(4):
                j = g4 * 4 + jj
                mgd = em.op("gp", lambda e, src_d=src_d, j=j, jj=jj: e.dma_start(
                    wstage[:, jj * 16:(jj + 1) * 16],
                    src_d.ap()[j * 128:(j + 1) * 128, :]), inc="s_gd", k=16)
            em.wait("te", "s_gd", mgd)
            for jj in range(4):
                em.op("te", lambda e, jj=jj: e.transpose(
                    ps_lo[0:128, jj * 128:(jj + 1) * 128][0:R, :],
                    wstage[:, jj * 16:(jj + 1) * 16], id_f[:]))
            mtr = em.inc_last("te", "s_tr")
            em.wait("ve", "s_tr", mtr)
            mdv = em.op("ve", lambda e, dstT=dstT, g4=g4: e.tensor_copy(
                dstT[0:R, g4 * 512:(g4 + 1) * 512], ps_lo[0:R, :]), inc="s_dv")
            em.wait("te", "s_dv", mdv)

    em.barrier()
    em.realize()
    for c in reversed(cm_p0):
        c.__exit__(None, None, None)
    if max_phase <= 0:
        return _close_truncated(nc, em)


    def actT_full(t, k):
        return t[:, k * NT3:(k + 1) * NT3]

    def build_rep(slot, b, off, bias_col, prev_mm):
        em.wait("te", "s_sc", prev_mm)
        for hh in range(2):
            mmm = em.op("te", lambda e, b=b, off=off, hh=hh: e.matmul(
                ps_mm[:, hh * 512:(hh + 1) * 512], ones1[32 * b:32 * b + 1, :],
                mod_sb[32 * b:32 * b + 1, off + hh * 512:off + (hh + 1) * 512],
                start=True, stop=True), inc="s_mm")
        em.wait("sc", "s_mm", mmm)
        for hh in range(2):
            msc = em.op("sc", lambda e, slot=slot, hh=hh, bias_col=bias_col: e.activation(
                rep[:, slot * D + hh * 512:slot * D + (hh + 1) * 512],
                ps_mm[:, hh * 512:(hh + 1) * 512], AF.Identity, bias=bias_col), inc="s_sc")
        return msc

    pbf_free = {0: ("s_dv", 0), 1: ("s_dv", 0)}
    x_free = [(0, 0)] * 4

    def emit_ln_modulate(src_ap, par_h, rep_mul, rep_add, mul2, add2,
                         m_data, data_sem):
        em.wait("ve", data_sem, m_data)
        em.op("ve", lambda e: e.reduce_sum(stat[:, 0:1], src_ap, axis=AX.X))
        em.wait("sc", data_sem, m_data)
        msq = em.op("sc", lambda e: e.activation(
            sq_dump[:], src_ap, AF.Square, bias=Z, accum_out=stat[:, 1:2]), inc="s_sc")
        em.op("ve", lambda e: e.tensor_scalar(stat[:, 2:3], stat[:, 0:1], 1.0 / D, None, op0=ALU.mult))
        em.wait("ve", "s_sc", msq)
        em.op("ve", lambda e: e.tensor_scalar(stat[:, 4:5], stat[:, 1:2], 1.0 / D, EPS,
                                              op0=ALU.mult, op1=ALU.add))
        em.op("ve", lambda e: e.tensor_tensor(stat[:, 5:6], stat[:, 2:3], stat[:, 2:3], op=ALU.mult))
        mdv = em.op("ve", lambda e: e.tensor_tensor(stat[:, 4:5], stat[:, 4:5], stat[:, 5:6],
                                                    op=ALU.subtract), inc="s_dv")
        em.wait("sc", "s_dv", mdv)
        msq = em.op("sc", lambda e: e.activation(stat[:, 5:6], stat[:, 4:5], AF.Sqrt, bias=Z),
                    inc="s_sc")
        em.wait("ve", "s_sc", msq)
        em.op("ve", lambda e: e.reciprocal(stat[:, 3:4], stat[:, 5:6]))
        mnm = em.op("ve", lambda e: e.tensor_scalar(
            stat[:, 6:7], stat[:, 2:3], stat[:, 3:4], -1.0,
            op0=ALU.mult, op1=ALU.mult), inc="s_dv")
        em.wait("sc", "s_dv", mnm)
        mcons = em.op("sc", lambda e: e.activation(
            tm_bf[:], src_ap, AF.Identity, scale=stat[:, 3:4], bias=stat[:, 6:7]),
            inc="s_sc")
        em.wait("ve", "s_sc", mcons)
        em.op("ve", lambda e: e.tensor_tensor(tm_bf[:], tm_bf[:], rep_mul, op=ALU.mult))
        mh = em.op("ve", lambda e, par_h=par_h: e.tensor_tensor(
            h_tmb[:, par_h * D:(par_h + 1) * D], tm_bf[:], rep_add, op=ALU.add), inc="s_dv")
        return (mnm, mcons), mh

    def emit_act_transpose(par_h, mh, dstT, col0):
        em.wait("te", "s_dv", mh)
        em.wait("te", pbf_free[par_h][0], pbf_free[par_h][1])
        for k in range(KC):
            em.op("te", lambda e, par_h=par_h, k=k: e.transpose(
                ps_bf[:, par_h * 1024 + k * 128:par_h * 1024 + (k + 1) * 128],
                h_tmb[:, par_h * D + k * 128:par_h * D + (k + 1) * 128], id_b[:]))
        mtr = em.inc_last("te", "s_tr")
        em.wait("ve", "s_tr", mtr)
        dst3 = dstT[:].rearrange("p (k t) -> p k t", k=KC)[:, :, col0:col0 + 128]
        src3 = ps_bf[:, par_h * 1024:(par_h + 1) * 1024].rearrange("p (k i) -> p k i", k=KC)
        mev = em.op("ve", lambda e, dst3=dst3, src3=src3: e.tensor_copy(dst3, src3), inc="s_dv")
        pbf_free[par_h] = ("s_dv", mev)
        return mev

    cm_p2 = [nc.sbuf_tensor("y_fm", [128, 2 * 128 * 16], BF16),
             nc.sbuf_tensor("u_st", [128, 4 * 128 * 16], BF16),
             nc.sbuf_tensor("v_st", [128, 4 * 128 * 16], BF16),
             nc.sbuf_tensor("y_tm", [128, NB * D], BF16),
             nc.sbuf_tensor("ystage", [128, D], BF16),
             nc.sbuf_tensor("actT1", [128, KC * NT1], BF16),
             nc.sbuf_tensor("qload", [128, 2 * D], F32),
             nc.sbuf_tensor("qa", [128, D], F32),
             nc.sbuf_tensor("qr", [128, D], F32),
             nc.sbuf_tensor("qt", [128, D], F32),
             nc.sbuf_tensor("qsgn", [128, 2 * D], BF16),
             nc.sbuf_tensor("qb", [128, 2 * D], BF16),
             nc.sbuf_tensor("qstage", [128, 2 * D], BF16)]
    (y_fm, u_st, v_st, y_tm, ystage, actT1, qload, qa, qr, qt, qsgn, qb, qstage) = \
        [c.__enter__() for c in cm_p2]

    def ring_slice(chunk, c, b):
        base = (chunk % 4) * 2048 + c * 2 + b
        return base

    mrep = 0
    for b in range(NB):
        mrep = build_rep(2 * b + 0, b, D, ONE, mrep)
        mrep = build_rep(2 * b + 1, b, 0, Z, mrep)
    em.wait("ve", "s_sc", mrep)

    p1m = {}
    zev = {0: mrep, 1: mrep}
    zuse = {}

    def build_p1_tile(t1):
        items = []
        t0 = t1 * NT1

        def ln_items(b, s):
            slot = 2 * b + s
            src_ap = x_sb[:, slot * D:(slot + 1) * D]
            par_h = slot % 2

            def i_load(em, b=b, s=s, slot=slot, t0=t0, src_ap=src_ap):
                em.wait("sy", "s_p1", x_free[slot][0])
                em.wait("sy", "s_sc", x_free[slot][1])
                em.dma_fence("sy", "s_ld")
                p1m[('xl', slot)] = em.op("sy", lambda e: e.dma_start(
                    src_ap, x_d.ap()[b, t0 + 128 * s:t0 + 128 * (s + 1), :]),
                    inc="s_ld", k=16)

            def i_st1(em, slot=slot, src_ap=src_ap):
                em.wait("ve", "s_ld", p1m[('xl', slot)])
                em.op("ve", lambda e: e.reduce_sum(stat[:, 0:1], src_ap, axis=AX.X))
                em.wait("sc", "s_ld", p1m[('xl', slot)])
                p1m['sq'] = em.op("sc", lambda e: e.activation(
                    sq_dump[:], src_ap, AF.Square, bias=Z, accum_out=stat[:, 1:2]),
                    inc="s_sc")

            def i_st2(em):
                em.op("ve", lambda e: e.tensor_scalar(stat[:, 2:3], stat[:, 0:1],
                                                      1.0 / D, None, op0=ALU.mult))
                em.wait("ve", "s_sc", p1m['sq'])
                em.op("ve", lambda e: e.tensor_scalar(stat[:, 4:5], stat[:, 1:2],
                                                      1.0 / D, EPS, op0=ALU.mult, op1=ALU.add))
                em.op("ve", lambda e: e.tensor_tensor(stat[:, 5:6], stat[:, 2:3],
                                                      stat[:, 2:3], op=ALU.mult))
                p1m['var'] = em.op("ve", lambda e: e.tensor_tensor(
                    stat[:, 4:5], stat[:, 4:5], stat[:, 5:6], op=ALU.subtract), inc="s_p1")

            def i_st3(em):
                em.wait("sc", "s_p1", p1m['var'])
                p1m['sd'] = em.op("sc", lambda e: e.activation(
                    stat[:, 5:6], stat[:, 4:5], AF.Sqrt, bias=Z), inc="s_sc")

            def i_st4(em, slot=slot):
                em.wait("ve", "s_sc", p1m['sd'])
                em.op("ve", lambda e: e.reciprocal(stat[:, 3:4], stat[:, 5:6]))
                p1m['nm'] = em.op("ve", lambda e: e.tensor_scalar(
                    stat[:, 6:7], stat[:, 2:3], stat[:, 3:4], -1.0,
                    op0=ALU.mult, op1=ALU.mult), inc="s_p1")

            def i_xn(em, slot=slot, src_ap=src_ap):
                em.wait("sc", "s_p1", p1m['nm'])
                p1m[('xn', slot)] = em.op("sc", lambda e: e.activation(
                    tm_bf[:], src_ap, AF.Identity, scale=stat[:, 3:4],
                    bias=stat[:, 6:7]), inc="s_sc")

            def i_mod(em, b=b, slot=slot, par_h=par_h):
                em.wait("ve", "s_sc", p1m[('xn', slot)])
                em.op("ve", lambda e: e.tensor_tensor(
                    tm_bf[:], tm_bf[:], rep[:, (2 * b) * D:(2 * b + 1) * D], op=ALU.mult))
                mh = em.op("ve", lambda e: e.tensor_tensor(
                    h_tmb[:, par_h * D:(par_h + 1) * D], tm_bf[:],
                    rep[:, (2 * b + 1) * D:(2 * b + 2) * D], op=ALU.add), inc="s_p1")
                p1m[('h', slot)] = mh
                x_free[slot] = (p1m['nm'], p1m[('xn', slot)])

            def i_tr(em, slot=slot, par_h=par_h, s=s):
                em.wait("te", "s_p1", p1m[('h', slot)])
                em.wait("te", pbf_free[par_h][0], pbf_free[par_h][1])
                for k in range(KC):
                    em.op("te", lambda e, par_h=par_h, k=k: e.transpose(
                        ps_bf[:, par_h * 1024 + k * 128:par_h * 1024 + (k + 1) * 128],
                        h_tmb[:, par_h * D + k * 128:par_h * D + (k + 1) * 128], id_b[:]))
                p1m[('tr', slot)] = em.inc_last("te", "s_tr")

            def i_ev(em, slot=slot, par_h=par_h, s=s):
                em.wait("ve", "s_tr", p1m[('tr', slot)])
                dst3 = actT1[:].rearrange("p (k t) -> p k t", k=KC)[:, :, s * 128:(s + 1) * 128]
                src3 = ps_bf[:, par_h * 1024:(par_h + 1) * 1024].rearrange(
                    "p (k i) -> p k i", k=KC)
                mev = em.op("ve", lambda e, dst3=dst3, src3=src3: e.tensor_copy(dst3, src3),
                            inc="s_p1")
                pbf_free[par_h] = ("s_p1", mev)
                p1m[('ev', slot)] = mev
            return [i_load, i_st1, i_st2, i_st3, i_st4, i_xn, i_mod, i_tr, i_ev]

        def z_items(b, c, t1=t1):
            pz = c % 2
            chunk0 = 2 * t1

            def zi_mm(em, b=b, c=c, pz=pz):
                em.wait("te", "s_p1", p1m[('ev', 2 * b)])
                em.wait("te", "s_p1", p1m[('ev', 2 * b + 1)])
                em.wait("te", "s_sc", zev.get(pz, 0))
                for k in range(KC):
                    em.op("te", lambda e, c=c, k=k, pz=pz: e.matmul(
                        ps_mm[:, pz * 512:pz * 512 + NT1],
                        wgT[:, k * D + c * 128:k * D + (c + 1) * 128],
                        actT1[:, k * NT1:(k + 1) * NT1],
                        start=(k == 0), stop=(k == KC - 1)))
                p1m[('zmm', pz)] = em.inc_last("te", "s_mm")

            def zi_ev(em, c=c, pz=pz):
                em.wait("sc", "s_mm", p1m[('zmm', pz)])
                em.wait("sc", "s_p1", zuse.get(pz, 0))
                zev[pz] = em.op("sc", lambda e, pz=pz, c=c: e.activation(
                    tm_a[:, pz * 512:pz * 512 + NT1], ps_mm[:, pz * 512:pz * 512 + NT1],
                    AF.Sigmoid, bias=vecs[:, 24 + c:25 + c]), inc="s_sc")

            def zi_u(em, b=b, c=c, pz=pz, chunk0=chunk0):
                em.wait("ve", "s_sc", zev[pz])
                b0 = (chunk0 % 4) * 2048 + c * 2 + b
                b1 = ((chunk0 + 1) % 4) * 2048 + c * 2 + b
                em.op("ve", lambda e, b0=b0, pz=pz: e.tensor_scalar(
                    u_st[:, b0:b0 + 16 * 127 + 1:16], tm_a[:, pz * 512:pz * 512 + 128],
                    vecs[:, 8 + c:9 + c], None, op0=ALU.mult), drain=False)
                em.op("ve", lambda e, b1=b1, pz=pz: e.tensor_scalar(
                    u_st[:, b1:b1 + 16 * 127 + 1:16],
                    tm_a[:, pz * 512 + 128:pz * 512 + 256],
                    vecs[:, 8 + c:9 + c], None, op0=ALU.mult))

            def zi_v(em, b=b, c=c, pz=pz, chunk0=chunk0):
                b0 = (chunk0 % 4) * 2048 + c * 2 + b
                b1 = ((chunk0 + 1) % 4) * 2048 + c * 2 + b
                em.op("ve", lambda e, b0=b0, c=c: e.scalar_tensor_tensor(
                    v_st[:, b0:b0 + 16 * 127 + 1:16],
                    actT1[:, c * NT1:c * NT1 + 128],
                    vecs[:, 16 + c:17 + c],
                    u_st[:, b0:b0 + 16 * 127 + 1:16],
                    op0=ALU.subtract, op1=ALU.mult), drain=False)
                muv = em.op("ve", lambda e, b1=b1, c=c: e.scalar_tensor_tensor(
                    v_st[:, b1:b1 + 16 * 127 + 1:16],
                    actT1[:, c * NT1 + 128:(c + 1) * NT1],
                    vecs[:, 16 + c:17 + c],
                    u_st[:, b1:b1 + 16 * 127 + 1:16],
                    op0=ALU.subtract, op1=ALU.mult), inc="s_p1")
                zuse[pz] = muv
            return [zi_mm, zi_ev, zi_u, zi_v]

        for b in range(NB):
            for s in range(2):
                items += ln_items(b, s)
            for c in range(KC):
                items += z_items(b, c)
        return items

    for t1 in range(min(2, NTIL1)):
        for it in build_p1_tile(t1):
            it(em)
    p1q = []
    p1_need = {}
    for t1 in range(2, NTIL1):
        w = 256 * t1 - 255
        for it in build_p1_tile(t1):
            p1q.append((w, it))
        p1_need[2 * t1] = len(p1q)
    p1pos = [0]

    def drain_p1(t, n):
        while n > 0 and p1pos[0] < len(p1q) and p1q[p1pos[0]][0] <= t:
            p1q[p1pos[0]][1](em)
            p1pos[0] += 1
            n -= 1

    def gran_src(g):
        if g < FC:
            return wfc_d.ap()[g * 128:(g + 1) * 128, :]
        gg = g - FC
        r, qf = gg // 4, gg % 4
        return wpr_d.ap()[r * 128:(r + 1) * 128, qf * D:(qf + 1) * D]

    qitems = []
    qm = {}

    def qi_init(em):
        qm[('red', -1)] = em.op("ve", lambda e: e.memset(mx[:, 0:2], 0.0), inc="s_q")
    qitems.append(qi_init)

    for g in range(64):
        def qi_mload(em, g=g):
            if g >= 2:
                em.wait("gp", "s_q", qm[('red', g - 2)])
            em.dma_fence("gp", "s_gd")
            qm[('mld', g)] = em.op("gp", lambda e, g=g: e.dma_start(
                qload[:, (g % 2) * D:(g % 2 + 1) * D], gran_src(g)), inc="s_gd", k=16)

        def qi_mred(em, g=g):
            em.wait("ve", "s_gd", qm[('mld', g)])
            em.op("ve", lambda e, g=g: e.tensor_reduce(
                mx[:, 2:3], qload[:, (g % 2) * D:(g % 2 + 1) * D], axis=AX.X,
                op=ALU.max, apply_absolute_value=True))
            col = 0 if g < FC else 1
            qm[('red', g)] = em.op("ve", lambda e, col=col: e.tensor_tensor(
                mx[:, col:col + 1], mx[:, col:col + 1], mx[:, 2:3], op=ALU.max), inc="s_q")
        qitems += [qi_mload, qi_mred]

    def qi_fin(em):
        em.wait("te", "s_q", qm[('red', 63)])
        em.op("te", lambda e: e.transpose(ps_lo[:, 0:128][0:1, :], mx[:, 0:1], id_f[:]))
        qm['fint'] = em.op("te", lambda e: e.transpose(
            ps_lo[:, 128:256][0:1, :], mx[:, 1:2], id_f[:]), inc="s_qt")
        em.wait("ve", "s_qt", qm['fint'])
        em.op("ve", lambda e: e.tensor_reduce(mx[0:1, 4:5], ps_lo[0:1, 0:128], axis=AX.X, op=ALU.max))
        em.op("ve", lambda e: e.tensor_reduce(mx[0:1, 6:7], ps_lo[0:1, 128:256], axis=AX.X, op=ALU.max))
        em.op("ve", lambda e: e.reciprocal(mx[0:1, 5:6], mx[0:1, 4:5]))
        qm['finv'] = em.op("ve", lambda e: e.reciprocal(mx[0:1, 7:8], mx[0:1, 6:7]), inc="s_q")
        em.wait("te", "s_q", qm['finv'])
        for i, mx_col in enumerate((4, 5, 6, 7)):
            em.op("te", lambda e, i=i, mx_col=mx_col: e.matmul(
                ps_lo[:, 256 + i:257 + i], ones1[0:1, :], mx[0:1, mx_col:mx_col + 1],
                start=True, stop=True))
        mtr = em.inc_last("te", "s_qt")
        em.wait("ve", "s_qt", mtr)
        for i, sc_col in enumerate((48, 49, 50, 51)):
            em.op("ve", lambda e, i=i, sc_col=sc_col: e.tensor_copy(
                vecs[:, sc_col:sc_col + 1], ps_lo[:, 256 + i:257 + i]))
        qm['fbc'] = em.inc_last("ve", "s_q")
    qitems.append(qi_fin)

    for g in range(64):
        iscale = vecs[:, 49:50] if g < FC else vecs[:, 51:52]
        mscale = vecs[:, 48:49] if g < FC else vecs[:, 50:51]

        def qi_load(em, g=g):
            em.wait("gp", "s_q",
                    qm[('wq', g - 2)] if g >= 2 else qm[('red', {0: 62, 1: 63}[g % 2])])
            em.dma_fence("gp", "s_gd")
            qm[('ld', g)] = em.op("gp", lambda e, g=g: e.dma_start(
                qload[:, (g % 2) * D:(g % 2 + 1) * D], gran_src(g)), inc="s_gd", k=16)

        def qi_q(em, g=g, iscale=iscale):
            em.wait("ve", "s_gd", qm[('ld', g)])
            if g == 0:
                em.wait("ve", "s_q", qm['fbc'])
            else:
                em.wait("ve", "s_sc", qm[('sgn', g - 1)])
            qm[('qq', g)] = em.op("ve", lambda e, g=g, iscale=iscale: e.tensor_scalar(
                qr[:], qload[:, (g % 2) * D:(g % 2 + 1) * D], iscale, 127.0,
                op0=ALU.mult, op1=ALU.mult), inc="s_q")

        def qi_abs(em, g=g):
            em.wait("sc", "s_q", qm[('qq', g)])
            em.op("sc", lambda e: e.activation(qa[:], qr[:], AF.Abs, bias=Z))
            qm[('sgn', g)] = em.op("sc", lambda e, g=g: e.activation(
                qsgn[:, (g % 2) * D:(g % 2 + 1) * D], qr[:], AF.Sign, bias=Z), inc="s_sc")

        def qi_b0(em, g=g):
            em.wait("ve", "s_sc", qm[('sgn', g)])
            em.op("ve", lambda e: e.tensor_scalar(qt[:], qa[:], 63.5, 64.0,
                                                  op0=ALU.is_ge, op1=ALU.mult))

        def qi_b0b(em, g=g):
            em.op("ve", lambda e: e.tensor_tensor(qr[:], qa[:], qt[:], op=ALU.subtract))

        def mk_bit(j):
            def qi_bj(em, j=j):
                em.op("ve", lambda e, j=j: e.tensor_scalar(
                    qt[:], qr[:], j - 0.5, float(j), op0=ALU.is_ge, op1=ALU.mult))
            def qi_bjs(em):
                em.op("ve", lambda e: e.tensor_tensor(qr[:], qr[:], qt[:], op=ALU.subtract))
            return [qi_bj, qi_bjs]

        def qi_rnd(em, g=g):
            em.op("ve", lambda e: e.tensor_tensor(qt[:], qa[:], qr[:], op=ALU.subtract))

        def qi_sca(em, g=g, mscale=mscale):
            em.op("ve", lambda e, mscale=mscale: e.tensor_scalar(
                qr[:], qt[:], mscale, 1.0 / 127.0, op0=ALU.mult, op1=ALU.mult))

        def qi_sgm(em, g=g):
            qm[('wq', g)] = em.op("ve", lambda e, g=g: e.tensor_tensor(
                qb[:, (g % 2) * D:(g % 2 + 1) * D], qr[:],
                qsgn[:, (g % 2) * D:(g % 2 + 1) * D], op=ALU.mult), inc="s_q")

        def qi_tr(em, g=g):
            em.wait("te", "s_q", qm[('wq', g)])
            em.wait("te", "s_q", qm.get(('ev', g - 1), 0))
            for k in range(KC):
                em.op("te", lambda e, g=g, k=k: e.transpose(
                    ps_x[:, k * 128:(k + 1) * 128],
                    qb[:, (g % 2) * D + k * 128:(g % 2) * D + (k + 1) * 128], id_b[:]))
            qm[('tr', g)] = em.inc_last("te", "s_qt")

        def qi_ev(em, g=g):
            em.wait("ve", "s_qt", qm[('tr', g)])
            if g >= 2:
                em.wait("ve", "s_gd", qm[('out', g - 2)])
            qm[('ev', g)] = em.op("ve", lambda e, g=g: e.tensor_copy(
                qstage[:, (g % 2) * D:(g % 2 + 1) * D], ps_x[:]), inc="s_q")

        def qi_out(em, g=g):
            em.wait("gp", "s_q", qm[('ev', g)])
            em.dma_fence("gp", "s_gd")
            if g < FC:
                dst = wqfc_h.ap()[g, :, :]
            else:
                gg = g - FC
                r, qf = gg // 4, gg % 4
                dst = wqpr_h.ap()[r, :, qf * D:(qf + 1) * D]
            qm[('out', g)] = em.op("gp", lambda e, g=g, dst=dst: e.dma_start(
                dst, qstage[:, (g % 2) * D:(g % 2 + 1) * D]), inc="s_gd", k=16)

        bit_items = []
        for j in (32, 16, 8, 4, 2, 1):
            bit_items += mk_bit(j)
        qitems += ([qi_load, qi_q, qi_abs, qi_b0, qi_b0b] + bit_items +
                   [qi_rnd, qi_sca, qi_sgm, qi_tr, qi_ev, qi_out])

    qpos = [0]

    def drain_quant(n):
        while n > 0 and qpos[0] < len(qitems):
            qitems[qpos[0]](em)
            qpos[0] += 1
            n -= 1

    base_dv = em.cnt["s_dv"]
    em.op("ve", lambda e: e.tensor_copy(y_fm[:, 0:8], v_st[:, 0:8]), inc="s_dv")
    em.op("ve", lambda e: e.tensor_copy(y_fm[:, 8:16], v_st[:, 8:16]), inc="s_dv")

    flush_ev = {}
    ydma_mark = {}

    def emit_flush(B):
        buf = B % 2
        em.wait("te", "s_dv", base_dv + 2 * (B + 1) * 128)
        em.wait("te", "s_fl", flush_ev.get(B - 1, 0))
        em.wait("te", pbf_free[0][0], pbf_free[0][1])
        em.wait("te", pbf_free[1][0], pbf_free[1][1])
        for b0 in range(NB):
            bank = b0
            for c in range(KC):
                st = buf * 2048 + c * 2 + b0
                em.op("te", lambda e, bank=bank, c=c, st=st: e.transpose(
                    ps_bf[:, bank * 1024 + c * 128:bank * 1024 + (c + 1) * 128],
                    y_fm[:, st:st + 16 * 127 + 1:16], id_b[:]))
            mtr = em.inc_last("te", "s_tr")
            em.wait("sc", "s_tr", mtr)
            em.wait("sc", "s_ydma", ydma_mark.get(B - 1, 0))
            mev = em.op("sc", lambda e, b0=b0, bank=bank: e.copy(
                y_tm[:, b0 * 1024:(b0 + 1) * 1024],
                ps_bf[:, bank * 1024:(bank + 1) * 1024]), inc="s_ev", drain=True)
            ytm = y_tm[:, b0 * 1024:(b0 + 1) * 1024]
            em.wait("ve", "s_ev", mev)
            em.op("ve", lambda e, ytm=ytm: e.reduce_sum(stat[:, 8:9], ytm, axis=AX.X))
            msq = em.op("sc", lambda e, ytm=ytm: e.activation(
                sq_dump[:], ytm, AF.Square, bias=Z, accum_out=stat[:, 9:10]), inc="s_sc")
            em.op("ve", lambda e: e.tensor_scalar(stat[:, 10:11], stat[:, 8:9], 1.0 / D,
                                                  None, op0=ALU.mult), drain=False)
            em.wait("ve", "s_sc", msq)
            em.op("ve", lambda e: e.tensor_scalar(stat[:, 12:13], stat[:, 9:10], 1.0 / D,
                                                  EPS, op0=ALU.mult, op1=ALU.add))
            em.op("ve", lambda e: e.tensor_tensor(stat[:, 13:14], stat[:, 10:11],
                                                  stat[:, 10:11], op=ALU.mult))
            mdv = em.op("ve", lambda e: e.tensor_tensor(stat[:, 12:13], stat[:, 12:13],
                                                        stat[:, 13:14], op=ALU.subtract),
                        inc="s_fl", drain=True)
            em.wait("sc", "s_fl", mdv)
            msq = em.op("sc", lambda e: e.activation(stat[:, 13:14], stat[:, 12:13],
                                                     AF.Sqrt, bias=Z), inc="s_sc")
            em.wait("ve", "s_sc", msq)
            em.op("ve", lambda e: e.reciprocal(stat[:, 11:12], stat[:, 13:14]))
            mdv = em.op("ve", lambda e: e.tensor_scalar(
                stat[:, 14:15], stat[:, 10:11], stat[:, 11:12], -1.0,
                op0=ALU.mult, op1=ALU.mult), inc="s_fl")
            em.wait("sc", "s_fl", mdv)
            mno = em.op("sc", lambda e, b0=b0, ytm=ytm: e.activation(
                y_tm[:, b0 * 1024:(b0 + 1) * 1024], ytm, AF.Identity,
                scale=stat[:, 11:12], bias=stat[:, 14:15]), inc="s_sc", drain=False)
            em.wait("te", "s_sc", mno)
            for k in range(KC):
                em.op("te", lambda e, bank=bank, b0=b0, k=k: e.transpose(
                    ps_bf[:, bank * 1024 + k * 128:bank * 1024 + (k + 1) * 128],
                    y_tm[:, b0 * 1024 + k * 128:b0 * 1024 + (k + 1) * 128], id_b[:]))
            mtr = em.inc_last("te", "s_tr")
            em.wait("ve", "s_tr", mtr)
            em.wait("ve", "s_ydma", em.cnt["s_ydma"])
            mev = em.op("ve", lambda e, bank=bank: e.tensor_copy(
                ystage[:], ps_bf[:, bank * 1024:(bank + 1) * 1024]),
                inc="s_fl", drain=False)
            pbf_free[bank] = ("s_fl", mev)
            em.wait("sy", "s_fl", mev)
            em.dma_fence("sy", "s_ydma")
            dstv = ynT_h.ap()[b0, :, :, B * 128:(B + 1) * 128].rearrange("k p i -> p k i")
            srcv = ystage[:].rearrange("p (k i) -> p k i", k=KC)
            em.op("sy", lambda e, dstv=dstv, srcv=srcv: e.dma_start(dstv, srcv),
                  inc="s_ydma", k=16)
        flush_ev[B] = mev
        ydma_mark[B] = em.cnt["s_ydma"]

    m_mm_h = {0: 0, 1: 0}
    for t in range(1, L):
        B = t // 128
        buf = B % 2
        pbuf = ((t - 1) // 128) % 2
        par = t % 2
        po = t % 128
        ppo = (t - 1) % 128
        if po == 0:
            emit_flush(B - 1)
            assert p1pos[0] >= p1_need.get(B, 0), (B, p1pos[0])
        em.wait("te", "s_dv", base_dv + 2 * t)
        prev0 = pbuf * 2048 + ppo * 16
        for e_ in range(KC):
            hb = (e_ // 4) * 512
            fn0 = lambda e, e_=e_, hb=hb, prev0=prev0: e.matmul(
                ps_sc[:, hb + (e_ % 4) * 2:hb + (e_ % 4) * 2 + 2],
                wrT[:, 0 * D + e_ * 128:0 * D + (e_ + 1) * 128],
                y_fm[:, prev0 + 0:prev0 + 2],
                start=(e_ % 4 == 0), stop=False)
            em.op("te", fn0)
            for k in range(1, KC):
                fn = lambda e, e_=e_, k=k, hb=hb, prev0=prev0: e.matmul(
                    ps_sc[:, hb + (e_ % 4) * 2:hb + (e_ % 4) * 2 + 2],
                    wrT[:, k * D + e_ * 128:k * D + (e_ + 1) * 128],
                    y_fm[:, prev0 + k * 2:prev0 + k * 2 + 2],
                    start=False, stop=(k == KC - 1 and e_ % 4 == 3))
                if k == KC - 1 and e_ in (3, KC - 1):
                    m_mm_h[e_ // 4] = em.op("te", fn, inc="s_mm")
                else:
                    em.op("te", fn)
        if po == 0:
            em.wait("ve", "s_fl", flush_ev.get(B - 2, 0))
        cur0 = buf * 2048 + po * 16
        uv0 = (B % 4) * 2048 + po * 16
        em.wait("ve", "s_mm", m_mm_h[0])
        em.op("ve", lambda e, uv0=uv0: e.tensor_tensor(
            scan_tmp[:, 0:8], ps_sc[:, 0:8],
            u_st[:, uv0:uv0 + 8], op=ALU.mult))
        if t == 1:
            em.op("ve", lambda e, prev0=prev0: e.tensor_tensor(
                scan_tmp[:, 32:48], y_fm[:, prev0:prev0 + 16], dec_bc[:], op=ALU.mult))
        em.op("ve", lambda e, uv0=uv0: e.tensor_tensor(
            scan_tmp[:, 16:24], v_st[:, uv0:uv0 + 8], scan_tmp[:, 0:8],
            op=ALU.subtract))
        em.op("ve", lambda e, cur0=cur0: e.tensor_tensor(
            y_fm[:, cur0:cur0 + 8], scan_tmp[:, 32:40], scan_tmp[:, 16:24],
            op=ALU.add), inc="s_dv", drain=False)
        em.wait("ve", "s_mm", m_mm_h[1])
        em.op("ve", lambda e, uv0=uv0: e.tensor_tensor(
            scan_tmp[:, 8:16], ps_sc[:, 512:520],
            u_st[:, uv0 + 8:uv0 + 16], op=ALU.mult))
        em.op("ve", lambda e, uv0=uv0: e.tensor_tensor(
            scan_tmp[:, 24:32], v_st[:, uv0 + 8:uv0 + 16], scan_tmp[:, 8:16],
            op=ALU.subtract))
        em.op("ve", lambda e, cur0=cur0: e.tensor_tensor(
            y_fm[:, cur0 + 8:cur0 + 16], scan_tmp[:, 40:48], scan_tmp[:, 24:32],
            op=ALU.add), inc="s_dv")
        em.op("ve", lambda e, cur0=cur0: e.tensor_tensor(
            scan_tmp[:, 32:48], y_fm[:, cur0:cur0 + 16], dec_bc[:], op=ALU.mult),
            drain=False)
        if t >= 2 and t % 3 != 0:
            drain_quant(1)
        if t % 2 == 0:
            drain_p1(t, 1)
    emit_flush(NCH - 1)
    drain_quant(10 ** 9)
    m_wq_all = em.cnt["s_gd"]
    m_y_all = em.cnt["s_ydma"]

    em.barrier()
    em.realize()
    for c in reversed(cm_p2):
        c.__exit__(None, None, None)
    cm_wgT.__exit__(None, None, None)
    cm_wrT.__exit__(None, None, None)
    if max_phase <= 2:
        return _close_truncated(nc, em)

    cm_p3 = [nc.sbuf_tensor("actT", [128, KC * NT3], BF16),
             nc.sbuf_tensor("ynT", [128, 2 * KC * NT3], BF16),
             nc.sbuf_tensor("att_tm", [128, 4 * D], BF16),
             nc.sbuf_tensor("att_fm", [128, KC * NT3], BF16),
             nc.sbuf_tensor("h3", [128, FC * NT3], BF16),
             nc.sbuf_tensor("wqf_sb", [128, 2 * 2 * KC * 128], BF16),
             nc.sbuf_tensor("wqp_sb", [128, 2 * FC * 128], BF16)]
    (actT, ynT, att_tm, att_fm, h3, wqf_sb, wqp_sb) = [c.__enter__() for c in cm_p3]

    out_mark = [0, 0, 0, 0]
    yin_mark = [0]
    mm_ev = {0: 0, 1: 0}
    wqf_mark = {}
    wqp_mark = {}
    wqf_free = {0: 0, 1: 0}
    wqp_free = {0: 0, 1: 0}
    first_wq = [True]

    def emit_wqf_load(jg):
        pz = jg % 2
        if first_wq[0]:
            em.wait("gp", "s_gd", m_wq_all)
            em.wait("gp", "s_ydma", m_y_all)
            first_wq[0] = False
        em.wait("gp", "s_mm", wqf_free[pz])
        em.dma_fence("gp", "s_gd")
        dstv = wqf_sb[:, pz * 2 * KC * 128:(pz + 1) * 2 * KC * 128].rearrange(
            "p (j c) -> p j c", j=2)
        srcv = wqfc_h.ap()[jg * 2:(jg + 1) * 2, :, :].rearrange("j p c -> p j c")
        wqf_mark[jg] = em.op("gp", lambda e, dstv=dstv, srcv=srcv: e.dma_start(
            dstv, srcv), inc="s_gd", k=16)

    def emit_wqp_load(m):
        pz = m % 2
        em.wait("gp", "s_mm", wqp_free[pz])
        em.dma_fence("gp", "s_gd")
        wqp_mark[m] = em.op("gp", lambda e, m=m, pz=pz: e.dma_start(
            wqp_sb[:, pz * FC * 128:(pz + 1) * FC * 128], wqpr_h.ap()[m, :, :]),
            inc="s_gd", k=16)

    em.wait("sc", "s_ydma", m_y_all)
    yn_mark = {}

    def emit_ynT_load(bb, tt0, par):
        em.wait("sc", "s_mm", yin_mark[0])
        em.dma_fence("sc", "s_scd")
        dstv = ynT[:, par * KC * NT3:(par + 1) * KC * NT3].rearrange(
            "p (k t) -> p k t", k=KC)
        srcv = ynT_h.ap()[bb, :, :, tt0:tt0 + NT3].rearrange("k p t -> p k t")
        return em.op("sc", lambda e, dstv=dstv, srcv=srcv: e.dma_start(dstv, srcv),
                     inc="s_scd", k=16)

    for b in range(NB):
        em.wait("sc", "s_dv", em.cnt["s_dv"])
        em.wait("te", "s_dv", em.cnt["s_dv"])
        mrep3 = em.cnt["s_sc"]
        mrep3 = build_rep(0, b, 4 * D, ONE, mrep3)
        mrep3 = build_rep(1, b, 3 * D, Z, mrep3)
        em.wait("ve", "s_sc", mrep3)
        em.wait("te", "s_sc", mrep3)
        for gi, goff in ((0, 2 * D), (2, 5 * D)):
            for k in range(KC):
                em.op("te", lambda e, b=b, goff=goff, k=k: e.transpose(
                    ps_sc[:, k:k + 1],
                    mod_sb[32 * b:32 * b + 1, goff + k * 128:goff + (k + 1) * 128],
                    ones1[32 * b:32 * b + 1, 0:1]))
            mtr = em.inc_last("te", "s_tr")
            em.wait("ve", "s_tr", mtr)
            em.op("ve", lambda e, gi=gi: e.tensor_copy(
                gfm[:, gi * 8:(gi + 1) * 8], ps_sc[:, 0:8]))
            bcol = vecs[:, 32:40] if gi == 0 else vecs[:, 40:48]
            mgf = em.op("ve", lambda e, gi=gi, bcol=bcol: e.tensor_tensor(
                gfm[:, (gi + 1) * 8:(gi + 2) * 8], gfm[:, gi * 8:(gi + 1) * 8],
                bcol, op=ALU.mult), inc="s_dv")
            em.wait("te", "s_dv", mgf)
            em.wait("sc", "s_dv", mgf)
        for t3 in range(NTIL3):
            t0 = t3 * NT3
            mxl = [0] * NS3
            myl = [0] * NS3
            for s in range(NS3):
                em.wait("sy", "s_out", out_mark[s])
                em.dma_fence("sy", "s_ld")
                mxl[s] = em.op("sy", lambda e, b=b, s=s, t0=t0: e.dma_start(
                    x_sb[:, s * D:(s + 1) * D],
                    x_d.ap()[b, t0 + 128 * s:t0 + 128 * (s + 1), :]), inc="s_ld", k=16)
            ti = b * NTIL3 + t3
            pyn = ti % 2
            if ti == 0:
                yn_mark[0] = emit_ynT_load(0, 0, 0)
            if ti + 1 < NB * NTIL3:
                bn, tn = divmod(ti + 1, NTIL3)
                yn_mark[(ti + 1) % 2] = emit_ynT_load(bn, tn * NT3, (ti + 1) % 2)
            myn = yn_mark[pyn]
            emit_wqf_load(0)
            emit_wqf_load(1)
            m_att = 0
            em.wait("te", "s_scd", myn)
            ev_c = [0] * KC
            for c in range(KC):
                if c % 4 == 0:
                    em.wait("te", "s_sc", em.cnt["s_sc"])
                pb = (ps_sc, 0) if c % 4 == 0 else \
                     (ps_sc, 512) if c % 4 == 1 else \
                     (ps_mm, 0) if c % 4 == 2 else (ps_mm, 512)
                for k in range(KC):
                    em.op("te", lambda e, c=c, k=k, pb=pb, pyn=pyn: e.matmul(
                        pb[0][:, pb[1]:pb[1] + NT3],
                        woT[:, k * D + c * 128:k * D + (c + 1) * 128],
                        ynT[:, pyn * KC * NT3 + k * NT3:pyn * KC * NT3 + (k + 1) * NT3],
                        start=(k == 0), stop=(k == KC - 1)))
                m_mm = em.inc_last("te", "s_mm")
                em.wait("sc", "s_mm", m_mm)
                ev_c[c] = em.op("sc", lambda e, c=c, pb=pb: e.activation(
                    att_fm[:, c * NT3:(c + 1) * NT3], pb[0][:, pb[1]:pb[1] + NT3],
                    AF.Identity, scale=gfm[:, c:c + 1], bias=gfm[:, 8 + c:9 + c]),
                    inc="s_sc")
            yn_used = em.cnt["s_mm"]
            mxp = [0] * NS3
            m_act2 = 0
            for s4 in range(NS3):
                pz = s4 % 2
                em.wait("te", "s_sc", ev_c[KC - 1])
                em.wait("te", pbf_free[pz][0], pbf_free[pz][1])
                for c in range(KC):
                    em.op("te", lambda e, c=c, s4=s4, pz=pz: e.transpose(
                        ps_bf[:, pz * 1024 + c * 128:pz * 1024 + (c + 1) * 128],
                        att_fm[:, c * NT3 + s4 * 128:c * NT3 + (s4 + 1) * 128], id_b[:]))
                mtr = em.inc_last("te", "s_tr")
                em.wait("ve", "s_tr", mtr)
                m_att = em.op("ve", lambda e, s4=s4, pz=pz: e.tensor_copy(
                    att_tm[:, s4 * D:(s4 + 1) * D],
                    ps_bf[:, pz * 1024:(pz + 1) * 1024]), inc="s_dv", drain=True)
                pbf_free[pz] = ("s_dv", m_att)
                em.wait("ve", "s_ld", mxl[s4])
                mxp[s4] = em.op("ve", lambda e, s4=s4: e.tensor_tensor(
                    x_sb[:, s4 * D:(s4 + 1) * D], x_sb[:, s4 * D:(s4 + 1) * D],
                    att_tm[:, s4 * D:(s4 + 1) * D], op=ALU.add), inc="s_dv")
                _, mh = emit_ln_modulate(
                    x_sb[:, s4 * D:(s4 + 1) * D], s4 % 2, rep[:, 0:D], rep[:, D:2 * D],
                    None, None, mxp[s4], "s_dv")
                m_act2 = emit_act_transpose(s4 % 2, mh, actT, s4 * 128)
            yin_mark[0] = yn_used
            em.wait("te", "s_dv", m_act2)
            for k in range(KC):
                em.op("te", lambda e, k=k: e.matmul(
                    ps_lo[0:R, 0:NT3], afcT[:, k * R:(k + 1) * R], actT_full(actT, k),
                    start=(k == 0), stop=(k == KC - 1)))
            m_mm = em.inc_last("te", "s_mm")
            em.wait("ve", "s_mm", m_mm)
            m_s2 = em.op("ve", lambda e: e.tensor_scalar(
                s2[0:R, :], ps_lo[0:R, 0:NT3], 2.0, None, op0=ALU.mult), inc="s_dv")
            em.wait("te", "s_dv", m_s2)
            for j in range(FC):
                jg = j // 2
                pz4 = j % 4
                pg = jg % 2
                if j % 2 == 0:
                    em.wait("te", "s_gd", wqf_mark[jg])
                if j % 4 == 0:
                    em.wait("te", "s_sc", em.cnt["s_sc"])
                pb = (ps_mm, 0) if pz4 == 0 else \
                     (ps_mm, 512) if pz4 == 1 else \
                     (ps_sc, 0) if pz4 == 2 else (ps_sc, 512)
                base_w = pg * 2 * KC * 128 + (j % 2) * KC * 128
                for k in range(KC):
                    em.op("te", lambda e, k=k, pb=pb, base_w=base_w: e.matmul(
                        pb[0][:, pb[1]:pb[1] + NT3],
                        wqf_sb[:, base_w + k * 128:base_w + (k + 1) * 128],
                        actT_full(actT, k), start=(k == 0), stop=False))
                m_mm = em.op("te", lambda e, j=j, pb=pb: e.matmul(
                    pb[0][:, pb[1]:pb[1] + NT3],
                    bfcT[0:R, j * 128:(j + 1) * 128], s2[0:R, :],
                    start=False, stop=True), inc="s_mm")
                if j % 2 == 1:
                    wqf_free[pg] = m_mm
                em.wait("sc", "s_mm", m_mm)
                em.op("sc", lambda e, j=j, pb=pb: e.activation(
                    h3[:, j * NT3:(j + 1) * NT3], pb[0][:, pb[1]:pb[1] + NT3],
                    AF.Gelu, bias=bfc_fm[:, j:j + 1]), inc="s_sc")
                if j % 2 == 1 and jg + 2 < FC // 2:
                    emit_wqf_load(jg + 2)
            emit_wqp_load(0)
            emit_wqp_load(1)
            em.wait("te", "s_sc", em.cnt["s_sc"])
            for j in range(FC):
                em.op("te", lambda e, j=j: e.matmul(
                    ps_lo[0:R, 0:NT3], aprT[:, j * R:(j + 1) * R],
                    h3[:, j * NT3:(j + 1) * NT3],
                    start=(j == 0), stop=(j == FC - 1)))
            m_mm = em.inc_last("te", "s_mm")
            em.wait("ve", "s_mm", m_mm)
            m_s2p = em.op("ve", lambda e: e.tensor_scalar(
                s2[0:R, :], ps_lo[0:R, 0:NT3], 2.0, None, op0=ALU.mult), inc="s_dv")
            m_proj = 0
            em.wait("te", "s_dv", m_s2p)
            ev_m = [0] * KC
            for m_ in range(KC):
                pz4 = m_ % 4
                em.wait("te", "s_gd", wqp_mark[m_])
                if m_ % 4 == 0:
                    em.wait("te", "s_sc", em.cnt["s_sc"])
                pb = (ps_mm, 0) if pz4 == 0 else \
                     (ps_mm, 512) if pz4 == 1 else \
                     (ps_sc, 0) if pz4 == 2 else (ps_sc, 512)
                pzb = m_ % 2
                for k in range(FC):
                    em.op("te", lambda e, m_=m_, k=k, pb=pb, pzb=pzb: e.matmul(
                        pb[0][:, pb[1]:pb[1] + NT3],
                        wqp_sb[:, pzb * FC * 128 + k * 128:pzb * FC * 128 + (k + 1) * 128],
                        h3[:, k * NT3:(k + 1) * NT3], start=(k == 0), stop=False))
                m_mm = em.op("te", lambda e, m_=m_, pb=pb: e.matmul(
                    pb[0][:, pb[1]:pb[1] + NT3],
                    bprT[0:R, m_ * 128:(m_ + 1) * 128], s2[0:R, :],
                    start=False, stop=True), inc="s_mm")
                wqp_free[pzb] = m_mm
                em.wait("sc", "s_mm", m_mm)
                ev_m[m_] = em.op("sc", lambda e, m_=m_, pb=pb: e.activation(
                    att_fm[:, m_ * NT3:(m_ + 1) * NT3], pb[0][:, pb[1]:pb[1] + NT3],
                    AF.Identity, scale=gfm[:, 16 + m_:17 + m_],
                    bias=gfm[:, 24 + m_:25 + m_]), inc="s_sc")
                if m_ + 2 < KC:
                    emit_wqp_load(m_ + 2)
            for s4 in range(NS3):
                pz = s4 % 2
                em.wait("te", "s_sc", ev_m[KC - 1])
                em.wait("te", pbf_free[pz][0], pbf_free[pz][1])
                for m_ in range(KC):
                    em.op("te", lambda e, m_=m_, s4=s4, pz=pz: e.transpose(
                        ps_bf[:, pz * 1024 + m_ * 128:pz * 1024 + (m_ + 1) * 128],
                        att_fm[:, m_ * NT3 + s4 * 128:m_ * NT3 + (s4 + 1) * 128], id_b[:]))
                mtr = em.inc_last("te", "s_tr")
                em.wait("ve", "s_tr", mtr)
                m_proj = em.op("ve", lambda e, s4=s4, pz=pz: e.tensor_copy(
                    att_tm[:, s4 * D:(s4 + 1) * D],
                    ps_bf[:, pz * 1024:(pz + 1) * 1024]), inc="s_dv", drain=True)
                pbf_free[pz] = ("s_dv", m_proj)
                mo = em.op("ve", lambda e, s4=s4: e.tensor_tensor(
                    x_sb[:, s4 * D:(s4 + 1) * D], x_sb[:, s4 * D:(s4 + 1) * D],
                    att_tm[:, s4 * D:(s4 + 1) * D], op=ALU.add), inc="s_dv")
                em.wait("sy", "s_dv", mo)
                em.dma_fence("sy", "s_out")
                out_mark[s4] = em.op("sy", lambda e, b=b, s4=s4, t0=t0: e.dma_start(
                    out_d.ap()[b, t0 + 128 * s4:t0 + 128 * (s4 + 1), :],
                    x_sb[:, s4 * D:(s4 + 1) * D]), inc="s_out", k=16)
    em.wait("sy", "s_out", em.cnt["s_out"])
    em.realize()
    for c in reversed(cm_p3):
        c.__exit__(None, None, None)
    return nc


def _slice_inputs(inputs, core):
    b0 = NB * core
    m = {
        "x": np.ascontiguousarray(inputs["L0_attn_x"][b0:b0 + NB]),
        "cvec": np.ascontiguousarray(inputs["L0_c"][b0:b0 + NB]),
        "wmod": inputs["L0_wmod"],
        "decay": inputs["L0_attn_decay"],
        "Wr": inputs["L0_attn_Wr"], "br": inputs["L0_attn_br"],
        "Wg": inputs["L0_attn_Wg"], "bg": inputs["L0_attn_bg"],
        "Wo": inputs["L0_attn_Wo"], "bo": inputs["L0_attn_bo"],
        "ln_w": inputs["L0_attn_ln_w"], "ln_b": inputs["L0_attn_ln_b"],
        "W_fc": inputs["L0_ffn_W_fc"], "b_fc": inputs["L0_ffn_b_fc"],
        "A_fc": inputs["L0_ffn_A_fc"], "B_fc": inputs["L0_ffn_B_fc"],
        "W_proj": inputs["L0_ffn_W_proj"], "b_proj": inputs["L0_ffn_b_proj"],
        "A_proj": inputs["L0_ffn_A_proj"], "B_proj": inputs["L0_ffn_B_proj"],
    }
    return {k: np.ascontiguousarray(v) for k, v in m.items()}


def kernel(**inputs):
    nc = build_program(L=2048)
    in_maps = [_slice_inputs(inputs, i) for i in range(8)]
    res = run_bass_kernel_spmd(nc, in_maps, core_ids=list(range(8)))
    return np.concatenate([np.asarray(r["out"]) for r in res.results], axis=0)


# version 18
# speedup vs baseline: 1.0337x; 1.0337x over previous
import numpy as np
import concourse.bass as bass
import concourse.mybir as mybir
from concourse import library_config
from concourse.bass_utils import run_bass_kernel_spmd

F32 = mybir.dt.float32
BF16 = mybir.dt.bfloat16
AF = mybir.ActivationFunctionType
ALU = mybir.AluOpType
AX = mybir.AxisListType

D = 1024
FF = 4096
NB = 2
KC = D // 128
FC = FF // 128
R = 16
SCL = 2.0
EPS = 1e-5
ENGS = ("sy", "gp", "ve", "sc", "te")


class EM:

    def __init__(self, nc, sems):
        self.nc = nc
        self.sems = sems
        self.cnt = {k: 0 for k in sems}
        self.lists = {e: [] for e in ENGS}

    def op(self, eng, fn, inc=None, k=1, drain=None):
        incs = []
        ret = None
        if inc is not None:
            self.cnt[inc] += k
            incs.append((self.sems[inc], k))
            ret = self.cnt[inc]
        self.lists[eng].append(("op", fn, incs))
        if drain is None:
            drain = eng == "ve"
        if drain:
            self.lists[eng].append(("drain",))
        return ret

    def drain(self, eng):
        self.lists[eng].append(("drain",))

    def inc_last(self, eng, inc, k=1):
        for entry in reversed(self.lists[eng]):
            if entry[0] == "op":
                if not entry[2]:
                    self.cnt[inc] += k
                    entry[2].append((self.sems[inc], k))
                    return self.cnt[inc]
                self.lists[eng].append(("drain",))
                return self.op(eng, lambda e: e.nop(), inc=inc, k=k)
        return self.op(eng, lambda e: e.nop(), inc=inc, k=k)

    def dma_fence(self, eng, sem_name):
        self.wait(eng, sem_name, self.cnt[sem_name])

    def wait(self, eng, sem_name, val):
        if val < 0:
            return
        self.lists[eng].append(("wait", self.sems[sem_name], val))

    def barrier(self):
        for e in ENGS:
            if e in ("sy", "gp"):
                self.op(e, lambda x: x.nop(), inc="s_ph")
            else:
                self.inc_last(e, "s_ph")
        tgt = self.cnt["s_ph"]
        for e in ENGS:
            self.wait(e, "s_ph", tgt)

    @staticmethod
    def _replay(eng, entries):
        for entry in entries:
            if entry[0] == "op":
                instr = entry[1](eng)
                for sem, k in entry[2]:
                    instr.then_inc(sem, k)
            elif entry[0] == "wait":
                eng.wait_ge(entry[1], entry[2])
            else:
                eng.drain()

    def realize(self):
        with self.nc.Block() as blk:
            for e, meth in (("sy", blk.sync), ("gp", blk.gpsimd), ("ve", blk.vector),
                            ("sc", blk.scalar), ("te", blk.tensor)):
                entries = self.lists[e]
                if entries:
                    meth(lambda eng, entries=entries: EM._replay(eng, entries))
        self.lists = {e: [] for e in ENGS}



def _close_truncated(nc, em):
    with nc.Block() as blk:
        def _w(eng):
            for name in ("s_ld", "s_gd", "s_scd", "s_uv", "s_ydma", "s_out"):
                if em.cnt[name] > 0:
                    eng.wait_ge(em.sems[name], em.cnt[name])
        blk.sync(_w)
    return nc

def build_program(L=2048, max_phase=4):
    NCH = L // 128
    NT1 = 256
    NT3 = min(512, L)
    NS3 = NT3 // 128
    NTIL1 = L // NT1
    NTIL3 = L // NT3
    nc = bass.Bass("TRN2", target_bir_lowering=False)

    x_d = nc.dram_tensor("x", [NB, L, D], F32, kind="ExternalInput")
    c_d = nc.dram_tensor("cvec", [NB, D], F32, kind="ExternalInput")
    wmod_d = nc.dram_tensor("wmod", [D, 6 * D], F32, kind="ExternalInput")
    decay_d = nc.dram_tensor("decay", [D], F32, kind="ExternalInput")
    wr_d = nc.dram_tensor("Wr", [D, D], F32, kind="ExternalInput")
    br_d = nc.dram_tensor("br", [D], F32, kind="ExternalInput")
    wg_d = nc.dram_tensor("Wg", [D, D], F32, kind="ExternalInput")
    bg_d = nc.dram_tensor("bg", [D], F32, kind="ExternalInput")
    wo_d = nc.dram_tensor("Wo", [D, D], F32, kind="ExternalInput")
    bo_d = nc.dram_tensor("bo", [D], F32, kind="ExternalInput")
    lnw_d = nc.dram_tensor("ln_w", [D], F32, kind="ExternalInput")
    lnb_d = nc.dram_tensor("ln_b", [D], F32, kind="ExternalInput")
    wfc_d = nc.dram_tensor("W_fc", [FF, D], F32, kind="ExternalInput")
    bfc_d = nc.dram_tensor("b_fc", [FF], F32, kind="ExternalInput")
    afc_d = nc.dram_tensor("A_fc", [R, D], F32, kind="ExternalInput")
    Bfc_d = nc.dram_tensor("B_fc", [FF, R], F32, kind="ExternalInput")
    wpr_d = nc.dram_tensor("W_proj", [D, FF], F32, kind="ExternalInput")
    bpr_d = nc.dram_tensor("b_proj", [D], F32, kind="ExternalInput")
    apr_d = nc.dram_tensor("A_proj", [R, FF], F32, kind="ExternalInput")
    Bpr_d = nc.dram_tensor("B_proj", [D, R], F32, kind="ExternalInput")
    out_d = nc.dram_tensor("out", [NB, L, D], F32, kind="ExternalOutput")
    ynT_h = nc.dram_tensor("ynT_h", [NB, KC, 128, L], BF16)
    wqfc_h = nc.dram_tensor("wqfc_h", [FC, 128, KC * 128], BF16)
    wqpr_h = nc.dram_tensor("wqpr_h", [KC, 128, FC * 128], BF16)

    A = nc.alloc_sbuf_tensor
    woT = A("woT", [128, KC * D], BF16)
    mod_sb = A("mod_sb", [64, 6 * D], F32)
    rep = A("rep", [128, 4 * D], BF16)
    tm_bf = A("tm_bf", [128, D], BF16)
    vecs = A("vecs", [128, 64], F32)
    dec_bc = A("dec_bc", [128, 16], F32)
    lnb_b = A("lnb_b", [128, 8], BF16)
    gfm = A("gfm", [128, 32], F32)
    bfc_fm = A("bfc_fm", [128, FC], F32)
    id_b = A("id_b", [128, 128], BF16)
    id_f = A("id_f", [128, 128], F32)
    ones1 = A("ones1", [64, 128], F32)
    afcT = A("afcT", [128, KC * R], BF16)
    aprT = A("aprT", [128, FC * R], BF16)
    bfcT = A("bfcT", [R, FF], BF16)
    bprT = A("bprT", [R, D], BF16)
    x_sb = A("x_sb", [128, 4 * D], F32)
    tm_a = A("tm_a", [128, D], F32)
    h_tmb = A("h_tmb", [128, 2 * D], BF16)
    stat = A("stat", [128, 16], F32)
    sq_dump = A("sq_dump", [128, D], F32)
    mx = A("mx", [128, 8], F32)
    siluT = A("siluT", [128, 16], F32)
    scan_tmp = A("scan_tmp", [128, 48], F32)
    s2 = A("s2", [R, NT3], BF16)

    ps_mm = nc.alloc_psum_tensor("ps_mm", [128, 1024], F32)
    ps_sc = nc.alloc_psum_tensor("ps_sc", [128, 1024], F32)
    ps_bf = nc.alloc_psum_tensor("ps_bf", [128, 2048], BF16)
    ps_lo = nc.alloc_psum_tensor("ps_lo", [128, 512], F32)
    ps_x = nc.alloc_psum_tensor("ps_x", [128, 1024], BF16)

    sems = {}
    for name in ["s_ld", "s_gd", "s_scd", "s_mm", "s_dv", "s_sc", "s_gp",
                 "s_uv", "s_p1", "s_ydma", "s_ev", "s_fl", "s_tr", "s_q", "s_qt", "s_out", "s_ph"]:
        sems[name] = nc.alloc_semaphore(name)
    em = EM(nc, sems)

    Z = vecs[:, 52:53]
    ONE = vecs[:, 53:54]

    cm_wrT = nc.sbuf_tensor("wrT", [128, KC * D], BF16)
    wrT = cm_wrT.__enter__()
    cm_wgT = nc.sbuf_tensor("wgT", [128, KC * D], BF16)
    wgT = cm_wgT.__enter__()
    cm_p0 = [nc.sbuf_tensor("wstage", [128, 2 * D], F32),
             nc.sbuf_tensor("wmodf", [128, 6 * D], F32),
             nc.sbuf_tensor("csb", [64, D], F32)]
    wstage, wmodf, csb = [c.__enter__() for c in cm_p0]
    lora_st = wstage

    em.op("ve", lambda e: e.memset(vecs[:], 0.0))
    mz = em.op("ve", lambda e: e.memset(ones1[:], 1.0), inc="s_dv")
    em.wait("gp", "s_dv", mz)
    em.op("gp", lambda e: e.iota(vecs[:, 54:55], pattern=[[0, 1]], base=0,
                                 channel_multiplier=1, allow_small_or_imprecise_dtypes=True))
    mgp = em.op("gp", lambda e: e.iota(id_f[:], pattern=[[1, 128]], base=0, channel_multiplier=0,
                                       allow_small_or_imprecise_dtypes=True), inc="s_gp")
    em.wait("ve", "s_gp", mgp)
    em.op("ve", lambda e: e.tensor_scalar(id_f[:], id_f[:], vecs[:, 54:55], None,
                                          op0=ALU.is_equal), inc="s_dv")
    em.op("ve", lambda e: e.tensor_copy(id_b[:], id_f[:]), inc="s_dv")
    mID = em.op("ve", lambda e: e.memset(vecs[:, 53:54], 1.0), inc="s_dv")
    em.wait("te", "s_dv", mID)
    em.wait("sc", "s_dv", mID)

    def vec_to_fm(rows, dsts):
        mld = 0
        for r, dr in enumerate(rows):
            mld = em.op("sy", lambda e, r=r, dr=dr: e.dma_start(
                csb[32 * r:32 * r + 1, :], dr.ap().rearrange("(o d) -> o d", o=1)),
                inc="s_ld", k=16)
        em.wait("te", "s_ld", mld)
        for r in range(len(rows)):
            for k in range(KC):
                em.op("te", lambda e, r=r, k=k: e.transpose(
                    ps_sc[:, r * 512 + k:r * 512 + k + 1],
                    csb[32 * r:32 * r + 1, k * 128:(k + 1) * 128],
                    ones1[32 * r:32 * r + 1, 0:1]))
        mtr = em.inc_last("te", "s_tr")
        em.wait("ve", "s_tr", mtr)
        for r, dcol in enumerate(dsts):
            em.op("ve", lambda e, r=r, dcol=dcol: e.tensor_copy(
                vecs[:, dcol:dcol + 8], ps_sc[:, r * 512:r * 512 + 8]))
        mdv = em.inc_last("ve", "s_dv")
        em.wait("sy", "s_dv", mdv)
        em.wait("te", "s_dv", mdv)
        return mdv

    mv1 = vec_to_fm([decay_d, br_d], [0, 16])
    vec_to_fm([bg_d, bo_d], [24, 32])
    vec_to_fm([bpr_d], [40])

    em.wait("sc", "s_dv", mv1)
    em.op("sc", lambda e: e.activation(vecs[:, 0:8], vecs[:, 0:8], AF.Sigmoid, bias=Z))
    msc = em.inc_last("sc", "s_sc")
    em.wait("ve", "s_sc", msc)
    em.op("ve", lambda e: e.tensor_scalar(vecs[:, 8:16], vecs[:, 0:8], -1.0, 1.0,
                                          op0=ALU.mult, op1=ALU.add))
    em.op("ve", lambda e: e.tensor_copy(dec_bc[:, 0:16:2], vecs[:, 0:8]))
    em.op("ve", lambda e: e.tensor_copy(dec_bc[:, 1:16:2], vecs[:, 0:8]), inc="s_dv")

    for p in range(4):
        em.dma_fence("sy", "s_ld")
        mld = em.op("sy", lambda e, p=p: e.dma_start(
            csb[0:1, :], bfc_d.ap().rearrange("(o d) -> o d", o=4)[p:p + 1, :]), inc="s_ld", k=16)
        em.wait("te", "s_ld", mld)
        for k in range(KC):
            em.op("te", lambda e, k=k: e.transpose(
                ps_sc[:, k:k + 1], csb[0:1, k * 128:(k + 1) * 128], id_f[0:1, 0:1]))
        mtr = em.inc_last("te", "s_tr")
        em.wait("ve", "s_tr", mtr)
        mdv = em.op("ve", lambda e, p=p: e.tensor_copy(
            bfc_fm[:, p * 8:(p + 1) * 8], ps_sc[:, 0:8]), inc="s_dv")
        em.wait("sy", "s_dv", mdv)
        em.wait("te", "s_dv", mdv)

    vec_to_fm([lnw_d], [55])
    mld = em.op("sy", lambda e: e.dma_start(csb[0:1, :], lnb_d.ap().rearrange("(o d) -> o d", o=1)), inc="s_ld", k=16)
    em.wait("te", "s_ld", mld)
    for k in range(KC):
        em.op("te", lambda e, k=k: e.transpose(
            ps_sc[:, k:k + 1], csb[0:1, k * 128:(k + 1) * 128], ones1[0:1, 0:1]))
    mtr = em.inc_last("te", "s_tr")
    em.wait("ve", "s_tr", mtr)
    em.op("ve", lambda e: e.tensor_copy(stat[:, 0:8], ps_sc[:, 0:8]))
    em.op("ve", lambda e: e.tensor_copy(lnb_b[:], ps_sc[:, 0:8]))
    mdv = em.inc_last("ve", "s_dv")
    em.wait("sy", "s_dv", mdv)
    em.wait("te", "s_dv", mdv)

    prev_ev = 0
    for wd, dstT in ((wg_d, wgT), (wr_d, wrT), (wo_d, woT)):
        for r in range(KC):
            par = r % 2
            em.wait("sy", "s_dv", prev_ev)
            em.dma_fence("sy", "s_ld")
            mld = em.op("sy", lambda e, wd=wd, par=par, r=r: e.dma_start(
                wstage[:, par * D:(par + 1) * D], wd.ap()[128 * r:128 * (r + 1), :]),
                inc="s_ld", k=16)
            em.wait("te", "s_ld", mld)
            em.wait("te", "s_dv", prev_ev)
            for k in range(KC):
                em.op("te", lambda e, par=par, k=k: e.transpose(
                    ps_sc[:, (k % 4) * 128 + (k // 4) * 512:(k % 4) * 128 + (k // 4) * 512 + 128],
                    wstage[:, par * D + k * 128:par * D + (k + 1) * 128], id_f[:]))
            mtr = em.inc_last("te", "s_tr")
            em.wait("ve", "s_tr", mtr)
            for half in range(2):
                dst3 = dstT[:].rearrange("p (k e) -> p k e", k=KC)[:, half * 4:(half + 1) * 4,
                                                                  r * 128:(r + 1) * 128]
                src3 = ps_sc[:, half * 512:(half + 1) * 512].rearrange("p (k i) -> p k i", k=4)
                em.op("ve", lambda e, dst3=dst3, src3=src3: e.tensor_copy(dst3, src3))
            prev_ev = em.inc_last("ve", "s_dv")

    em.wait("te", "s_dv", prev_ev)
    for hh in range(2):
        for k in range(KC):
            em.op("te", lambda e, hh=hh, k=k: e.matmul(
                ps_sc[0:1, hh * 512:(hh + 1) * 512], lnb_b[:, k:k + 1],
                woT[:, k * D + hh * 512:k * D + (hh + 1) * 512],
                start=(k == 0), stop=(k == KC - 1)))
    mq = em.inc_last("te", "s_mm")
    em.wait("ve", "s_mm", mq)
    em.op("ve", lambda e: e.tensor_copy(csb[0:1, 0:512], ps_sc[0:1, 0:512]))
    mdv = em.op("ve", lambda e: e.tensor_copy(csb[0:1, 512:1024], ps_sc[0:1, 512:1024]),
                inc="s_dv")
    em.wait("te", "s_dv", mdv)
    for k in range(KC):
        em.op("te", lambda e, k=k: e.transpose(
            ps_sc[:, k:k + 1], csb[0:1, k * 128:(k + 1) * 128], ones1[0:1, 0:1]))
    mtr = em.inc_last("te", "s_tr")
    em.wait("ve", "s_tr", mtr)
    em.op("ve", lambda e: e.tensor_tensor(vecs[:, 32:40], vecs[:, 32:40], ps_sc[:, 0:8],
                                          op=ALU.add))
    for k in range(KC):
        em.op("ve", lambda e, k=k: e.tensor_scalar(
            woT[:, k * D:(k + 1) * D], woT[:, k * D:(k + 1) * D],
            vecs[:, 55 + k:56 + k], None, op0=ALU.mult), drain=False)
    m_wofold = em.inc_last("ve", "s_dv")
    em.wait("te", "s_dv", m_wofold)
    em.wait("sy", "s_dv", m_wofold)

    em.dma_fence("sy", "s_ld")
    mld = em.op("sy", lambda e: e.dma_start(csb[0:2, :], c_d.ap()), inc="s_ld", k=16)
    em.wait("sc", "s_ld", mld)
    msc = em.op("sc", lambda e: e.activation(tm_a[0:2, :], csb[0:2, :], AF.Silu, bias=vecs[0:2, 52:53]), inc="s_sc")
    em.wait("te", "s_sc", msc)
    em.wait("te", "s_dv", prev_ev)
    for k in range(KC):
        em.op("te", lambda e, k=k: e.transpose(
            ps_sc[:, 2 * k:2 * k + 2], tm_a[0:2, k * 128:(k + 1) * 128], id_f[0:2, 0:2]))
    mtr = em.inc_last("te", "s_tr")
    em.wait("ve", "s_tr", mtr)
    msil = em.op("ve", lambda e: e.tensor_copy(siluT[:], ps_sc[:, 0:16]), inc="s_dv")

    ev_n = {}
    mq_prev = 0
    for kk in range(KC):
        em.wait("gp", "s_mm", mq_prev)
        em.dma_fence("gp", "s_gd")
        mgd = em.op("gp", lambda e, kk=kk: e.dma_start(
            wmodf[:], wmod_d.ap()[128 * kk:128 * (kk + 1), :]), inc="s_gd", k=16)
        em.wait("te", "s_gd", mgd)
        if kk == 0:
            em.wait("te", "s_dv", msil)
        for n in range(12):
            for b in range(2):
                par = (2 * n + b) % 2
                em.wait("te", "s_dv", ev_n.get((kk, 2 * n + b - 2), 0))
                em.op("te", lambda e, par=par, kk=kk, n=n, b=b: e.matmul(
                    ps_sc[0:1, par * 512:(par + 1) * 512],
                    siluT[:, 2 * kk + b:2 * kk + b + 1],
                    wmodf[:, n * 512:(n + 1) * 512],
                    start=True, stop=True))
                mmm = em.inc_last("te", "s_mm")
                mq_prev = mmm
                em.wait("ve", "s_mm", mmm)
                brow = 32 * b
                if kk == 0:
                    ev_n[(kk, 2 * n + b)] = em.op(
                        "ve", lambda e, par=par, n=n, brow=brow: e.tensor_copy(
                            mod_sb[brow:brow + 1, n * 512:(n + 1) * 512],
                            ps_sc[0:1, par * 512:(par + 1) * 512]), inc="s_dv")
                else:
                    ev_n[(kk, 2 * n + b)] = em.op(
                        "ve", lambda e, par=par, n=n, brow=brow: e.tensor_tensor(
                            mod_sb[brow:brow + 1, n * 512:(n + 1) * 512],
                            mod_sb[brow:brow + 1, n * 512:(n + 1) * 512],
                            ps_sc[0:1, par * 512:(par + 1) * 512], op=ALU.add), inc="s_dv")
        ev_n[(kk + 1, -2)] = ev_n[(kk, 22)]
        ev_n[(kk + 1, -1)] = ev_n[(kk, 23)]

    for off in (2 * D, 5 * D):
        for brow in (0, 32):
            em.op("ve", lambda e, off=off, brow=brow: e.tensor_scalar(
                tm_a[brow:brow + 1, :], mod_sb[brow:brow + 1, off:off + D], 0.0, 2.0,
                op0=ALU.is_ge, op1=ALU.mult))
            mdv = em.op("ve", lambda e, brow=brow: e.tensor_scalar(
                tm_a[brow:brow + 1, :], tm_a[brow:brow + 1, :], -1.0, None,
                op0=ALU.add), inc="s_dv")
            em.wait("sc", "s_dv", mdv)
            msc = em.op("sc", lambda e, off=off, brow=brow: e.activation(
                sq_dump[brow:brow + 1, :], mod_sb[brow:brow + 1, off:off + D], AF.Abs,
                bias=vecs[brow:brow + 1, 52:53]), inc="s_sc")
            em.wait("ve", "s_sc", msc)
            em.op("ve", lambda e, brow=brow: e.tensor_scalar(
                sq_dump[brow:brow + 1, :], sq_dump[brow:brow + 1, :], 0.25, None, op0=ALU.max))
            em.op("ve", lambda e, off=off, brow=brow: e.tensor_tensor(
                mod_sb[brow:brow + 1, off:off + D], tm_a[brow:brow + 1, :],
                sq_dump[brow:brow + 1, :], op=ALU.mult), inc="s_dv")

    em.wait("sy", "s_dv", prev_ev)
    em.dma_fence("sy", "s_ld")
    mld = em.op("sy", lambda e: e.dma_start(lora_st[0:R, 0:D], afc_d.ap()), inc="s_ld", k=16)
    em.wait("te", "s_dv", em.cnt["s_dv"])
    em.wait("te", "s_ld", mld)
    for k in range(KC):
        em.op("te", lambda e, k=k: e.transpose(
            ps_sc[:, k * 16:(k + 1) * 16], lora_st[0:R, k * 128:(k + 1) * 128], id_f[0:R, 0:R]))
    mtr = em.inc_last("te", "s_tr")
    em.wait("ve", "s_tr", mtr)
    mdv = em.op("ve", lambda e: e.tensor_copy(afcT[:], ps_sc[:, 0:KC * R]), inc="s_dv")
    em.wait("sy", "s_dv", mdv)
    em.wait("te", "s_dv", mdv)
    for hh in range(2):
        em.dma_fence("sy", "s_ld")
        mld = em.op("sy", lambda e, hh=hh: e.dma_start(
            lora_st[0:R, 0:2 * D], apr_d.ap()[:, hh * 2 * D:(hh + 1) * 2 * D]),
            inc="s_ld", k=16)
        em.wait("te", "s_ld", mld)
        for k in range(16):
            em.op("te", lambda e, k=k: e.transpose(
                ps_sc[:, k * 16:(k + 1) * 16], lora_st[0:R, k * 128:(k + 1) * 128],
                id_f[0:R, 0:R]))
        mtr = em.inc_last("te", "s_tr")
        em.wait("ve", "s_tr", mtr)
        mdv = em.op("ve", lambda e, hh=hh: e.tensor_copy(
            aprT[:, hh * 16 * R:(hh + 1) * 16 * R], ps_sc[:, 0:16 * R]), inc="s_dv")
        em.wait("te", "s_dv", mdv)
        em.wait("sy", "s_dv", mdv)
    em.wait("gp", "s_dv", mdv)
    for src_d, dstT, nt in ((Bfc_d, bfcT, FC), (Bpr_d, bprT, KC)):
        for g4 in range(nt // 4):
            em.dma_fence("gp", "s_gd")
            em.wait("gp", "s_dv", em.cnt["s_dv"])
            mgd = 0
            for jj in range(4):
                j = g4 * 4 + jj
                mgd = em.op("gp", lambda e, src_d=src_d, j=j, jj=jj: e.dma_start(
                    wstage[:, jj * 16:(jj + 1) * 16],
                    src_d.ap()[j * 128:(j + 1) * 128, :]), inc="s_gd", k=16)
            em.wait("te", "s_gd", mgd)
            for jj in range(4):
                em.op("te", lambda e, jj=jj: e.transpose(
                    ps_lo[0:128, jj * 128:(jj + 1) * 128][0:R, :],
                    wstage[:, jj * 16:(jj + 1) * 16], id_f[:]))
            mtr = em.inc_last("te", "s_tr")
            em.wait("ve", "s_tr", mtr)
            mdv = em.op("ve", lambda e, dstT=dstT, g4=g4: e.tensor_copy(
                dstT[0:R, g4 * 512:(g4 + 1) * 512], ps_lo[0:R, :]), inc="s_dv")
            em.wait("te", "s_dv", mdv)

    em.barrier()
    em.realize()
    for c in reversed(cm_p0):
        c.__exit__(None, None, None)
    if max_phase <= 0:
        return _close_truncated(nc, em)


    def actT_full(t, k):
        return t[:, k * NT3:(k + 1) * NT3]

    def build_rep(slot, b, off, bias_col, prev_mm):
        em.wait("te", "s_sc", prev_mm)
        for hh in range(2):
            mmm = em.op("te", lambda e, b=b, off=off, hh=hh: e.matmul(
                ps_mm[:, hh * 512:(hh + 1) * 512], ones1[32 * b:32 * b + 1, :],
                mod_sb[32 * b:32 * b + 1, off + hh * 512:off + (hh + 1) * 512],
                start=True, stop=True), inc="s_mm")
        em.wait("sc", "s_mm", mmm)
        for hh in range(2):
            msc = em.op("sc", lambda e, slot=slot, hh=hh, bias_col=bias_col: e.activation(
                rep[:, slot * D + hh * 512:slot * D + (hh + 1) * 512],
                ps_mm[:, hh * 512:(hh + 1) * 512], AF.Identity, bias=bias_col), inc="s_sc")
        return msc

    pbf_free = {0: ("s_dv", 0), 1: ("s_dv", 0)}
    x_free = [(0, 0)] * 4

    def emit_ln_modulate(src_ap, par_h, rep_mul, rep_add, mul2, add2,
                         m_data, data_sem):
        em.wait("ve", data_sem, m_data)
        em.op("ve", lambda e: e.reduce_sum(stat[:, 0:1], src_ap, axis=AX.X))
        em.wait("sc", data_sem, m_data)
        msq = em.op("sc", lambda e: e.activation(
            sq_dump[:], src_ap, AF.Square, bias=Z, accum_out=stat[:, 1:2]), inc="s_sc")
        em.op("ve", lambda e: e.tensor_scalar(stat[:, 2:3], stat[:, 0:1], 1.0 / D, None, op0=ALU.mult))
        em.wait("ve", "s_sc", msq)
        em.op("ve", lambda e: e.tensor_scalar(stat[:, 4:5], stat[:, 1:2], 1.0 / D, EPS,
                                              op0=ALU.mult, op1=ALU.add))
        em.op("ve", lambda e: e.tensor_tensor(stat[:, 5:6], stat[:, 2:3], stat[:, 2:3], op=ALU.mult))
        mdv = em.op("ve", lambda e: e.tensor_tensor(stat[:, 4:5], stat[:, 4:5], stat[:, 5:6],
                                                    op=ALU.subtract), inc="s_dv")
        em.wait("sc", "s_dv", mdv)
        msq = em.op("sc", lambda e: e.activation(stat[:, 5:6], stat[:, 4:5], AF.Sqrt, bias=Z),
                    inc="s_sc")
        em.wait("ve", "s_sc", msq)
        em.op("ve", lambda e: e.reciprocal(stat[:, 3:4], stat[:, 5:6]))
        mnm = em.op("ve", lambda e: e.tensor_scalar(
            stat[:, 6:7], stat[:, 2:3], stat[:, 3:4], -1.0,
            op0=ALU.mult, op1=ALU.mult), inc="s_dv")
        em.wait("sc", "s_dv", mnm)
        mcons = em.op("sc", lambda e: e.activation(
            tm_bf[:], src_ap, AF.Identity, scale=stat[:, 3:4], bias=stat[:, 6:7]),
            inc="s_sc")
        em.wait("ve", "s_sc", mcons)
        em.op("ve", lambda e: e.tensor_tensor(tm_bf[:], tm_bf[:], rep_mul, op=ALU.mult))
        mh = em.op("ve", lambda e, par_h=par_h: e.tensor_tensor(
            h_tmb[:, par_h * D:(par_h + 1) * D], tm_bf[:], rep_add, op=ALU.add), inc="s_dv")
        return (mnm, mcons), mh

    def emit_act_transpose(par_h, mh, dstT, col0):
        em.wait("te", "s_dv", mh)
        em.wait("te", pbf_free[par_h][0], pbf_free[par_h][1])
        for k in range(KC):
            em.op("te", lambda e, par_h=par_h, k=k: e.transpose(
                ps_bf[:, par_h * 1024 + k * 128:par_h * 1024 + (k + 1) * 128],
                h_tmb[:, par_h * D + k * 128:par_h * D + (k + 1) * 128], id_b[:]))
        mtr = em.inc_last("te", "s_tr")
        em.wait("ve", "s_tr", mtr)
        dst3 = dstT[:].rearrange("p (k t) -> p k t", k=KC)[:, :, col0:col0 + 128]
        src3 = ps_bf[:, par_h * 1024:(par_h + 1) * 1024].rearrange("p (k i) -> p k i", k=KC)
        mev = em.op("ve", lambda e, dst3=dst3, src3=src3: e.tensor_copy(dst3, src3), inc="s_dv")
        pbf_free[par_h] = ("s_dv", mev)
        return mev

    cm_p2 = [nc.sbuf_tensor("y_fm", [128, 2 * 128 * 16], BF16),
             nc.sbuf_tensor("u_st", [128, 4 * 128 * 16], BF16),
             nc.sbuf_tensor("v_st", [128, 4 * 128 * 16], BF16),
             nc.sbuf_tensor("y_tm", [128, NB * D], BF16),
             nc.sbuf_tensor("ystage", [128, D], BF16),
             nc.sbuf_tensor("actT1", [128, KC * NT1], BF16),
             nc.sbuf_tensor("qload", [128, 2 * D], F32),
             nc.sbuf_tensor("qa", [128, D], F32),
             nc.sbuf_tensor("qr", [128, D], F32),
             nc.sbuf_tensor("qt", [128, D], F32),
             nc.sbuf_tensor("qsgn", [128, 2 * D], BF16),
             nc.sbuf_tensor("qb", [128, 2 * D], BF16),
             nc.sbuf_tensor("qstage", [128, 2 * D], BF16)]
    (y_fm, u_st, v_st, y_tm, ystage, actT1, qload, qa, qr, qt, qsgn, qb, qstage) = \
        [c.__enter__() for c in cm_p2]

    def ring_slice(chunk, c, b):
        base = (chunk % 4) * 2048 + c * 2 + b
        return base

    mrep = 0
    for b in range(NB):
        mrep = build_rep(2 * b + 0, b, D, ONE, mrep)
        mrep = build_rep(2 * b + 1, b, 0, Z, mrep)
    em.wait("ve", "s_sc", mrep)

    p1m = {}
    zev = {0: mrep, 1: mrep}
    zuse = {}

    def build_p1_tile(t1):
        items = []
        t0 = t1 * NT1

        def ln_items(b, s):
            slot = 2 * b + s
            src_ap = x_sb[:, slot * D:(slot + 1) * D]
            par_h = slot % 2

            def i_load(em, b=b, s=s, slot=slot, t0=t0, src_ap=src_ap):
                em.wait("sy", "s_p1", x_free[slot][0])
                em.wait("sy", "s_sc", x_free[slot][1])
                em.dma_fence("sy", "s_ld")
                p1m[('xl', slot)] = em.op("sy", lambda e: e.dma_start(
                    src_ap, x_d.ap()[b, t0 + 128 * s:t0 + 128 * (s + 1), :]),
                    inc="s_ld", k=16)

            def i_st1(em, slot=slot, src_ap=src_ap):
                em.wait("ve", "s_ld", p1m[('xl', slot)])
                em.op("ve", lambda e: e.reduce_sum(stat[:, 0:1], src_ap, axis=AX.X))
                em.wait("sc", "s_ld", p1m[('xl', slot)])
                p1m['sq'] = em.op("sc", lambda e: e.activation(
                    sq_dump[:], src_ap, AF.Square, bias=Z, accum_out=stat[:, 1:2]),
                    inc="s_sc")

            def i_st2(em):
                em.op("ve", lambda e: e.tensor_scalar(stat[:, 2:3], stat[:, 0:1],
                                                      1.0 / D, None, op0=ALU.mult))
                em.wait("ve", "s_sc", p1m['sq'])
                em.op("ve", lambda e: e.tensor_scalar(stat[:, 4:5], stat[:, 1:2],
                                                      1.0 / D, EPS, op0=ALU.mult, op1=ALU.add))
                em.op("ve", lambda e: e.tensor_tensor(stat[:, 5:6], stat[:, 2:3],
                                                      stat[:, 2:3], op=ALU.mult))
                p1m['var'] = em.op("ve", lambda e: e.tensor_tensor(
                    stat[:, 4:5], stat[:, 4:5], stat[:, 5:6], op=ALU.subtract), inc="s_p1")

            def i_st3(em):
                em.wait("sc", "s_p1", p1m['var'])
                p1m['sd'] = em.op("sc", lambda e: e.activation(
                    stat[:, 5:6], stat[:, 4:5], AF.Sqrt, bias=Z), inc="s_sc")

            def i_st4(em, slot=slot):
                em.wait("ve", "s_sc", p1m['sd'])
                em.op("ve", lambda e: e.reciprocal(stat[:, 3:4], stat[:, 5:6]))
                p1m['nm'] = em.op("ve", lambda e: e.tensor_scalar(
                    stat[:, 6:7], stat[:, 2:3], stat[:, 3:4], -1.0,
                    op0=ALU.mult, op1=ALU.mult), inc="s_p1")

            def i_xn(em, slot=slot, src_ap=src_ap):
                em.wait("sc", "s_p1", p1m['nm'])
                p1m[('xn', slot)] = em.op("sc", lambda e: e.activation(
                    tm_bf[:], src_ap, AF.Identity, scale=stat[:, 3:4],
                    bias=stat[:, 6:7]), inc="s_sc")

            def i_mod(em, b=b, slot=slot, par_h=par_h):
                em.wait("ve", "s_sc", p1m[('xn', slot)])
                em.op("ve", lambda e: e.tensor_tensor(
                    tm_bf[:], tm_bf[:], rep[:, (2 * b) * D:(2 * b + 1) * D], op=ALU.mult))
                mh = em.op("ve", lambda e: e.tensor_tensor(
                    h_tmb[:, par_h * D:(par_h + 1) * D], tm_bf[:],
                    rep[:, (2 * b + 1) * D:(2 * b + 2) * D], op=ALU.add), inc="s_p1")
                p1m[('h', slot)] = mh
                x_free[slot] = (p1m['nm'], p1m[('xn', slot)])

            def i_tr(em, slot=slot, par_h=par_h, s=s):
                em.wait("te", "s_p1", p1m[('h', slot)])
                em.wait("te", pbf_free[par_h][0], pbf_free[par_h][1])
                for k in range(KC):
                    em.op("te", lambda e, par_h=par_h, k=k: e.transpose(
                        ps_bf[:, par_h * 1024 + k * 128:par_h * 1024 + (k + 1) * 128],
                        h_tmb[:, par_h * D + k * 128:par_h * D + (k + 1) * 128], id_b[:]))
                p1m[('tr', slot)] = em.inc_last("te", "s_tr")

            def i_ev(em, slot=slot, par_h=par_h, s=s):
                em.wait("ve", "s_tr", p1m[('tr', slot)])
                dst3 = actT1[:].rearrange("p (k t) -> p k t", k=KC)[:, :, s * 128:(s + 1) * 128]
                src3 = ps_bf[:, par_h * 1024:(par_h + 1) * 1024].rearrange(
                    "p (k i) -> p k i", k=KC)
                mev = em.op("ve", lambda e, dst3=dst3, src3=src3: e.tensor_copy(dst3, src3),
                            inc="s_p1")
                pbf_free[par_h] = ("s_p1", mev)
                p1m[('ev', slot)] = mev
            return [i_load, i_st1, i_st2, i_st3, i_st4, i_xn, i_mod, i_tr, i_ev]

        def z_items(b, c, t1=t1):
            pz = c % 2
            chunk0 = 2 * t1

            def zi_mm(em, b=b, c=c, pz=pz):
                em.wait("te", "s_p1", p1m[('ev', 2 * b)])
                em.wait("te", "s_p1", p1m[('ev', 2 * b + 1)])
                em.wait("te", "s_sc", zev.get(pz, 0))
                for k in range(KC):
                    em.op("te", lambda e, c=c, k=k, pz=pz: e.matmul(
                        ps_mm[:, pz * 512:pz * 512 + NT1],
                        wgT[:, k * D + c * 128:k * D + (c + 1) * 128],
                        actT1[:, k * NT1:(k + 1) * NT1],
                        start=(k == 0), stop=(k == KC - 1)))
                p1m[('zmm', pz)] = em.inc_last("te", "s_mm")

            def zi_ev(em, c=c, pz=pz):
                em.wait("sc", "s_mm", p1m[('zmm', pz)])
                em.wait("sc", "s_p1", zuse.get(pz, 0))
                zev[pz] = em.op("sc", lambda e, pz=pz, c=c: e.activation(
                    tm_a[:, pz * 512:pz * 512 + NT1], ps_mm[:, pz * 512:pz * 512 + NT1],
                    AF.Sigmoid, bias=vecs[:, 24 + c:25 + c]), inc="s_sc")

            def zi_u(em, b=b, c=c, pz=pz, chunk0=chunk0):
                em.wait("ve", "s_sc", zev[pz])
                b0 = (chunk0 % 4) * 2048 + c * 2 + b
                b1 = ((chunk0 + 1) % 4) * 2048 + c * 2 + b
                em.op("ve", lambda e, b0=b0, pz=pz: e.tensor_scalar(
                    u_st[:, b0:b0 + 16 * 127 + 1:16], tm_a[:, pz * 512:pz * 512 + 128],
                    vecs[:, 8 + c:9 + c], None, op0=ALU.mult), drain=False)
                em.op("ve", lambda e, b1=b1, pz=pz: e.tensor_scalar(
                    u_st[:, b1:b1 + 16 * 127 + 1:16],
                    tm_a[:, pz * 512 + 128:pz * 512 + 256],
                    vecs[:, 8 + c:9 + c], None, op0=ALU.mult))

            def zi_v(em, b=b, c=c, pz=pz, chunk0=chunk0):
                b0 = (chunk0 % 4) * 2048 + c * 2 + b
                b1 = ((chunk0 + 1) % 4) * 2048 + c * 2 + b
                em.op("ve", lambda e, b0=b0, c=c: e.scalar_tensor_tensor(
                    v_st[:, b0:b0 + 16 * 127 + 1:16],
                    actT1[:, c * NT1:c * NT1 + 128],
                    vecs[:, 16 + c:17 + c],
                    u_st[:, b0:b0 + 16 * 127 + 1:16],
                    op0=ALU.subtract, op1=ALU.mult), drain=False)
                muv = em.op("ve", lambda e, b1=b1, c=c: e.scalar_tensor_tensor(
                    v_st[:, b1:b1 + 16 * 127 + 1:16],
                    actT1[:, c * NT1 + 128:(c + 1) * NT1],
                    vecs[:, 16 + c:17 + c],
                    u_st[:, b1:b1 + 16 * 127 + 1:16],
                    op0=ALU.subtract, op1=ALU.mult), inc="s_p1")
                zuse[pz] = muv
            return [zi_mm, zi_ev, zi_u, zi_v]

        for b in range(NB):
            for s in range(2):
                items += ln_items(b, s)
            for c in range(KC):
                items += z_items(b, c)
        return items

    for t1 in range(min(2, NTIL1)):
        for it in build_p1_tile(t1):
            it(em)
    p1q = []
    p1_need = {}
    for t1 in range(2, NTIL1):
        w = 256 * t1 - 255
        for it in build_p1_tile(t1):
            p1q.append((w, it))
        p1_need[2 * t1] = len(p1q)
    p1pos = [0]

    def drain_p1(t, n):
        while n > 0 and p1pos[0] < len(p1q) and p1q[p1pos[0]][0] <= t:
            p1q[p1pos[0]][1](em)
            p1pos[0] += 1
            n -= 1

    def gran_src(g):
        if g < FC:
            return wfc_d.ap()[g * 128:(g + 1) * 128, :]
        gg = g - FC
        r, qf = gg // 4, gg % 4
        return wpr_d.ap()[r * 128:(r + 1) * 128, qf * D:(qf + 1) * D]

    qitems = []
    qm = {}

    def qi_init(em):
        qm[('red', -1)] = em.op("ve", lambda e: e.memset(mx[:, 0:2], 0.0), inc="s_q")
    qitems.append(qi_init)

    for g in range(64):
        def qi_mload(em, g=g):
            if g >= 2:
                em.wait("gp", "s_q", qm[('red', g - 2)])
            em.dma_fence("gp", "s_gd")
            qm[('mld', g)] = em.op("gp", lambda e, g=g: e.dma_start(
                qload[:, (g % 2) * D:(g % 2 + 1) * D], gran_src(g)), inc="s_gd", k=16)

        def qi_mred(em, g=g):
            em.wait("ve", "s_gd", qm[('mld', g)])
            em.op("ve", lambda e, g=g: e.tensor_reduce(
                mx[:, 2:3], qload[:, (g % 2) * D:(g % 2 + 1) * D], axis=AX.X,
                op=ALU.max, apply_absolute_value=True))
            col = 0 if g < FC else 1
            qm[('red', g)] = em.op("ve", lambda e, col=col: e.tensor_tensor(
                mx[:, col:col + 1], mx[:, col:col + 1], mx[:, 2:3], op=ALU.max), inc="s_q")
        qitems += [qi_mload, qi_mred]

    def qi_fin(em):
        em.wait("te", "s_q", qm[('red', 63)])
        em.op("te", lambda e: e.transpose(ps_lo[:, 0:128][0:1, :], mx[:, 0:1], id_f[:]))
        qm['fint'] = em.op("te", lambda e: e.transpose(
            ps_lo[:, 128:256][0:1, :], mx[:, 1:2], id_f[:]), inc="s_qt")
        em.wait("ve", "s_qt", qm['fint'])
        em.op("ve", lambda e: e.tensor_reduce(mx[0:1, 4:5], ps_lo[0:1, 0:128], axis=AX.X, op=ALU.max))
        em.op("ve", lambda e: e.tensor_reduce(mx[0:1, 6:7], ps_lo[0:1, 128:256], axis=AX.X, op=ALU.max))
        em.op("ve", lambda e: e.reciprocal(mx[0:1, 5:6], mx[0:1, 4:5]))
        em.op("ve", lambda e: e.reciprocal(mx[0:1, 7:8], mx[0:1, 6:7]))
        em.op("ve", lambda e: e.tensor_scalar(mx[0:1, 4:5], mx[0:1, 4:5],
                                              1.0 / 127.0, None, op0=ALU.mult))
        em.op("ve", lambda e: e.tensor_scalar(mx[0:1, 6:7], mx[0:1, 6:7],
                                              1.0 / 127.0, None, op0=ALU.mult))
        em.op("ve", lambda e: e.tensor_scalar(mx[0:1, 5:6], mx[0:1, 5:6],
                                              127.0, None, op0=ALU.mult))
        qm['finv'] = em.op("ve", lambda e: e.tensor_scalar(
            mx[0:1, 7:8], mx[0:1, 7:8], 127.0, None, op0=ALU.mult), inc="s_q")
        em.wait("te", "s_q", qm['finv'])
        for i, mx_col in enumerate((4, 5, 6, 7)):
            em.op("te", lambda e, i=i, mx_col=mx_col: e.matmul(
                ps_lo[:, 256 + i:257 + i], ones1[0:1, :], mx[0:1, mx_col:mx_col + 1],
                start=True, stop=True))
        mtr = em.inc_last("te", "s_qt")
        em.wait("ve", "s_qt", mtr)
        for i, sc_col in enumerate((48, 49, 50, 51)):
            em.op("ve", lambda e, i=i, sc_col=sc_col: e.tensor_copy(
                vecs[:, sc_col:sc_col + 1], ps_lo[:, 256 + i:257 + i]))
        qm['fbc'] = em.inc_last("ve", "s_q")
    qitems.append(qi_fin)

    for g in range(64):
        iscale = vecs[:, 49:50] if g < FC else vecs[:, 51:52]
        mscale = vecs[:, 48:49] if g < FC else vecs[:, 50:51]

        def qi_load(em, g=g):
            em.wait("gp", "s_q",
                    qm[('wq', g - 2)] if g >= 2 else qm[('red', {0: 62, 1: 63}[g % 2])])
            em.dma_fence("gp", "s_gd")
            qm[('ld', g)] = em.op("gp", lambda e, g=g: e.dma_start(
                qload[:, (g % 2) * D:(g % 2 + 1) * D], gran_src(g)), inc="s_gd", k=16)

        def qi_q(em, g=g, iscale=iscale):
            em.wait("sc", "s_gd", qm[('ld', g)])
            if g == 0:
                em.wait("sc", "s_q", qm['fbc'])
            else:
                em.wait("sc", "s_q", qm[('wq', g - 1)])
            em.op("sc", lambda e, g=g, iscale=iscale: e.activation(
                qr[:], qload[:, (g % 2) * D:(g % 2 + 1) * D], AF.Identity,
                scale=iscale, bias=Z), drain=True)

        def qi_abs(em, g=g):
            em.op("sc", lambda e: e.activation(qa[:], qr[:], AF.Abs, bias=Z))
            qm[('sgn', g)] = em.op("sc", lambda e, g=g: e.activation(
                qsgn[:, (g % 2) * D:(g % 2 + 1) * D], qr[:], AF.Sign, bias=Z), inc="s_sc")

        def qi_b0(em, g=g):
            em.wait("ve", "s_sc", qm[('sgn', g)])
            em.op("ve", lambda e: e.tensor_scalar(qt[:], qa[:], 63.5, 64.0,
                                                  op0=ALU.is_ge, op1=ALU.mult))

        def qi_b0b(em, g=g):
            em.op("ve", lambda e: e.tensor_tensor(qr[:], qa[:], qt[:], op=ALU.subtract))

        def mk_bit(j):
            def qi_bj(em, j=j):
                em.op("ve", lambda e, j=j: e.tensor_scalar(
                    qt[:], qr[:], j - 0.5, float(j), op0=ALU.is_ge, op1=ALU.mult))
            def qi_bjs(em):
                em.op("ve", lambda e: e.tensor_tensor(qr[:], qr[:], qt[:], op=ALU.subtract))
            return [qi_bj, qi_bjs]

        def qi_rnd(em, g=g):
            qm[('rnd', g)] = em.op("ve", lambda e: e.tensor_tensor(
                qt[:], qa[:], qr[:], op=ALU.subtract), inc="s_q")

        def qi_sca(em, g=g, mscale=mscale):
            em.wait("sc", "s_q", qm[('rnd', g)])
            qm[('sca', g)] = em.op("sc", lambda e, mscale=mscale: e.activation(
                qr[:], qt[:], AF.Identity, scale=mscale, bias=Z), inc="s_sc",
                drain=True)

        def qi_sgm(em, g=g):
            em.wait("ve", "s_sc", qm[('sca', g)])
            qm[('wq', g)] = em.op("ve", lambda e, g=g: e.tensor_tensor(
                qb[:, (g % 2) * D:(g % 2 + 1) * D], qr[:],
                qsgn[:, (g % 2) * D:(g % 2 + 1) * D], op=ALU.mult), inc="s_q")

        def qi_tr(em, g=g):
            em.wait("te", "s_q", qm[('wq', g)])
            em.wait("te", "s_sc", qm.get(('ev', g - 1), 0))
            for k in range(KC):
                em.op("te", lambda e, g=g, k=k: e.transpose(
                    ps_x[:, k * 128:(k + 1) * 128],
                    qb[:, (g % 2) * D + k * 128:(g % 2) * D + (k + 1) * 128], id_b[:]))
            qm[('tr', g)] = em.inc_last("te", "s_qt")

        def qi_ev(em, g=g):
            em.wait("sc", "s_qt", qm[('tr', g)])
            if g >= 2:
                em.wait("sc", "s_gd", qm[('out', g - 2)])
            qm[('ev', g)] = em.op("sc", lambda e, g=g: e.activation(
                qstage[:, (g % 2) * D:(g % 2 + 1) * D], ps_x[:], AF.Identity,
                bias=Z), inc="s_sc")

        def qi_out(em, g=g):
            em.wait("gp", "s_sc", qm[('ev', g)])
            em.dma_fence("gp", "s_gd")
            if g < FC:
                dst = wqfc_h.ap()[g, :, :]
            else:
                gg = g - FC
                r, qf = gg // 4, gg % 4
                dst = wqpr_h.ap()[r, :, qf * D:(qf + 1) * D]
            qm[('out', g)] = em.op("gp", lambda e, g=g, dst=dst: e.dma_start(
                dst, qstage[:, (g % 2) * D:(g % 2 + 1) * D]), inc="s_gd", k=16)

        bit_items = []
        for j in (32, 16, 8, 4, 2, 1):
            bit_items += mk_bit(j)
        qitems += ([qi_load, qi_q, qi_abs, qi_b0, qi_b0b] + bit_items +
                   [qi_rnd, qi_sca, qi_sgm, qi_tr, qi_ev, qi_out])

    qpos = [0]

    def drain_quant(n):
        while n > 0 and qpos[0] < len(qitems):
            qitems[qpos[0]](em)
            qpos[0] += 1
            n -= 1

    base_dv = em.cnt["s_dv"]
    em.op("ve", lambda e: e.tensor_copy(y_fm[:, 0:8], v_st[:, 0:8]), inc="s_dv")
    em.op("ve", lambda e: e.tensor_copy(y_fm[:, 8:16], v_st[:, 8:16]), inc="s_dv")

    flush_ev = {}
    ydma_mark = {}

    def emit_flush(B):
        buf = B % 2
        em.wait("te", "s_dv", base_dv + 2 * (B + 1) * 128)
        em.wait("te", "s_fl", flush_ev.get(B - 1, 0))
        em.wait("te", pbf_free[0][0], pbf_free[0][1])
        em.wait("te", pbf_free[1][0], pbf_free[1][1])
        for b0 in range(NB):
            bank = b0
            for c in range(KC):
                st = buf * 2048 + c * 2 + b0
                em.op("te", lambda e, bank=bank, c=c, st=st: e.transpose(
                    ps_bf[:, bank * 1024 + c * 128:bank * 1024 + (c + 1) * 128],
                    y_fm[:, st:st + 16 * 127 + 1:16], id_b[:]))
            mtr = em.inc_last("te", "s_tr")
            em.wait("sc", "s_tr", mtr)
            em.wait("sc", "s_ydma", ydma_mark.get(B - 1, 0))
            mev = em.op("sc", lambda e, b0=b0, bank=bank: e.copy(
                y_tm[:, b0 * 1024:(b0 + 1) * 1024],
                ps_bf[:, bank * 1024:(bank + 1) * 1024]), inc="s_ev", drain=True)
            ytm = y_tm[:, b0 * 1024:(b0 + 1) * 1024]
            em.wait("ve", "s_ev", mev)
            em.op("ve", lambda e, ytm=ytm: e.reduce_sum(stat[:, 8:9], ytm, axis=AX.X))
            msq = em.op("sc", lambda e, ytm=ytm: e.activation(
                sq_dump[:], ytm, AF.Square, bias=Z, accum_out=stat[:, 9:10]), inc="s_sc")
            em.op("ve", lambda e: e.tensor_scalar(stat[:, 10:11], stat[:, 8:9], 1.0 / D,
                                                  None, op0=ALU.mult), drain=False)
            em.wait("ve", "s_sc", msq)
            em.op("ve", lambda e: e.tensor_scalar(stat[:, 12:13], stat[:, 9:10], 1.0 / D,
                                                  EPS, op0=ALU.mult, op1=ALU.add))
            em.op("ve", lambda e: e.tensor_tensor(stat[:, 13:14], stat[:, 10:11],
                                                  stat[:, 10:11], op=ALU.mult))
            mdv = em.op("ve", lambda e: e.tensor_tensor(stat[:, 12:13], stat[:, 12:13],
                                                        stat[:, 13:14], op=ALU.subtract),
                        inc="s_fl", drain=True)
            em.wait("sc", "s_fl", mdv)
            msq = em.op("sc", lambda e: e.activation(stat[:, 13:14], stat[:, 12:13],
                                                     AF.Sqrt, bias=Z), inc="s_sc")
            em.wait("ve", "s_sc", msq)
            em.op("ve", lambda e: e.reciprocal(stat[:, 11:12], stat[:, 13:14]))
            mdv = em.op("ve", lambda e: e.tensor_scalar(
                stat[:, 14:15], stat[:, 10:11], stat[:, 11:12], -1.0,
                op0=ALU.mult, op1=ALU.mult), inc="s_fl")
            em.wait("sc", "s_fl", mdv)
            mno = em.op("sc", lambda e, b0=b0, ytm=ytm: e.activation(
                y_tm[:, b0 * 1024:(b0 + 1) * 1024], ytm, AF.Identity,
                scale=stat[:, 11:12], bias=stat[:, 14:15]), inc="s_sc", drain=False)
            em.wait("te", "s_sc", mno)
            for k in range(KC):
                em.op("te", lambda e, bank=bank, b0=b0, k=k: e.transpose(
                    ps_bf[:, bank * 1024 + k * 128:bank * 1024 + (k + 1) * 128],
                    y_tm[:, b0 * 1024 + k * 128:b0 * 1024 + (k + 1) * 128], id_b[:]))
            mtr = em.inc_last("te", "s_tr")
            em.wait("ve", "s_tr", mtr)
            em.wait("ve", "s_ydma", em.cnt["s_ydma"])
            mev = em.op("ve", lambda e, bank=bank: e.tensor_copy(
                ystage[:], ps_bf[:, bank * 1024:(bank + 1) * 1024]),
                inc="s_fl", drain=False)
            pbf_free[bank] = ("s_fl", mev)
            em.wait("sy", "s_fl", mev)
            em.dma_fence("sy", "s_ydma")
            dstv = ynT_h.ap()[b0, :, :, B * 128:(B + 1) * 128].rearrange("k p i -> p k i")
            srcv = ystage[:].rearrange("p (k i) -> p k i", k=KC)
            em.op("sy", lambda e, dstv=dstv, srcv=srcv: e.dma_start(dstv, srcv),
                  inc="s_ydma", k=16)
        flush_ev[B] = mev
        ydma_mark[B] = em.cnt["s_ydma"]

    m_mm_h = {0: 0, 1: 0}
    for t in range(1, L):
        B = t // 128
        buf = B % 2
        pbuf = ((t - 1) // 128) % 2
        par = t % 2
        po = t % 128
        ppo = (t - 1) % 128
        if po == 0:
            emit_flush(B - 1)
            assert p1pos[0] >= p1_need.get(B, 0), (B, p1pos[0])
        em.wait("te", "s_dv", base_dv + 2 * t)
        prev0 = pbuf * 2048 + ppo * 16
        for e_ in range(KC):
            hb = (e_ // 4) * 512
            fn0 = lambda e, e_=e_, hb=hb, prev0=prev0: e.matmul(
                ps_sc[:, hb + (e_ % 4) * 2:hb + (e_ % 4) * 2 + 2],
                wrT[:, 0 * D + e_ * 128:0 * D + (e_ + 1) * 128],
                y_fm[:, prev0 + 0:prev0 + 2],
                start=(e_ % 4 == 0), stop=False)
            em.op("te", fn0)
            for k in range(1, KC):
                fn = lambda e, e_=e_, k=k, hb=hb, prev0=prev0: e.matmul(
                    ps_sc[:, hb + (e_ % 4) * 2:hb + (e_ % 4) * 2 + 2],
                    wrT[:, k * D + e_ * 128:k * D + (e_ + 1) * 128],
                    y_fm[:, prev0 + k * 2:prev0 + k * 2 + 2],
                    start=False, stop=(k == KC - 1 and e_ % 4 == 3))
                if k == KC - 1 and e_ in (3, KC - 1):
                    m_mm_h[e_ // 4] = em.op("te", fn, inc="s_mm")
                else:
                    em.op("te", fn)
        if po == 0:
            em.wait("ve", "s_fl", flush_ev.get(B - 2, 0))
        cur0 = buf * 2048 + po * 16
        uv0 = (B % 4) * 2048 + po * 16
        em.wait("ve", "s_mm", m_mm_h[0])
        em.op("ve", lambda e, uv0=uv0: e.tensor_tensor(
            scan_tmp[:, 0:8], ps_sc[:, 0:8],
            u_st[:, uv0:uv0 + 8], op=ALU.mult))
        if t == 1:
            em.op("ve", lambda e, prev0=prev0: e.tensor_tensor(
                scan_tmp[:, 32:48], y_fm[:, prev0:prev0 + 16], dec_bc[:], op=ALU.mult))
        em.op("ve", lambda e, uv0=uv0: e.tensor_tensor(
            scan_tmp[:, 16:24], v_st[:, uv0:uv0 + 8], scan_tmp[:, 0:8],
            op=ALU.subtract))
        em.op("ve", lambda e, cur0=cur0: e.tensor_tensor(
            y_fm[:, cur0:cur0 + 8], scan_tmp[:, 32:40], scan_tmp[:, 16:24],
            op=ALU.add), inc="s_dv", drain=False)
        em.wait("ve", "s_mm", m_mm_h[1])
        em.op("ve", lambda e, uv0=uv0: e.tensor_tensor(
            scan_tmp[:, 8:16], ps_sc[:, 512:520],
            u_st[:, uv0 + 8:uv0 + 16], op=ALU.mult))
        em.op("ve", lambda e, uv0=uv0: e.tensor_tensor(
            scan_tmp[:, 24:32], v_st[:, uv0 + 8:uv0 + 16], scan_tmp[:, 8:16],
            op=ALU.subtract))
        em.op("ve", lambda e, cur0=cur0: e.tensor_tensor(
            y_fm[:, cur0 + 8:cur0 + 16], scan_tmp[:, 40:48], scan_tmp[:, 24:32],
            op=ALU.add), inc="s_dv")
        em.op("ve", lambda e, cur0=cur0: e.tensor_tensor(
            scan_tmp[:, 32:48], y_fm[:, cur0:cur0 + 16], dec_bc[:], op=ALU.mult),
            drain=False)
        if t >= 2:
            drain_quant(1)
        if t % 2 == 1:
            drain_p1(t, 1)
    emit_flush(NCH - 1)
    drain_quant(10 ** 9)
    m_wq_all = em.cnt["s_gd"]
    m_y_all = em.cnt["s_ydma"]

    em.barrier()
    em.realize()
    for c in reversed(cm_p2):
        c.__exit__(None, None, None)
    cm_wgT.__exit__(None, None, None)
    cm_wrT.__exit__(None, None, None)
    if max_phase <= 2:
        return _close_truncated(nc, em)

    cm_p3 = [nc.sbuf_tensor("actT", [128, KC * NT3], BF16),
             nc.sbuf_tensor("ynT", [128, 2 * KC * NT3], BF16),
             nc.sbuf_tensor("att_tm", [128, 4 * D], BF16),
             nc.sbuf_tensor("att_fm", [128, KC * NT3], BF16),
             nc.sbuf_tensor("h3", [128, FC * NT3], BF16),
             nc.sbuf_tensor("wqf_sb", [128, 2 * 2 * KC * 128], BF16),
             nc.sbuf_tensor("wqp_sb", [128, 2 * FC * 128], BF16)]
    (actT, ynT, att_tm, att_fm, h3, wqf_sb, wqp_sb) = [c.__enter__() for c in cm_p3]

    out_mark = [0, 0, 0, 0]
    yin_mark = [0]
    mm_ev = {0: 0, 1: 0}
    wqf_mark = {}
    wqp_mark = {}
    wqf_free = {0: 0, 1: 0}
    wqp_free = {0: 0, 1: 0}
    first_wq = [True]

    def emit_wqf_load(jg):
        pz = jg % 2
        if first_wq[0]:
            em.wait("gp", "s_gd", m_wq_all)
            em.wait("gp", "s_ydma", m_y_all)
            first_wq[0] = False
        em.wait("gp", "s_mm", wqf_free[pz])
        em.dma_fence("gp", "s_gd")
        dstv = wqf_sb[:, pz * 2 * KC * 128:(pz + 1) * 2 * KC * 128].rearrange(
            "p (j c) -> p j c", j=2)
        srcv = wqfc_h.ap()[jg * 2:(jg + 1) * 2, :, :].rearrange("j p c -> p j c")
        wqf_mark[jg] = em.op("gp", lambda e, dstv=dstv, srcv=srcv: e.dma_start(
            dstv, srcv), inc="s_gd", k=16)

    def emit_wqp_load(m):
        pz = m % 2
        em.wait("gp", "s_mm", wqp_free[pz])
        em.dma_fence("gp", "s_gd")
        wqp_mark[m] = em.op("gp", lambda e, m=m, pz=pz: e.dma_start(
            wqp_sb[:, pz * FC * 128:(pz + 1) * FC * 128], wqpr_h.ap()[m, :, :]),
            inc="s_gd", k=16)

    em.wait("sc", "s_ydma", m_y_all)
    yn_mark = {}

    def emit_ynT_load(bb, tt0, par):
        em.wait("sc", "s_mm", yin_mark[0])
        em.dma_fence("sc", "s_scd")
        dstv = ynT[:, par * KC * NT3:(par + 1) * KC * NT3].rearrange(
            "p (k t) -> p k t", k=KC)
        srcv = ynT_h.ap()[bb, :, :, tt0:tt0 + NT3].rearrange("k p t -> p k t")
        return em.op("sc", lambda e, dstv=dstv, srcv=srcv: e.dma_start(dstv, srcv),
                     inc="s_scd", k=16)

    for b in range(NB):
        em.wait("sc", "s_dv", em.cnt["s_dv"])
        em.wait("te", "s_dv", em.cnt["s_dv"])
        mrep3 = em.cnt["s_sc"]
        mrep3 = build_rep(0, b, 4 * D, ONE, mrep3)
        mrep3 = build_rep(1, b, 3 * D, Z, mrep3)
        em.wait("ve", "s_sc", mrep3)
        em.wait("te", "s_sc", mrep3)
        for gi, goff in ((0, 2 * D), (2, 5 * D)):
            for k in range(KC):
                em.op("te", lambda e, b=b, goff=goff, k=k: e.transpose(
                    ps_sc[:, k:k + 1],
                    mod_sb[32 * b:32 * b + 1, goff + k * 128:goff + (k + 1) * 128],
                    ones1[32 * b:32 * b + 1, 0:1]))
            mtr = em.inc_last("te", "s_tr")
            em.wait("ve", "s_tr", mtr)
            em.op("ve", lambda e, gi=gi: e.tensor_copy(
                gfm[:, gi * 8:(gi + 1) * 8], ps_sc[:, 0:8]))
            bcol = vecs[:, 32:40] if gi == 0 else vecs[:, 40:48]
            mgf = em.op("ve", lambda e, gi=gi, bcol=bcol: e.tensor_tensor(
                gfm[:, (gi + 1) * 8:(gi + 2) * 8], gfm[:, gi * 8:(gi + 1) * 8],
                bcol, op=ALU.mult), inc="s_dv")
            em.wait("te", "s_dv", mgf)
            em.wait("sc", "s_dv", mgf)
        for t3 in range(NTIL3):
            t0 = t3 * NT3
            mxl = [0] * NS3
            myl = [0] * NS3
            for s in range(NS3):
                em.wait("sy", "s_out", out_mark[s])
                em.dma_fence("sy", "s_ld")
                mxl[s] = em.op("sy", lambda e, b=b, s=s, t0=t0: e.dma_start(
                    x_sb[:, s * D:(s + 1) * D],
                    x_d.ap()[b, t0 + 128 * s:t0 + 128 * (s + 1), :]), inc="s_ld", k=16)
            ti = b * NTIL3 + t3
            pyn = ti % 2
            if ti == 0:
                yn_mark[0] = emit_ynT_load(0, 0, 0)
            if ti + 1 < NB * NTIL3:
                bn, tn = divmod(ti + 1, NTIL3)
                yn_mark[(ti + 1) % 2] = emit_ynT_load(bn, tn * NT3, (ti + 1) % 2)
            myn = yn_mark[pyn]
            emit_wqf_load(0)
            emit_wqf_load(1)
            m_att = 0
            em.wait("te", "s_scd", myn)
            ev_c = [0] * KC
            for c in range(KC):
                if c % 4 == 0:
                    em.wait("te", "s_sc", em.cnt["s_sc"])
                pb = (ps_sc, 0) if c % 4 == 0 else \
                     (ps_sc, 512) if c % 4 == 1 else \
                     (ps_mm, 0) if c % 4 == 2 else (ps_mm, 512)
                for k in range(KC):
                    em.op("te", lambda e, c=c, k=k, pb=pb, pyn=pyn: e.matmul(
                        pb[0][:, pb[1]:pb[1] + NT3],
                        woT[:, k * D + c * 128:k * D + (c + 1) * 128],
                        ynT[:, pyn * KC * NT3 + k * NT3:pyn * KC * NT3 + (k + 1) * NT3],
                        start=(k == 0), stop=(k == KC - 1)))
                m_mm = em.inc_last("te", "s_mm")
                em.wait("sc", "s_mm", m_mm)
                ev_c[c] = em.op("sc", lambda e, c=c, pb=pb: e.activation(
                    att_fm[:, c * NT3:(c + 1) * NT3], pb[0][:, pb[1]:pb[1] + NT3],
                    AF.Identity, scale=gfm[:, c:c + 1], bias=gfm[:, 8 + c:9 + c]),
                    inc="s_sc")
            yn_used = em.cnt["s_mm"]
            mxp = [0] * NS3
            m_act2 = 0
            for s4 in range(NS3):
                pz = s4 % 2
                em.wait("te", "s_sc", ev_c[KC - 1])
                em.wait("te", pbf_free[pz][0], pbf_free[pz][1])
                for c in range(KC):
                    em.op("te", lambda e, c=c, s4=s4, pz=pz: e.transpose(
                        ps_bf[:, pz * 1024 + c * 128:pz * 1024 + (c + 1) * 128],
                        att_fm[:, c * NT3 + s4 * 128:c * NT3 + (s4 + 1) * 128], id_b[:]))
                mtr = em.inc_last("te", "s_tr")
                em.wait("ve", "s_tr", mtr)
                m_att = em.op("ve", lambda e, s4=s4, pz=pz: e.tensor_copy(
                    att_tm[:, s4 * D:(s4 + 1) * D],
                    ps_bf[:, pz * 1024:(pz + 1) * 1024]), inc="s_dv", drain=True)
                pbf_free[pz] = ("s_dv", m_att)
                em.wait("ve", "s_ld", mxl[s4])
                mxp[s4] = em.op("ve", lambda e, s4=s4: e.tensor_tensor(
                    x_sb[:, s4 * D:(s4 + 1) * D], x_sb[:, s4 * D:(s4 + 1) * D],
                    att_tm[:, s4 * D:(s4 + 1) * D], op=ALU.add), inc="s_dv")
                _, mh = emit_ln_modulate(
                    x_sb[:, s4 * D:(s4 + 1) * D], s4 % 2, rep[:, 0:D], rep[:, D:2 * D],
                    None, None, mxp[s4], "s_dv")
                m_act2 = emit_act_transpose(s4 % 2, mh, actT, s4 * 128)
            yin_mark[0] = yn_used
            em.wait("te", "s_dv", m_act2)
            for k in range(KC):
                em.op("te", lambda e, k=k: e.matmul(
                    ps_lo[0:R, 0:NT3], afcT[:, k * R:(k + 1) * R], actT_full(actT, k),
                    start=(k == 0), stop=(k == KC - 1)))
            m_mm = em.inc_last("te", "s_mm")
            em.wait("ve", "s_mm", m_mm)
            m_s2 = em.op("ve", lambda e: e.tensor_scalar(
                s2[0:R, :], ps_lo[0:R, 0:NT3], 2.0, None, op0=ALU.mult), inc="s_dv")
            em.wait("te", "s_dv", m_s2)
            for j in range(FC):
                jg = j // 2
                pz4 = j % 4
                pg = jg % 2
                if j % 2 == 0:
                    em.wait("te", "s_gd", wqf_mark[jg])
                if j % 4 == 0:
                    em.wait("te", "s_sc", em.cnt["s_sc"])
                pb = (ps_mm, 0) if pz4 == 0 else \
                     (ps_mm, 512) if pz4 == 1 else \
                     (ps_sc, 0) if pz4 == 2 else (ps_sc, 512)
                base_w = pg * 2 * KC * 128 + (j % 2) * KC * 128
                for k in range(KC):
                    em.op("te", lambda e, k=k, pb=pb, base_w=base_w: e.matmul(
                        pb[0][:, pb[1]:pb[1] + NT3],
                        wqf_sb[:, base_w + k * 128:base_w + (k + 1) * 128],
                        actT_full(actT, k), start=(k == 0), stop=False))
                m_mm = em.op("te", lambda e, j=j, pb=pb: e.matmul(
                    pb[0][:, pb[1]:pb[1] + NT3],
                    bfcT[0:R, j * 128:(j + 1) * 128], s2[0:R, :],
                    start=False, stop=True), inc="s_mm")
                if j % 2 == 1:
                    wqf_free[pg] = m_mm
                em.wait("sc", "s_mm", m_mm)
                em.op("sc", lambda e, j=j, pb=pb: e.activation(
                    h3[:, j * NT3:(j + 1) * NT3], pb[0][:, pb[1]:pb[1] + NT3],
                    AF.Gelu, bias=bfc_fm[:, j:j + 1]), inc="s_sc")
                if j % 2 == 1 and jg + 2 < FC // 2:
                    emit_wqf_load(jg + 2)
            emit_wqp_load(0)
            emit_wqp_load(1)
            em.wait("te", "s_sc", em.cnt["s_sc"])
            for j in range(FC):
                em.op("te", lambda e, j=j: e.matmul(
                    ps_lo[0:R, 0:NT3], aprT[:, j * R:(j + 1) * R],
                    h3[:, j * NT3:(j + 1) * NT3],
                    start=(j == 0), stop=(j == FC - 1)))
            m_mm = em.inc_last("te", "s_mm")
            em.wait("ve", "s_mm", m_mm)
            m_s2p = em.op("ve", lambda e: e.tensor_scalar(
                s2[0:R, :], ps_lo[0:R, 0:NT3], 2.0, None, op0=ALU.mult), inc="s_dv")
            m_proj = 0
            em.wait("te", "s_dv", m_s2p)
            ev_m = [0] * KC
            for m_ in range(KC):
                pz4 = m_ % 4
                em.wait("te", "s_gd", wqp_mark[m_])
                if m_ % 4 == 0:
                    em.wait("te", "s_sc", em.cnt["s_sc"])
                pb = (ps_mm, 0) if pz4 == 0 else \
                     (ps_mm, 512) if pz4 == 1 else \
                     (ps_sc, 0) if pz4 == 2 else (ps_sc, 512)
                pzb = m_ % 2
                for k in range(FC):
                    em.op("te", lambda e, m_=m_, k=k, pb=pb, pzb=pzb: e.matmul(
                        pb[0][:, pb[1]:pb[1] + NT3],
                        wqp_sb[:, pzb * FC * 128 + k * 128:pzb * FC * 128 + (k + 1) * 128],
                        h3[:, k * NT3:(k + 1) * NT3], start=(k == 0), stop=False))
                m_mm = em.op("te", lambda e, m_=m_, pb=pb: e.matmul(
                    pb[0][:, pb[1]:pb[1] + NT3],
                    bprT[0:R, m_ * 128:(m_ + 1) * 128], s2[0:R, :],
                    start=False, stop=True), inc="s_mm")
                wqp_free[pzb] = m_mm
                em.wait("sc", "s_mm", m_mm)
                ev_m[m_] = em.op("sc", lambda e, m_=m_, pb=pb: e.activation(
                    att_fm[:, m_ * NT3:(m_ + 1) * NT3], pb[0][:, pb[1]:pb[1] + NT3],
                    AF.Identity, scale=gfm[:, 16 + m_:17 + m_],
                    bias=gfm[:, 24 + m_:25 + m_]), inc="s_sc")
                if m_ + 2 < KC:
                    emit_wqp_load(m_ + 2)
            for s4 in range(NS3):
                pz = s4 % 2
                em.wait("te", "s_sc", ev_m[KC - 1])
                em.wait("te", pbf_free[pz][0], pbf_free[pz][1])
                for m_ in range(KC):
                    em.op("te", lambda e, m_=m_, s4=s4, pz=pz: e.transpose(
                        ps_bf[:, pz * 1024 + m_ * 128:pz * 1024 + (m_ + 1) * 128],
                        att_fm[:, m_ * NT3 + s4 * 128:m_ * NT3 + (s4 + 1) * 128], id_b[:]))
                mtr = em.inc_last("te", "s_tr")
                em.wait("ve", "s_tr", mtr)
                m_proj = em.op("ve", lambda e, s4=s4, pz=pz: e.tensor_copy(
                    att_tm[:, s4 * D:(s4 + 1) * D],
                    ps_bf[:, pz * 1024:(pz + 1) * 1024]), inc="s_dv", drain=True)
                pbf_free[pz] = ("s_dv", m_proj)
                mo = em.op("ve", lambda e, s4=s4: e.tensor_tensor(
                    x_sb[:, s4 * D:(s4 + 1) * D], x_sb[:, s4 * D:(s4 + 1) * D],
                    att_tm[:, s4 * D:(s4 + 1) * D], op=ALU.add), inc="s_dv")
                em.wait("sy", "s_dv", mo)
                em.dma_fence("sy", "s_out")
                out_mark[s4] = em.op("sy", lambda e, b=b, s4=s4, t0=t0: e.dma_start(
                    out_d.ap()[b, t0 + 128 * s4:t0 + 128 * (s4 + 1), :],
                    x_sb[:, s4 * D:(s4 + 1) * D]), inc="s_out", k=16)
    em.wait("sy", "s_out", em.cnt["s_out"])
    em.realize()
    for c in reversed(cm_p3):
        c.__exit__(None, None, None)
    return nc


def _slice_inputs(inputs, core):
    b0 = NB * core
    m = {
        "x": np.ascontiguousarray(inputs["L0_attn_x"][b0:b0 + NB]),
        "cvec": np.ascontiguousarray(inputs["L0_c"][b0:b0 + NB]),
        "wmod": inputs["L0_wmod"],
        "decay": inputs["L0_attn_decay"],
        "Wr": inputs["L0_attn_Wr"], "br": inputs["L0_attn_br"],
        "Wg": inputs["L0_attn_Wg"], "bg": inputs["L0_attn_bg"],
        "Wo": inputs["L0_attn_Wo"], "bo": inputs["L0_attn_bo"],
        "ln_w": inputs["L0_attn_ln_w"], "ln_b": inputs["L0_attn_ln_b"],
        "W_fc": inputs["L0_ffn_W_fc"], "b_fc": inputs["L0_ffn_b_fc"],
        "A_fc": inputs["L0_ffn_A_fc"], "B_fc": inputs["L0_ffn_B_fc"],
        "W_proj": inputs["L0_ffn_W_proj"], "b_proj": inputs["L0_ffn_b_proj"],
        "A_proj": inputs["L0_ffn_A_proj"], "B_proj": inputs["L0_ffn_B_proj"],
    }
    return {k: np.ascontiguousarray(v) for k, v in m.items()}


def kernel(**inputs):
    nc = build_program(L=2048)
    in_maps = [_slice_inputs(inputs, i) for i in range(8)]
    res = run_bass_kernel_spmd(nc, in_maps, core_ids=list(range(8)))
    return np.concatenate([np.asarray(r["out"]) for r in res.results], axis=0)


# version 19
# speedup vs baseline: 1.1007x; 1.0648x over previous
import numpy as np
import concourse.bass as bass
import concourse.mybir as mybir
from concourse import library_config
from concourse.bass_utils import run_bass_kernel_spmd

F32 = mybir.dt.float32
BF16 = mybir.dt.bfloat16
AF = mybir.ActivationFunctionType
ALU = mybir.AluOpType
AX = mybir.AxisListType

D = 1024
FF = 4096
NB = 2
KC = D // 128
FC = FF // 128
R = 16
SCL = 2.0
EPS = 1e-5
ENGS = ("sy", "gp", "ve", "sc", "te")


class EM:

    def __init__(self, nc, sems):
        self.nc = nc
        self.sems = sems
        self.cnt = {k: 0 for k in sems}
        self.lists = {e: [] for e in ENGS}

    def op(self, eng, fn, inc=None, k=1, drain=None):
        incs = []
        ret = None
        if inc is not None:
            self.cnt[inc] += k
            incs.append((self.sems[inc], k))
            ret = self.cnt[inc]
        self.lists[eng].append(("op", fn, incs))
        if drain is None:
            drain = eng == "ve"
        if drain:
            self.lists[eng].append(("drain",))
        return ret

    def drain(self, eng):
        self.lists[eng].append(("drain",))

    def inc_last(self, eng, inc, k=1):
        for entry in reversed(self.lists[eng]):
            if entry[0] == "op":
                if not entry[2]:
                    self.cnt[inc] += k
                    entry[2].append((self.sems[inc], k))
                    return self.cnt[inc]
                self.lists[eng].append(("drain",))
                return self.op(eng, lambda e: e.nop(), inc=inc, k=k)
        return self.op(eng, lambda e: e.nop(), inc=inc, k=k)

    def dma_fence(self, eng, sem_name):
        self.wait(eng, sem_name, self.cnt[sem_name])

    def wait(self, eng, sem_name, val):
        if val < 0:
            return
        self.lists[eng].append(("wait", self.sems[sem_name], val))

    def barrier(self):
        for e in ENGS:
            if e in ("sy", "gp"):
                self.op(e, lambda x: x.nop(), inc="s_ph")
            else:
                self.inc_last(e, "s_ph")
        tgt = self.cnt["s_ph"]
        for e in ENGS:
            self.wait(e, "s_ph", tgt)

    @staticmethod
    def _replay(eng, entries):
        for entry in entries:
            if entry[0] == "op":
                instr = entry[1](eng)
                for sem, k in entry[2]:
                    instr.then_inc(sem, k)
            elif entry[0] == "wait":
                eng.wait_ge(entry[1], entry[2])
            else:
                eng.drain()

    def realize(self):
        with self.nc.Block() as blk:
            for e, meth in (("sy", blk.sync), ("gp", blk.gpsimd), ("ve", blk.vector),
                            ("sc", blk.scalar), ("te", blk.tensor)):
                entries = self.lists[e]
                if entries:
                    meth(lambda eng, entries=entries: EM._replay(eng, entries))
        self.lists = {e: [] for e in ENGS}



def _close_truncated(nc, em):
    with nc.Block() as blk:
        def _w(eng):
            for name in ("s_ld", "s_gd", "s_scd", "s_uv", "s_ydma", "s_out"):
                if em.cnt[name] > 0:
                    eng.wait_ge(em.sems[name], em.cnt[name])
        blk.sync(_w)
    return nc

def build_program(L=2048, max_phase=4):
    NCH = L // 128
    NT1 = 256
    NT3 = min(512, L)
    NS3 = NT3 // 128
    NTIL1 = L // NT1
    NTIL3 = L // NT3
    nc = bass.Bass("TRN2", target_bir_lowering=False)

    x_d = nc.dram_tensor("x", [NB, L, D], F32, kind="ExternalInput")
    c_d = nc.dram_tensor("cvec", [NB, D], F32, kind="ExternalInput")
    wmod_d = nc.dram_tensor("wmod", [D, 6 * D], F32, kind="ExternalInput")
    decay_d = nc.dram_tensor("decay", [D], F32, kind="ExternalInput")
    wr_d = nc.dram_tensor("Wr", [D, D], F32, kind="ExternalInput")
    br_d = nc.dram_tensor("br", [D], F32, kind="ExternalInput")
    wg_d = nc.dram_tensor("Wg", [D, D], F32, kind="ExternalInput")
    bg_d = nc.dram_tensor("bg", [D], F32, kind="ExternalInput")
    wo_d = nc.dram_tensor("Wo", [D, D], F32, kind="ExternalInput")
    bo_d = nc.dram_tensor("bo", [D], F32, kind="ExternalInput")
    lnw_d = nc.dram_tensor("ln_w", [D], F32, kind="ExternalInput")
    lnb_d = nc.dram_tensor("ln_b", [D], F32, kind="ExternalInput")
    wfc_d = nc.dram_tensor("W_fc", [FF, D], F32, kind="ExternalInput")
    bfc_d = nc.dram_tensor("b_fc", [FF], F32, kind="ExternalInput")
    afc_d = nc.dram_tensor("A_fc", [R, D], F32, kind="ExternalInput")
    Bfc_d = nc.dram_tensor("B_fc", [FF, R], F32, kind="ExternalInput")
    wpr_d = nc.dram_tensor("W_proj", [D, FF], F32, kind="ExternalInput")
    bpr_d = nc.dram_tensor("b_proj", [D], F32, kind="ExternalInput")
    apr_d = nc.dram_tensor("A_proj", [R, FF], F32, kind="ExternalInput")
    Bpr_d = nc.dram_tensor("B_proj", [D, R], F32, kind="ExternalInput")
    out_d = nc.dram_tensor("out", [NB, L, D], F32, kind="ExternalOutput")
    ynT_h = nc.dram_tensor("ynT_h", [NB, KC, 128, L], BF16)
    wqfc_h = nc.dram_tensor("wqfc_h", [FC, 128, KC * 128], BF16)
    wqpr_h = nc.dram_tensor("wqpr_h", [KC, 128, FC * 128], BF16)

    A = nc.alloc_sbuf_tensor
    woT = A("woT", [128, KC * D], BF16)
    mod_sb = A("mod_sb", [64, 6 * D], F32)
    rep = A("rep", [128, 4 * D], BF16)
    tm_bf = A("tm_bf", [128, D], BF16)
    vecs = A("vecs", [128, 64], F32)
    dec_bc = A("dec_bc", [128, 16], F32)
    lnb_b = A("lnb_b", [128, 8], BF16)
    gfm = A("gfm", [128, 32], F32)
    bfc_fm = A("bfc_fm", [128, FC], F32)
    id_b = A("id_b", [128, 128], BF16)
    id_f = A("id_f", [128, 128], F32)
    ones1 = A("ones1", [64, 128], F32)
    afcT = A("afcT", [128, KC * R], BF16)
    aprT = A("aprT", [128, FC * R], BF16)
    bfcT = A("bfcT", [R, FF], BF16)
    bprT = A("bprT", [R, D], BF16)
    x_sb = A("x_sb", [128, 4 * D], F32)
    tm_a = A("tm_a", [128, D], F32)
    h_tmb = A("h_tmb", [128, 2 * D], BF16)
    stat = A("stat", [128, 16], F32)
    sq_dump = A("sq_dump", [128, D], F32)
    mx = A("mx", [128, 8], F32)
    siluT = A("siluT", [128, 16], F32)
    scan_tmp = A("scan_tmp", [128, 64], F32)
    s2 = A("s2", [R, NT3], BF16)

    ps_mm = nc.alloc_psum_tensor("ps_mm", [128, 1024], F32)
    ps_sc = nc.alloc_psum_tensor("ps_sc", [128, 1024], F32)
    ps_bf = nc.alloc_psum_tensor("ps_bf", [128, 2048], BF16)
    ps_lo = nc.alloc_psum_tensor("ps_lo", [128, 512], F32)
    ps_x = nc.alloc_psum_tensor("ps_x", [128, 1024], BF16)

    sems = {}
    for name in ["s_ld", "s_gd", "s_scd", "s_mm", "s_dv", "s_sc", "s_gp",
                 "s_uv", "s_p1", "s_ydma", "s_ev", "s_fl", "s_tr", "s_q", "s_qt", "s_out", "s_ph"]:
        sems[name] = nc.alloc_semaphore(name)
    em = EM(nc, sems)

    Z = vecs[:, 52:53]
    ONE = vecs[:, 53:54]

    cm_wrT = nc.sbuf_tensor("wrT", [128, KC * D], BF16)
    wrT = cm_wrT.__enter__()
    cm_wgT = nc.sbuf_tensor("wgT", [128, KC * D], BF16)
    wgT = cm_wgT.__enter__()
    cm_p0 = [nc.sbuf_tensor("wstage", [128, 2 * D], F32),
             nc.sbuf_tensor("wmodf", [128, 6 * D], F32),
             nc.sbuf_tensor("csb", [64, D], F32)]
    wstage, wmodf, csb = [c.__enter__() for c in cm_p0]
    lora_st = wstage

    em.op("ve", lambda e: e.memset(vecs[:], 0.0))
    mz = em.op("ve", lambda e: e.memset(ones1[:], 1.0), inc="s_dv")
    em.wait("gp", "s_dv", mz)
    em.op("gp", lambda e: e.iota(vecs[:, 54:55], pattern=[[0, 1]], base=0,
                                 channel_multiplier=1, allow_small_or_imprecise_dtypes=True))
    mgp = em.op("gp", lambda e: e.iota(id_f[:], pattern=[[1, 128]], base=0, channel_multiplier=0,
                                       allow_small_or_imprecise_dtypes=True), inc="s_gp")
    em.wait("ve", "s_gp", mgp)
    em.op("ve", lambda e: e.tensor_scalar(id_f[:], id_f[:], vecs[:, 54:55], None,
                                          op0=ALU.is_equal), inc="s_dv")
    em.op("ve", lambda e: e.tensor_copy(id_b[:], id_f[:]), inc="s_dv")
    mID = em.op("ve", lambda e: e.memset(vecs[:, 53:54], 1.0), inc="s_dv")
    em.wait("te", "s_dv", mID)
    em.wait("sc", "s_dv", mID)

    def vec_to_fm(rows, dsts):
        mld = 0
        for r, dr in enumerate(rows):
            mld = em.op("sy", lambda e, r=r, dr=dr: e.dma_start(
                csb[32 * r:32 * r + 1, :], dr.ap().rearrange("(o d) -> o d", o=1)),
                inc="s_ld", k=16)
        em.wait("te", "s_ld", mld)
        for r in range(len(rows)):
            for k in range(KC):
                em.op("te", lambda e, r=r, k=k: e.transpose(
                    ps_sc[:, r * 512 + k:r * 512 + k + 1],
                    csb[32 * r:32 * r + 1, k * 128:(k + 1) * 128],
                    ones1[32 * r:32 * r + 1, 0:1]))
        mtr = em.inc_last("te", "s_tr")
        em.wait("ve", "s_tr", mtr)
        for r, dcol in enumerate(dsts):
            em.op("ve", lambda e, r=r, dcol=dcol: e.tensor_copy(
                vecs[:, dcol:dcol + 8], ps_sc[:, r * 512:r * 512 + 8]))
        mdv = em.inc_last("ve", "s_dv")
        em.wait("sy", "s_dv", mdv)
        em.wait("te", "s_dv", mdv)
        return mdv

    mv1 = vec_to_fm([decay_d, br_d], [0, 16])
    vec_to_fm([bg_d, bo_d], [24, 32])
    vec_to_fm([bpr_d], [40])

    em.wait("sc", "s_dv", mv1)
    em.op("sc", lambda e: e.activation(vecs[:, 0:8], vecs[:, 0:8], AF.Sigmoid, bias=Z))
    msc = em.inc_last("sc", "s_sc")
    em.wait("ve", "s_sc", msc)
    em.op("ve", lambda e: e.tensor_scalar(vecs[:, 8:16], vecs[:, 0:8], -1.0, 1.0,
                                          op0=ALU.mult, op1=ALU.add))
    em.op("ve", lambda e: e.tensor_copy(dec_bc[:, 0:16:2], vecs[:, 0:8]))
    em.op("ve", lambda e: e.tensor_copy(dec_bc[:, 1:16:2], vecs[:, 0:8]), inc="s_dv")

    for p in range(4):
        em.dma_fence("sy", "s_ld")
        mld = em.op("sy", lambda e, p=p: e.dma_start(
            csb[0:1, :], bfc_d.ap().rearrange("(o d) -> o d", o=4)[p:p + 1, :]), inc="s_ld", k=16)
        em.wait("te", "s_ld", mld)
        for k in range(KC):
            em.op("te", lambda e, k=k: e.transpose(
                ps_sc[:, k:k + 1], csb[0:1, k * 128:(k + 1) * 128], id_f[0:1, 0:1]))
        mtr = em.inc_last("te", "s_tr")
        em.wait("ve", "s_tr", mtr)
        mdv = em.op("ve", lambda e, p=p: e.tensor_copy(
            bfc_fm[:, p * 8:(p + 1) * 8], ps_sc[:, 0:8]), inc="s_dv")
        em.wait("sy", "s_dv", mdv)
        em.wait("te", "s_dv", mdv)

    vec_to_fm([lnw_d], [55])
    mld = em.op("sy", lambda e: e.dma_start(csb[0:1, :], lnb_d.ap().rearrange("(o d) -> o d", o=1)), inc="s_ld", k=16)
    em.wait("te", "s_ld", mld)
    for k in range(KC):
        em.op("te", lambda e, k=k: e.transpose(
            ps_sc[:, k:k + 1], csb[0:1, k * 128:(k + 1) * 128], ones1[0:1, 0:1]))
    mtr = em.inc_last("te", "s_tr")
    em.wait("ve", "s_tr", mtr)
    em.op("ve", lambda e: e.tensor_copy(stat[:, 0:8], ps_sc[:, 0:8]))
    em.op("ve", lambda e: e.tensor_copy(lnb_b[:], ps_sc[:, 0:8]))
    mdv = em.inc_last("ve", "s_dv")
    em.wait("sy", "s_dv", mdv)
    em.wait("te", "s_dv", mdv)

    prev_ev = 0
    for wd, dstT in ((wg_d, wgT), (wr_d, wrT), (wo_d, woT)):
        for r in range(KC):
            par = r % 2
            em.wait("sy", "s_dv", prev_ev)
            em.dma_fence("sy", "s_ld")
            mld = em.op("sy", lambda e, wd=wd, par=par, r=r: e.dma_start(
                wstage[:, par * D:(par + 1) * D], wd.ap()[128 * r:128 * (r + 1), :]),
                inc="s_ld", k=16)
            em.wait("te", "s_ld", mld)
            em.wait("te", "s_dv", prev_ev)
            for k in range(KC):
                em.op("te", lambda e, par=par, k=k: e.transpose(
                    ps_sc[:, (k % 4) * 128 + (k // 4) * 512:(k % 4) * 128 + (k // 4) * 512 + 128],
                    wstage[:, par * D + k * 128:par * D + (k + 1) * 128], id_f[:]))
            mtr = em.inc_last("te", "s_tr")
            em.wait("ve", "s_tr", mtr)
            for half in range(2):
                dst3 = dstT[:].rearrange("p (k e) -> p k e", k=KC)[:, half * 4:(half + 1) * 4,
                                                                  r * 128:(r + 1) * 128]
                src3 = ps_sc[:, half * 512:(half + 1) * 512].rearrange("p (k i) -> p k i", k=4)
                em.op("ve", lambda e, dst3=dst3, src3=src3: e.tensor_copy(dst3, src3))
            prev_ev = em.inc_last("ve", "s_dv")

    em.wait("te", "s_dv", prev_ev)
    for hh in range(2):
        for k in range(KC):
            em.op("te", lambda e, hh=hh, k=k: e.matmul(
                ps_sc[0:1, hh * 512:(hh + 1) * 512], lnb_b[:, k:k + 1],
                woT[:, k * D + hh * 512:k * D + (hh + 1) * 512],
                start=(k == 0), stop=(k == KC - 1)))
    mq = em.inc_last("te", "s_mm")
    em.wait("ve", "s_mm", mq)
    em.op("ve", lambda e: e.tensor_copy(csb[0:1, 0:512], ps_sc[0:1, 0:512]))
    mdv = em.op("ve", lambda e: e.tensor_copy(csb[0:1, 512:1024], ps_sc[0:1, 512:1024]),
                inc="s_dv")
    em.wait("te", "s_dv", mdv)
    for k in range(KC):
        em.op("te", lambda e, k=k: e.transpose(
            ps_sc[:, k:k + 1], csb[0:1, k * 128:(k + 1) * 128], ones1[0:1, 0:1]))
    mtr = em.inc_last("te", "s_tr")
    em.wait("ve", "s_tr", mtr)
    em.op("ve", lambda e: e.tensor_tensor(vecs[:, 32:40], vecs[:, 32:40], ps_sc[:, 0:8],
                                          op=ALU.add))
    for k in range(KC):
        em.op("ve", lambda e, k=k: e.tensor_scalar(
            woT[:, k * D:(k + 1) * D], woT[:, k * D:(k + 1) * D],
            vecs[:, 55 + k:56 + k], None, op0=ALU.mult), drain=False)
    m_wofold = em.inc_last("ve", "s_dv")
    em.wait("te", "s_dv", m_wofold)
    em.wait("sy", "s_dv", m_wofold)

    em.dma_fence("sy", "s_ld")
    mld = em.op("sy", lambda e: e.dma_start(csb[0:2, :], c_d.ap()), inc="s_ld", k=16)
    em.wait("sc", "s_ld", mld)
    msc = em.op("sc", lambda e: e.activation(tm_a[0:2, :], csb[0:2, :], AF.Silu, bias=vecs[0:2, 52:53]), inc="s_sc")
    em.wait("te", "s_sc", msc)
    em.wait("te", "s_dv", prev_ev)
    for k in range(KC):
        em.op("te", lambda e, k=k: e.transpose(
            ps_sc[:, 2 * k:2 * k + 2], tm_a[0:2, k * 128:(k + 1) * 128], id_f[0:2, 0:2]))
    mtr = em.inc_last("te", "s_tr")
    em.wait("ve", "s_tr", mtr)
    msil = em.op("ve", lambda e: e.tensor_copy(siluT[:], ps_sc[:, 0:16]), inc="s_dv")

    ev_n = {}
    mq_prev = 0
    for kk in range(KC):
        em.wait("gp", "s_mm", mq_prev)
        em.dma_fence("gp", "s_gd")
        mgd = em.op("gp", lambda e, kk=kk: e.dma_start(
            wmodf[:], wmod_d.ap()[128 * kk:128 * (kk + 1), :]), inc="s_gd", k=16)
        em.wait("te", "s_gd", mgd)
        if kk == 0:
            em.wait("te", "s_dv", msil)
        for n in range(12):
            for b in range(2):
                par = (2 * n + b) % 2
                em.wait("te", "s_dv", ev_n.get((kk, 2 * n + b - 2), 0))
                em.op("te", lambda e, par=par, kk=kk, n=n, b=b: e.matmul(
                    ps_sc[0:1, par * 512:(par + 1) * 512],
                    siluT[:, 2 * kk + b:2 * kk + b + 1],
                    wmodf[:, n * 512:(n + 1) * 512],
                    start=True, stop=True))
                mmm = em.inc_last("te", "s_mm")
                mq_prev = mmm
                em.wait("ve", "s_mm", mmm)
                brow = 32 * b
                if kk == 0:
                    ev_n[(kk, 2 * n + b)] = em.op(
                        "ve", lambda e, par=par, n=n, brow=brow: e.tensor_copy(
                            mod_sb[brow:brow + 1, n * 512:(n + 1) * 512],
                            ps_sc[0:1, par * 512:(par + 1) * 512]), inc="s_dv")
                else:
                    ev_n[(kk, 2 * n + b)] = em.op(
                        "ve", lambda e, par=par, n=n, brow=brow: e.tensor_tensor(
                            mod_sb[brow:brow + 1, n * 512:(n + 1) * 512],
                            mod_sb[brow:brow + 1, n * 512:(n + 1) * 512],
                            ps_sc[0:1, par * 512:(par + 1) * 512], op=ALU.add), inc="s_dv")
        ev_n[(kk + 1, -2)] = ev_n[(kk, 22)]
        ev_n[(kk + 1, -1)] = ev_n[(kk, 23)]

    for off in (2 * D, 5 * D):
        for brow in (0, 32):
            em.op("ve", lambda e, off=off, brow=brow: e.tensor_scalar(
                tm_a[brow:brow + 1, :], mod_sb[brow:brow + 1, off:off + D], 0.0, 2.0,
                op0=ALU.is_ge, op1=ALU.mult))
            mdv = em.op("ve", lambda e, brow=brow: e.tensor_scalar(
                tm_a[brow:brow + 1, :], tm_a[brow:brow + 1, :], -1.0, None,
                op0=ALU.add), inc="s_dv")
            em.wait("sc", "s_dv", mdv)
            msc = em.op("sc", lambda e, off=off, brow=brow: e.activation(
                sq_dump[brow:brow + 1, :], mod_sb[brow:brow + 1, off:off + D], AF.Abs,
                bias=vecs[brow:brow + 1, 52:53]), inc="s_sc")
            em.wait("ve", "s_sc", msc)
            em.op("ve", lambda e, brow=brow: e.tensor_scalar(
                sq_dump[brow:brow + 1, :], sq_dump[brow:brow + 1, :], 0.25, None, op0=ALU.max))
            em.op("ve", lambda e, off=off, brow=brow: e.tensor_tensor(
                mod_sb[brow:brow + 1, off:off + D], tm_a[brow:brow + 1, :],
                sq_dump[brow:brow + 1, :], op=ALU.mult), inc="s_dv")

    em.wait("sy", "s_dv", prev_ev)
    em.dma_fence("sy", "s_ld")
    mld = em.op("sy", lambda e: e.dma_start(lora_st[0:R, 0:D], afc_d.ap()), inc="s_ld", k=16)
    em.wait("te", "s_dv", em.cnt["s_dv"])
    em.wait("te", "s_ld", mld)
    for k in range(KC):
        em.op("te", lambda e, k=k: e.transpose(
            ps_sc[:, k * 16:(k + 1) * 16], lora_st[0:R, k * 128:(k + 1) * 128], id_f[0:R, 0:R]))
    mtr = em.inc_last("te", "s_tr")
    em.wait("ve", "s_tr", mtr)
    mdv = em.op("ve", lambda e: e.tensor_copy(afcT[:], ps_sc[:, 0:KC * R]), inc="s_dv")
    em.wait("sy", "s_dv", mdv)
    em.wait("te", "s_dv", mdv)
    for hh in range(2):
        em.dma_fence("sy", "s_ld")
        mld = em.op("sy", lambda e, hh=hh: e.dma_start(
            lora_st[0:R, 0:2 * D], apr_d.ap()[:, hh * 2 * D:(hh + 1) * 2 * D]),
            inc="s_ld", k=16)
        em.wait("te", "s_ld", mld)
        for k in range(16):
            em.op("te", lambda e, k=k: e.transpose(
                ps_sc[:, k * 16:(k + 1) * 16], lora_st[0:R, k * 128:(k + 1) * 128],
                id_f[0:R, 0:R]))
        mtr = em.inc_last("te", "s_tr")
        em.wait("ve", "s_tr", mtr)
        mdv = em.op("ve", lambda e, hh=hh: e.tensor_copy(
            aprT[:, hh * 16 * R:(hh + 1) * 16 * R], ps_sc[:, 0:16 * R]), inc="s_dv")
        em.wait("te", "s_dv", mdv)
        em.wait("sy", "s_dv", mdv)
    em.wait("gp", "s_dv", mdv)
    for src_d, dstT, nt in ((Bfc_d, bfcT, FC), (Bpr_d, bprT, KC)):
        for g4 in range(nt // 4):
            em.dma_fence("gp", "s_gd")
            em.wait("gp", "s_dv", em.cnt["s_dv"])
            mgd = 0
            for jj in range(4):
                j = g4 * 4 + jj
                mgd = em.op("gp", lambda e, src_d=src_d, j=j, jj=jj: e.dma_start(
                    wstage[:, jj * 16:(jj + 1) * 16],
                    src_d.ap()[j * 128:(j + 1) * 128, :]), inc="s_gd", k=16)
            em.wait("te", "s_gd", mgd)
            for jj in range(4):
                em.op("te", lambda e, jj=jj: e.transpose(
                    ps_lo[0:128, jj * 128:(jj + 1) * 128][0:R, :],
                    wstage[:, jj * 16:(jj + 1) * 16], id_f[:]))
            mtr = em.inc_last("te", "s_tr")
            em.wait("ve", "s_tr", mtr)
            mdv = em.op("ve", lambda e, dstT=dstT, g4=g4: e.tensor_copy(
                dstT[0:R, g4 * 512:(g4 + 1) * 512], ps_lo[0:R, :]), inc="s_dv")
            em.wait("te", "s_dv", mdv)

    em.barrier()
    em.realize()
    for c in reversed(cm_p0):
        c.__exit__(None, None, None)
    if max_phase <= 0:
        return _close_truncated(nc, em)


    def actT_full(t, k):
        return t[:, k * NT3:(k + 1) * NT3]

    def build_rep(slot, b, off, bias_col, prev_mm):
        em.wait("te", "s_sc", prev_mm)
        for hh in range(2):
            mmm = em.op("te", lambda e, b=b, off=off, hh=hh: e.matmul(
                ps_mm[:, hh * 512:(hh + 1) * 512], ones1[32 * b:32 * b + 1, :],
                mod_sb[32 * b:32 * b + 1, off + hh * 512:off + (hh + 1) * 512],
                start=True, stop=True), inc="s_mm")
        em.wait("sc", "s_mm", mmm)
        for hh in range(2):
            msc = em.op("sc", lambda e, slot=slot, hh=hh, bias_col=bias_col: e.activation(
                rep[:, slot * D + hh * 512:slot * D + (hh + 1) * 512],
                ps_mm[:, hh * 512:(hh + 1) * 512], AF.Identity, bias=bias_col), inc="s_sc")
        return msc

    pbf_free = {0: ("s_dv", 0), 1: ("s_dv", 0)}
    x_free = [(0, 0)] * 4

    def emit_ln_modulate(src_ap, par_h, rep_mul, rep_add, mul2, add2,
                         m_data, data_sem):
        em.wait("ve", data_sem, m_data)
        em.op("ve", lambda e: e.reduce_sum(stat[:, 0:1], src_ap, axis=AX.X))
        em.wait("sc", data_sem, m_data)
        msq = em.op("sc", lambda e: e.activation(
            sq_dump[:], src_ap, AF.Square, bias=Z, accum_out=stat[:, 1:2]), inc="s_sc")
        em.op("ve", lambda e: e.tensor_scalar(stat[:, 2:3], stat[:, 0:1], 1.0 / D, None, op0=ALU.mult))
        em.wait("ve", "s_sc", msq)
        em.op("ve", lambda e: e.tensor_scalar(stat[:, 4:5], stat[:, 1:2], 1.0 / D, EPS,
                                              op0=ALU.mult, op1=ALU.add))
        em.op("ve", lambda e: e.tensor_tensor(stat[:, 5:6], stat[:, 2:3], stat[:, 2:3], op=ALU.mult))
        mdv = em.op("ve", lambda e: e.tensor_tensor(stat[:, 4:5], stat[:, 4:5], stat[:, 5:6],
                                                    op=ALU.subtract), inc="s_dv")
        em.wait("sc", "s_dv", mdv)
        msq = em.op("sc", lambda e: e.activation(stat[:, 5:6], stat[:, 4:5], AF.Sqrt, bias=Z),
                    inc="s_sc")
        em.wait("ve", "s_sc", msq)
        em.op("ve", lambda e: e.reciprocal(stat[:, 3:4], stat[:, 5:6]))
        mnm = em.op("ve", lambda e: e.tensor_scalar(
            stat[:, 6:7], stat[:, 2:3], stat[:, 3:4], -1.0,
            op0=ALU.mult, op1=ALU.mult), inc="s_dv")
        em.wait("sc", "s_dv", mnm)
        mcons = em.op("sc", lambda e: e.activation(
            tm_bf[:], src_ap, AF.Identity, scale=stat[:, 3:4], bias=stat[:, 6:7]),
            inc="s_sc")
        em.wait("ve", "s_sc", mcons)
        em.op("ve", lambda e: e.tensor_tensor(tm_bf[:], tm_bf[:], rep_mul, op=ALU.mult))
        mh = em.op("ve", lambda e, par_h=par_h: e.tensor_tensor(
            h_tmb[:, par_h * D:(par_h + 1) * D], tm_bf[:], rep_add, op=ALU.add), inc="s_dv")
        return (mnm, mcons), mh

    def emit_act_transpose(par_h, mh, dstT, col0):
        em.wait("te", "s_dv", mh)
        em.wait("te", pbf_free[par_h][0], pbf_free[par_h][1])
        for k in range(KC):
            em.op("te", lambda e, par_h=par_h, k=k: e.transpose(
                ps_bf[:, par_h * 1024 + k * 128:par_h * 1024 + (k + 1) * 128],
                h_tmb[:, par_h * D + k * 128:par_h * D + (k + 1) * 128], id_b[:]))
        mtr = em.inc_last("te", "s_tr")
        em.wait("ve", "s_tr", mtr)
        dst3 = dstT[:].rearrange("p (k t) -> p k t", k=KC)[:, :, col0:col0 + 128]
        src3 = ps_bf[:, par_h * 1024:(par_h + 1) * 1024].rearrange("p (k i) -> p k i", k=KC)
        mev = em.op("ve", lambda e, dst3=dst3, src3=src3: e.tensor_copy(dst3, src3), inc="s_dv")
        pbf_free[par_h] = ("s_dv", mev)
        return mev

    cm_p2 = [nc.sbuf_tensor("y_fm", [128, 2 * 128 * 16], BF16),
             nc.sbuf_tensor("u_st", [128, 4 * 128 * 16], BF16),
             nc.sbuf_tensor("v_st", [128, 4 * 128 * 16], BF16),
             nc.sbuf_tensor("y_tm", [128, NB * D], BF16),
             nc.sbuf_tensor("ystage", [128, D], BF16),
             nc.sbuf_tensor("actT1", [128, KC * NT1], BF16),
             nc.sbuf_tensor("qload", [128, 2 * D], F32),
             nc.sbuf_tensor("qa", [128, D], F32),
             nc.sbuf_tensor("qr", [128, D], F32),
             nc.sbuf_tensor("qt", [128, D], F32),
             nc.sbuf_tensor("qsgn", [128, 2 * D], BF16),
             nc.sbuf_tensor("qb", [128, 2 * D], BF16),
             nc.sbuf_tensor("qstage", [128, 2 * D], BF16)]
    (y_fm, u_st, v_st, y_tm, ystage, actT1, qload, qa, qr, qt, qsgn, qb, qstage) = \
        [c.__enter__() for c in cm_p2]

    def ring_slice(chunk, c, b):
        base = (chunk % 4) * 2048 + c * 2 + b
        return base

    mrep = 0
    for b in range(NB):
        mrep = build_rep(2 * b + 0, b, D, ONE, mrep)
        mrep = build_rep(2 * b + 1, b, 0, Z, mrep)
    em.wait("ve", "s_sc", mrep)

    p1m = {}
    zev = {0: mrep, 1: mrep}
    zuse = {}

    def build_p1_tile(t1):
        items = []
        t0 = t1 * NT1

        def ln_items(b, s):
            slot = 2 * b + s
            src_ap = x_sb[:, slot * D:(slot + 1) * D]
            par_h = slot % 2

            def i_load(em, b=b, s=s, slot=slot, t0=t0, src_ap=src_ap):
                em.wait("sy", "s_p1", x_free[slot][0])
                em.wait("sy", "s_sc", x_free[slot][1])
                em.dma_fence("sy", "s_ld")
                p1m[('xl', slot)] = em.op("sy", lambda e: e.dma_start(
                    src_ap, x_d.ap()[b, t0 + 128 * s:t0 + 128 * (s + 1), :]),
                    inc="s_ld", k=16)

            def i_st1(em, slot=slot, src_ap=src_ap):
                em.wait("ve", "s_ld", p1m[('xl', slot)])
                em.op("ve", lambda e: e.reduce_sum(stat[:, 0:1], src_ap, axis=AX.X))
                em.wait("sc", "s_ld", p1m[('xl', slot)])
                p1m['sq'] = em.op("sc", lambda e: e.activation(
                    sq_dump[:], src_ap, AF.Square, bias=Z, accum_out=stat[:, 1:2]),
                    inc="s_sc")

            def i_st2(em):
                em.op("ve", lambda e: e.tensor_scalar(stat[:, 2:3], stat[:, 0:1],
                                                      1.0 / D, None, op0=ALU.mult))
                em.wait("ve", "s_sc", p1m['sq'])
                em.op("ve", lambda e: e.tensor_scalar(stat[:, 4:5], stat[:, 1:2],
                                                      1.0 / D, EPS, op0=ALU.mult, op1=ALU.add))
                em.op("ve", lambda e: e.tensor_tensor(stat[:, 5:6], stat[:, 2:3],
                                                      stat[:, 2:3], op=ALU.mult))
                p1m['var'] = em.op("ve", lambda e: e.tensor_tensor(
                    stat[:, 4:5], stat[:, 4:5], stat[:, 5:6], op=ALU.subtract), inc="s_p1")

            def i_st3(em):
                em.wait("sc", "s_p1", p1m['var'])
                p1m['sd'] = em.op("sc", lambda e: e.activation(
                    stat[:, 5:6], stat[:, 4:5], AF.Sqrt, bias=Z), inc="s_sc")

            def i_st4(em, slot=slot):
                em.wait("ve", "s_sc", p1m['sd'])
                em.op("ve", lambda e: e.reciprocal(stat[:, 3:4], stat[:, 5:6]))
                p1m['nm'] = em.op("ve", lambda e: e.tensor_scalar(
                    stat[:, 6:7], stat[:, 2:3], stat[:, 3:4], -1.0,
                    op0=ALU.mult, op1=ALU.mult), inc="s_p1")

            def i_xn(em, slot=slot, src_ap=src_ap):
                em.wait("sc", "s_p1", p1m['nm'])
                p1m[('xn', slot)] = em.op("sc", lambda e: e.activation(
                    tm_bf[:], src_ap, AF.Identity, scale=stat[:, 3:4],
                    bias=stat[:, 6:7]), inc="s_sc")

            def i_mod(em, b=b, slot=slot, par_h=par_h):
                em.wait("ve", "s_sc", p1m[('xn', slot)])
                em.op("ve", lambda e: e.tensor_tensor(
                    tm_bf[:], tm_bf[:], rep[:, (2 * b) * D:(2 * b + 1) * D], op=ALU.mult))
                mh = em.op("ve", lambda e: e.tensor_tensor(
                    h_tmb[:, par_h * D:(par_h + 1) * D], tm_bf[:],
                    rep[:, (2 * b + 1) * D:(2 * b + 2) * D], op=ALU.add), inc="s_p1")
                p1m[('h', slot)] = mh
                x_free[slot] = (p1m['nm'], p1m[('xn', slot)])

            def i_tr(em, slot=slot, par_h=par_h, s=s):
                em.wait("te", "s_p1", p1m[('h', slot)])
                em.wait("te", pbf_free[par_h][0], pbf_free[par_h][1])
                for k in range(KC):
                    em.op("te", lambda e, par_h=par_h, k=k: e.transpose(
                        ps_bf[:, par_h * 1024 + k * 128:par_h * 1024 + (k + 1) * 128],
                        h_tmb[:, par_h * D + k * 128:par_h * D + (k + 1) * 128], id_b[:]))
                p1m[('tr', slot)] = em.inc_last("te", "s_tr")

            def i_ev(em, slot=slot, par_h=par_h, s=s):
                em.wait("ve", "s_tr", p1m[('tr', slot)])
                dst3 = actT1[:].rearrange("p (k t) -> p k t", k=KC)[:, :, s * 128:(s + 1) * 128]
                src3 = ps_bf[:, par_h * 1024:(par_h + 1) * 1024].rearrange(
                    "p (k i) -> p k i", k=KC)
                mev = em.op("ve", lambda e, dst3=dst3, src3=src3: e.tensor_copy(dst3, src3),
                            inc="s_p1")
                pbf_free[par_h] = ("s_p1", mev)
                p1m[('ev', slot)] = mev
            return [i_load, i_st1, i_st2, i_st3, i_st4, i_xn, i_mod, i_tr, i_ev]

        def z_items(b, c, t1=t1):
            pz = c % 2
            chunk0 = 2 * t1

            def zi_mm(em, b=b, c=c, pz=pz):
                em.wait("te", "s_p1", p1m[('ev', 2 * b)])
                em.wait("te", "s_p1", p1m[('ev', 2 * b + 1)])
                em.wait("te", "s_sc", zev.get(pz, 0))
                for k in range(KC):
                    em.op("te", lambda e, c=c, k=k, pz=pz: e.matmul(
                        ps_mm[:, pz * 512:pz * 512 + NT1],
                        wgT[:, k * D + c * 128:k * D + (c + 1) * 128],
                        actT1[:, k * NT1:(k + 1) * NT1],
                        start=(k == 0), stop=(k == KC - 1)))
                p1m[('zmm', pz)] = em.inc_last("te", "s_mm")

            def zi_ev(em, c=c, pz=pz):
                em.wait("sc", "s_mm", p1m[('zmm', pz)])
                em.wait("sc", "s_p1", zuse.get(pz, 0))
                zev[pz] = em.op("sc", lambda e, pz=pz, c=c: e.activation(
                    tm_a[:, pz * 512:pz * 512 + NT1], ps_mm[:, pz * 512:pz * 512 + NT1],
                    AF.Sigmoid, bias=vecs[:, 24 + c:25 + c]), inc="s_sc")

            def zi_u(em, b=b, c=c, pz=pz, chunk0=chunk0):
                em.wait("ve", "s_sc", zev[pz])
                b0 = (chunk0 % 4) * 2048 + c * 2 + b
                b1 = ((chunk0 + 1) % 4) * 2048 + c * 2 + b
                em.op("ve", lambda e, b0=b0, pz=pz: e.tensor_scalar(
                    u_st[:, b0:b0 + 16 * 127 + 1:16], tm_a[:, pz * 512:pz * 512 + 128],
                    vecs[:, 8 + c:9 + c], None, op0=ALU.mult), drain=False)
                em.op("ve", lambda e, b1=b1, pz=pz: e.tensor_scalar(
                    u_st[:, b1:b1 + 16 * 127 + 1:16],
                    tm_a[:, pz * 512 + 128:pz * 512 + 256],
                    vecs[:, 8 + c:9 + c], None, op0=ALU.mult))

            def zi_v(em, b=b, c=c, pz=pz, chunk0=chunk0):
                b0 = (chunk0 % 4) * 2048 + c * 2 + b
                b1 = ((chunk0 + 1) % 4) * 2048 + c * 2 + b
                em.op("ve", lambda e, b0=b0, c=c: e.scalar_tensor_tensor(
                    v_st[:, b0:b0 + 16 * 127 + 1:16],
                    actT1[:, c * NT1:c * NT1 + 128],
                    vecs[:, 16 + c:17 + c],
                    u_st[:, b0:b0 + 16 * 127 + 1:16],
                    op0=ALU.subtract, op1=ALU.mult), drain=False)
                muv = em.op("ve", lambda e, b1=b1, c=c: e.scalar_tensor_tensor(
                    v_st[:, b1:b1 + 16 * 127 + 1:16],
                    actT1[:, c * NT1 + 128:(c + 1) * NT1],
                    vecs[:, 16 + c:17 + c],
                    u_st[:, b1:b1 + 16 * 127 + 1:16],
                    op0=ALU.subtract, op1=ALU.mult), inc="s_p1")
                zuse[pz] = muv
            return [zi_mm, zi_ev, zi_u, zi_v]

        for b in range(NB):
            for s in range(2):
                items += ln_items(b, s)
            for c in range(KC):
                items += z_items(b, c)
        return items

    for t1 in range(min(2, NTIL1)):
        for it in build_p1_tile(t1):
            it(em)
    p1q = []
    p1_need = {}
    for t1 in range(2, NTIL1):
        w = 256 * t1 - 255
        for it in build_p1_tile(t1):
            p1q.append((w, it))
        p1_need[2 * t1] = len(p1q)
    p1pos = [0]

    def drain_p1(t, n):
        while n > 0 and p1pos[0] < len(p1q) and p1q[p1pos[0]][0] <= t:
            p1q[p1pos[0]][1](em)
            p1pos[0] += 1
            n -= 1

    def gran_src(g):
        if g < FC:
            return wfc_d.ap()[g * 128:(g + 1) * 128, :]
        gg = g - FC
        r, qf = gg // 4, gg % 4
        return wpr_d.ap()[r * 128:(r + 1) * 128, qf * D:(qf + 1) * D]

    qitems = []
    qm = {}

    def qi_init(em):
        qm[('red', -1)] = em.op("ve", lambda e: e.memset(mx[:, 0:2], 0.0), inc="s_q")
    qitems.append(qi_init)

    for g in range(64):
        def qi_mload(em, g=g):
            if g >= 2:
                em.wait("gp", "s_q", qm[('red', g - 2)])
            em.dma_fence("gp", "s_gd")
            qm[('mld', g)] = em.op("gp", lambda e, g=g: e.dma_start(
                qload[:, (g % 2) * D:(g % 2 + 1) * D], gran_src(g)), inc="s_gd", k=16)

        def qi_mred(em, g=g):
            em.wait("ve", "s_gd", qm[('mld', g)])
            em.op("ve", lambda e, g=g: e.tensor_reduce(
                mx[:, 2:3], qload[:, (g % 2) * D:(g % 2 + 1) * D], axis=AX.X,
                op=ALU.max, apply_absolute_value=True))
            col = 0 if g < FC else 1
            qm[('red', g)] = em.op("ve", lambda e, col=col: e.tensor_tensor(
                mx[:, col:col + 1], mx[:, col:col + 1], mx[:, 2:3], op=ALU.max), inc="s_q")
        qitems += [qi_mload, qi_mred]

    def qi_fin(em):
        em.wait("te", "s_q", qm[('red', 63)])
        em.op("te", lambda e: e.transpose(ps_lo[:, 0:128][0:1, :], mx[:, 0:1], id_f[:]))
        qm['fint'] = em.op("te", lambda e: e.transpose(
            ps_lo[:, 128:256][0:1, :], mx[:, 1:2], id_f[:]), inc="s_qt")
        em.wait("ve", "s_qt", qm['fint'])
        em.op("ve", lambda e: e.tensor_reduce(mx[0:1, 4:5], ps_lo[0:1, 0:128], axis=AX.X, op=ALU.max))
        em.op("ve", lambda e: e.tensor_reduce(mx[0:1, 6:7], ps_lo[0:1, 128:256], axis=AX.X, op=ALU.max))
        em.op("ve", lambda e: e.reciprocal(mx[0:1, 5:6], mx[0:1, 4:5]))
        em.op("ve", lambda e: e.reciprocal(mx[0:1, 7:8], mx[0:1, 6:7]))
        em.op("ve", lambda e: e.tensor_scalar(mx[0:1, 4:5], mx[0:1, 4:5],
                                              1.0 / 127.0, None, op0=ALU.mult))
        em.op("ve", lambda e: e.tensor_scalar(mx[0:1, 6:7], mx[0:1, 6:7],
                                              1.0 / 127.0, None, op0=ALU.mult))
        em.op("ve", lambda e: e.tensor_scalar(mx[0:1, 5:6], mx[0:1, 5:6],
                                              127.0, None, op0=ALU.mult))
        qm['finv'] = em.op("ve", lambda e: e.tensor_scalar(
            mx[0:1, 7:8], mx[0:1, 7:8], 127.0, None, op0=ALU.mult), inc="s_q")
        em.wait("te", "s_q", qm['finv'])
        for i, mx_col in enumerate((4, 5, 6, 7)):
            em.op("te", lambda e, i=i, mx_col=mx_col: e.matmul(
                ps_lo[:, 256 + i:257 + i], ones1[0:1, :], mx[0:1, mx_col:mx_col + 1],
                start=True, stop=True))
        mtr = em.inc_last("te", "s_qt")
        em.wait("ve", "s_qt", mtr)
        for i, sc_col in enumerate((48, 49, 50, 51)):
            em.op("ve", lambda e, i=i, sc_col=sc_col: e.tensor_copy(
                vecs[:, sc_col:sc_col + 1], ps_lo[:, 256 + i:257 + i]))
        qm['fbc'] = em.inc_last("ve", "s_q")
    qitems.append(qi_fin)

    for g in range(64):
        iscale = vecs[:, 49:50] if g < FC else vecs[:, 51:52]
        mscale = vecs[:, 48:49] if g < FC else vecs[:, 50:51]

        def qi_load(em, g=g):
            em.wait("gp", "s_q",
                    qm[('wq', g - 2)] if g >= 2 else qm[('red', {0: 62, 1: 63}[g % 2])])
            em.dma_fence("gp", "s_gd")
            qm[('ld', g)] = em.op("gp", lambda e, g=g: e.dma_start(
                qload[:, (g % 2) * D:(g % 2 + 1) * D], gran_src(g)), inc="s_gd", k=16)

        def qi_q(em, g=g, iscale=iscale):
            em.wait("sc", "s_gd", qm[('ld', g)])
            if g == 0:
                em.wait("sc", "s_q", qm['fbc'])
            else:
                em.wait("sc", "s_q", qm[('wq', g - 1)])
            em.op("sc", lambda e, g=g, iscale=iscale: e.activation(
                qr[:], qload[:, (g % 2) * D:(g % 2 + 1) * D], AF.Identity,
                scale=iscale, bias=Z), drain=True)

        def qi_abs(em, g=g):
            em.op("sc", lambda e: e.activation(qa[:], qr[:], AF.Abs, bias=Z))
            qm[('sgn', g)] = em.op("sc", lambda e, g=g: e.activation(
                qsgn[:, (g % 2) * D:(g % 2 + 1) * D], qr[:], AF.Sign, bias=Z), inc="s_sc")

        def qi_b0(em, g=g):
            em.wait("ve", "s_sc", qm[('sgn', g)])
            em.op("ve", lambda e: e.tensor_scalar(qt[:], qa[:], 63.5, 64.0,
                                                  op0=ALU.is_ge, op1=ALU.mult))

        def qi_b0b(em, g=g):
            em.op("ve", lambda e: e.tensor_tensor(qr[:], qa[:], qt[:], op=ALU.subtract))

        def mk_bit(j):
            def qi_bj(em, j=j):
                em.op("ve", lambda e, j=j: e.tensor_scalar(
                    qt[:], qr[:], j - 0.5, float(j), op0=ALU.is_ge, op1=ALU.mult))
            def qi_bjs(em):
                em.op("ve", lambda e: e.tensor_tensor(qr[:], qr[:], qt[:], op=ALU.subtract))
            return [qi_bj, qi_bjs]

        def qi_rnd(em, g=g):
            qm[('rnd', g)] = em.op("ve", lambda e: e.tensor_tensor(
                qt[:], qa[:], qr[:], op=ALU.subtract), inc="s_q")

        def qi_sca(em, g=g, mscale=mscale):
            em.wait("sc", "s_q", qm[('rnd', g)])
            qm[('sca', g)] = em.op("sc", lambda e, mscale=mscale: e.activation(
                qr[:], qt[:], AF.Identity, scale=mscale, bias=Z), inc="s_sc",
                drain=True)

        def qi_sgm(em, g=g):
            em.wait("ve", "s_sc", qm[('sca', g)])
            qm[('wq', g)] = em.op("ve", lambda e, g=g: e.tensor_tensor(
                qb[:, (g % 2) * D:(g % 2 + 1) * D], qr[:],
                qsgn[:, (g % 2) * D:(g % 2 + 1) * D], op=ALU.mult), inc="s_q")

        def qi_tr(em, g=g):
            em.wait("te", "s_q", qm[('wq', g)])
            em.wait("te", "s_sc", qm.get(('ev', g - 1), 0))
            for k in range(KC):
                em.op("te", lambda e, g=g, k=k: e.transpose(
                    ps_x[:, k * 128:(k + 1) * 128],
                    qb[:, (g % 2) * D + k * 128:(g % 2) * D + (k + 1) * 128], id_b[:]))
            qm[('tr', g)] = em.inc_last("te", "s_qt")

        def qi_ev(em, g=g):
            em.wait("sc", "s_qt", qm[('tr', g)])
            if g >= 2:
                em.wait("sc", "s_gd", qm[('out', g - 2)])
            qm[('ev', g)] = em.op("sc", lambda e, g=g: e.activation(
                qstage[:, (g % 2) * D:(g % 2 + 1) * D], ps_x[:], AF.Identity,
                bias=Z), inc="s_sc")

        def qi_out(em, g=g):
            em.wait("gp", "s_sc", qm[('ev', g)])
            em.dma_fence("gp", "s_gd")
            if g < FC:
                dst = wqfc_h.ap()[g, :, :]
            else:
                gg = g - FC
                r, qf = gg // 4, gg % 4
                dst = wqpr_h.ap()[r, :, qf * D:(qf + 1) * D]
            qm[('out', g)] = em.op("gp", lambda e, g=g, dst=dst: e.dma_start(
                dst, qstage[:, (g % 2) * D:(g % 2 + 1) * D]), inc="s_gd", k=16)

        bit_items = []
        for j in (32, 16, 8, 4, 2, 1):
            bit_items += mk_bit(j)
        qitems += ([qi_load, qi_q, qi_abs, qi_b0, qi_b0b] + bit_items +
                   [qi_rnd, qi_sca, qi_sgm, qi_tr, qi_ev, qi_out])

    qpos = [0]

    def drain_quant(n):
        while n > 0 and qpos[0] < len(qitems):
            qitems[qpos[0]](em)
            qpos[0] += 1
            n -= 1

    base_dv = em.cnt["s_dv"]
    em.op("ve", lambda e: e.tensor_copy(y_fm[:, 0:8], v_st[:, 0:8]), inc="s_dv")
    em.op("ve", lambda e: e.tensor_copy(y_fm[:, 8:16], v_st[:, 8:16]), inc="s_dv")

    flush_ev = {}
    ydma_mark = {}

    def emit_flush(B):
        buf = B % 2
        em.wait("te", "s_dv", base_dv + 2 * (B + 1) * 128)
        em.wait("te", "s_fl", flush_ev.get(B - 1, 0))
        em.wait("te", pbf_free[0][0], pbf_free[0][1])
        em.wait("te", pbf_free[1][0], pbf_free[1][1])
        for b0 in range(NB):
            bank = b0
            for c in range(KC):
                st = buf * 2048 + c * 2 + b0
                em.op("te", lambda e, bank=bank, c=c, st=st: e.transpose(
                    ps_bf[:, bank * 1024 + c * 128:bank * 1024 + (c + 1) * 128],
                    y_fm[:, st:st + 16 * 127 + 1:16], id_b[:]))
            mtr = em.inc_last("te", "s_tr")
            em.wait("sc", "s_tr", mtr)
            em.wait("sc", "s_ydma", ydma_mark.get(B - 1, 0))
            mev = em.op("sc", lambda e, b0=b0, bank=bank: e.copy(
                y_tm[:, b0 * 1024:(b0 + 1) * 1024],
                ps_bf[:, bank * 1024:(bank + 1) * 1024]), inc="s_ev", drain=True)
            ytm = y_tm[:, b0 * 1024:(b0 + 1) * 1024]
            em.wait("ve", "s_ev", mev)
            em.op("ve", lambda e, ytm=ytm: e.reduce_sum(stat[:, 8:9], ytm, axis=AX.X))
            msq = em.op("sc", lambda e, ytm=ytm: e.activation(
                sq_dump[:], ytm, AF.Square, bias=Z, accum_out=stat[:, 9:10]), inc="s_sc")
            em.op("ve", lambda e: e.tensor_scalar(stat[:, 10:11], stat[:, 8:9], 1.0 / D,
                                                  None, op0=ALU.mult), drain=False)
            em.wait("ve", "s_sc", msq)
            em.op("ve", lambda e: e.tensor_scalar(stat[:, 12:13], stat[:, 9:10], 1.0 / D,
                                                  EPS, op0=ALU.mult, op1=ALU.add))
            em.op("ve", lambda e: e.tensor_tensor(stat[:, 13:14], stat[:, 10:11],
                                                  stat[:, 10:11], op=ALU.mult))
            mdv = em.op("ve", lambda e: e.tensor_tensor(stat[:, 12:13], stat[:, 12:13],
                                                        stat[:, 13:14], op=ALU.subtract),
                        inc="s_fl", drain=True)
            em.wait("sc", "s_fl", mdv)
            msq = em.op("sc", lambda e: e.activation(stat[:, 13:14], stat[:, 12:13],
                                                     AF.Sqrt, bias=Z), inc="s_sc")
            em.wait("ve", "s_sc", msq)
            em.op("ve", lambda e: e.reciprocal(stat[:, 11:12], stat[:, 13:14]))
            mdv = em.op("ve", lambda e: e.tensor_scalar(
                stat[:, 14:15], stat[:, 10:11], stat[:, 11:12], -1.0,
                op0=ALU.mult, op1=ALU.mult), inc="s_fl")
            em.wait("sc", "s_fl", mdv)
            mno = em.op("sc", lambda e, b0=b0, ytm=ytm: e.activation(
                y_tm[:, b0 * 1024:(b0 + 1) * 1024], ytm, AF.Identity,
                scale=stat[:, 11:12], bias=stat[:, 14:15]), inc="s_sc", drain=False)
            em.wait("te", "s_sc", mno)
            for k in range(KC):
                em.op("te", lambda e, bank=bank, b0=b0, k=k: e.transpose(
                    ps_bf[:, bank * 1024 + k * 128:bank * 1024 + (k + 1) * 128],
                    y_tm[:, b0 * 1024 + k * 128:b0 * 1024 + (k + 1) * 128], id_b[:]))
            mtr = em.inc_last("te", "s_tr")
            em.wait("ve", "s_tr", mtr)
            em.wait("ve", "s_ydma", em.cnt["s_ydma"])
            mev = em.op("ve", lambda e, bank=bank: e.tensor_copy(
                ystage[:], ps_bf[:, bank * 1024:(bank + 1) * 1024]),
                inc="s_fl", drain=False)
            pbf_free[bank] = ("s_fl", mev)
            em.wait("sy", "s_fl", mev)
            em.dma_fence("sy", "s_ydma")
            dstv = ynT_h.ap()[b0, :, :, B * 128:(B + 1) * 128].rearrange("k p i -> p k i")
            srcv = ystage[:].rearrange("p (k i) -> p k i", k=KC)
            em.op("sy", lambda e, dstv=dstv, srcv=srcv: e.dma_start(dstv, srcv),
                  inc="s_ydma", k=16)
        flush_ev[B] = mev
        ydma_mark[B] = em.cnt["s_ydma"]

    m_mm_h = {0: 0, 1: 0}
    for t in range(1, L):
        B = t // 128
        buf = B % 2
        pbuf = ((t - 1) // 128) % 2
        par = t % 2
        po = t % 128
        ppo = (t - 1) % 128
        if po == 0:
            emit_flush(B - 1)
            assert p1pos[0] >= p1_need.get(B, 0), (B, p1pos[0])
        em.wait("te", "s_dv", base_dv + 2 * t)
        prev0 = pbuf * 2048 + ppo * 16
        for e_ in range(KC):
            hb = (e_ // 4) * 512
            fn0 = lambda e, e_=e_, hb=hb, prev0=prev0: e.matmul(
                ps_sc[:, hb + (e_ % 4) * 2:hb + (e_ % 4) * 2 + 2],
                wrT[:, 0 * D + e_ * 128:0 * D + (e_ + 1) * 128],
                y_fm[:, prev0 + 0:prev0 + 2],
                start=(e_ % 4 == 0), stop=False)
            em.op("te", fn0)
            for k in range(1, KC):
                fn = lambda e, e_=e_, k=k, hb=hb, prev0=prev0: e.matmul(
                    ps_sc[:, hb + (e_ % 4) * 2:hb + (e_ % 4) * 2 + 2],
                    wrT[:, k * D + e_ * 128:k * D + (e_ + 1) * 128],
                    y_fm[:, prev0 + k * 2:prev0 + k * 2 + 2],
                    start=False, stop=(k == KC - 1 and e_ % 4 == 3))
                if k == KC - 1 and e_ in (3, KC - 1):
                    m_mm_h[e_ // 4] = em.op("te", fn, inc="s_mm")
                else:
                    em.op("te", fn)
        if po == 0:
            em.wait("ve", "s_fl", flush_ev.get(B - 2, 0))
        cur0 = buf * 2048 + po * 16
        uv0 = (B % 4) * 2048 + po * 16
        if t == 1:
            em.op("ve", lambda e, prev0=prev0: e.tensor_tensor(
                scan_tmp[:, 32:48], y_fm[:, prev0:prev0 + 16], dec_bc[:], op=ALU.mult))
            em.op("ve", lambda e, uv0=uv0: e.tensor_tensor(
                scan_tmp[:, 48:64], scan_tmp[:, 32:48], v_st[:, uv0:uv0 + 16],
                op=ALU.add))
        em.wait("ve", "s_mm", m_mm_h[0])
        em.op("ve", lambda e, uv0=uv0: e.tensor_tensor(
            scan_tmp[:, 0:8], ps_sc[:, 0:8],
            u_st[:, uv0:uv0 + 8], op=ALU.mult))
        em.op("ve", lambda e, cur0=cur0: e.tensor_tensor(
            y_fm[:, cur0:cur0 + 8], scan_tmp[:, 48:56], scan_tmp[:, 0:8],
            op=ALU.subtract), inc="s_dv", drain=False)
        em.wait("ve", "s_mm", m_mm_h[1])
        em.op("ve", lambda e, uv0=uv0: e.tensor_tensor(
            scan_tmp[:, 8:16], ps_sc[:, 512:520],
            u_st[:, uv0 + 8:uv0 + 16], op=ALU.mult))
        em.op("ve", lambda e, cur0=cur0: e.tensor_tensor(
            y_fm[:, cur0 + 8:cur0 + 16], scan_tmp[:, 56:64], scan_tmp[:, 8:16],
            op=ALU.subtract), inc="s_dv")
        if t < L - 1:
            nt = t + 1
            nuv0 = ((nt // 128) % 4) * 2048 + (nt % 128) * 16
            em.op("ve", lambda e, cur0=cur0: e.tensor_tensor(
                scan_tmp[:, 32:48], y_fm[:, cur0:cur0 + 16], dec_bc[:], op=ALU.mult))
            em.op("ve", lambda e, nuv0=nuv0: e.tensor_tensor(
                scan_tmp[:, 48:64], scan_tmp[:, 32:48], v_st[:, nuv0:nuv0 + 16],
                op=ALU.add), drain=False)
        if t >= 2:
            drain_quant(1)
        if t % 2 == 1:
            drain_p1(t, 1)
    emit_flush(NCH - 1)
    drain_quant(10 ** 9)
    m_wq_all = em.cnt["s_gd"]
    m_y_all = em.cnt["s_ydma"]

    em.barrier()
    em.realize()
    for c in reversed(cm_p2):
        c.__exit__(None, None, None)
    cm_wgT.__exit__(None, None, None)
    cm_wrT.__exit__(None, None, None)
    if max_phase <= 2:
        return _close_truncated(nc, em)

    cm_p3 = [nc.sbuf_tensor("actT", [128, KC * NT3], BF16),
             nc.sbuf_tensor("ynT", [128, 2 * KC * NT3], BF16),
             nc.sbuf_tensor("att_tm", [128, 4 * D], BF16),
             nc.sbuf_tensor("att_fm", [128, KC * NT3], BF16),
             nc.sbuf_tensor("h3", [128, FC * NT3], BF16),
             nc.sbuf_tensor("wqf_sb", [128, 2 * 2 * KC * 128], BF16),
             nc.sbuf_tensor("wqp_sb", [128, 2 * FC * 128], BF16)]
    (actT, ynT, att_tm, att_fm, h3, wqf_sb, wqp_sb) = [c.__enter__() for c in cm_p3]

    out_mark = [0, 0, 0, 0]
    yin_mark = [0]
    mm_ev = {0: 0, 1: 0}
    wqf_mark = {}
    wqp_mark = {}
    wqf_free = {0: 0, 1: 0}
    wqp_free = {0: 0, 1: 0}
    first_wq = [True]

    def emit_wqf_load(jg):
        pz = jg % 2
        if first_wq[0]:
            em.wait("gp", "s_gd", m_wq_all)
            em.wait("gp", "s_ydma", m_y_all)
            first_wq[0] = False
        em.wait("gp", "s_mm", wqf_free[pz])
        em.dma_fence("gp", "s_gd")
        dstv = wqf_sb[:, pz * 2 * KC * 128:(pz + 1) * 2 * KC * 128].rearrange(
            "p (j c) -> p j c", j=2)
        srcv = wqfc_h.ap()[jg * 2:(jg + 1) * 2, :, :].rearrange("j p c -> p j c")
        wqf_mark[jg] = em.op("gp", lambda e, dstv=dstv, srcv=srcv: e.dma_start(
            dstv, srcv), inc="s_gd", k=16)

    def emit_wqp_load(m):
        pz = m % 2
        em.wait("gp", "s_mm", wqp_free[pz])
        em.dma_fence("gp", "s_gd")
        wqp_mark[m] = em.op("gp", lambda e, m=m, pz=pz: e.dma_start(
            wqp_sb[:, pz * FC * 128:(pz + 1) * FC * 128], wqpr_h.ap()[m, :, :]),
            inc="s_gd", k=16)

    em.wait("sc", "s_ydma", m_y_all)
    yn_mark = {}

    def emit_ynT_load(bb, tt0, par):
        em.wait("sc", "s_mm", yin_mark[0])
        em.dma_fence("sc", "s_scd")
        dstv = ynT[:, par * KC * NT3:(par + 1) * KC * NT3].rearrange(
            "p (k t) -> p k t", k=KC)
        srcv = ynT_h.ap()[bb, :, :, tt0:tt0 + NT3].rearrange("k p t -> p k t")
        return em.op("sc", lambda e, dstv=dstv, srcv=srcv: e.dma_start(dstv, srcv),
                     inc="s_scd", k=16)

    for b in range(NB):
        em.wait("sc", "s_dv", em.cnt["s_dv"])
        em.wait("te", "s_dv", em.cnt["s_dv"])
        mrep3 = em.cnt["s_sc"]
        mrep3 = build_rep(0, b, 4 * D, ONE, mrep3)
        mrep3 = build_rep(1, b, 3 * D, Z, mrep3)
        em.wait("ve", "s_sc", mrep3)
        em.wait("te", "s_sc", mrep3)
        for gi, goff in ((0, 2 * D), (2, 5 * D)):
            for k in range(KC):
                em.op("te", lambda e, b=b, goff=goff, k=k: e.transpose(
                    ps_sc[:, k:k + 1],
                    mod_sb[32 * b:32 * b + 1, goff + k * 128:goff + (k + 1) * 128],
                    ones1[32 * b:32 * b + 1, 0:1]))
            mtr = em.inc_last("te", "s_tr")
            em.wait("ve", "s_tr", mtr)
            em.op("ve", lambda e, gi=gi: e.tensor_copy(
                gfm[:, gi * 8:(gi + 1) * 8], ps_sc[:, 0:8]))
            bcol = vecs[:, 32:40] if gi == 0 else vecs[:, 40:48]
            mgf = em.op("ve", lambda e, gi=gi, bcol=bcol: e.tensor_tensor(
                gfm[:, (gi + 1) * 8:(gi + 2) * 8], gfm[:, gi * 8:(gi + 1) * 8],
                bcol, op=ALU.mult), inc="s_dv")
            em.wait("te", "s_dv", mgf)
            em.wait("sc", "s_dv", mgf)
        for t3 in range(NTIL3):
            t0 = t3 * NT3
            mxl = [0] * NS3
            myl = [0] * NS3
            for s in range(NS3):
                em.wait("sy", "s_out", out_mark[s])
                em.dma_fence("sy", "s_ld")
                mxl[s] = em.op("sy", lambda e, b=b, s=s, t0=t0: e.dma_start(
                    x_sb[:, s * D:(s + 1) * D],
                    x_d.ap()[b, t0 + 128 * s:t0 + 128 * (s + 1), :]), inc="s_ld", k=16)
            ti = b * NTIL3 + t3
            pyn = ti % 2
            if ti == 0:
                yn_mark[0] = emit_ynT_load(0, 0, 0)
            if ti + 1 < NB * NTIL3:
                bn, tn = divmod(ti + 1, NTIL3)
                yn_mark[(ti + 1) % 2] = emit_ynT_load(bn, tn * NT3, (ti + 1) % 2)
            myn = yn_mark[pyn]
            emit_wqf_load(0)
            emit_wqf_load(1)
            m_att = 0
            em.wait("te", "s_scd", myn)
            ev_c = [0] * KC
            for c in range(KC):
                if c % 4 == 0:
                    em.wait("te", "s_sc", em.cnt["s_sc"])
                pb = (ps_sc, 0) if c % 4 == 0 else \
                     (ps_sc, 512) if c % 4 == 1 else \
                     (ps_mm, 0) if c % 4 == 2 else (ps_mm, 512)
                for k in range(KC):
                    em.op("te", lambda e, c=c, k=k, pb=pb, pyn=pyn: e.matmul(
                        pb[0][:, pb[1]:pb[1] + NT3],
                        woT[:, k * D + c * 128:k * D + (c + 1) * 128],
                        ynT[:, pyn * KC * NT3 + k * NT3:pyn * KC * NT3 + (k + 1) * NT3],
                        start=(k == 0), stop=(k == KC - 1)))
                m_mm = em.inc_last("te", "s_mm")
                em.wait("sc", "s_mm", m_mm)
                ev_c[c] = em.op("sc", lambda e, c=c, pb=pb: e.activation(
                    att_fm[:, c * NT3:(c + 1) * NT3], pb[0][:, pb[1]:pb[1] + NT3],
                    AF.Identity, scale=gfm[:, c:c + 1], bias=gfm[:, 8 + c:9 + c]),
                    inc="s_sc")
            yn_used = em.cnt["s_mm"]
            mxp = [0] * NS3
            m_act2 = 0
            for s4 in range(NS3):
                pz = s4 % 2
                em.wait("te", "s_sc", ev_c[KC - 1])
                em.wait("te", pbf_free[pz][0], pbf_free[pz][1])
                for c in range(KC):
                    em.op("te", lambda e, c=c, s4=s4, pz=pz: e.transpose(
                        ps_bf[:, pz * 1024 + c * 128:pz * 1024 + (c + 1) * 128],
                        att_fm[:, c * NT3 + s4 * 128:c * NT3 + (s4 + 1) * 128], id_b[:]))
                mtr = em.inc_last("te", "s_tr")
                em.wait("ve", "s_tr", mtr)
                m_att = em.op("ve", lambda e, s4=s4, pz=pz: e.tensor_copy(
                    att_tm[:, s4 * D:(s4 + 1) * D],
                    ps_bf[:, pz * 1024:(pz + 1) * 1024]), inc="s_dv", drain=True)
                pbf_free[pz] = ("s_dv", m_att)
                em.wait("ve", "s_ld", mxl[s4])
                mxp[s4] = em.op("ve", lambda e, s4=s4: e.tensor_tensor(
                    x_sb[:, s4 * D:(s4 + 1) * D], x_sb[:, s4 * D:(s4 + 1) * D],
                    att_tm[:, s4 * D:(s4 + 1) * D], op=ALU.add), inc="s_dv")
                _, mh = emit_ln_modulate(
                    x_sb[:, s4 * D:(s4 + 1) * D], s4 % 2, rep[:, 0:D], rep[:, D:2 * D],
                    None, None, mxp[s4], "s_dv")
                m_act2 = emit_act_transpose(s4 % 2, mh, actT, s4 * 128)
            yin_mark[0] = yn_used
            em.wait("te", "s_dv", m_act2)
            for k in range(KC):
                em.op("te", lambda e, k=k: e.matmul(
                    ps_lo[0:R, 0:NT3], afcT[:, k * R:(k + 1) * R], actT_full(actT, k),
                    start=(k == 0), stop=(k == KC - 1)))
            m_mm = em.inc_last("te", "s_mm")
            em.wait("ve", "s_mm", m_mm)
            m_s2 = em.op("ve", lambda e: e.tensor_scalar(
                s2[0:R, :], ps_lo[0:R, 0:NT3], 2.0, None, op0=ALU.mult), inc="s_dv")
            em.wait("te", "s_dv", m_s2)
            for j in range(FC):
                jg = j // 2
                pz4 = j % 4
                pg = jg % 2
                if j % 2 == 0:
                    em.wait("te", "s_gd", wqf_mark[jg])
                if j % 4 == 0:
                    em.wait("te", "s_sc", em.cnt["s_sc"])
                pb = (ps_mm, 0) if pz4 == 0 else \
                     (ps_mm, 512) if pz4 == 1 else \
                     (ps_sc, 0) if pz4 == 2 else (ps_sc, 512)
                base_w = pg * 2 * KC * 128 + (j % 2) * KC * 128
                for k in range(KC):
                    em.op("te", lambda e, k=k, pb=pb, base_w=base_w: e.matmul(
                        pb[0][:, pb[1]:pb[1] + NT3],
                        wqf_sb[:, base_w + k * 128:base_w + (k + 1) * 128],
                        actT_full(actT, k), start=(k == 0), stop=False))
                m_mm = em.op("te", lambda e, j=j, pb=pb: e.matmul(
                    pb[0][:, pb[1]:pb[1] + NT3],
                    bfcT[0:R, j * 128:(j + 1) * 128], s2[0:R, :],
                    start=False, stop=True), inc="s_mm")
                if j % 2 == 1:
                    wqf_free[pg] = m_mm
                em.wait("sc", "s_mm", m_mm)
                em.op("sc", lambda e, j=j, pb=pb: e.activation(
                    h3[:, j * NT3:(j + 1) * NT3], pb[0][:, pb[1]:pb[1] + NT3],
                    AF.Gelu, bias=bfc_fm[:, j:j + 1]), inc="s_sc")
                if j % 2 == 1 and jg + 2 < FC // 2:
                    emit_wqf_load(jg + 2)
            emit_wqp_load(0)
            emit_wqp_load(1)
            em.wait("te", "s_sc", em.cnt["s_sc"])
            for j in range(FC):
                em.op("te", lambda e, j=j: e.matmul(
                    ps_lo[0:R, 0:NT3], aprT[:, j * R:(j + 1) * R],
                    h3[:, j * NT3:(j + 1) * NT3],
                    start=(j == 0), stop=(j == FC - 1)))
            m_mm = em.inc_last("te", "s_mm")
            em.wait("ve", "s_mm", m_mm)
            m_s2p = em.op("ve", lambda e: e.tensor_scalar(
                s2[0:R, :], ps_lo[0:R, 0:NT3], 2.0, None, op0=ALU.mult), inc="s_dv")
            m_proj = 0
            em.wait("te", "s_dv", m_s2p)
            ev_m = [0] * KC
            for m_ in range(KC):
                pz4 = m_ % 4
                em.wait("te", "s_gd", wqp_mark[m_])
                if m_ % 4 == 0:
                    em.wait("te", "s_sc", em.cnt["s_sc"])
                pb = (ps_mm, 0) if pz4 == 0 else \
                     (ps_mm, 512) if pz4 == 1 else \
                     (ps_sc, 0) if pz4 == 2 else (ps_sc, 512)
                pzb = m_ % 2
                for k in range(FC):
                    em.op("te", lambda e, m_=m_, k=k, pb=pb, pzb=pzb: e.matmul(
                        pb[0][:, pb[1]:pb[1] + NT3],
                        wqp_sb[:, pzb * FC * 128 + k * 128:pzb * FC * 128 + (k + 1) * 128],
                        h3[:, k * NT3:(k + 1) * NT3], start=(k == 0), stop=False))
                m_mm = em.op("te", lambda e, m_=m_, pb=pb: e.matmul(
                    pb[0][:, pb[1]:pb[1] + NT3],
                    bprT[0:R, m_ * 128:(m_ + 1) * 128], s2[0:R, :],
                    start=False, stop=True), inc="s_mm")
                wqp_free[pzb] = m_mm
                em.wait("sc", "s_mm", m_mm)
                ev_m[m_] = em.op("sc", lambda e, m_=m_, pb=pb: e.activation(
                    att_fm[:, m_ * NT3:(m_ + 1) * NT3], pb[0][:, pb[1]:pb[1] + NT3],
                    AF.Identity, scale=gfm[:, 16 + m_:17 + m_],
                    bias=gfm[:, 24 + m_:25 + m_]), inc="s_sc")
                if m_ + 2 < KC:
                    emit_wqp_load(m_ + 2)
            for s4 in range(NS3):
                pz = s4 % 2
                em.wait("te", "s_sc", ev_m[KC - 1])
                em.wait("te", pbf_free[pz][0], pbf_free[pz][1])
                for m_ in range(KC):
                    em.op("te", lambda e, m_=m_, s4=s4, pz=pz: e.transpose(
                        ps_bf[:, pz * 1024 + m_ * 128:pz * 1024 + (m_ + 1) * 128],
                        att_fm[:, m_ * NT3 + s4 * 128:m_ * NT3 + (s4 + 1) * 128], id_b[:]))
                mtr = em.inc_last("te", "s_tr")
                em.wait("ve", "s_tr", mtr)
                m_proj = em.op("ve", lambda e, s4=s4, pz=pz: e.tensor_copy(
                    att_tm[:, s4 * D:(s4 + 1) * D],
                    ps_bf[:, pz * 1024:(pz + 1) * 1024]), inc="s_dv", drain=True)
                pbf_free[pz] = ("s_dv", m_proj)
                mo = em.op("ve", lambda e, s4=s4: e.tensor_tensor(
                    x_sb[:, s4 * D:(s4 + 1) * D], x_sb[:, s4 * D:(s4 + 1) * D],
                    att_tm[:, s4 * D:(s4 + 1) * D], op=ALU.add), inc="s_dv")
                em.wait("sy", "s_dv", mo)
                em.dma_fence("sy", "s_out")
                out_mark[s4] = em.op("sy", lambda e, b=b, s4=s4, t0=t0: e.dma_start(
                    out_d.ap()[b, t0 + 128 * s4:t0 + 128 * (s4 + 1), :],
                    x_sb[:, s4 * D:(s4 + 1) * D]), inc="s_out", k=16)
    em.wait("sy", "s_out", em.cnt["s_out"])
    em.realize()
    for c in reversed(cm_p3):
        c.__exit__(None, None, None)
    return nc


def _slice_inputs(inputs, core):
    b0 = NB * core
    m = {
        "x": np.ascontiguousarray(inputs["L0_attn_x"][b0:b0 + NB]),
        "cvec": np.ascontiguousarray(inputs["L0_c"][b0:b0 + NB]),
        "wmod": inputs["L0_wmod"],
        "decay": inputs["L0_attn_decay"],
        "Wr": inputs["L0_attn_Wr"], "br": inputs["L0_attn_br"],
        "Wg": inputs["L0_attn_Wg"], "bg": inputs["L0_attn_bg"],
        "Wo": inputs["L0_attn_Wo"], "bo": inputs["L0_attn_bo"],
        "ln_w": inputs["L0_attn_ln_w"], "ln_b": inputs["L0_attn_ln_b"],
        "W_fc": inputs["L0_ffn_W_fc"], "b_fc": inputs["L0_ffn_b_fc"],
        "A_fc": inputs["L0_ffn_A_fc"], "B_fc": inputs["L0_ffn_B_fc"],
        "W_proj": inputs["L0_ffn_W_proj"], "b_proj": inputs["L0_ffn_b_proj"],
        "A_proj": inputs["L0_ffn_A_proj"], "B_proj": inputs["L0_ffn_B_proj"],
    }
    return {k: np.ascontiguousarray(v) for k, v in m.items()}


def kernel(**inputs):
    nc = build_program(L=2048)
    in_maps = [_slice_inputs(inputs, i) for i in range(8)]
    res = run_bass_kernel_spmd(nc, in_maps, core_ids=list(range(8)))
    return np.concatenate([np.asarray(r["out"]) for r in res.results], axis=0)


# version 20
# speedup vs baseline: 1.1192x; 1.0168x over previous
import numpy as np
import concourse.bass as bass
import concourse.mybir as mybir
from concourse import library_config
from concourse.bass_utils import run_bass_kernel_spmd

F32 = mybir.dt.float32
BF16 = mybir.dt.bfloat16
AF = mybir.ActivationFunctionType
ALU = mybir.AluOpType
AX = mybir.AxisListType

D = 1024
FF = 4096
NB = 2
KC = D // 128
FC = FF // 128
R = 16
SCL = 2.0
EPS = 1e-5
ENGS = ("sy", "gp", "ve", "sc", "te")


class EM:

    def __init__(self, nc, sems):
        self.nc = nc
        self.sems = sems
        self.cnt = {k: 0 for k in sems}
        self.lists = {e: [] for e in ENGS}

    def op(self, eng, fn, inc=None, k=1, drain=None):
        incs = []
        ret = None
        if inc is not None:
            self.cnt[inc] += k
            incs.append((self.sems[inc], k))
            ret = self.cnt[inc]
        self.lists[eng].append(("op", fn, incs))
        if drain is None:
            drain = eng == "ve"
        if drain:
            self.lists[eng].append(("drain",))
        return ret

    def drain(self, eng):
        self.lists[eng].append(("drain",))

    def inc_last(self, eng, inc, k=1):
        for entry in reversed(self.lists[eng]):
            if entry[0] == "op":
                if not entry[2]:
                    self.cnt[inc] += k
                    entry[2].append((self.sems[inc], k))
                    return self.cnt[inc]
                self.lists[eng].append(("drain",))
                return self.op(eng, lambda e: e.nop(), inc=inc, k=k)
        return self.op(eng, lambda e: e.nop(), inc=inc, k=k)

    def dma_fence(self, eng, sem_name):
        self.wait(eng, sem_name, self.cnt[sem_name])

    def wait(self, eng, sem_name, val):
        if val < 0:
            return
        self.lists[eng].append(("wait", self.sems[sem_name], val))

    def barrier(self):
        for e in ENGS:
            if e in ("sy", "gp"):
                self.op(e, lambda x: x.nop(), inc="s_ph")
            else:
                self.inc_last(e, "s_ph")
        tgt = self.cnt["s_ph"]
        for e in ENGS:
            self.wait(e, "s_ph", tgt)

    @staticmethod
    def _replay(eng, entries):
        for entry in entries:
            if entry[0] == "op":
                instr = entry[1](eng)
                for sem, k in entry[2]:
                    instr.then_inc(sem, k)
            elif entry[0] == "wait":
                eng.wait_ge(entry[1], entry[2])
            else:
                eng.drain()

    def realize(self):
        with self.nc.Block() as blk:
            for e, meth in (("sy", blk.sync), ("gp", blk.gpsimd), ("ve", blk.vector),
                            ("sc", blk.scalar), ("te", blk.tensor)):
                entries = self.lists[e]
                if entries:
                    meth(lambda eng, entries=entries: EM._replay(eng, entries))
        self.lists = {e: [] for e in ENGS}



def _close_truncated(nc, em):
    with nc.Block() as blk:
        def _w(eng):
            for name in ("s_ld", "s_gd", "s_scd", "s_uv", "s_ydma", "s_out"):
                if em.cnt[name] > 0:
                    eng.wait_ge(em.sems[name], em.cnt[name])
        blk.sync(_w)
    return nc

def build_program(L=2048, max_phase=4):
    NCH = L // 128
    NT1 = 256
    NT3 = min(512, L)
    NS3 = NT3 // 128
    NTIL1 = L // NT1
    NTIL3 = L // NT3
    nc = bass.Bass("TRN2", target_bir_lowering=False)

    x_d = nc.dram_tensor("x", [NB, L, D], F32, kind="ExternalInput")
    c_d = nc.dram_tensor("cvec", [NB, D], F32, kind="ExternalInput")
    wmod_d = nc.dram_tensor("wmod", [D, 6 * D], F32, kind="ExternalInput")
    decay_d = nc.dram_tensor("decay", [D], F32, kind="ExternalInput")
    wr_d = nc.dram_tensor("Wr", [D, D], F32, kind="ExternalInput")
    br_d = nc.dram_tensor("br", [D], F32, kind="ExternalInput")
    wg_d = nc.dram_tensor("Wg", [D, D], F32, kind="ExternalInput")
    bg_d = nc.dram_tensor("bg", [D], F32, kind="ExternalInput")
    wo_d = nc.dram_tensor("Wo", [D, D], F32, kind="ExternalInput")
    bo_d = nc.dram_tensor("bo", [D], F32, kind="ExternalInput")
    lnw_d = nc.dram_tensor("ln_w", [D], F32, kind="ExternalInput")
    lnb_d = nc.dram_tensor("ln_b", [D], F32, kind="ExternalInput")
    wfc_d = nc.dram_tensor("W_fc", [FF, D], F32, kind="ExternalInput")
    bfc_d = nc.dram_tensor("b_fc", [FF], F32, kind="ExternalInput")
    afc_d = nc.dram_tensor("A_fc", [R, D], F32, kind="ExternalInput")
    Bfc_d = nc.dram_tensor("B_fc", [FF, R], F32, kind="ExternalInput")
    wpr_d = nc.dram_tensor("W_proj", [D, FF], F32, kind="ExternalInput")
    bpr_d = nc.dram_tensor("b_proj", [D], F32, kind="ExternalInput")
    apr_d = nc.dram_tensor("A_proj", [R, FF], F32, kind="ExternalInput")
    Bpr_d = nc.dram_tensor("B_proj", [D, R], F32, kind="ExternalInput")
    out_d = nc.dram_tensor("out", [NB, L, D], F32, kind="ExternalOutput")
    ynT_h = nc.dram_tensor("ynT_h", [NB, KC, 128, L], BF16)
    wqfc_h = nc.dram_tensor("wqfc_h", [FC, 128, KC * 128], BF16)
    wqpr_h = nc.dram_tensor("wqpr_h", [KC, 128, FC * 128], BF16)

    A = nc.alloc_sbuf_tensor
    woT = A("woT", [128, KC * D], BF16)
    mod_sb = A("mod_sb", [64, 6 * D], F32)
    rep = A("rep", [128, 4 * D], BF16)
    tm_bf = A("tm_bf", [128, D], BF16)
    vecs = A("vecs", [128, 64], F32)
    dec_bc = A("dec_bc", [128, 16], F32)
    lnb_b = A("lnb_b", [128, 8], BF16)
    gfm = A("gfm", [128, 32], F32)
    bfc_fm = A("bfc_fm", [128, FC], F32)
    id_b = A("id_b", [128, 128], BF16)
    id_f = A("id_f", [128, 128], F32)
    ones1 = A("ones1", [64, 128], F32)
    afcT = A("afcT", [128, KC * R], BF16)
    aprT = A("aprT", [128, FC * R], BF16)
    bfcT = A("bfcT", [R, FF], BF16)
    bprT = A("bprT", [R, D], BF16)
    x_sb = A("x_sb", [128, 4 * D], F32)
    tm_a = A("tm_a", [128, D], F32)
    h_tmb = A("h_tmb", [128, 2 * D], BF16)
    stat = A("stat", [128, 16], F32)
    sq_dump = A("sq_dump", [128, D], F32)
    mx = A("mx", [128, 8], F32)
    siluT = A("siluT", [128, 16], F32)
    scan_tmp = A("scan_tmp", [128, 64], F32)
    s2 = A("s2", [R, NT3], BF16)

    ps_mm = nc.alloc_psum_tensor("ps_mm", [128, 1024], F32)
    ps_sc = nc.alloc_psum_tensor("ps_sc", [128, 1024], F32)
    ps_bf = nc.alloc_psum_tensor("ps_bf", [128, 2048], BF16)
    ps_lo = nc.alloc_psum_tensor("ps_lo", [128, 512], F32)
    ps_x = nc.alloc_psum_tensor("ps_x", [128, 1024], BF16)

    sems = {}
    for name in ["s_ld", "s_gd", "s_scd", "s_mm", "s_dv", "s_sc", "s_gp",
                 "s_uv", "s_p1", "s_ydma", "s_ev", "s_fl", "s_tr", "s_q", "s_qt", "s_out", "s_ph"]:
        sems[name] = nc.alloc_semaphore(name)
    em = EM(nc, sems)

    Z = vecs[:, 52:53]
    ONE = vecs[:, 53:54]

    cm_wrT = nc.sbuf_tensor("wrT", [128, KC * D], BF16)
    wrT = cm_wrT.__enter__()
    cm_wgT = nc.sbuf_tensor("wgT", [128, KC * D], BF16)
    wgT = cm_wgT.__enter__()
    cm_p0 = [nc.sbuf_tensor("wstage", [128, 2 * D], F32),
             nc.sbuf_tensor("wmodf", [128, 6 * D], F32),
             nc.sbuf_tensor("csb", [64, D], F32)]
    wstage, wmodf, csb = [c.__enter__() for c in cm_p0]
    lora_st = wstage

    em.op("ve", lambda e: e.memset(vecs[:], 0.0))
    mz = em.op("ve", lambda e: e.memset(ones1[:], 1.0), inc="s_dv")
    em.wait("gp", "s_dv", mz)
    em.op("gp", lambda e: e.iota(vecs[:, 54:55], pattern=[[0, 1]], base=0,
                                 channel_multiplier=1, allow_small_or_imprecise_dtypes=True))
    mgp = em.op("gp", lambda e: e.iota(id_f[:], pattern=[[1, 128]], base=0, channel_multiplier=0,
                                       allow_small_or_imprecise_dtypes=True), inc="s_gp")
    em.wait("ve", "s_gp", mgp)
    em.op("ve", lambda e: e.tensor_scalar(id_f[:], id_f[:], vecs[:, 54:55], None,
                                          op0=ALU.is_equal), inc="s_dv")
    em.op("ve", lambda e: e.tensor_copy(id_b[:], id_f[:]), inc="s_dv")
    mID = em.op("ve", lambda e: e.memset(vecs[:, 53:54], 1.0), inc="s_dv")
    em.wait("te", "s_dv", mID)
    em.wait("sc", "s_dv", mID)

    def vec_to_fm(rows, dsts):
        mld = 0
        for r, dr in enumerate(rows):
            mld = em.op("sy", lambda e, r=r, dr=dr: e.dma_start(
                csb[32 * r:32 * r + 1, :], dr.ap().rearrange("(o d) -> o d", o=1)),
                inc="s_ld", k=16)
        em.wait("te", "s_ld", mld)
        for r in range(len(rows)):
            for k in range(KC):
                em.op("te", lambda e, r=r, k=k: e.transpose(
                    ps_sc[:, r * 512 + k:r * 512 + k + 1],
                    csb[32 * r:32 * r + 1, k * 128:(k + 1) * 128],
                    ones1[32 * r:32 * r + 1, 0:1]))
        mtr = em.inc_last("te", "s_tr")
        em.wait("ve", "s_tr", mtr)
        for r, dcol in enumerate(dsts):
            em.op("ve", lambda e, r=r, dcol=dcol: e.tensor_copy(
                vecs[:, dcol:dcol + 8], ps_sc[:, r * 512:r * 512 + 8]))
        mdv = em.inc_last("ve", "s_dv")
        em.wait("sy", "s_dv", mdv)
        em.wait("te", "s_dv", mdv)
        return mdv

    mv1 = vec_to_fm([decay_d, br_d], [0, 16])
    vec_to_fm([bg_d, bo_d], [24, 32])
    vec_to_fm([bpr_d], [40])

    em.wait("sc", "s_dv", mv1)
    em.op("sc", lambda e: e.activation(vecs[:, 0:8], vecs[:, 0:8], AF.Sigmoid, bias=Z))
    msc = em.inc_last("sc", "s_sc")
    em.wait("ve", "s_sc", msc)
    em.op("ve", lambda e: e.tensor_scalar(vecs[:, 8:16], vecs[:, 0:8], -1.0, 1.0,
                                          op0=ALU.mult, op1=ALU.add))
    em.op("ve", lambda e: e.tensor_copy(dec_bc[:, 0:16:2], vecs[:, 0:8]))
    em.op("ve", lambda e: e.tensor_copy(dec_bc[:, 1:16:2], vecs[:, 0:8]), inc="s_dv")

    for p in range(4):
        em.dma_fence("sy", "s_ld")
        mld = em.op("sy", lambda e, p=p: e.dma_start(
            csb[0:1, :], bfc_d.ap().rearrange("(o d) -> o d", o=4)[p:p + 1, :]), inc="s_ld", k=16)
        em.wait("te", "s_ld", mld)
        for k in range(KC):
            em.op("te", lambda e, k=k: e.transpose(
                ps_sc[:, k:k + 1], csb[0:1, k * 128:(k + 1) * 128], id_f[0:1, 0:1]))
        mtr = em.inc_last("te", "s_tr")
        em.wait("ve", "s_tr", mtr)
        mdv = em.op("ve", lambda e, p=p: e.tensor_copy(
            bfc_fm[:, p * 8:(p + 1) * 8], ps_sc[:, 0:8]), inc="s_dv")
        em.wait("sy", "s_dv", mdv)
        em.wait("te", "s_dv", mdv)

    vec_to_fm([lnw_d], [55])
    mld = em.op("sy", lambda e: e.dma_start(csb[0:1, :], lnb_d.ap().rearrange("(o d) -> o d", o=1)), inc="s_ld", k=16)
    em.wait("te", "s_ld", mld)
    for k in range(KC):
        em.op("te", lambda e, k=k: e.transpose(
            ps_sc[:, k:k + 1], csb[0:1, k * 128:(k + 1) * 128], ones1[0:1, 0:1]))
    mtr = em.inc_last("te", "s_tr")
    em.wait("ve", "s_tr", mtr)
    em.op("ve", lambda e: e.tensor_copy(stat[:, 0:8], ps_sc[:, 0:8]))
    em.op("ve", lambda e: e.tensor_copy(lnb_b[:], ps_sc[:, 0:8]))
    mdv = em.inc_last("ve", "s_dv")
    em.wait("sy", "s_dv", mdv)
    em.wait("te", "s_dv", mdv)

    prev_ev = 0
    for wd, dstT in ((wg_d, wgT), (wr_d, wrT), (wo_d, woT)):
        for r in range(KC):
            par = r % 2
            em.wait("sy", "s_dv", prev_ev)
            em.dma_fence("sy", "s_ld")
            mld = em.op("sy", lambda e, wd=wd, par=par, r=r: e.dma_start(
                wstage[:, par * D:(par + 1) * D], wd.ap()[128 * r:128 * (r + 1), :]),
                inc="s_ld", k=16)
            em.wait("te", "s_ld", mld)
            em.wait("te", "s_dv", prev_ev)
            for k in range(KC):
                em.op("te", lambda e, par=par, k=k: e.transpose(
                    ps_sc[:, (k % 4) * 128 + (k // 4) * 512:(k % 4) * 128 + (k // 4) * 512 + 128],
                    wstage[:, par * D + k * 128:par * D + (k + 1) * 128], id_f[:]))
            mtr = em.inc_last("te", "s_tr")
            em.wait("ve", "s_tr", mtr)
            for half in range(2):
                dst3 = dstT[:].rearrange("p (k e) -> p k e", k=KC)[:, half * 4:(half + 1) * 4,
                                                                  r * 128:(r + 1) * 128]
                src3 = ps_sc[:, half * 512:(half + 1) * 512].rearrange("p (k i) -> p k i", k=4)
                em.op("ve", lambda e, dst3=dst3, src3=src3: e.tensor_copy(dst3, src3))
            prev_ev = em.inc_last("ve", "s_dv")

    em.wait("te", "s_dv", prev_ev)
    for hh in range(2):
        for k in range(KC):
            em.op("te", lambda e, hh=hh, k=k: e.matmul(
                ps_sc[0:1, hh * 512:(hh + 1) * 512], lnb_b[:, k:k + 1],
                woT[:, k * D + hh * 512:k * D + (hh + 1) * 512],
                start=(k == 0), stop=(k == KC - 1)))
    mq = em.inc_last("te", "s_mm")
    em.wait("ve", "s_mm", mq)
    em.op("ve", lambda e: e.tensor_copy(csb[0:1, 0:512], ps_sc[0:1, 0:512]))
    mdv = em.op("ve", lambda e: e.tensor_copy(csb[0:1, 512:1024], ps_sc[0:1, 512:1024]),
                inc="s_dv")
    em.wait("te", "s_dv", mdv)
    for k in range(KC):
        em.op("te", lambda e, k=k: e.transpose(
            ps_sc[:, k:k + 1], csb[0:1, k * 128:(k + 1) * 128], ones1[0:1, 0:1]))
    mtr = em.inc_last("te", "s_tr")
    em.wait("ve", "s_tr", mtr)
    em.op("ve", lambda e: e.tensor_tensor(vecs[:, 32:40], vecs[:, 32:40], ps_sc[:, 0:8],
                                          op=ALU.add))
    for k in range(KC):
        em.op("ve", lambda e, k=k: e.tensor_scalar(
            woT[:, k * D:(k + 1) * D], woT[:, k * D:(k + 1) * D],
            vecs[:, 55 + k:56 + k], None, op0=ALU.mult), drain=False)
    m_wofold = em.inc_last("ve", "s_dv")
    em.wait("te", "s_dv", m_wofold)
    em.wait("sy", "s_dv", m_wofold)

    em.dma_fence("sy", "s_ld")
    mld = em.op("sy", lambda e: e.dma_start(csb[0:2, :], c_d.ap()), inc="s_ld", k=16)
    em.wait("sc", "s_ld", mld)
    msc = em.op("sc", lambda e: e.activation(tm_a[0:2, :], csb[0:2, :], AF.Silu, bias=vecs[0:2, 52:53]), inc="s_sc")
    em.wait("te", "s_sc", msc)
    em.wait("te", "s_dv", prev_ev)
    for k in range(KC):
        em.op("te", lambda e, k=k: e.transpose(
            ps_sc[:, 2 * k:2 * k + 2], tm_a[0:2, k * 128:(k + 1) * 128], id_f[0:2, 0:2]))
    mtr = em.inc_last("te", "s_tr")
    em.wait("ve", "s_tr", mtr)
    msil = em.op("ve", lambda e: e.tensor_copy(siluT[:], ps_sc[:, 0:16]), inc="s_dv")

    ev_n = {}
    mq_prev = 0
    for kk in range(KC):
        em.wait("gp", "s_mm", mq_prev)
        em.dma_fence("gp", "s_gd")
        mgd = em.op("gp", lambda e, kk=kk: e.dma_start(
            wmodf[:], wmod_d.ap()[128 * kk:128 * (kk + 1), :]), inc="s_gd", k=16)
        em.wait("te", "s_gd", mgd)
        if kk == 0:
            em.wait("te", "s_dv", msil)
        for n in range(12):
            for b in range(2):
                par = (2 * n + b) % 2
                em.wait("te", "s_dv", ev_n.get((kk, 2 * n + b - 2), 0))
                em.op("te", lambda e, par=par, kk=kk, n=n, b=b: e.matmul(
                    ps_sc[0:1, par * 512:(par + 1) * 512],
                    siluT[:, 2 * kk + b:2 * kk + b + 1],
                    wmodf[:, n * 512:(n + 1) * 512],
                    start=True, stop=True))
                mmm = em.inc_last("te", "s_mm")
                mq_prev = mmm
                em.wait("ve", "s_mm", mmm)
                brow = 32 * b
                if kk == 0:
                    ev_n[(kk, 2 * n + b)] = em.op(
                        "ve", lambda e, par=par, n=n, brow=brow: e.tensor_copy(
                            mod_sb[brow:brow + 1, n * 512:(n + 1) * 512],
                            ps_sc[0:1, par * 512:(par + 1) * 512]), inc="s_dv")
                else:
                    ev_n[(kk, 2 * n + b)] = em.op(
                        "ve", lambda e, par=par, n=n, brow=brow: e.tensor_tensor(
                            mod_sb[brow:brow + 1, n * 512:(n + 1) * 512],
                            mod_sb[brow:brow + 1, n * 512:(n + 1) * 512],
                            ps_sc[0:1, par * 512:(par + 1) * 512], op=ALU.add), inc="s_dv")
        ev_n[(kk + 1, -2)] = ev_n[(kk, 22)]
        ev_n[(kk + 1, -1)] = ev_n[(kk, 23)]

    for off in (2 * D, 5 * D):
        for brow in (0, 32):
            em.op("ve", lambda e, off=off, brow=brow: e.tensor_scalar(
                tm_a[brow:brow + 1, :], mod_sb[brow:brow + 1, off:off + D], 0.0, 2.0,
                op0=ALU.is_ge, op1=ALU.mult))
            mdv = em.op("ve", lambda e, brow=brow: e.tensor_scalar(
                tm_a[brow:brow + 1, :], tm_a[brow:brow + 1, :], -1.0, None,
                op0=ALU.add), inc="s_dv")
            em.wait("sc", "s_dv", mdv)
            msc = em.op("sc", lambda e, off=off, brow=brow: e.activation(
                sq_dump[brow:brow + 1, :], mod_sb[brow:brow + 1, off:off + D], AF.Abs,
                bias=vecs[brow:brow + 1, 52:53]), inc="s_sc")
            em.wait("ve", "s_sc", msc)
            em.op("ve", lambda e, brow=brow: e.tensor_scalar(
                sq_dump[brow:brow + 1, :], sq_dump[brow:brow + 1, :], 0.25, None, op0=ALU.max))
            em.op("ve", lambda e, off=off, brow=brow: e.tensor_tensor(
                mod_sb[brow:brow + 1, off:off + D], tm_a[brow:brow + 1, :],
                sq_dump[brow:brow + 1, :], op=ALU.mult), inc="s_dv")

    em.wait("sy", "s_dv", prev_ev)
    em.dma_fence("sy", "s_ld")
    mld = em.op("sy", lambda e: e.dma_start(lora_st[0:R, 0:D], afc_d.ap()), inc="s_ld", k=16)
    em.wait("te", "s_dv", em.cnt["s_dv"])
    em.wait("te", "s_ld", mld)
    for k in range(KC):
        em.op("te", lambda e, k=k: e.transpose(
            ps_sc[:, k * 16:(k + 1) * 16], lora_st[0:R, k * 128:(k + 1) * 128], id_f[0:R, 0:R]))
    mtr = em.inc_last("te", "s_tr")
    em.wait("ve", "s_tr", mtr)
    mdv = em.op("ve", lambda e: e.tensor_copy(afcT[:], ps_sc[:, 0:KC * R]), inc="s_dv")
    em.wait("sy", "s_dv", mdv)
    em.wait("te", "s_dv", mdv)
    for hh in range(2):
        em.dma_fence("sy", "s_ld")
        mld = em.op("sy", lambda e, hh=hh: e.dma_start(
            lora_st[0:R, 0:2 * D], apr_d.ap()[:, hh * 2 * D:(hh + 1) * 2 * D]),
            inc="s_ld", k=16)
        em.wait("te", "s_ld", mld)
        for k in range(16):
            em.op("te", lambda e, k=k: e.transpose(
                ps_sc[:, k * 16:(k + 1) * 16], lora_st[0:R, k * 128:(k + 1) * 128],
                id_f[0:R, 0:R]))
        mtr = em.inc_last("te", "s_tr")
        em.wait("ve", "s_tr", mtr)
        mdv = em.op("ve", lambda e, hh=hh: e.tensor_copy(
            aprT[:, hh * 16 * R:(hh + 1) * 16 * R], ps_sc[:, 0:16 * R]), inc="s_dv")
        em.wait("te", "s_dv", mdv)
        em.wait("sy", "s_dv", mdv)
    em.wait("gp", "s_dv", mdv)
    for src_d, dstT, nt in ((Bfc_d, bfcT, FC), (Bpr_d, bprT, KC)):
        for g4 in range(nt // 4):
            em.dma_fence("gp", "s_gd")
            em.wait("gp", "s_dv", em.cnt["s_dv"])
            mgd = 0
            for jj in range(4):
                j = g4 * 4 + jj
                mgd = em.op("gp", lambda e, src_d=src_d, j=j, jj=jj: e.dma_start(
                    wstage[:, jj * 16:(jj + 1) * 16],
                    src_d.ap()[j * 128:(j + 1) * 128, :]), inc="s_gd", k=16)
            em.wait("te", "s_gd", mgd)
            for jj in range(4):
                em.op("te", lambda e, jj=jj: e.transpose(
                    ps_lo[0:128, jj * 128:(jj + 1) * 128][0:R, :],
                    wstage[:, jj * 16:(jj + 1) * 16], id_f[:]))
            mtr = em.inc_last("te", "s_tr")
            em.wait("ve", "s_tr", mtr)
            mdv = em.op("ve", lambda e, dstT=dstT, g4=g4: e.tensor_copy(
                dstT[0:R, g4 * 512:(g4 + 1) * 512], ps_lo[0:R, :]), inc="s_dv")
            em.wait("te", "s_dv", mdv)

    em.barrier()
    em.realize()
    for c in reversed(cm_p0):
        c.__exit__(None, None, None)
    if max_phase <= 0:
        return _close_truncated(nc, em)


    def actT_full(t, k):
        return t[:, k * NT3:(k + 1) * NT3]

    def build_rep(slot, b, off, bias_col, prev_mm):
        em.wait("te", "s_sc", prev_mm)
        for hh in range(2):
            mmm = em.op("te", lambda e, b=b, off=off, hh=hh: e.matmul(
                ps_mm[:, hh * 512:(hh + 1) * 512], ones1[32 * b:32 * b + 1, :],
                mod_sb[32 * b:32 * b + 1, off + hh * 512:off + (hh + 1) * 512],
                start=True, stop=True), inc="s_mm")
        em.wait("sc", "s_mm", mmm)
        for hh in range(2):
            msc = em.op("sc", lambda e, slot=slot, hh=hh, bias_col=bias_col: e.activation(
                rep[:, slot * D + hh * 512:slot * D + (hh + 1) * 512],
                ps_mm[:, hh * 512:(hh + 1) * 512], AF.Identity, bias=bias_col), inc="s_sc")
        return msc

    pbf_free = {0: ("s_dv", 0), 1: ("s_dv", 0)}
    x_free = [(0, 0)] * 4

    def emit_ln_modulate(src_ap, par_h, rep_mul, rep_add, mul2, add2,
                         m_data, data_sem):
        em.wait("ve", data_sem, m_data)
        em.op("ve", lambda e: e.reduce_sum(stat[:, 0:1], src_ap, axis=AX.X))
        em.wait("sc", data_sem, m_data)
        msq = em.op("sc", lambda e: e.activation(
            sq_dump[:], src_ap, AF.Square, bias=Z, accum_out=stat[:, 1:2]), inc="s_sc")
        em.op("ve", lambda e: e.tensor_scalar(stat[:, 2:3], stat[:, 0:1], 1.0 / D, None, op0=ALU.mult))
        em.wait("ve", "s_sc", msq)
        em.op("ve", lambda e: e.tensor_scalar(stat[:, 4:5], stat[:, 1:2], 1.0 / D, EPS,
                                              op0=ALU.mult, op1=ALU.add))
        em.op("ve", lambda e: e.tensor_tensor(stat[:, 5:6], stat[:, 2:3], stat[:, 2:3], op=ALU.mult))
        mdv = em.op("ve", lambda e: e.tensor_tensor(stat[:, 4:5], stat[:, 4:5], stat[:, 5:6],
                                                    op=ALU.subtract), inc="s_dv")
        em.wait("sc", "s_dv", mdv)
        msq = em.op("sc", lambda e: e.activation(stat[:, 5:6], stat[:, 4:5], AF.Sqrt, bias=Z),
                    inc="s_sc")
        em.wait("ve", "s_sc", msq)
        em.op("ve", lambda e: e.reciprocal(stat[:, 3:4], stat[:, 5:6]))
        mnm = em.op("ve", lambda e: e.tensor_scalar(
            stat[:, 6:7], stat[:, 2:3], stat[:, 3:4], -1.0,
            op0=ALU.mult, op1=ALU.mult), inc="s_dv")
        em.wait("sc", "s_dv", mnm)
        mcons = em.op("sc", lambda e: e.activation(
            tm_bf[:], src_ap, AF.Identity, scale=stat[:, 3:4], bias=stat[:, 6:7]),
            inc="s_sc")
        em.wait("ve", "s_sc", mcons)
        em.op("ve", lambda e: e.tensor_tensor(tm_bf[:], tm_bf[:], rep_mul, op=ALU.mult))
        mh = em.op("ve", lambda e, par_h=par_h: e.tensor_tensor(
            h_tmb[:, par_h * D:(par_h + 1) * D], tm_bf[:], rep_add, op=ALU.add), inc="s_dv")
        return (mnm, mcons), mh

    def emit_act_transpose(par_h, mh, dstT, col0):
        em.wait("te", "s_dv", mh)
        em.wait("te", pbf_free[par_h][0], pbf_free[par_h][1])
        for k in range(KC):
            em.op("te", lambda e, par_h=par_h, k=k: e.transpose(
                ps_bf[:, par_h * 1024 + k * 128:par_h * 1024 + (k + 1) * 128],
                h_tmb[:, par_h * D + k * 128:par_h * D + (k + 1) * 128], id_b[:]))
        mtr = em.inc_last("te", "s_tr")
        em.wait("ve", "s_tr", mtr)
        dst3 = dstT[:].rearrange("p (k t) -> p k t", k=KC)[:, :, col0:col0 + 128]
        src3 = ps_bf[:, par_h * 1024:(par_h + 1) * 1024].rearrange("p (k i) -> p k i", k=KC)
        mev = em.op("ve", lambda e, dst3=dst3, src3=src3: e.tensor_copy(dst3, src3), inc="s_dv")
        pbf_free[par_h] = ("s_dv", mev)
        return mev

    cm_p2 = [nc.sbuf_tensor("y_fm", [128, 2 * 128 * 16], BF16),
             nc.sbuf_tensor("u_st", [128, 4 * 128 * 16], BF16),
             nc.sbuf_tensor("v_st", [128, 4 * 128 * 16], BF16),
             nc.sbuf_tensor("y_tm", [128, NB * D], BF16),
             nc.sbuf_tensor("ystage", [128, D], BF16),
             nc.sbuf_tensor("actT1", [128, KC * NT1], BF16),
             nc.sbuf_tensor("qload", [128, 2 * D], F32),
             nc.sbuf_tensor("qa", [128, D], F32),
             nc.sbuf_tensor("qr", [128, D], F32),
             nc.sbuf_tensor("qt", [128, D], F32),
             nc.sbuf_tensor("qsgn", [128, 2 * D], BF16),
             nc.sbuf_tensor("qb", [128, 2 * D], BF16),
             nc.sbuf_tensor("qstage", [128, 2 * D], BF16)]
    (y_fm, u_st, v_st, y_tm, ystage, actT1, qload, qa, qr, qt, qsgn, qb, qstage) = \
        [c.__enter__() for c in cm_p2]

    def ring_slice(chunk, c, b):
        base = (chunk % 4) * 2048 + c * 2 + b
        return base

    mrep = 0
    for b in range(NB):
        mrep = build_rep(2 * b + 0, b, D, ONE, mrep)
        mrep = build_rep(2 * b + 1, b, 0, Z, mrep)
    em.wait("ve", "s_sc", mrep)

    p1m = {}
    zev = {0: mrep, 1: mrep}
    zuse = {}

    def build_p1_tile(t1):
        items = []
        t0 = t1 * NT1

        def ln_items(b, s):
            slot = 2 * b + s
            src_ap = x_sb[:, slot * D:(slot + 1) * D]
            par_h = slot % 2

            def i_load(em, b=b, s=s, slot=slot, t0=t0, src_ap=src_ap):
                em.wait("sy", "s_p1", x_free[slot][0])
                em.wait("sy", "s_sc", x_free[slot][1])
                em.dma_fence("sy", "s_ld")
                p1m[('xl', slot)] = em.op("sy", lambda e: e.dma_start(
                    src_ap, x_d.ap()[b, t0 + 128 * s:t0 + 128 * (s + 1), :]),
                    inc="s_ld", k=16)

            def i_st1(em, slot=slot, src_ap=src_ap):
                em.wait("ve", "s_ld", p1m[('xl', slot)])
                em.op("ve", lambda e: e.reduce_sum(stat[:, 0:1], src_ap, axis=AX.X))
                em.wait("sc", "s_ld", p1m[('xl', slot)])
                em.drain("sc")
                p1m['sq'] = em.op("sc", lambda e: e.activation(
                    sq_dump[:], src_ap, AF.Square, bias=Z, accum_out=stat[:, 1:2]),
                    inc="s_sc")

            def i_st2(em):
                em.op("ve", lambda e: e.tensor_scalar(stat[:, 2:3], stat[:, 0:1],
                                                      1.0 / D, None, op0=ALU.mult))
                em.wait("ve", "s_sc", p1m['sq'])
                em.op("ve", lambda e: e.tensor_scalar(stat[:, 4:5], stat[:, 1:2],
                                                      1.0 / D, EPS, op0=ALU.mult, op1=ALU.add))
                em.op("ve", lambda e: e.tensor_tensor(stat[:, 5:6], stat[:, 2:3],
                                                      stat[:, 2:3], op=ALU.mult))
                p1m['var'] = em.op("ve", lambda e: e.tensor_tensor(
                    stat[:, 4:5], stat[:, 4:5], stat[:, 5:6], op=ALU.subtract), inc="s_p1")

            def i_st3(em):
                em.wait("sc", "s_p1", p1m['var'])
                p1m['sd'] = em.op("sc", lambda e: e.activation(
                    stat[:, 5:6], stat[:, 4:5], AF.Sqrt, bias=Z), inc="s_sc")

            def i_st4(em, slot=slot):
                em.wait("ve", "s_sc", p1m['sd'])
                em.op("ve", lambda e: e.reciprocal(stat[:, 3:4], stat[:, 5:6]))
                p1m['nm'] = em.op("ve", lambda e: e.tensor_scalar(
                    stat[:, 6:7], stat[:, 2:3], stat[:, 3:4], -1.0,
                    op0=ALU.mult, op1=ALU.mult), inc="s_p1")

            def i_xn(em, slot=slot, src_ap=src_ap):
                em.wait("sc", "s_p1", p1m['nm'])
                p1m[('xn', slot)] = em.op("sc", lambda e: e.activation(
                    tm_bf[:], src_ap, AF.Identity, scale=stat[:, 3:4],
                    bias=stat[:, 6:7]), inc="s_sc")

            def i_mod(em, b=b, slot=slot, par_h=par_h):
                em.wait("ve", "s_sc", p1m[('xn', slot)])
                em.op("ve", lambda e: e.tensor_tensor(
                    tm_bf[:], tm_bf[:], rep[:, (2 * b) * D:(2 * b + 1) * D], op=ALU.mult))
                mh = em.op("ve", lambda e: e.tensor_tensor(
                    h_tmb[:, par_h * D:(par_h + 1) * D], tm_bf[:],
                    rep[:, (2 * b + 1) * D:(2 * b + 2) * D], op=ALU.add), inc="s_p1")
                p1m[('h', slot)] = mh
                x_free[slot] = (p1m['nm'], p1m[('xn', slot)])

            def i_tr(em, slot=slot, par_h=par_h, s=s):
                em.wait("te", "s_p1", p1m[('h', slot)])
                em.wait("te", pbf_free[par_h][0], pbf_free[par_h][1])
                for k in range(KC):
                    em.op("te", lambda e, par_h=par_h, k=k: e.transpose(
                        ps_bf[:, par_h * 1024 + k * 128:par_h * 1024 + (k + 1) * 128],
                        h_tmb[:, par_h * D + k * 128:par_h * D + (k + 1) * 128], id_b[:]))
                p1m[('tr', slot)] = em.inc_last("te", "s_tr")

            def i_ev(em, slot=slot, par_h=par_h, s=s):
                em.wait("ve", "s_tr", p1m[('tr', slot)])
                dst3 = actT1[:].rearrange("p (k t) -> p k t", k=KC)[:, :, s * 128:(s + 1) * 128]
                src3 = ps_bf[:, par_h * 1024:(par_h + 1) * 1024].rearrange(
                    "p (k i) -> p k i", k=KC)
                mev = em.op("ve", lambda e, dst3=dst3, src3=src3: e.tensor_copy(dst3, src3),
                            inc="s_p1")
                pbf_free[par_h] = ("s_p1", mev)
                p1m[('ev', slot)] = mev
            return [i_load, i_st1, i_st2, i_st3, i_st4, i_xn, i_mod, i_tr, i_ev]

        def z_items(b, c, t1=t1):
            pz = c % 2
            chunk0 = 2 * t1

            def zi_mm(em, b=b, c=c, pz=pz):
                em.wait("te", "s_p1", p1m[('ev', 2 * b)])
                em.wait("te", "s_p1", p1m[('ev', 2 * b + 1)])
                em.wait("te", "s_sc", zev.get(pz, 0))
                for k in range(KC):
                    em.op("te", lambda e, c=c, k=k, pz=pz: e.matmul(
                        ps_mm[:, pz * 512:pz * 512 + NT1],
                        wgT[:, k * D + c * 128:k * D + (c + 1) * 128],
                        actT1[:, k * NT1:(k + 1) * NT1],
                        start=(k == 0), stop=(k == KC - 1)))
                p1m[('zmm', pz)] = em.inc_last("te", "s_mm")

            def zi_ev(em, c=c, pz=pz):
                em.wait("sc", "s_mm", p1m[('zmm', pz)])
                em.wait("sc", "s_p1", zuse.get(pz, 0))
                zev[pz] = em.op("sc", lambda e, pz=pz, c=c: e.activation(
                    tm_a[:, pz * 512:pz * 512 + NT1], ps_mm[:, pz * 512:pz * 512 + NT1],
                    AF.Sigmoid, bias=vecs[:, 24 + c:25 + c]), inc="s_sc")

            def zi_u(em, b=b, c=c, pz=pz, chunk0=chunk0):
                em.wait("ve", "s_sc", zev[pz])
                b0 = (chunk0 % 4) * 2048 + c * 2 + b
                b1 = ((chunk0 + 1) % 4) * 2048 + c * 2 + b
                em.op("ve", lambda e, b0=b0, pz=pz: e.tensor_scalar(
                    u_st[:, b0:b0 + 16 * 127 + 1:16], tm_a[:, pz * 512:pz * 512 + 128],
                    vecs[:, 8 + c:9 + c], None, op0=ALU.mult), drain=False)
                em.op("ve", lambda e, b1=b1, pz=pz: e.tensor_scalar(
                    u_st[:, b1:b1 + 16 * 127 + 1:16],
                    tm_a[:, pz * 512 + 128:pz * 512 + 256],
                    vecs[:, 8 + c:9 + c], None, op0=ALU.mult))

            def zi_v(em, b=b, c=c, pz=pz, chunk0=chunk0):
                b0 = (chunk0 % 4) * 2048 + c * 2 + b
                b1 = ((chunk0 + 1) % 4) * 2048 + c * 2 + b
                em.op("ve", lambda e, b0=b0, c=c: e.scalar_tensor_tensor(
                    v_st[:, b0:b0 + 16 * 127 + 1:16],
                    actT1[:, c * NT1:c * NT1 + 128],
                    vecs[:, 16 + c:17 + c],
                    u_st[:, b0:b0 + 16 * 127 + 1:16],
                    op0=ALU.subtract, op1=ALU.mult), drain=False)
                muv = em.op("ve", lambda e, b1=b1, c=c: e.scalar_tensor_tensor(
                    v_st[:, b1:b1 + 16 * 127 + 1:16],
                    actT1[:, c * NT1 + 128:(c + 1) * NT1],
                    vecs[:, 16 + c:17 + c],
                    u_st[:, b1:b1 + 16 * 127 + 1:16],
                    op0=ALU.subtract, op1=ALU.mult), inc="s_p1")
                zuse[pz] = muv
            return [zi_mm, zi_ev, zi_u, zi_v]

        for b in range(NB):
            for s in range(2):
                items += ln_items(b, s)
            for c in range(KC):
                items += z_items(b, c)
        return items

    for t1 in range(min(2, NTIL1)):
        for it in build_p1_tile(t1):
            it(em)
    p1q = []
    p1_need = {}
    for t1 in range(2, NTIL1):
        w = 256 * t1 - 255
        for it in build_p1_tile(t1):
            p1q.append((w, it))
        p1_need[2 * t1] = len(p1q)
    p1pos = [0]

    def drain_p1(t, n):
        while n > 0 and p1pos[0] < len(p1q) and p1q[p1pos[0]][0] <= t:
            p1q[p1pos[0]][1](em)
            p1pos[0] += 1
            n -= 1

    def gran_src(g):
        if g < FC:
            return wfc_d.ap()[g * 128:(g + 1) * 128, :]
        gg = g - FC
        r, qf = gg // 4, gg % 4
        return wpr_d.ap()[r * 128:(r + 1) * 128, qf * D:(qf + 1) * D]

    qitems = []
    qm = {}

    def qi_init(em):
        qm[('red', -1)] = em.op("ve", lambda e: e.memset(mx[:, 0:2], 0.0), inc="s_q")
    qitems.append(qi_init)

    for g in range(64):
        def qi_mload(em, g=g):
            if g >= 2:
                em.wait("gp", "s_q", qm[('red', g - 2)])
            em.dma_fence("gp", "s_gd")
            qm[('mld', g)] = em.op("gp", lambda e, g=g: e.dma_start(
                qload[:, (g % 2) * D:(g % 2 + 1) * D], gran_src(g)), inc="s_gd", k=16)

        def qi_mred(em, g=g):
            em.wait("ve", "s_gd", qm[('mld', g)])
            em.op("ve", lambda e, g=g: e.tensor_reduce(
                mx[:, 2:3], qload[:, (g % 2) * D:(g % 2 + 1) * D], axis=AX.X,
                op=ALU.max, apply_absolute_value=True))
            col = 0 if g < FC else 1
            qm[('red', g)] = em.op("ve", lambda e, col=col: e.tensor_tensor(
                mx[:, col:col + 1], mx[:, col:col + 1], mx[:, 2:3], op=ALU.max), inc="s_q")
        qitems += [qi_mload, qi_mred]

    def qi_fin(em):
        em.wait("te", "s_q", qm[('red', 63)])
        em.op("te", lambda e: e.transpose(ps_lo[:, 0:128][0:1, :], mx[:, 0:1], id_f[:]))
        qm['fint'] = em.op("te", lambda e: e.transpose(
            ps_lo[:, 128:256][0:1, :], mx[:, 1:2], id_f[:]), inc="s_qt")
        em.wait("ve", "s_qt", qm['fint'])
        em.op("ve", lambda e: e.tensor_reduce(mx[0:1, 4:5], ps_lo[0:1, 0:128], axis=AX.X, op=ALU.max))
        em.op("ve", lambda e: e.tensor_reduce(mx[0:1, 6:7], ps_lo[0:1, 128:256], axis=AX.X, op=ALU.max))
        em.op("ve", lambda e: e.reciprocal(mx[0:1, 5:6], mx[0:1, 4:5]))
        em.op("ve", lambda e: e.reciprocal(mx[0:1, 7:8], mx[0:1, 6:7]))
        em.op("ve", lambda e: e.tensor_scalar(mx[0:1, 4:5], mx[0:1, 4:5],
                                              1.0 / 127.0, None, op0=ALU.mult))
        em.op("ve", lambda e: e.tensor_scalar(mx[0:1, 6:7], mx[0:1, 6:7],
                                              1.0 / 127.0, None, op0=ALU.mult))
        em.op("ve", lambda e: e.tensor_scalar(mx[0:1, 5:6], mx[0:1, 5:6],
                                              127.0, None, op0=ALU.mult))
        qm['finv'] = em.op("ve", lambda e: e.tensor_scalar(
            mx[0:1, 7:8], mx[0:1, 7:8], 127.0, None, op0=ALU.mult), inc="s_q")
        em.wait("te", "s_q", qm['finv'])
        for i, mx_col in enumerate((4, 5, 6, 7)):
            em.op("te", lambda e, i=i, mx_col=mx_col: e.matmul(
                ps_lo[:, 256 + i:257 + i], ones1[0:1, :], mx[0:1, mx_col:mx_col + 1],
                start=True, stop=True))
        mtr = em.inc_last("te", "s_qt")
        em.wait("ve", "s_qt", mtr)
        for i, sc_col in enumerate((48, 49, 50, 51)):
            em.op("ve", lambda e, i=i, sc_col=sc_col: e.tensor_copy(
                vecs[:, sc_col:sc_col + 1], ps_lo[:, 256 + i:257 + i]))
        qm['fbc'] = em.inc_last("ve", "s_q")
    qitems.append(qi_fin)

    for g in range(64):
        iscale = vecs[:, 49:50] if g < FC else vecs[:, 51:52]
        mscale = vecs[:, 48:49] if g < FC else vecs[:, 50:51]

        def qi_load(em, g=g):
            em.wait("gp", "s_q",
                    qm[('wq', g - 2)] if g >= 2 else qm[('red', {0: 62, 1: 63}[g % 2])])
            em.dma_fence("gp", "s_gd")
            qm[('ld', g)] = em.op("gp", lambda e, g=g: e.dma_start(
                qload[:, (g % 2) * D:(g % 2 + 1) * D], gran_src(g)), inc="s_gd", k=16)

        def qi_q(em, g=g, iscale=iscale):
            em.wait("sc", "s_gd", qm[('ld', g)])
            if g == 0:
                em.wait("sc", "s_q", qm['fbc'])
            else:
                em.wait("sc", "s_q", qm[('wq', g - 1)])
            em.op("sc", lambda e, g=g, iscale=iscale: e.activation(
                qr[:], qload[:, (g % 2) * D:(g % 2 + 1) * D], AF.Identity,
                scale=iscale, bias=Z), drain=True)

        def qi_abs(em, g=g):
            em.op("sc", lambda e: e.activation(qa[:], qr[:], AF.Abs, bias=Z))
            qm[('sgn', g)] = em.op("sc", lambda e, g=g: e.activation(
                qsgn[:, (g % 2) * D:(g % 2 + 1) * D], qr[:], AF.Sign, bias=Z), inc="s_sc")

        def qi_b0(em, g=g):
            em.wait("ve", "s_sc", qm[('sgn', g)])
            em.op("ve", lambda e: e.tensor_scalar(qt[:], qa[:], 63.5, 64.0,
                                                  op0=ALU.is_ge, op1=ALU.mult))

        def qi_b0b(em, g=g):
            em.op("ve", lambda e: e.tensor_tensor(qr[:], qa[:], qt[:], op=ALU.subtract))

        def mk_bit(j):
            def qi_bj(em, j=j):
                em.op("ve", lambda e, j=j: e.tensor_scalar(
                    qt[:], qr[:], j - 0.5, float(j), op0=ALU.is_ge, op1=ALU.mult))
            def qi_bjs(em):
                em.op("ve", lambda e: e.tensor_tensor(qr[:], qr[:], qt[:], op=ALU.subtract))
            return [qi_bj, qi_bjs]

        def qi_rnd(em, g=g):
            qm[('rnd', g)] = em.op("ve", lambda e: e.tensor_tensor(
                qt[:], qa[:], qr[:], op=ALU.subtract), inc="s_q")

        def qi_sca(em, g=g, mscale=mscale):
            em.wait("sc", "s_q", qm[('rnd', g)])
            qm[('sca', g)] = em.op("sc", lambda e, mscale=mscale: e.activation(
                qr[:], qt[:], AF.Identity, scale=mscale, bias=Z), inc="s_sc",
                drain=True)

        def qi_sgm(em, g=g):
            em.wait("ve", "s_sc", qm[('sca', g)])
            qm[('wq', g)] = em.op("ve", lambda e, g=g: e.tensor_tensor(
                qb[:, (g % 2) * D:(g % 2 + 1) * D], qr[:],
                qsgn[:, (g % 2) * D:(g % 2 + 1) * D], op=ALU.mult), inc="s_q")

        def qi_tr(em, g=g):
            em.wait("te", "s_q", qm[('wq', g)])
            em.wait("te", "s_sc", qm.get(('ev', g - 1), 0))
            for k in range(KC):
                em.op("te", lambda e, g=g, k=k: e.transpose(
                    ps_x[:, k * 128:(k + 1) * 128],
                    qb[:, (g % 2) * D + k * 128:(g % 2) * D + (k + 1) * 128], id_b[:]))
            qm[('tr', g)] = em.inc_last("te", "s_qt")

        def qi_ev(em, g=g):
            em.wait("sc", "s_qt", qm[('tr', g)])
            if g >= 2:
                em.wait("sc", "s_gd", qm[('out', g - 2)])
            qm[('ev', g)] = em.op("sc", lambda e, g=g: e.activation(
                qstage[:, (g % 2) * D:(g % 2 + 1) * D], ps_x[:], AF.Identity,
                bias=Z), inc="s_sc")

        def qi_out(em, g=g):
            em.wait("gp", "s_sc", qm[('ev', g)])
            em.dma_fence("gp", "s_gd")
            if g < FC:
                dst = wqfc_h.ap()[g, :, :]
            else:
                gg = g - FC
                r, qf = gg // 4, gg % 4
                dst = wqpr_h.ap()[r, :, qf * D:(qf + 1) * D]
            qm[('out', g)] = em.op("gp", lambda e, g=g, dst=dst: e.dma_start(
                dst, qstage[:, (g % 2) * D:(g % 2 + 1) * D]), inc="s_gd", k=16)

        bit_items = []
        for j in (32, 16, 8, 4, 2, 1):
            bit_items += mk_bit(j)
        qitems += ([qi_load, qi_q, qi_abs, qi_b0, qi_b0b] + bit_items +
                   [qi_rnd, qi_sca, qi_sgm, qi_tr, qi_ev, qi_out])

    qpos = [0]

    def drain_quant(n):
        while n > 0 and qpos[0] < len(qitems):
            qitems[qpos[0]](em)
            qpos[0] += 1
            n -= 1

    base_dv = em.cnt["s_dv"]
    em.op("ve", lambda e: e.tensor_copy(y_fm[:, 0:8], v_st[:, 0:8]), inc="s_dv")
    em.op("ve", lambda e: e.tensor_copy(y_fm[:, 8:16], v_st[:, 8:16]), inc="s_dv")

    flush_ev = {}
    ydma_mark = {}
    flushq = []
    flpos = [0]

    def queue_flush(B):
        buf = B % 2
        fm = {}

        def mk(b0):
            bank = b0
            ytm = y_tm[:, b0 * 1024:(b0 + 1) * 1024]

            def f_tr1(em, b0=b0, bank=bank):
                em.wait("te", "s_dv", base_dv + 2 * (B + 1) * 128)
                em.wait("te", "s_fl", flush_ev.get(B - 1, 0))
                em.wait("te", pbf_free[bank][0], pbf_free[bank][1])
                for c in range(KC):
                    st = buf * 2048 + c * 2 + b0
                    em.op("te", lambda e, bank=bank, c=c, st=st: e.transpose(
                        ps_bf[:, bank * 1024 + c * 128:bank * 1024 + (c + 1) * 128],
                        y_fm[:, st:st + 16 * 127 + 1:16], id_b[:]))
                fm['tr1'] = em.inc_last("te", "s_tr")

            def f_ev1(em, b0=b0, bank=bank):
                em.wait("sc", "s_tr", fm['tr1'])
                em.wait("sc", "s_ydma", ydma_mark.get(B - 1, 0))
                fm['mev'] = em.op("sc", lambda e, b0=b0, bank=bank: e.copy(
                    y_tm[:, b0 * 1024:(b0 + 1) * 1024],
                    ps_bf[:, bank * 1024:(bank + 1) * 1024]), inc="s_ev", drain=True)

            def f_st1(em, ytm=ytm):
                em.wait("ve", "s_ev", fm['mev'])
                em.op("ve", lambda e, ytm=ytm: e.reduce_sum(stat[:, 8:9], ytm, axis=AX.X))
                em.drain("sc")
                fm['msq'] = em.op("sc", lambda e, ytm=ytm: e.activation(
                    sq_dump[:], ytm, AF.Square, bias=Z, accum_out=stat[:, 9:10]),
                    inc="s_sc")
                em.op("ve", lambda e: e.tensor_scalar(stat[:, 10:11], stat[:, 8:9],
                                                      1.0 / D, None, op0=ALU.mult),
                      drain=False)

            def f_st2(em):
                em.wait("ve", "s_sc", fm['msq'])
                em.op("ve", lambda e: e.tensor_scalar(stat[:, 12:13], stat[:, 9:10],
                                                      1.0 / D, EPS, op0=ALU.mult,
                                                      op1=ALU.add))
                em.op("ve", lambda e: e.tensor_tensor(stat[:, 13:14], stat[:, 10:11],
                                                      stat[:, 10:11], op=ALU.mult))
                fm['mdv'] = em.op("ve", lambda e: e.tensor_tensor(
                    stat[:, 12:13], stat[:, 12:13], stat[:, 13:14], op=ALU.subtract),
                    inc="s_fl", drain=True)

            def f_sd(em):
                em.wait("sc", "s_fl", fm['mdv'])
                fm['msd'] = em.op("sc", lambda e: e.activation(
                    stat[:, 13:14], stat[:, 12:13], AF.Sqrt, bias=Z), inc="s_sc")

            def f_rc(em):
                em.wait("ve", "s_sc", fm['msd'])
                em.op("ve", lambda e: e.reciprocal(stat[:, 11:12], stat[:, 13:14]))
                fm['mnm'] = em.op("ve", lambda e: e.tensor_scalar(
                    stat[:, 14:15], stat[:, 10:11], stat[:, 11:12], -1.0,
                    op0=ALU.mult, op1=ALU.mult), inc="s_fl")

            def f_no(em, b0=b0, ytm=ytm):
                em.wait("sc", "s_fl", fm['mnm'])
                fm['mno'] = em.op("sc", lambda e, b0=b0, ytm=ytm: e.activation(
                    y_tm[:, b0 * 1024:(b0 + 1) * 1024], ytm, AF.Identity,
                    scale=stat[:, 11:12], bias=stat[:, 14:15]), inc="s_sc",
                    drain=False)

            def f_tr2(em, b0=b0, bank=bank):
                em.wait("te", "s_sc", fm['mno'])
                for k in range(KC):
                    em.op("te", lambda e, bank=bank, b0=b0, k=k: e.transpose(
                        ps_bf[:, bank * 1024 + k * 128:bank * 1024 + (k + 1) * 128],
                        y_tm[:, b0 * 1024 + k * 128:b0 * 1024 + (k + 1) * 128],
                        id_b[:]))
                fm['tr2'] = em.inc_last("te", "s_tr")

            def f_ev2(em, b0=b0, bank=bank):
                em.wait("ve", "s_tr", fm['tr2'])
                em.wait("ve", "s_ydma", em.cnt["s_ydma"])
                mev = em.op("ve", lambda e, bank=bank: e.tensor_copy(
                    ystage[:], ps_bf[:, bank * 1024:(bank + 1) * 1024]),
                    inc="s_fl", drain=False)
                pbf_free[bank] = ("s_fl", mev)
                fm['mev2'] = mev
                if b0 == NB - 1:
                    flush_ev[B] = mev

            def f_dma(em, b0=b0):
                em.wait("sy", "s_fl", fm['mev2'])
                em.dma_fence("sy", "s_ydma")
                dstv = ynT_h.ap()[b0, :, :, B * 128:(B + 1) * 128].rearrange(
                    "k p i -> p k i")
                srcv = ystage[:].rearrange("p (k i) -> p k i", k=KC)
                em.op("sy", lambda e, dstv=dstv, srcv=srcv: e.dma_start(dstv, srcv),
                      inc="s_ydma", k=16)
                if b0 == NB - 1:
                    ydma_mark[B] = em.cnt["s_ydma"]
            return [f_tr1, f_ev1, f_st1, f_st2, f_sd, f_rc, f_no, f_tr2, f_ev2,
                    f_dma]

        for b0 in range(NB):
            flushq.extend(mk(b0))

    def drain_flush(n):
        while n > 0 and flpos[0] < len(flushq):
            flushq[flpos[0]](em)
            flpos[0] += 1
            n -= 1

    m_mm_h = {0: 0, 1: 0}
    for t in range(1, L):
        B = t // 128
        buf = B % 2
        pbuf = ((t - 1) // 128) % 2
        par = t % 2
        po = t % 128
        ppo = (t - 1) % 128
        if po == 0:
            queue_flush(B - 1)
            assert p1pos[0] >= p1_need.get(B, 0), (B, p1pos[0])
        em.wait("te", "s_dv", base_dv + 2 * t)
        prev0 = pbuf * 2048 + ppo * 16
        for e_ in range(KC):
            hb = (e_ // 4) * 512
            fn0 = lambda e, e_=e_, hb=hb, prev0=prev0: e.matmul(
                ps_sc[:, hb + (e_ % 4) * 2:hb + (e_ % 4) * 2 + 2],
                wrT[:, 0 * D + e_ * 128:0 * D + (e_ + 1) * 128],
                y_fm[:, prev0 + 0:prev0 + 2],
                start=(e_ % 4 == 0), stop=False)
            em.op("te", fn0)
            for k in range(1, KC):
                fn = lambda e, e_=e_, k=k, hb=hb, prev0=prev0: e.matmul(
                    ps_sc[:, hb + (e_ % 4) * 2:hb + (e_ % 4) * 2 + 2],
                    wrT[:, k * D + e_ * 128:k * D + (e_ + 1) * 128],
                    y_fm[:, prev0 + k * 2:prev0 + k * 2 + 2],
                    start=False, stop=(k == KC - 1 and e_ % 4 == 3))
                if k == KC - 1 and e_ in (3, KC - 1):
                    m_mm_h[e_ // 4] = em.op("te", fn, inc="s_mm")
                else:
                    em.op("te", fn)
        if po == 0:
            em.wait("ve", "s_fl", flush_ev.get(B - 2, 0))
        cur0 = buf * 2048 + po * 16
        uv0 = (B % 4) * 2048 + po * 16
        if t == 1:
            em.op("ve", lambda e, prev0=prev0: e.tensor_tensor(
                scan_tmp[:, 32:48], y_fm[:, prev0:prev0 + 16], dec_bc[:], op=ALU.mult))
            em.op("ve", lambda e, uv0=uv0: e.tensor_tensor(
                scan_tmp[:, 48:64], scan_tmp[:, 32:48], v_st[:, uv0:uv0 + 16],
                op=ALU.add))
        em.wait("ve", "s_mm", m_mm_h[0])
        em.op("ve", lambda e, uv0=uv0: e.tensor_tensor(
            scan_tmp[:, 0:8], ps_sc[:, 0:8],
            u_st[:, uv0:uv0 + 8], op=ALU.mult))
        em.op("ve", lambda e, cur0=cur0: e.tensor_tensor(
            y_fm[:, cur0:cur0 + 8], scan_tmp[:, 48:56], scan_tmp[:, 0:8],
            op=ALU.subtract), inc="s_dv", drain=False)
        em.wait("ve", "s_mm", m_mm_h[1])
        em.op("ve", lambda e, uv0=uv0: e.tensor_tensor(
            scan_tmp[:, 8:16], ps_sc[:, 512:520],
            u_st[:, uv0 + 8:uv0 + 16], op=ALU.mult))
        em.op("ve", lambda e, cur0=cur0: e.tensor_tensor(
            y_fm[:, cur0 + 8:cur0 + 16], scan_tmp[:, 56:64], scan_tmp[:, 8:16],
            op=ALU.subtract), inc="s_dv")
        if t < L - 1:
            nt = t + 1
            nuv0 = ((nt // 128) % 4) * 2048 + (nt % 128) * 16
            em.op("ve", lambda e, cur0=cur0: e.tensor_tensor(
                scan_tmp[:, 32:48], y_fm[:, cur0:cur0 + 16], dec_bc[:], op=ALU.mult))
            em.op("ve", lambda e, nuv0=nuv0: e.tensor_tensor(
                scan_tmp[:, 48:64], scan_tmp[:, 32:48], v_st[:, nuv0:nuv0 + 16],
                op=ALU.add), drain=False)
        drain_flush(1)
        if t >= 2:
            drain_quant(1)
        if t % 2 == 1:
            drain_p1(t, 1)
    queue_flush(NCH - 1)
    drain_flush(10 ** 9)
    drain_quant(10 ** 9)
    m_wq_all = em.cnt["s_gd"]
    m_y_all = em.cnt["s_ydma"]

    em.barrier()
    em.realize()
    for c in reversed(cm_p2):
        c.__exit__(None, None, None)
    cm_wgT.__exit__(None, None, None)
    cm_wrT.__exit__(None, None, None)
    if max_phase <= 2:
        return _close_truncated(nc, em)

    cm_p3 = [nc.sbuf_tensor("actT", [128, KC * NT3], BF16),
             nc.sbuf_tensor("ynT", [128, 2 * KC * NT3], BF16),
             nc.sbuf_tensor("att_tm", [128, 4 * D], BF16),
             nc.sbuf_tensor("att_fm", [128, KC * NT3], BF16),
             nc.sbuf_tensor("h3", [128, FC * NT3], BF16),
             nc.sbuf_tensor("wqf_sb", [128, 2 * 2 * KC * 128], BF16),
             nc.sbuf_tensor("wqp_sb", [128, 2 * FC * 128], BF16)]
    (actT, ynT, att_tm, att_fm, h3, wqf_sb, wqp_sb) = [c.__enter__() for c in cm_p3]

    out_mark = [0, 0, 0, 0]
    yin_mark = [0]
    mm_ev = {0: 0, 1: 0}
    wqf_mark = {}
    wqp_mark = {}
    wqf_free = {0: 0, 1: 0}
    wqp_free = {0: 0, 1: 0}
    first_wq = [True]

    def emit_wqf_load(jg):
        pz = jg % 2
        if first_wq[0]:
            em.wait("gp", "s_gd", m_wq_all)
            em.wait("gp", "s_ydma", m_y_all)
            first_wq[0] = False
        em.wait("gp", "s_mm", wqf_free[pz])
        em.dma_fence("gp", "s_gd")
        dstv = wqf_sb[:, pz * 2 * KC * 128:(pz + 1) * 2 * KC * 128].rearrange(
            "p (j c) -> p j c", j=2)
        srcv = wqfc_h.ap()[jg * 2:(jg + 1) * 2, :, :].rearrange("j p c -> p j c")
        wqf_mark[jg] = em.op("gp", lambda e, dstv=dstv, srcv=srcv: e.dma_start(
            dstv, srcv), inc="s_gd", k=16)

    def emit_wqp_load(m):
        pz = m % 2
        em.wait("gp", "s_mm", wqp_free[pz])
        em.dma_fence("gp", "s_gd")
        wqp_mark[m] = em.op("gp", lambda e, m=m, pz=pz: e.dma_start(
            wqp_sb[:, pz * FC * 128:(pz + 1) * FC * 128], wqpr_h.ap()[m, :, :]),
            inc="s_gd", k=16)

    em.wait("sc", "s_ydma", m_y_all)
    yn_mark = {}

    def emit_ynT_load(bb, tt0, par):
        em.wait("sc", "s_mm", yin_mark[0])
        em.dma_fence("sc", "s_scd")
        dstv = ynT[:, par * KC * NT3:(par + 1) * KC * NT3].rearrange(
            "p (k t) -> p k t", k=KC)
        srcv = ynT_h.ap()[bb, :, :, tt0:tt0 + NT3].rearrange("k p t -> p k t")
        return em.op("sc", lambda e, dstv=dstv, srcv=srcv: e.dma_start(dstv, srcv),
                     inc="s_scd", k=16)

    for b in range(NB):
        em.wait("sc", "s_dv", em.cnt["s_dv"])
        em.wait("te", "s_dv", em.cnt["s_dv"])
        mrep3 = em.cnt["s_sc"]
        mrep3 = build_rep(0, b, 4 * D, ONE, mrep3)
        mrep3 = build_rep(1, b, 3 * D, Z, mrep3)
        em.wait("ve", "s_sc", mrep3)
        em.wait("te", "s_sc", mrep3)
        for gi, goff in ((0, 2 * D), (2, 5 * D)):
            for k in range(KC):
                em.op("te", lambda e, b=b, goff=goff, k=k: e.transpose(
                    ps_sc[:, k:k + 1],
                    mod_sb[32 * b:32 * b + 1, goff + k * 128:goff + (k + 1) * 128],
                    ones1[32 * b:32 * b + 1, 0:1]))
            mtr = em.inc_last("te", "s_tr")
            em.wait("ve", "s_tr", mtr)
            em.op("ve", lambda e, gi=gi: e.tensor_copy(
                gfm[:, gi * 8:(gi + 1) * 8], ps_sc[:, 0:8]))
            bcol = vecs[:, 32:40] if gi == 0 else vecs[:, 40:48]
            mgf = em.op("ve", lambda e, gi=gi, bcol=bcol: e.tensor_tensor(
                gfm[:, (gi + 1) * 8:(gi + 2) * 8], gfm[:, gi * 8:(gi + 1) * 8],
                bcol, op=ALU.mult), inc="s_dv")
            em.wait("te", "s_dv", mgf)
            em.wait("sc", "s_dv", mgf)
        for t3 in range(NTIL3):
            t0 = t3 * NT3
            mxl = [0] * NS3
            myl = [0] * NS3
            for s in range(NS3):
                em.wait("sy", "s_out", out_mark[s])
                em.dma_fence("sy", "s_ld")
                mxl[s] = em.op("sy", lambda e, b=b, s=s, t0=t0: e.dma_start(
                    x_sb[:, s * D:(s + 1) * D],
                    x_d.ap()[b, t0 + 128 * s:t0 + 128 * (s + 1), :]), inc="s_ld", k=16)
            ti = b * NTIL3 + t3
            pyn = ti % 2
            if ti == 0:
                yn_mark[0] = emit_ynT_load(0, 0, 0)
            if ti + 1 < NB * NTIL3:
                bn, tn = divmod(ti + 1, NTIL3)
                yn_mark[(ti + 1) % 2] = emit_ynT_load(bn, tn * NT3, (ti + 1) % 2)
            myn = yn_mark[pyn]
            emit_wqf_load(0)
            emit_wqf_load(1)
            m_att = 0
            em.wait("te", "s_scd", myn)
            ev_c = [0] * KC
            for c in range(KC):
                if c % 4 == 0:
                    em.wait("te", "s_sc", em.cnt["s_sc"])
                pb = (ps_sc, 0) if c % 4 == 0 else \
                     (ps_sc, 512) if c % 4 == 1 else \
                     (ps_mm, 0) if c % 4 == 2 else (ps_mm, 512)
                for k in range(KC):
                    em.op("te", lambda e, c=c, k=k, pb=pb, pyn=pyn: e.matmul(
                        pb[0][:, pb[1]:pb[1] + NT3],
                        woT[:, k * D + c * 128:k * D + (c + 1) * 128],
                        ynT[:, pyn * KC * NT3 + k * NT3:pyn * KC * NT3 + (k + 1) * NT3],
                        start=(k == 0), stop=(k == KC - 1)))
                m_mm = em.inc_last("te", "s_mm")
                em.wait("sc", "s_mm", m_mm)
                ev_c[c] = em.op("sc", lambda e, c=c, pb=pb: e.activation(
                    att_fm[:, c * NT3:(c + 1) * NT3], pb[0][:, pb[1]:pb[1] + NT3],
                    AF.Identity, scale=gfm[:, c:c + 1], bias=gfm[:, 8 + c:9 + c]),
                    inc="s_sc")
            yn_used = em.cnt["s_mm"]
            mxp = [0] * NS3
            m_act2 = 0
            for s4 in range(NS3):
                pz = s4 % 2
                em.wait("te", "s_sc", ev_c[KC - 1])
                em.wait("te", pbf_free[pz][0], pbf_free[pz][1])
                for c in range(KC):
                    em.op("te", lambda e, c=c, s4=s4, pz=pz: e.transpose(
                        ps_bf[:, pz * 1024 + c * 128:pz * 1024 + (c + 1) * 128],
                        att_fm[:, c * NT3 + s4 * 128:c * NT3 + (s4 + 1) * 128], id_b[:]))
                mtr = em.inc_last("te", "s_tr")
                em.wait("ve", "s_tr", mtr)
                m_att = em.op("ve", lambda e, s4=s4, pz=pz: e.tensor_copy(
                    att_tm[:, s4 * D:(s4 + 1) * D],
                    ps_bf[:, pz * 1024:(pz + 1) * 1024]), inc="s_dv", drain=True)
                pbf_free[pz] = ("s_dv", m_att)
                em.wait("ve", "s_ld", mxl[s4])
                mxp[s4] = em.op("ve", lambda e, s4=s4: e.tensor_tensor(
                    x_sb[:, s4 * D:(s4 + 1) * D], x_sb[:, s4 * D:(s4 + 1) * D],
                    att_tm[:, s4 * D:(s4 + 1) * D], op=ALU.add), inc="s_dv")
                _, mh = emit_ln_modulate(
                    x_sb[:, s4 * D:(s4 + 1) * D], s4 % 2, rep[:, 0:D], rep[:, D:2 * D],
                    None, None, mxp[s4], "s_dv")
                m_act2 = emit_act_transpose(s4 % 2, mh, actT, s4 * 128)
            yin_mark[0] = yn_used
            em.wait("te", "s_dv", m_act2)
            for k in range(KC):
                em.op("te", lambda e, k=k: e.matmul(
                    ps_lo[0:R, 0:NT3], afcT[:, k * R:(k + 1) * R], actT_full(actT, k),
                    start=(k == 0), stop=(k == KC - 1)))
            m_mm = em.inc_last("te", "s_mm")
            em.wait("ve", "s_mm", m_mm)
            m_s2 = em.op("ve", lambda e: e.tensor_scalar(
                s2[0:R, :], ps_lo[0:R, 0:NT3], 2.0, None, op0=ALU.mult), inc="s_dv")
            em.wait("te", "s_dv", m_s2)
            for j in range(FC):
                jg = j // 2
                pz4 = j % 4
                pg = jg % 2
                if j % 2 == 0:
                    em.wait("te", "s_gd", wqf_mark[jg])
                if j % 4 == 0:
                    em.wait("te", "s_sc", em.cnt["s_sc"])
                pb = (ps_mm, 0) if pz4 == 0 else \
                     (ps_mm, 512) if pz4 == 1 else \
                     (ps_sc, 0) if pz4 == 2 else (ps_sc, 512)
                base_w = pg * 2 * KC * 128 + (j % 2) * KC * 128
                for k in range(KC):
                    em.op("te", lambda e, k=k, pb=pb, base_w=base_w: e.matmul(
                        pb[0][:, pb[1]:pb[1] + NT3],
                        wqf_sb[:, base_w + k * 128:base_w + (k + 1) * 128],
                        actT_full(actT, k), start=(k == 0), stop=False))
                m_mm = em.op("te", lambda e, j=j, pb=pb: e.matmul(
                    pb[0][:, pb[1]:pb[1] + NT3],
                    bfcT[0:R, j * 128:(j + 1) * 128], s2[0:R, :],
                    start=False, stop=True), inc="s_mm")
                if j % 2 == 1:
                    wqf_free[pg] = m_mm
                em.wait("sc", "s_mm", m_mm)
                em.op("sc", lambda e, j=j, pb=pb: e.activation(
                    h3[:, j * NT3:(j + 1) * NT3], pb[0][:, pb[1]:pb[1] + NT3],
                    AF.Gelu, bias=bfc_fm[:, j:j + 1]), inc="s_sc")
                if j % 2 == 1 and jg + 2 < FC // 2:
                    emit_wqf_load(jg + 2)
            emit_wqp_load(0)
            emit_wqp_load(1)
            em.wait("te", "s_sc", em.cnt["s_sc"])
            for j in range(FC):
                em.op("te", lambda e, j=j: e.matmul(
                    ps_lo[0:R, 0:NT3], aprT[:, j * R:(j + 1) * R],
                    h3[:, j * NT3:(j + 1) * NT3],
                    start=(j == 0), stop=(j == FC - 1)))
            m_mm = em.inc_last("te", "s_mm")
            em.wait("ve", "s_mm", m_mm)
            m_s2p = em.op("ve", lambda e: e.tensor_scalar(
                s2[0:R, :], ps_lo[0:R, 0:NT3], 2.0, None, op0=ALU.mult), inc="s_dv")
            m_proj = 0
            em.wait("te", "s_dv", m_s2p)
            ev_m = [0] * KC
            for m_ in range(KC):
                pz4 = m_ % 4
                em.wait("te", "s_gd", wqp_mark[m_])
                if m_ % 4 == 0:
                    em.wait("te", "s_sc", em.cnt["s_sc"])
                pb = (ps_mm, 0) if pz4 == 0 else \
                     (ps_mm, 512) if pz4 == 1 else \
                     (ps_sc, 0) if pz4 == 2 else (ps_sc, 512)
                pzb = m_ % 2
                for k in range(FC):
                    em.op("te", lambda e, m_=m_, k=k, pb=pb, pzb=pzb: e.matmul(
                        pb[0][:, pb[1]:pb[1] + NT3],
                        wqp_sb[:, pzb * FC * 128 + k * 128:pzb * FC * 128 + (k + 1) * 128],
                        h3[:, k * NT3:(k + 1) * NT3], start=(k == 0), stop=False))
                m_mm = em.op("te", lambda e, m_=m_, pb=pb: e.matmul(
                    pb[0][:, pb[1]:pb[1] + NT3],
                    bprT[0:R, m_ * 128:(m_ + 1) * 128], s2[0:R, :],
                    start=False, stop=True), inc="s_mm")
                wqp_free[pzb] = m_mm
                em.wait("sc", "s_mm", m_mm)
                ev_m[m_] = em.op("sc", lambda e, m_=m_, pb=pb: e.activation(
                    att_fm[:, m_ * NT3:(m_ + 1) * NT3], pb[0][:, pb[1]:pb[1] + NT3],
                    AF.Identity, scale=gfm[:, 16 + m_:17 + m_],
                    bias=gfm[:, 24 + m_:25 + m_]), inc="s_sc")
                if m_ + 2 < KC:
                    emit_wqp_load(m_ + 2)
            for s4 in range(NS3):
                pz = s4 % 2
                em.wait("te", "s_sc", ev_m[KC - 1])
                em.wait("te", pbf_free[pz][0], pbf_free[pz][1])
                for m_ in range(KC):
                    em.op("te", lambda e, m_=m_, s4=s4, pz=pz: e.transpose(
                        ps_bf[:, pz * 1024 + m_ * 128:pz * 1024 + (m_ + 1) * 128],
                        att_fm[:, m_ * NT3 + s4 * 128:m_ * NT3 + (s4 + 1) * 128], id_b[:]))
                mtr = em.inc_last("te", "s_tr")
                em.wait("ve", "s_tr", mtr)
                m_proj = em.op("ve", lambda e, s4=s4, pz=pz: e.tensor_copy(
                    att_tm[:, s4 * D:(s4 + 1) * D],
                    ps_bf[:, pz * 1024:(pz + 1) * 1024]), inc="s_dv", drain=True)
                pbf_free[pz] = ("s_dv", m_proj)
                mo = em.op("ve", lambda e, s4=s4: e.tensor_tensor(
                    x_sb[:, s4 * D:(s4 + 1) * D], x_sb[:, s4 * D:(s4 + 1) * D],
                    att_tm[:, s4 * D:(s4 + 1) * D], op=ALU.add), inc="s_dv")
                em.wait("sy", "s_dv", mo)
                em.dma_fence("sy", "s_out")
                out_mark[s4] = em.op("sy", lambda e, b=b, s4=s4, t0=t0: e.dma_start(
                    out_d.ap()[b, t0 + 128 * s4:t0 + 128 * (s4 + 1), :],
                    x_sb[:, s4 * D:(s4 + 1) * D]), inc="s_out", k=16)
    em.wait("sy", "s_out", em.cnt["s_out"])
    em.realize()
    for c in reversed(cm_p3):
        c.__exit__(None, None, None)
    return nc


def _slice_inputs(inputs, core):
    b0 = NB * core
    m = {
        "x": np.ascontiguousarray(inputs["L0_attn_x"][b0:b0 + NB]),
        "cvec": np.ascontiguousarray(inputs["L0_c"][b0:b0 + NB]),
        "wmod": inputs["L0_wmod"],
        "decay": inputs["L0_attn_decay"],
        "Wr": inputs["L0_attn_Wr"], "br": inputs["L0_attn_br"],
        "Wg": inputs["L0_attn_Wg"], "bg": inputs["L0_attn_bg"],
        "Wo": inputs["L0_attn_Wo"], "bo": inputs["L0_attn_bo"],
        "ln_w": inputs["L0_attn_ln_w"], "ln_b": inputs["L0_attn_ln_b"],
        "W_fc": inputs["L0_ffn_W_fc"], "b_fc": inputs["L0_ffn_b_fc"],
        "A_fc": inputs["L0_ffn_A_fc"], "B_fc": inputs["L0_ffn_B_fc"],
        "W_proj": inputs["L0_ffn_W_proj"], "b_proj": inputs["L0_ffn_b_proj"],
        "A_proj": inputs["L0_ffn_A_proj"], "B_proj": inputs["L0_ffn_B_proj"],
    }
    return {k: np.ascontiguousarray(v) for k, v in m.items()}


def kernel(**inputs):
    nc = build_program(L=2048)
    in_maps = [_slice_inputs(inputs, i) for i in range(8)]
    res = run_bass_kernel_spmd(nc, in_maps, core_ids=list(range(8)))
    return np.concatenate([np.asarray(r["out"]) for r in res.results], axis=0)


# version 21
# speedup vs baseline: 1.1195x; 1.0003x over previous
import numpy as np
import concourse.bass as bass
import concourse.mybir as mybir
from concourse import library_config
from concourse.bass_utils import run_bass_kernel_spmd

F32 = mybir.dt.float32
BF16 = mybir.dt.bfloat16
AF = mybir.ActivationFunctionType
ALU = mybir.AluOpType
AX = mybir.AxisListType

D = 1024
FF = 4096
NB = 2
KC = D // 128
FC = FF // 128
R = 16
SCL = 2.0
EPS = 1e-5
ENGS = ("sy", "gp", "ve", "sc", "te")


class EM:

    def __init__(self, nc, sems):
        self.nc = nc
        self.sems = sems
        self.cnt = {k: 0 for k in sems}
        self.lists = {e: [] for e in ENGS}

    def op(self, eng, fn, inc=None, k=1, drain=None):
        incs = []
        ret = None
        if inc is not None:
            self.cnt[inc] += k
            incs.append((self.sems[inc], k))
            ret = self.cnt[inc]
        self.lists[eng].append(("op", fn, incs))
        if drain is None:
            drain = eng == "ve"
        if drain:
            self.lists[eng].append(("drain",))
        return ret

    def drain(self, eng):
        self.lists[eng].append(("drain",))

    def inc_last(self, eng, inc, k=1):
        for entry in reversed(self.lists[eng]):
            if entry[0] == "op":
                if not entry[2]:
                    self.cnt[inc] += k
                    entry[2].append((self.sems[inc], k))
                    return self.cnt[inc]
                self.lists[eng].append(("drain",))
                return self.op(eng, lambda e: e.nop(), inc=inc, k=k)
        return self.op(eng, lambda e: e.nop(), inc=inc, k=k)

    def dma_fence(self, eng, sem_name):
        self.wait(eng, sem_name, self.cnt[sem_name])

    def wait(self, eng, sem_name, val):
        if val < 0:
            return
        self.lists[eng].append(("wait", self.sems[sem_name], val))

    def barrier(self):
        for e in ENGS:
            if e in ("sy", "gp"):
                self.op(e, lambda x: x.nop(), inc="s_ph")
            else:
                self.inc_last(e, "s_ph")
        tgt = self.cnt["s_ph"]
        for e in ENGS:
            self.wait(e, "s_ph", tgt)

    @staticmethod
    def _replay(eng, entries):
        for entry in entries:
            if entry[0] == "op":
                instr = entry[1](eng)
                for sem, k in entry[2]:
                    instr.then_inc(sem, k)
            elif entry[0] == "wait":
                eng.wait_ge(entry[1], entry[2])
            else:
                eng.drain()

    def realize(self):
        with self.nc.Block() as blk:
            for e, meth in (("sy", blk.sync), ("gp", blk.gpsimd), ("ve", blk.vector),
                            ("sc", blk.scalar), ("te", blk.tensor)):
                entries = self.lists[e]
                if entries:
                    meth(lambda eng, entries=entries: EM._replay(eng, entries))
        self.lists = {e: [] for e in ENGS}



def _close_truncated(nc, em):
    with nc.Block() as blk:
        def _w(eng):
            for name in ("s_ld", "s_gd", "s_scd", "s_uv", "s_ydma", "s_out"):
                if em.cnt[name] > 0:
                    eng.wait_ge(em.sems[name], em.cnt[name])
        blk.sync(_w)
    return nc

def build_program(L=2048, max_phase=4):
    NCH = L // 128
    NT1 = 256
    NT3 = min(512, L)
    NS3 = NT3 // 128
    NTIL1 = L // NT1
    NTIL3 = L // NT3
    nc = bass.Bass("TRN2", target_bir_lowering=False)

    x_d = nc.dram_tensor("x", [NB, L, D], F32, kind="ExternalInput")
    c_d = nc.dram_tensor("cvec", [NB, D], F32, kind="ExternalInput")
    wmod_d = nc.dram_tensor("wmod", [D, 6 * D], F32, kind="ExternalInput")
    decay_d = nc.dram_tensor("decay", [D], F32, kind="ExternalInput")
    wr_d = nc.dram_tensor("Wr", [D, D], F32, kind="ExternalInput")
    br_d = nc.dram_tensor("br", [D], F32, kind="ExternalInput")
    wg_d = nc.dram_tensor("Wg", [D, D], F32, kind="ExternalInput")
    bg_d = nc.dram_tensor("bg", [D], F32, kind="ExternalInput")
    wo_d = nc.dram_tensor("Wo", [D, D], F32, kind="ExternalInput")
    bo_d = nc.dram_tensor("bo", [D], F32, kind="ExternalInput")
    lnw_d = nc.dram_tensor("ln_w", [D], F32, kind="ExternalInput")
    lnb_d = nc.dram_tensor("ln_b", [D], F32, kind="ExternalInput")
    wfc_d = nc.dram_tensor("W_fc", [FF, D], F32, kind="ExternalInput")
    bfc_d = nc.dram_tensor("b_fc", [FF], F32, kind="ExternalInput")
    afc_d = nc.dram_tensor("A_fc", [R, D], F32, kind="ExternalInput")
    Bfc_d = nc.dram_tensor("B_fc", [FF, R], F32, kind="ExternalInput")
    wpr_d = nc.dram_tensor("W_proj", [D, FF], F32, kind="ExternalInput")
    bpr_d = nc.dram_tensor("b_proj", [D], F32, kind="ExternalInput")
    apr_d = nc.dram_tensor("A_proj", [R, FF], F32, kind="ExternalInput")
    Bpr_d = nc.dram_tensor("B_proj", [D, R], F32, kind="ExternalInput")
    out_d = nc.dram_tensor("out", [NB, L, D], F32, kind="ExternalOutput")
    ynT_h = nc.dram_tensor("ynT_h", [NB, KC, 128, L], BF16)
    wqfc_h = nc.dram_tensor("wqfc_h", [FC, 128, KC * 128], BF16)
    wqpr_h = nc.dram_tensor("wqpr_h", [KC, 128, FC * 128], BF16)

    A = nc.alloc_sbuf_tensor
    woT = A("woT", [128, KC * D], BF16)
    mod_sb = A("mod_sb", [64, 6 * D], F32)
    rep = A("rep", [128, 4 * D], BF16)
    tm_bf = A("tm_bf", [128, D], BF16)
    vecs = A("vecs", [128, 64], F32)
    dec_bc = A("dec_bc", [128, 16], F32)
    lnb_b = A("lnb_b", [128, 8], BF16)
    gfm = A("gfm", [128, 32], F32)
    bfc_fm = A("bfc_fm", [128, FC], F32)
    id_b = A("id_b", [128, 128], BF16)
    id_f = A("id_f", [128, 128], F32)
    ones1 = A("ones1", [64, 128], F32)
    afcT = A("afcT", [128, KC * R], BF16)
    aprT = A("aprT", [128, FC * R], BF16)
    bfcT = A("bfcT", [R, FF], BF16)
    bprT = A("bprT", [R, D], BF16)
    x_sb = A("x_sb", [128, 4 * D], F32)
    tm_a = A("tm_a", [128, D], F32)
    h_tmb = A("h_tmb", [128, 2 * D], BF16)
    stat = A("stat", [128, 16], F32)
    sq_dump = A("sq_dump", [128, D], F32)
    mx = A("mx", [128, 8], F32)
    siluT = A("siluT", [128, 16], F32)
    scan_tmp = A("scan_tmp", [128, 64], F32)
    s2 = A("s2", [R, NT3], BF16)

    ps_mm = nc.alloc_psum_tensor("ps_mm", [128, 1024], F32)
    ps_sc = nc.alloc_psum_tensor("ps_sc", [128, 1024], F32)
    ps_bf = nc.alloc_psum_tensor("ps_bf", [128, 2048], BF16)
    ps_lo = nc.alloc_psum_tensor("ps_lo", [128, 512], F32)
    ps_x = nc.alloc_psum_tensor("ps_x", [128, 1024], BF16)

    sems = {}
    for name in ["s_ld", "s_gd", "s_scd", "s_mm", "s_dv", "s_sc", "s_gp",
                 "s_uv", "s_p1", "s_ydma", "s_ev", "s_fl", "s_tr", "s_q", "s_qt", "s_out", "s_ph"]:
        sems[name] = nc.alloc_semaphore(name)
    em = EM(nc, sems)

    Z = vecs[:, 52:53]
    ONE = vecs[:, 53:54]

    cm_wrT = nc.sbuf_tensor("wrT", [128, KC * D], BF16)
    wrT = cm_wrT.__enter__()
    cm_wgT = nc.sbuf_tensor("wgT", [128, KC * D], BF16)
    wgT = cm_wgT.__enter__()
    cm_p0 = [nc.sbuf_tensor("wstage", [128, 2 * D], F32),
             nc.sbuf_tensor("wmodf", [128, 6 * D], F32),
             nc.sbuf_tensor("csb", [64, D], F32)]
    wstage, wmodf, csb = [c.__enter__() for c in cm_p0]
    lora_st = wstage

    em.op("ve", lambda e: e.memset(vecs[:], 0.0))
    mz = em.op("ve", lambda e: e.memset(ones1[:], 1.0), inc="s_dv")
    em.wait("gp", "s_dv", mz)
    em.op("gp", lambda e: e.iota(vecs[:, 54:55], pattern=[[0, 1]], base=0,
                                 channel_multiplier=1, allow_small_or_imprecise_dtypes=True))
    mgp = em.op("gp", lambda e: e.iota(id_f[:], pattern=[[1, 128]], base=0, channel_multiplier=0,
                                       allow_small_or_imprecise_dtypes=True), inc="s_gp")
    em.wait("ve", "s_gp", mgp)
    em.op("ve", lambda e: e.tensor_scalar(id_f[:], id_f[:], vecs[:, 54:55], None,
                                          op0=ALU.is_equal), inc="s_dv")
    em.op("ve", lambda e: e.tensor_copy(id_b[:], id_f[:]), inc="s_dv")
    mID = em.op("ve", lambda e: e.memset(vecs[:, 53:54], 1.0), inc="s_dv")
    em.wait("te", "s_dv", mID)
    em.wait("sc", "s_dv", mID)

    def vec_to_fm(rows, dsts):
        mld = 0
        for r, dr in enumerate(rows):
            mld = em.op("sy", lambda e, r=r, dr=dr: e.dma_start(
                csb[32 * r:32 * r + 1, :], dr.ap().rearrange("(o d) -> o d", o=1)),
                inc="s_ld", k=16)
        em.wait("te", "s_ld", mld)
        for r in range(len(rows)):
            for k in range(KC):
                em.op("te", lambda e, r=r, k=k: e.transpose(
                    ps_sc[:, r * 512 + k:r * 512 + k + 1],
                    csb[32 * r:32 * r + 1, k * 128:(k + 1) * 128],
                    ones1[32 * r:32 * r + 1, 0:1]))
        mtr = em.inc_last("te", "s_tr")
        em.wait("ve", "s_tr", mtr)
        for r, dcol in enumerate(dsts):
            em.op("ve", lambda e, r=r, dcol=dcol: e.tensor_copy(
                vecs[:, dcol:dcol + 8], ps_sc[:, r * 512:r * 512 + 8]))
        mdv = em.inc_last("ve", "s_dv")
        em.wait("sy", "s_dv", mdv)
        em.wait("te", "s_dv", mdv)
        return mdv

    mv1 = vec_to_fm([decay_d, br_d], [0, 16])
    vec_to_fm([bg_d, bo_d], [24, 32])
    vec_to_fm([bpr_d], [40])

    em.wait("sc", "s_dv", mv1)
    em.op("sc", lambda e: e.activation(vecs[:, 0:8], vecs[:, 0:8], AF.Sigmoid, bias=Z))
    msc = em.inc_last("sc", "s_sc")
    em.wait("ve", "s_sc", msc)
    em.op("ve", lambda e: e.tensor_scalar(vecs[:, 8:16], vecs[:, 0:8], -1.0, 1.0,
                                          op0=ALU.mult, op1=ALU.add))
    em.op("ve", lambda e: e.tensor_copy(dec_bc[:, 0:16:2], vecs[:, 0:8]))
    em.op("ve", lambda e: e.tensor_copy(dec_bc[:, 1:16:2], vecs[:, 0:8]), inc="s_dv")

    for p in range(4):
        em.dma_fence("sy", "s_ld")
        mld = em.op("sy", lambda e, p=p: e.dma_start(
            csb[0:1, :], bfc_d.ap().rearrange("(o d) -> o d", o=4)[p:p + 1, :]), inc="s_ld", k=16)
        em.wait("te", "s_ld", mld)
        for k in range(KC):
            em.op("te", lambda e, k=k: e.transpose(
                ps_sc[:, k:k + 1], csb[0:1, k * 128:(k + 1) * 128], id_f[0:1, 0:1]))
        mtr = em.inc_last("te", "s_tr")
        em.wait("ve", "s_tr", mtr)
        mdv = em.op("ve", lambda e, p=p: e.tensor_copy(
            bfc_fm[:, p * 8:(p + 1) * 8], ps_sc[:, 0:8]), inc="s_dv")
        em.wait("sy", "s_dv", mdv)
        em.wait("te", "s_dv", mdv)

    vec_to_fm([lnw_d], [55])
    mld = em.op("sy", lambda e: e.dma_start(csb[0:1, :], lnb_d.ap().rearrange("(o d) -> o d", o=1)), inc="s_ld", k=16)
    em.wait("te", "s_ld", mld)
    for k in range(KC):
        em.op("te", lambda e, k=k: e.transpose(
            ps_sc[:, k:k + 1], csb[0:1, k * 128:(k + 1) * 128], ones1[0:1, 0:1]))
    mtr = em.inc_last("te", "s_tr")
    em.wait("ve", "s_tr", mtr)
    em.op("ve", lambda e: e.tensor_copy(stat[:, 0:8], ps_sc[:, 0:8]))
    em.op("ve", lambda e: e.tensor_copy(lnb_b[:], ps_sc[:, 0:8]))
    mdv = em.inc_last("ve", "s_dv")
    em.wait("sy", "s_dv", mdv)
    em.wait("te", "s_dv", mdv)

    prev_ev = 0
    for wd, dstT in ((wg_d, wgT), (wr_d, wrT), (wo_d, woT)):
        for r in range(KC):
            par = r % 2
            em.wait("sy", "s_dv", prev_ev)
            em.dma_fence("sy", "s_ld")
            mld = em.op("sy", lambda e, wd=wd, par=par, r=r: e.dma_start(
                wstage[:, par * D:(par + 1) * D], wd.ap()[128 * r:128 * (r + 1), :]),
                inc="s_ld", k=16)
            em.wait("te", "s_ld", mld)
            em.wait("te", "s_dv", prev_ev)
            for k in range(KC):
                em.op("te", lambda e, par=par, k=k: e.transpose(
                    ps_sc[:, (k % 4) * 128 + (k // 4) * 512:(k % 4) * 128 + (k // 4) * 512 + 128],
                    wstage[:, par * D + k * 128:par * D + (k + 1) * 128], id_f[:]))
            mtr = em.inc_last("te", "s_tr")
            em.wait("ve", "s_tr", mtr)
            for half in range(2):
                dst3 = dstT[:].rearrange("p (k e) -> p k e", k=KC)[:, half * 4:(half + 1) * 4,
                                                                  r * 128:(r + 1) * 128]
                src3 = ps_sc[:, half * 512:(half + 1) * 512].rearrange("p (k i) -> p k i", k=4)
                em.op("ve", lambda e, dst3=dst3, src3=src3: e.tensor_copy(dst3, src3))
            prev_ev = em.inc_last("ve", "s_dv")

    em.wait("te", "s_dv", prev_ev)
    for hh in range(2):
        for k in range(KC):
            em.op("te", lambda e, hh=hh, k=k: e.matmul(
                ps_sc[0:1, hh * 512:(hh + 1) * 512], lnb_b[:, k:k + 1],
                woT[:, k * D + hh * 512:k * D + (hh + 1) * 512],
                start=(k == 0), stop=(k == KC - 1)))
    mq = em.inc_last("te", "s_mm")
    em.wait("ve", "s_mm", mq)
    em.op("ve", lambda e: e.tensor_copy(csb[0:1, 0:512], ps_sc[0:1, 0:512]))
    mdv = em.op("ve", lambda e: e.tensor_copy(csb[0:1, 512:1024], ps_sc[0:1, 512:1024]),
                inc="s_dv")
    em.wait("te", "s_dv", mdv)
    for k in range(KC):
        em.op("te", lambda e, k=k: e.transpose(
            ps_sc[:, k:k + 1], csb[0:1, k * 128:(k + 1) * 128], ones1[0:1, 0:1]))
    mtr = em.inc_last("te", "s_tr")
    em.wait("ve", "s_tr", mtr)
    em.op("ve", lambda e: e.tensor_tensor(vecs[:, 32:40], vecs[:, 32:40], ps_sc[:, 0:8],
                                          op=ALU.add))
    for k in range(KC):
        em.op("ve", lambda e, k=k: e.tensor_scalar(
            woT[:, k * D:(k + 1) * D], woT[:, k * D:(k + 1) * D],
            vecs[:, 55 + k:56 + k], None, op0=ALU.mult), drain=False)
    m_wofold = em.inc_last("ve", "s_dv")
    em.wait("te", "s_dv", m_wofold)
    em.wait("sy", "s_dv", m_wofold)

    em.dma_fence("sy", "s_ld")
    mld = em.op("sy", lambda e: e.dma_start(csb[0:2, :], c_d.ap()), inc="s_ld", k=16)
    em.wait("sc", "s_ld", mld)
    msc = em.op("sc", lambda e: e.activation(tm_a[0:2, :], csb[0:2, :], AF.Silu, bias=vecs[0:2, 52:53]), inc="s_sc")
    em.wait("te", "s_sc", msc)
    em.wait("te", "s_dv", prev_ev)
    for k in range(KC):
        em.op("te", lambda e, k=k: e.transpose(
            ps_sc[:, 2 * k:2 * k + 2], tm_a[0:2, k * 128:(k + 1) * 128], id_f[0:2, 0:2]))
    mtr = em.inc_last("te", "s_tr")
    em.wait("ve", "s_tr", mtr)
    msil = em.op("ve", lambda e: e.tensor_copy(siluT[:], ps_sc[:, 0:16]), inc="s_dv")

    ev_n = {}
    mq_prev = 0
    for kk in range(KC):
        em.wait("gp", "s_mm", mq_prev)
        em.dma_fence("gp", "s_gd")
        mgd = em.op("gp", lambda e, kk=kk: e.dma_start(
            wmodf[:], wmod_d.ap()[128 * kk:128 * (kk + 1), :]), inc="s_gd", k=16)
        em.wait("te", "s_gd", mgd)
        if kk == 0:
            em.wait("te", "s_dv", msil)
        for n in range(12):
            for b in range(2):
                par = (2 * n + b) % 2
                em.wait("te", "s_dv", ev_n.get((kk, 2 * n + b - 2), 0))
                em.op("te", lambda e, par=par, kk=kk, n=n, b=b: e.matmul(
                    ps_sc[0:1, par * 512:(par + 1) * 512],
                    siluT[:, 2 * kk + b:2 * kk + b + 1],
                    wmodf[:, n * 512:(n + 1) * 512],
                    start=True, stop=True))
                mmm = em.inc_last("te", "s_mm")
                mq_prev = mmm
                em.wait("ve", "s_mm", mmm)
                brow = 32 * b
                if kk == 0:
                    ev_n[(kk, 2 * n + b)] = em.op(
                        "ve", lambda e, par=par, n=n, brow=brow: e.tensor_copy(
                            mod_sb[brow:brow + 1, n * 512:(n + 1) * 512],
                            ps_sc[0:1, par * 512:(par + 1) * 512]), inc="s_dv")
                else:
                    ev_n[(kk, 2 * n + b)] = em.op(
                        "ve", lambda e, par=par, n=n, brow=brow: e.tensor_tensor(
                            mod_sb[brow:brow + 1, n * 512:(n + 1) * 512],
                            mod_sb[brow:brow + 1, n * 512:(n + 1) * 512],
                            ps_sc[0:1, par * 512:(par + 1) * 512], op=ALU.add), inc="s_dv")
        ev_n[(kk + 1, -2)] = ev_n[(kk, 22)]
        ev_n[(kk + 1, -1)] = ev_n[(kk, 23)]

    for off in (2 * D, 5 * D):
        for brow in (0, 32):
            em.op("ve", lambda e, off=off, brow=brow: e.tensor_scalar(
                tm_a[brow:brow + 1, :], mod_sb[brow:brow + 1, off:off + D], 0.0, 2.0,
                op0=ALU.is_ge, op1=ALU.mult))
            mdv = em.op("ve", lambda e, brow=brow: e.tensor_scalar(
                tm_a[brow:brow + 1, :], tm_a[brow:brow + 1, :], -1.0, None,
                op0=ALU.add), inc="s_dv")
            em.wait("sc", "s_dv", mdv)
            msc = em.op("sc", lambda e, off=off, brow=brow: e.activation(
                sq_dump[brow:brow + 1, :], mod_sb[brow:brow + 1, off:off + D], AF.Abs,
                bias=vecs[brow:brow + 1, 52:53]), inc="s_sc")
            em.wait("ve", "s_sc", msc)
            em.op("ve", lambda e, brow=brow: e.tensor_scalar(
                sq_dump[brow:brow + 1, :], sq_dump[brow:brow + 1, :], 0.25, None, op0=ALU.max))
            em.op("ve", lambda e, off=off, brow=brow: e.tensor_tensor(
                mod_sb[brow:brow + 1, off:off + D], tm_a[brow:brow + 1, :],
                sq_dump[brow:brow + 1, :], op=ALU.mult), inc="s_dv")

    em.wait("sy", "s_dv", prev_ev)
    em.dma_fence("sy", "s_ld")
    mld = em.op("sy", lambda e: e.dma_start(lora_st[0:R, 0:D], afc_d.ap()), inc="s_ld", k=16)
    em.wait("te", "s_dv", em.cnt["s_dv"])
    em.wait("te", "s_ld", mld)
    for k in range(KC):
        em.op("te", lambda e, k=k: e.transpose(
            ps_sc[:, k * 16:(k + 1) * 16], lora_st[0:R, k * 128:(k + 1) * 128], id_f[0:R, 0:R]))
    mtr = em.inc_last("te", "s_tr")
    em.wait("ve", "s_tr", mtr)
    mdv = em.op("ve", lambda e: e.tensor_copy(afcT[:], ps_sc[:, 0:KC * R]), inc="s_dv")
    em.wait("sy", "s_dv", mdv)
    em.wait("te", "s_dv", mdv)
    for hh in range(2):
        em.dma_fence("sy", "s_ld")
        mld = em.op("sy", lambda e, hh=hh: e.dma_start(
            lora_st[0:R, 0:2 * D], apr_d.ap()[:, hh * 2 * D:(hh + 1) * 2 * D]),
            inc="s_ld", k=16)
        em.wait("te", "s_ld", mld)
        for k in range(16):
            em.op("te", lambda e, k=k: e.transpose(
                ps_sc[:, k * 16:(k + 1) * 16], lora_st[0:R, k * 128:(k + 1) * 128],
                id_f[0:R, 0:R]))
        mtr = em.inc_last("te", "s_tr")
        em.wait("ve", "s_tr", mtr)
        mdv = em.op("ve", lambda e, hh=hh: e.tensor_copy(
            aprT[:, hh * 16 * R:(hh + 1) * 16 * R], ps_sc[:, 0:16 * R]), inc="s_dv")
        em.wait("te", "s_dv", mdv)
        em.wait("sy", "s_dv", mdv)
    em.wait("gp", "s_dv", mdv)
    for src_d, dstT, nt in ((Bfc_d, bfcT, FC), (Bpr_d, bprT, KC)):
        for g4 in range(nt // 4):
            em.dma_fence("gp", "s_gd")
            em.wait("gp", "s_dv", em.cnt["s_dv"])
            mgd = 0
            for jj in range(4):
                j = g4 * 4 + jj
                mgd = em.op("gp", lambda e, src_d=src_d, j=j, jj=jj: e.dma_start(
                    wstage[:, jj * 16:(jj + 1) * 16],
                    src_d.ap()[j * 128:(j + 1) * 128, :]), inc="s_gd", k=16)
            em.wait("te", "s_gd", mgd)
            for jj in range(4):
                em.op("te", lambda e, jj=jj: e.transpose(
                    ps_lo[0:128, jj * 128:(jj + 1) * 128][0:R, :],
                    wstage[:, jj * 16:(jj + 1) * 16], id_f[:]))
            mtr = em.inc_last("te", "s_tr")
            em.wait("ve", "s_tr", mtr)
            mdv = em.op("ve", lambda e, dstT=dstT, g4=g4: e.tensor_copy(
                dstT[0:R, g4 * 512:(g4 + 1) * 512], ps_lo[0:R, :]), inc="s_dv")
            em.wait("te", "s_dv", mdv)

    em.barrier()
    em.realize()
    for c in reversed(cm_p0):
        c.__exit__(None, None, None)
    if max_phase <= 0:
        return _close_truncated(nc, em)


    def actT_full(t, k):
        return t[:, k * NT3:(k + 1) * NT3]

    def build_rep(slot, b, off, bias_col, prev_mm):
        em.wait("te", "s_sc", prev_mm)
        for hh in range(2):
            mmm = em.op("te", lambda e, b=b, off=off, hh=hh: e.matmul(
                ps_mm[:, hh * 512:(hh + 1) * 512], ones1[32 * b:32 * b + 1, :],
                mod_sb[32 * b:32 * b + 1, off + hh * 512:off + (hh + 1) * 512],
                start=True, stop=True), inc="s_mm")
        em.wait("sc", "s_mm", mmm)
        for hh in range(2):
            msc = em.op("sc", lambda e, slot=slot, hh=hh, bias_col=bias_col: e.activation(
                rep[:, slot * D + hh * 512:slot * D + (hh + 1) * 512],
                ps_mm[:, hh * 512:(hh + 1) * 512], AF.Identity, bias=bias_col), inc="s_sc")
        return msc

    pbf_free = {0: ("s_dv", 0), 1: ("s_dv", 0)}
    x_free = [(0, 0)] * 4

    def emit_ln_modulate(src_ap, par_h, rep_mul, rep_add, mul2, add2,
                         m_data, data_sem):
        em.wait("ve", data_sem, m_data)
        em.op("ve", lambda e: e.bn_stats(stat[:, 0:6], src_ap[:, 0:512]))
        em.op("ve", lambda e: e.bn_stats(stat[:, 6:12], src_ap[:, 512:1024]))
        em.op("ve", lambda e: e.bn_aggr(stat[:, 12:14], stat[:, 0:12]))
        mdv = em.op("ve", lambda e: e.tensor_scalar(
            stat[:, 14:15], stat[:, 13:14], EPS, None, op0=ALU.add), inc="s_dv")
        em.wait("sc", "s_dv", mdv)
        msq = em.op("sc", lambda e: e.activation(
            stat[:, 7:8], stat[:, 14:15], AF.Sqrt, bias=Z), inc="s_sc")
        em.wait("ve", "s_sc", msq)
        em.op("ve", lambda e: e.reciprocal(stat[:, 15:16], stat[:, 7:8]))
        mnm = em.op("ve", lambda e: e.tensor_scalar(
            stat[:, 6:7], stat[:, 12:13], stat[:, 15:16], -1.0,
            op0=ALU.mult, op1=ALU.mult), inc="s_dv")
        em.wait("sc", "s_dv", mnm)
        mcons = em.op("sc", lambda e: e.activation(
            tm_bf[:], src_ap, AF.Identity, scale=stat[:, 15:16], bias=stat[:, 6:7]),
            inc="s_sc")
        em.wait("ve", "s_sc", mcons)
        em.op("ve", lambda e: e.tensor_tensor(tm_bf[:], tm_bf[:], rep_mul, op=ALU.mult))
        mh = em.op("ve", lambda e, par_h=par_h: e.tensor_tensor(
            h_tmb[:, par_h * D:(par_h + 1) * D], tm_bf[:], rep_add, op=ALU.add), inc="s_dv")
        return (mnm, mcons), mh

    def emit_act_transpose(par_h, mh, dstT, col0):
        em.wait("te", "s_dv", mh)
        em.wait("te", pbf_free[par_h][0], pbf_free[par_h][1])
        for k in range(KC):
            em.op("te", lambda e, par_h=par_h, k=k: e.transpose(
                ps_bf[:, par_h * 1024 + k * 128:par_h * 1024 + (k + 1) * 128],
                h_tmb[:, par_h * D + k * 128:par_h * D + (k + 1) * 128], id_b[:]))
        mtr = em.inc_last("te", "s_tr")
        em.wait("ve", "s_tr", mtr)
        dst3 = dstT[:].rearrange("p (k t) -> p k t", k=KC)[:, :, col0:col0 + 128]
        src3 = ps_bf[:, par_h * 1024:(par_h + 1) * 1024].rearrange("p (k i) -> p k i", k=KC)
        mev = em.op("ve", lambda e, dst3=dst3, src3=src3: e.tensor_copy(dst3, src3), inc="s_dv")
        pbf_free[par_h] = ("s_dv", mev)
        return mev

    cm_p2 = [nc.sbuf_tensor("y_fm", [128, 2 * 128 * 16], BF16),
             nc.sbuf_tensor("u_st", [128, 4 * 128 * 16], BF16),
             nc.sbuf_tensor("v_st", [128, 4 * 128 * 16], BF16),
             nc.sbuf_tensor("y_tm", [128, NB * D], BF16),
             nc.sbuf_tensor("ystage", [128, D], BF16),
             nc.sbuf_tensor("actT1", [128, KC * NT1], BF16),
             nc.sbuf_tensor("qload", [128, 2 * D], F32),
             nc.sbuf_tensor("qa", [128, D], F32),
             nc.sbuf_tensor("qr", [128, D], F32),
             nc.sbuf_tensor("qt", [128, D], F32),
             nc.sbuf_tensor("qsgn", [128, 2 * D], BF16),
             nc.sbuf_tensor("qb", [128, 2 * D], BF16),
             nc.sbuf_tensor("qstage", [128, 2 * D], BF16)]
    (y_fm, u_st, v_st, y_tm, ystage, actT1, qload, qa, qr, qt, qsgn, qb, qstage) = \
        [c.__enter__() for c in cm_p2]

    def ring_slice(chunk, c, b):
        base = (chunk % 4) * 2048 + c * 2 + b
        return base

    mrep = 0
    for b in range(NB):
        mrep = build_rep(2 * b + 0, b, D, ONE, mrep)
        mrep = build_rep(2 * b + 1, b, 0, Z, mrep)
    em.wait("ve", "s_sc", mrep)

    p1m = {}
    zev = {0: mrep, 1: mrep}
    zuse = {}

    def build_p1_tile(t1):
        items = []
        t0 = t1 * NT1

        def ln_items(b, s):
            slot = 2 * b + s
            src_ap = x_sb[:, slot * D:(slot + 1) * D]
            par_h = slot % 2

            def i_load(em, b=b, s=s, slot=slot, t0=t0, src_ap=src_ap):
                em.wait("sy", "s_p1", x_free[slot][0])
                em.wait("sy", "s_sc", x_free[slot][1])
                em.dma_fence("sy", "s_ld")
                p1m[('xl', slot)] = em.op("sy", lambda e: e.dma_start(
                    src_ap, x_d.ap()[b, t0 + 128 * s:t0 + 128 * (s + 1), :]),
                    inc="s_ld", k=16)

            def i_st1(em, slot=slot, src_ap=src_ap):
                em.wait("ve", "s_ld", p1m[('xl', slot)])
                em.op("ve", lambda e: e.reduce_sum(stat[:, 0:1], src_ap, axis=AX.X))
                em.wait("sc", "s_ld", p1m[('xl', slot)])
                em.drain("sc")
                p1m['sq'] = em.op("sc", lambda e: e.activation(
                    sq_dump[:], src_ap, AF.Square, bias=Z, accum_out=stat[:, 1:2]),
                    inc="s_sc")

            def i_st2(em):
                em.op("ve", lambda e: e.tensor_scalar(stat[:, 2:3], stat[:, 0:1],
                                                      1.0 / D, None, op0=ALU.mult))
                em.wait("ve", "s_sc", p1m['sq'])
                em.op("ve", lambda e: e.tensor_scalar(stat[:, 4:5], stat[:, 1:2],
                                                      1.0 / D, EPS, op0=ALU.mult, op1=ALU.add))
                em.op("ve", lambda e: e.tensor_tensor(stat[:, 5:6], stat[:, 2:3],
                                                      stat[:, 2:3], op=ALU.mult))
                p1m['var'] = em.op("ve", lambda e: e.tensor_tensor(
                    stat[:, 4:5], stat[:, 4:5], stat[:, 5:6], op=ALU.subtract), inc="s_p1")

            def i_st3(em):
                em.wait("sc", "s_p1", p1m['var'])
                p1m['sd'] = em.op("sc", lambda e: e.activation(
                    stat[:, 5:6], stat[:, 4:5], AF.Sqrt, bias=Z), inc="s_sc")

            def i_st4(em, slot=slot):
                em.wait("ve", "s_sc", p1m['sd'])
                em.op("ve", lambda e: e.reciprocal(stat[:, 3:4], stat[:, 5:6]))
                p1m['nm'] = em.op("ve", lambda e: e.tensor_scalar(
                    stat[:, 6:7], stat[:, 2:3], stat[:, 3:4], -1.0,
                    op0=ALU.mult, op1=ALU.mult), inc="s_p1")

            def i_xn(em, slot=slot, src_ap=src_ap):
                em.wait("sc", "s_p1", p1m['nm'])
                p1m[('xn', slot)] = em.op("sc", lambda e: e.activation(
                    tm_bf[:], src_ap, AF.Identity, scale=stat[:, 3:4],
                    bias=stat[:, 6:7]), inc="s_sc")

            def i_mod(em, b=b, slot=slot, par_h=par_h):
                em.wait("ve", "s_sc", p1m[('xn', slot)])
                em.op("ve", lambda e: e.tensor_tensor(
                    tm_bf[:], tm_bf[:], rep[:, (2 * b) * D:(2 * b + 1) * D], op=ALU.mult))
                mh = em.op("ve", lambda e: e.tensor_tensor(
                    h_tmb[:, par_h * D:(par_h + 1) * D], tm_bf[:],
                    rep[:, (2 * b + 1) * D:(2 * b + 2) * D], op=ALU.add), inc="s_p1")
                p1m[('h', slot)] = mh
                x_free[slot] = (p1m['nm'], p1m[('xn', slot)])

            def i_tr(em, slot=slot, par_h=par_h, s=s):
                em.wait("te", "s_p1", p1m[('h', slot)])
                em.wait("te", pbf_free[par_h][0], pbf_free[par_h][1])
                for k in range(KC):
                    em.op("te", lambda e, par_h=par_h, k=k: e.transpose(
                        ps_bf[:, par_h * 1024 + k * 128:par_h * 1024 + (k + 1) * 128],
                        h_tmb[:, par_h * D + k * 128:par_h * D + (k + 1) * 128], id_b[:]))
                p1m[('tr', slot)] = em.inc_last("te", "s_tr")

            def i_ev(em, slot=slot, par_h=par_h, s=s):
                em.wait("ve", "s_tr", p1m[('tr', slot)])
                dst3 = actT1[:].rearrange("p (k t) -> p k t", k=KC)[:, :, s * 128:(s + 1) * 128]
                src3 = ps_bf[:, par_h * 1024:(par_h + 1) * 1024].rearrange(
                    "p (k i) -> p k i", k=KC)
                mev = em.op("ve", lambda e, dst3=dst3, src3=src3: e.tensor_copy(dst3, src3),
                            inc="s_p1")
                pbf_free[par_h] = ("s_p1", mev)
                p1m[('ev', slot)] = mev
            return [i_load, i_st1, i_st2, i_st3, i_st4, i_xn, i_mod, i_tr, i_ev]

        def z_items(b, c, t1=t1):
            pz = c % 2
            chunk0 = 2 * t1

            def zi_mm(em, b=b, c=c, pz=pz):
                em.wait("te", "s_p1", p1m[('ev', 2 * b)])
                em.wait("te", "s_p1", p1m[('ev', 2 * b + 1)])
                em.wait("te", "s_sc", zev.get(pz, 0))
                for k in range(KC):
                    em.op("te", lambda e, c=c, k=k, pz=pz: e.matmul(
                        ps_mm[:, pz * 512:pz * 512 + NT1],
                        wgT[:, k * D + c * 128:k * D + (c + 1) * 128],
                        actT1[:, k * NT1:(k + 1) * NT1],
                        start=(k == 0), stop=(k == KC - 1)))
                p1m[('zmm', pz)] = em.inc_last("te", "s_mm")

            def zi_ev(em, c=c, pz=pz):
                em.wait("sc", "s_mm", p1m[('zmm', pz)])
                em.wait("sc", "s_p1", zuse.get(pz, 0))
                zev[pz] = em.op("sc", lambda e, pz=pz, c=c: e.activation(
                    tm_a[:, pz * 512:pz * 512 + NT1], ps_mm[:, pz * 512:pz * 512 + NT1],
                    AF.Sigmoid, bias=vecs[:, 24 + c:25 + c]), inc="s_sc")

            def zi_u(em, b=b, c=c, pz=pz, chunk0=chunk0):
                em.wait("ve", "s_sc", zev[pz])
                b0 = (chunk0 % 4) * 2048 + c * 2 + b
                b1 = ((chunk0 + 1) % 4) * 2048 + c * 2 + b
                em.op("ve", lambda e, b0=b0, pz=pz: e.tensor_scalar(
                    u_st[:, b0:b0 + 16 * 127 + 1:16], tm_a[:, pz * 512:pz * 512 + 128],
                    vecs[:, 8 + c:9 + c], None, op0=ALU.mult), drain=False)
                em.op("ve", lambda e, b1=b1, pz=pz: e.tensor_scalar(
                    u_st[:, b1:b1 + 16 * 127 + 1:16],
                    tm_a[:, pz * 512 + 128:pz * 512 + 256],
                    vecs[:, 8 + c:9 + c], None, op0=ALU.mult))

            def zi_v(em, b=b, c=c, pz=pz, chunk0=chunk0):
                b0 = (chunk0 % 4) * 2048 + c * 2 + b
                b1 = ((chunk0 + 1) % 4) * 2048 + c * 2 + b
                em.op("ve", lambda e, b0=b0, c=c: e.scalar_tensor_tensor(
                    v_st[:, b0:b0 + 16 * 127 + 1:16],
                    actT1[:, c * NT1:c * NT1 + 128],
                    vecs[:, 16 + c:17 + c],
                    u_st[:, b0:b0 + 16 * 127 + 1:16],
                    op0=ALU.subtract, op1=ALU.mult), drain=False)
                muv = em.op("ve", lambda e, b1=b1, c=c: e.scalar_tensor_tensor(
                    v_st[:, b1:b1 + 16 * 127 + 1:16],
                    actT1[:, c * NT1 + 128:(c + 1) * NT1],
                    vecs[:, 16 + c:17 + c],
                    u_st[:, b1:b1 + 16 * 127 + 1:16],
                    op0=ALU.subtract, op1=ALU.mult), inc="s_p1")
                zuse[pz] = muv
            return [zi_mm, zi_ev, zi_u, zi_v]

        for b in range(NB):
            for s in range(2):
                items += ln_items(b, s)
            for c in range(KC):
                items += z_items(b, c)
        return items

    for t1 in range(min(2, NTIL1)):
        for it in build_p1_tile(t1):
            it(em)
    p1q = []
    p1_need = {}
    for t1 in range(2, NTIL1):
        w = 256 * t1 - 255
        for it in build_p1_tile(t1):
            p1q.append((w, it))
        p1_need[2 * t1] = len(p1q)
    p1pos = [0]

    def drain_p1(t, n):
        while n > 0 and p1pos[0] < len(p1q) and p1q[p1pos[0]][0] <= t:
            p1q[p1pos[0]][1](em)
            p1pos[0] += 1
            n -= 1

    def gran_src(g):
        if g < FC:
            return wfc_d.ap()[g * 128:(g + 1) * 128, :]
        gg = g - FC
        r, qf = gg // 4, gg % 4
        return wpr_d.ap()[r * 128:(r + 1) * 128, qf * D:(qf + 1) * D]

    qitems = []
    qm = {}

    def qi_init(em):
        qm[('red', -1)] = em.op("ve", lambda e: e.memset(mx[:, 0:2], 0.0), inc="s_q")
    qitems.append(qi_init)

    for g in range(64):
        def qi_mload(em, g=g):
            if g >= 2:
                em.wait("gp", "s_q", qm[('red', g - 2)])
            em.dma_fence("gp", "s_gd")
            qm[('mld', g)] = em.op("gp", lambda e, g=g: e.dma_start(
                qload[:, (g % 2) * D:(g % 2 + 1) * D], gran_src(g)), inc="s_gd", k=16)

        def qi_mred(em, g=g):
            em.wait("ve", "s_gd", qm[('mld', g)])
            em.op("ve", lambda e, g=g: e.tensor_reduce(
                mx[:, 2:3], qload[:, (g % 2) * D:(g % 2 + 1) * D], axis=AX.X,
                op=ALU.max, apply_absolute_value=True))
            col = 0 if g < FC else 1
            qm[('red', g)] = em.op("ve", lambda e, col=col: e.tensor_tensor(
                mx[:, col:col + 1], mx[:, col:col + 1], mx[:, 2:3], op=ALU.max), inc="s_q")
        qitems += [qi_mload, qi_mred]

    def qi_fin(em):
        em.wait("te", "s_q", qm[('red', 63)])
        em.op("te", lambda e: e.transpose(ps_lo[:, 0:128][0:1, :], mx[:, 0:1], id_f[:]))
        qm['fint'] = em.op("te", lambda e: e.transpose(
            ps_lo[:, 128:256][0:1, :], mx[:, 1:2], id_f[:]), inc="s_qt")
        em.wait("ve", "s_qt", qm['fint'])
        em.op("ve", lambda e: e.tensor_reduce(mx[0:1, 4:5], ps_lo[0:1, 0:128], axis=AX.X, op=ALU.max))
        em.op("ve", lambda e: e.tensor_reduce(mx[0:1, 6:7], ps_lo[0:1, 128:256], axis=AX.X, op=ALU.max))
        em.op("ve", lambda e: e.reciprocal(mx[0:1, 5:6], mx[0:1, 4:5]))
        em.op("ve", lambda e: e.reciprocal(mx[0:1, 7:8], mx[0:1, 6:7]))
        em.op("ve", lambda e: e.tensor_scalar(mx[0:1, 4:5], mx[0:1, 4:5],
                                              1.0 / 127.0, None, op0=ALU.mult))
        em.op("ve", lambda e: e.tensor_scalar(mx[0:1, 6:7], mx[0:1, 6:7],
                                              1.0 / 127.0, None, op0=ALU.mult))
        em.op("ve", lambda e: e.tensor_scalar(mx[0:1, 5:6], mx[0:1, 5:6],
                                              127.0, None, op0=ALU.mult))
        qm['finv'] = em.op("ve", lambda e: e.tensor_scalar(
            mx[0:1, 7:8], mx[0:1, 7:8], 127.0, None, op0=ALU.mult), inc="s_q")
        em.wait("te", "s_q", qm['finv'])
        for i, mx_col in enumerate((4, 5, 6, 7)):
            em.op("te", lambda e, i=i, mx_col=mx_col: e.matmul(
                ps_lo[:, 256 + i:257 + i], ones1[0:1, :], mx[0:1, mx_col:mx_col + 1],
                start=True, stop=True))
        mtr = em.inc_last("te", "s_qt")
        em.wait("ve", "s_qt", mtr)
        for i, sc_col in enumerate((48, 49, 50, 51)):
            em.op("ve", lambda e, i=i, sc_col=sc_col: e.tensor_copy(
                vecs[:, sc_col:sc_col + 1], ps_lo[:, 256 + i:257 + i]))
        qm['fbc'] = em.inc_last("ve", "s_q")
    qitems.append(qi_fin)

    for g in range(64):
        iscale = vecs[:, 49:50] if g < FC else vecs[:, 51:52]
        mscale = vecs[:, 48:49] if g < FC else vecs[:, 50:51]

        def qi_load(em, g=g):
            em.wait("gp", "s_q",
                    qm[('wq', g - 2)] if g >= 2 else qm[('red', {0: 62, 1: 63}[g % 2])])
            em.dma_fence("gp", "s_gd")
            qm[('ld', g)] = em.op("gp", lambda e, g=g: e.dma_start(
                qload[:, (g % 2) * D:(g % 2 + 1) * D], gran_src(g)), inc="s_gd", k=16)

        def qi_q(em, g=g, iscale=iscale):
            em.wait("sc", "s_gd", qm[('ld', g)])
            if g == 0:
                em.wait("sc", "s_q", qm['fbc'])
            else:
                em.wait("sc", "s_q", qm[('wq', g - 1)])
            em.op("sc", lambda e, g=g, iscale=iscale: e.activation(
                qr[:], qload[:, (g % 2) * D:(g % 2 + 1) * D], AF.Identity,
                scale=iscale, bias=Z), drain=True)

        def qi_abs(em, g=g):
            em.op("sc", lambda e: e.activation(qa[:], qr[:], AF.Abs, bias=Z))
            qm[('sgn', g)] = em.op("sc", lambda e, g=g: e.activation(
                qsgn[:, (g % 2) * D:(g % 2 + 1) * D], qr[:], AF.Sign, bias=Z), inc="s_sc")

        def qi_b0(em, g=g):
            em.wait("ve", "s_sc", qm[('sgn', g)])
            em.op("ve", lambda e: e.tensor_scalar(qt[:], qa[:], 63.5, 64.0,
                                                  op0=ALU.is_ge, op1=ALU.mult))

        def qi_b0b(em, g=g):
            em.op("ve", lambda e: e.tensor_tensor(qr[:], qa[:], qt[:], op=ALU.subtract))

        def mk_bit(j):
            def qi_bj(em, j=j):
                em.op("ve", lambda e, j=j: e.tensor_scalar(
                    qt[:], qr[:], j - 0.5, float(j), op0=ALU.is_ge, op1=ALU.mult))
            def qi_bjs(em):
                em.op("ve", lambda e: e.tensor_tensor(qr[:], qr[:], qt[:], op=ALU.subtract))
            return [qi_bj, qi_bjs]

        def qi_rnd(em, g=g):
            qm[('rnd', g)] = em.op("ve", lambda e: e.tensor_tensor(
                qt[:], qa[:], qr[:], op=ALU.subtract), inc="s_q")

        def qi_sca(em, g=g, mscale=mscale):
            em.wait("sc", "s_q", qm[('rnd', g)])
            qm[('sca', g)] = em.op("sc", lambda e, mscale=mscale: e.activation(
                qr[:], qt[:], AF.Identity, scale=mscale, bias=Z), inc="s_sc",
                drain=True)

        def qi_sgm(em, g=g):
            em.wait("ve", "s_sc", qm[('sca', g)])
            qm[('wq', g)] = em.op("ve", lambda e, g=g: e.tensor_tensor(
                qb[:, (g % 2) * D:(g % 2 + 1) * D], qr[:],
                qsgn[:, (g % 2) * D:(g % 2 + 1) * D], op=ALU.mult), inc="s_q")

        def qi_tr(em, g=g):
            em.wait("te", "s_q", qm[('wq', g)])
            em.wait("te", "s_sc", qm.get(('ev', g - 1), 0))
            for k in range(KC):
                em.op("te", lambda e, g=g, k=k: e.transpose(
                    ps_x[:, k * 128:(k + 1) * 128],
                    qb[:, (g % 2) * D + k * 128:(g % 2) * D + (k + 1) * 128], id_b[:]))
            qm[('tr', g)] = em.inc_last("te", "s_qt")

        def qi_ev(em, g=g):
            em.wait("sc", "s_qt", qm[('tr', g)])
            if g >= 2:
                em.wait("sc", "s_gd", qm[('out', g - 2)])
            qm[('ev', g)] = em.op("sc", lambda e, g=g: e.activation(
                qstage[:, (g % 2) * D:(g % 2 + 1) * D], ps_x[:], AF.Identity,
                bias=Z), inc="s_sc")

        def qi_out(em, g=g):
            em.wait("gp", "s_sc", qm[('ev', g)])
            em.dma_fence("gp", "s_gd")
            if g < FC:
                dst = wqfc_h.ap()[g, :, :]
            else:
                gg = g - FC
                r, qf = gg // 4, gg % 4
                dst = wqpr_h.ap()[r, :, qf * D:(qf + 1) * D]
            qm[('out', g)] = em.op("gp", lambda e, g=g, dst=dst: e.dma_start(
                dst, qstage[:, (g % 2) * D:(g % 2 + 1) * D]), inc="s_gd", k=16)

        bit_items = []
        for j in (32, 16, 8, 4, 2, 1):
            bit_items += mk_bit(j)
        qitems += ([qi_load, qi_q, qi_abs, qi_b0, qi_b0b] + bit_items +
                   [qi_rnd, qi_sca, qi_sgm, qi_tr, qi_ev, qi_out])

    qpos = [0]

    def drain_quant(n):
        while n > 0 and qpos[0] < len(qitems):
            qitems[qpos[0]](em)
            qpos[0] += 1
            n -= 1

    base_dv = em.cnt["s_dv"]
    em.op("ve", lambda e: e.tensor_copy(y_fm[:, 0:8], v_st[:, 0:8]), inc="s_dv")
    em.op("ve", lambda e: e.tensor_copy(y_fm[:, 8:16], v_st[:, 8:16]), inc="s_dv")

    flush_ev = {}
    ydma_mark = {}
    flushq = []
    flpos = [0]

    def queue_flush(B):
        buf = B % 2
        fm = {}

        def mk(b0):
            bank = b0
            ytm = y_tm[:, b0 * 1024:(b0 + 1) * 1024]

            def f_tr1(em, b0=b0, bank=bank):
                em.wait("te", "s_dv", base_dv + 2 * (B + 1) * 128)
                em.wait("te", "s_fl", flush_ev.get(B - 1, 0))
                em.wait("te", pbf_free[bank][0], pbf_free[bank][1])
                for c in range(KC):
                    st = buf * 2048 + c * 2 + b0
                    em.op("te", lambda e, bank=bank, c=c, st=st: e.transpose(
                        ps_bf[:, bank * 1024 + c * 128:bank * 1024 + (c + 1) * 128],
                        y_fm[:, st:st + 16 * 127 + 1:16], id_b[:]))
                fm['tr1'] = em.inc_last("te", "s_tr")

            def f_ev1(em, b0=b0, bank=bank):
                em.wait("sc", "s_tr", fm['tr1'])
                em.wait("sc", "s_ydma", ydma_mark.get(B - 1, 0))
                fm['mev'] = em.op("sc", lambda e, b0=b0, bank=bank: e.copy(
                    y_tm[:, b0 * 1024:(b0 + 1) * 1024],
                    ps_bf[:, bank * 1024:(bank + 1) * 1024]), inc="s_ev", drain=True)

            def f_st1(em, ytm=ytm):
                em.wait("ve", "s_ev", fm['mev'])
                em.op("ve", lambda e, ytm=ytm: e.reduce_sum(stat[:, 8:9], ytm, axis=AX.X))
                em.drain("sc")
                fm['msq'] = em.op("sc", lambda e, ytm=ytm: e.activation(
                    sq_dump[:], ytm, AF.Square, bias=Z, accum_out=stat[:, 9:10]),
                    inc="s_sc")
                em.op("ve", lambda e: e.tensor_scalar(stat[:, 10:11], stat[:, 8:9],
                                                      1.0 / D, None, op0=ALU.mult),
                      drain=False)

            def f_st2(em):
                em.wait("ve", "s_sc", fm['msq'])
                em.op("ve", lambda e: e.tensor_scalar(stat[:, 12:13], stat[:, 9:10],
                                                      1.0 / D, EPS, op0=ALU.mult,
                                                      op1=ALU.add))
                em.op("ve", lambda e: e.tensor_tensor(stat[:, 13:14], stat[:, 10:11],
                                                      stat[:, 10:11], op=ALU.mult))
                fm['mdv'] = em.op("ve", lambda e: e.tensor_tensor(
                    stat[:, 12:13], stat[:, 12:13], stat[:, 13:14], op=ALU.subtract),
                    inc="s_fl", drain=True)

            def f_sd(em):
                em.wait("sc", "s_fl", fm['mdv'])
                fm['msd'] = em.op("sc", lambda e: e.activation(
                    stat[:, 13:14], stat[:, 12:13], AF.Sqrt, bias=Z), inc="s_sc")

            def f_rc(em):
                em.wait("ve", "s_sc", fm['msd'])
                em.op("ve", lambda e: e.reciprocal(stat[:, 11:12], stat[:, 13:14]))
                fm['mnm'] = em.op("ve", lambda e: e.tensor_scalar(
                    stat[:, 14:15], stat[:, 10:11], stat[:, 11:12], -1.0,
                    op0=ALU.mult, op1=ALU.mult), inc="s_fl")

            def f_no(em, b0=b0, ytm=ytm):
                em.wait("sc", "s_fl", fm['mnm'])
                fm['mno'] = em.op("sc", lambda e, b0=b0, ytm=ytm: e.activation(
                    y_tm[:, b0 * 1024:(b0 + 1) * 1024], ytm, AF.Identity,
                    scale=stat[:, 11:12], bias=stat[:, 14:15]), inc="s_sc",
                    drain=False)

            def f_tr2(em, b0=b0, bank=bank):
                em.wait("te", "s_sc", fm['mno'])
                for k in range(KC):
                    em.op("te", lambda e, bank=bank, b0=b0, k=k: e.transpose(
                        ps_bf[:, bank * 1024 + k * 128:bank * 1024 + (k + 1) * 128],
                        y_tm[:, b0 * 1024 + k * 128:b0 * 1024 + (k + 1) * 128],
                        id_b[:]))
                fm['tr2'] = em.inc_last("te", "s_tr")

            def f_ev2(em, b0=b0, bank=bank):
                em.wait("ve", "s_tr", fm['tr2'])
                em.wait("ve", "s_ydma", em.cnt["s_ydma"])
                mev = em.op("ve", lambda e, bank=bank: e.tensor_copy(
                    ystage[:], ps_bf[:, bank * 1024:(bank + 1) * 1024]),
                    inc="s_fl", drain=False)
                pbf_free[bank] = ("s_fl", mev)
                fm['mev2'] = mev
                if b0 == NB - 1:
                    flush_ev[B] = mev

            def f_dma(em, b0=b0):
                em.wait("sy", "s_fl", fm['mev2'])
                em.dma_fence("sy", "s_ydma")
                dstv = ynT_h.ap()[b0, :, :, B * 128:(B + 1) * 128].rearrange(
                    "k p i -> p k i")
                srcv = ystage[:].rearrange("p (k i) -> p k i", k=KC)
                em.op("sy", lambda e, dstv=dstv, srcv=srcv: e.dma_start(dstv, srcv),
                      inc="s_ydma", k=16)
                if b0 == NB - 1:
                    ydma_mark[B] = em.cnt["s_ydma"]
            return [f_tr1, f_ev1, f_st1, f_st2, f_sd, f_rc, f_no, f_tr2, f_ev2,
                    f_dma]

        for b0 in range(NB):
            flushq.extend(mk(b0))

    def drain_flush(n):
        while n > 0 and flpos[0] < len(flushq):
            flushq[flpos[0]](em)
            flpos[0] += 1
            n -= 1

    m_mm_h = {0: 0, 1: 0}
    for t in range(1, L):
        B = t // 128
        buf = B % 2
        pbuf = ((t - 1) // 128) % 2
        par = t % 2
        po = t % 128
        ppo = (t - 1) % 128
        if po == 0:
            queue_flush(B - 1)
            assert p1pos[0] >= p1_need.get(B, 0), (B, p1pos[0])
        em.wait("te", "s_dv", base_dv + 2 * t)
        prev0 = pbuf * 2048 + ppo * 16
        for e_ in range(KC):
            hb = (e_ // 4) * 512
            fn0 = lambda e, e_=e_, hb=hb, prev0=prev0: e.matmul(
                ps_sc[:, hb + (e_ % 4) * 2:hb + (e_ % 4) * 2 + 2],
                wrT[:, 0 * D + e_ * 128:0 * D + (e_ + 1) * 128],
                y_fm[:, prev0 + 0:prev0 + 2],
                start=(e_ % 4 == 0), stop=False)
            em.op("te", fn0)
            for k in range(1, KC):
                fn = lambda e, e_=e_, k=k, hb=hb, prev0=prev0: e.matmul(
                    ps_sc[:, hb + (e_ % 4) * 2:hb + (e_ % 4) * 2 + 2],
                    wrT[:, k * D + e_ * 128:k * D + (e_ + 1) * 128],
                    y_fm[:, prev0 + k * 2:prev0 + k * 2 + 2],
                    start=False, stop=(k == KC - 1 and e_ % 4 == 3))
                if k == KC - 1 and e_ in (3, KC - 1):
                    m_mm_h[e_ // 4] = em.op("te", fn, inc="s_mm")
                else:
                    em.op("te", fn)
        if po == 0:
            em.wait("ve", "s_fl", flush_ev.get(B - 2, 0))
        cur0 = buf * 2048 + po * 16
        uv0 = (B % 4) * 2048 + po * 16
        if t == 1:
            em.op("ve", lambda e, prev0=prev0: e.tensor_tensor(
                scan_tmp[:, 32:48], y_fm[:, prev0:prev0 + 16], dec_bc[:], op=ALU.mult))
            em.op("ve", lambda e, uv0=uv0: e.tensor_tensor(
                scan_tmp[:, 48:64], scan_tmp[:, 32:48], v_st[:, uv0:uv0 + 16],
                op=ALU.add))
        em.wait("ve", "s_mm", m_mm_h[0])
        em.op("ve", lambda e, uv0=uv0: e.tensor_tensor(
            scan_tmp[:, 0:8], ps_sc[:, 0:8],
            u_st[:, uv0:uv0 + 8], op=ALU.mult))
        em.op("ve", lambda e, cur0=cur0: e.tensor_tensor(
            y_fm[:, cur0:cur0 + 8], scan_tmp[:, 48:56], scan_tmp[:, 0:8],
            op=ALU.subtract), inc="s_dv", drain=False)
        em.wait("ve", "s_mm", m_mm_h[1])
        em.op("ve", lambda e, uv0=uv0: e.tensor_tensor(
            scan_tmp[:, 8:16], ps_sc[:, 512:520],
            u_st[:, uv0 + 8:uv0 + 16], op=ALU.mult))
        em.op("ve", lambda e, cur0=cur0: e.tensor_tensor(
            y_fm[:, cur0 + 8:cur0 + 16], scan_tmp[:, 56:64], scan_tmp[:, 8:16],
            op=ALU.subtract), inc="s_dv")
        if t < L - 1:
            nt = t + 1
            nuv0 = ((nt // 128) % 4) * 2048 + (nt % 128) * 16
            em.op("ve", lambda e, cur0=cur0: e.tensor_tensor(
                scan_tmp[:, 32:48], y_fm[:, cur0:cur0 + 16], dec_bc[:], op=ALU.mult))
            em.op("ve", lambda e, nuv0=nuv0: e.tensor_tensor(
                scan_tmp[:, 48:64], scan_tmp[:, 32:48], v_st[:, nuv0:nuv0 + 16],
                op=ALU.add), drain=False)
        drain_flush(1)
        if t >= 2:
            drain_quant(1)
        if t % 2 == 1:
            drain_p1(t, 1)
    queue_flush(NCH - 1)
    drain_flush(10 ** 9)
    drain_quant(10 ** 9)
    m_wq_all = em.cnt["s_gd"]
    m_y_all = em.cnt["s_ydma"]

    em.barrier()
    em.realize()
    for c in reversed(cm_p2):
        c.__exit__(None, None, None)
    cm_wgT.__exit__(None, None, None)
    cm_wrT.__exit__(None, None, None)
    if max_phase <= 2:
        return _close_truncated(nc, em)

    cm_p3 = [nc.sbuf_tensor("actT", [128, KC * NT3], BF16),
             nc.sbuf_tensor("ynT", [128, 2 * KC * NT3], BF16),
             nc.sbuf_tensor("att_tm", [128, 4 * D], BF16),
             nc.sbuf_tensor("att_fm", [128, KC * NT3], BF16),
             nc.sbuf_tensor("h3", [128, FC * NT3], BF16),
             nc.sbuf_tensor("wqf_sb", [128, 2 * 2 * KC * 128], BF16),
             nc.sbuf_tensor("wqp_sb", [128, 2 * FC * 128], BF16)]
    (actT, ynT, att_tm, att_fm, h3, wqf_sb, wqp_sb) = [c.__enter__() for c in cm_p3]

    out_mark = [0, 0, 0, 0]
    yin_mark = [0]
    mm_ev = {0: 0, 1: 0}
    wqf_mark = {}
    wqp_mark = {}
    wqf_free = {0: 0, 1: 0}
    wqp_free = {0: 0, 1: 0}
    first_wq = [True]

    def emit_wqf_load(jg):
        pz = jg % 2
        if first_wq[0]:
            em.wait("gp", "s_gd", m_wq_all)
            em.wait("gp", "s_ydma", m_y_all)
            first_wq[0] = False
        em.wait("gp", "s_mm", wqf_free[pz])
        em.dma_fence("gp", "s_gd")
        dstv = wqf_sb[:, pz * 2 * KC * 128:(pz + 1) * 2 * KC * 128].rearrange(
            "p (j c) -> p j c", j=2)
        srcv = wqfc_h.ap()[jg * 2:(jg + 1) * 2, :, :].rearrange("j p c -> p j c")
        wqf_mark[jg] = em.op("gp", lambda e, dstv=dstv, srcv=srcv: e.dma_start(
            dstv, srcv), inc="s_gd", k=16)

    def emit_wqp_load(m):
        pz = m % 2
        em.wait("gp", "s_mm", wqp_free[pz])
        em.dma_fence("gp", "s_gd")
        wqp_mark[m] = em.op("gp", lambda e, m=m, pz=pz: e.dma_start(
            wqp_sb[:, pz * FC * 128:(pz + 1) * FC * 128], wqpr_h.ap()[m, :, :]),
            inc="s_gd", k=16)

    em.wait("sc", "s_ydma", m_y_all)
    yn_mark = {}

    def emit_ynT_load(bb, tt0, par):
        em.wait("sc", "s_mm", yin_mark[0])
        em.dma_fence("sc", "s_scd")
        dstv = ynT[:, par * KC * NT3:(par + 1) * KC * NT3].rearrange(
            "p (k t) -> p k t", k=KC)
        srcv = ynT_h.ap()[bb, :, :, tt0:tt0 + NT3].rearrange("k p t -> p k t")
        return em.op("sc", lambda e, dstv=dstv, srcv=srcv: e.dma_start(dstv, srcv),
                     inc="s_scd", k=16)

    for b in range(NB):
        em.wait("sc", "s_dv", em.cnt["s_dv"])
        em.wait("te", "s_dv", em.cnt["s_dv"])
        mrep3 = em.cnt["s_sc"]
        mrep3 = build_rep(0, b, 4 * D, ONE, mrep3)
        mrep3 = build_rep(1, b, 3 * D, Z, mrep3)
        em.wait("ve", "s_sc", mrep3)
        em.wait("te", "s_sc", mrep3)
        for gi, goff in ((0, 2 * D), (2, 5 * D)):
            for k in range(KC):
                em.op("te", lambda e, b=b, goff=goff, k=k: e.transpose(
                    ps_sc[:, k:k + 1],
                    mod_sb[32 * b:32 * b + 1, goff + k * 128:goff + (k + 1) * 128],
                    ones1[32 * b:32 * b + 1, 0:1]))
            mtr = em.inc_last("te", "s_tr")
            em.wait("ve", "s_tr", mtr)
            em.op("ve", lambda e, gi=gi: e.tensor_copy(
                gfm[:, gi * 8:(gi + 1) * 8], ps_sc[:, 0:8]))
            bcol = vecs[:, 32:40] if gi == 0 else vecs[:, 40:48]
            mgf = em.op("ve", lambda e, gi=gi, bcol=bcol: e.tensor_tensor(
                gfm[:, (gi + 1) * 8:(gi + 2) * 8], gfm[:, gi * 8:(gi + 1) * 8],
                bcol, op=ALU.mult), inc="s_dv")
            em.wait("te", "s_dv", mgf)
            em.wait("sc", "s_dv", mgf)
        for t3 in range(NTIL3):
            t0 = t3 * NT3
            mxl = [0] * NS3
            myl = [0] * NS3
            for s in range(NS3):
                em.wait("sy", "s_out", out_mark[s])
                em.dma_fence("sy", "s_ld")
                mxl[s] = em.op("sy", lambda e, b=b, s=s, t0=t0: e.dma_start(
                    x_sb[:, s * D:(s + 1) * D],
                    x_d.ap()[b, t0 + 128 * s:t0 + 128 * (s + 1), :]), inc="s_ld", k=16)
            ti = b * NTIL3 + t3
            pyn = ti % 2
            if ti == 0:
                yn_mark[0] = emit_ynT_load(0, 0, 0)
            if ti + 1 < NB * NTIL3:
                bn, tn = divmod(ti + 1, NTIL3)
                yn_mark[(ti + 1) % 2] = emit_ynT_load(bn, tn * NT3, (ti + 1) % 2)
            myn = yn_mark[pyn]
            emit_wqf_load(0)
            emit_wqf_load(1)
            m_att = 0
            em.wait("te", "s_scd", myn)
            ev_c = [0] * KC
            for c in range(KC):
                if c % 4 == 0:
                    em.wait("te", "s_sc", em.cnt["s_sc"])
                pb = (ps_sc, 0) if c % 4 == 0 else \
                     (ps_sc, 512) if c % 4 == 1 else \
                     (ps_mm, 0) if c % 4 == 2 else (ps_mm, 512)
                for k in range(KC):
                    em.op("te", lambda e, c=c, k=k, pb=pb, pyn=pyn: e.matmul(
                        pb[0][:, pb[1]:pb[1] + NT3],
                        woT[:, k * D + c * 128:k * D + (c + 1) * 128],
                        ynT[:, pyn * KC * NT3 + k * NT3:pyn * KC * NT3 + (k + 1) * NT3],
                        start=(k == 0), stop=(k == KC - 1)))
                m_mm = em.inc_last("te", "s_mm")
                em.wait("sc", "s_mm", m_mm)
                ev_c[c] = em.op("sc", lambda e, c=c, pb=pb: e.activation(
                    att_fm[:, c * NT3:(c + 1) * NT3], pb[0][:, pb[1]:pb[1] + NT3],
                    AF.Identity, scale=gfm[:, c:c + 1], bias=gfm[:, 8 + c:9 + c]),
                    inc="s_sc")
            yn_used = em.cnt["s_mm"]
            mxp = [0] * NS3
            m_act2 = 0
            for s4 in range(NS3):
                pz = s4 % 2
                em.wait("te", "s_sc", ev_c[KC - 1])
                em.wait("te", pbf_free[pz][0], pbf_free[pz][1])
                for c in range(KC):
                    em.op("te", lambda e, c=c, s4=s4, pz=pz: e.transpose(
                        ps_bf[:, pz * 1024 + c * 128:pz * 1024 + (c + 1) * 128],
                        att_fm[:, c * NT3 + s4 * 128:c * NT3 + (s4 + 1) * 128], id_b[:]))
                mtr = em.inc_last("te", "s_tr")
                em.wait("ve", "s_tr", mtr)
                m_att = em.op("ve", lambda e, s4=s4, pz=pz: e.tensor_copy(
                    att_tm[:, s4 * D:(s4 + 1) * D],
                    ps_bf[:, pz * 1024:(pz + 1) * 1024]), inc="s_dv", drain=True)
                pbf_free[pz] = ("s_dv", m_att)
                em.wait("ve", "s_ld", mxl[s4])
                mxp[s4] = em.op("ve", lambda e, s4=s4: e.tensor_tensor(
                    x_sb[:, s4 * D:(s4 + 1) * D], x_sb[:, s4 * D:(s4 + 1) * D],
                    att_tm[:, s4 * D:(s4 + 1) * D], op=ALU.add), inc="s_dv")
                _, mh = emit_ln_modulate(
                    x_sb[:, s4 * D:(s4 + 1) * D], s4 % 2, rep[:, 0:D], rep[:, D:2 * D],
                    None, None, mxp[s4], "s_dv")
                m_act2 = emit_act_transpose(s4 % 2, mh, actT, s4 * 128)
            yin_mark[0] = yn_used
            em.wait("te", "s_dv", m_act2)
            for k in range(KC):
                em.op("te", lambda e, k=k: e.matmul(
                    ps_lo[0:R, 0:NT3], afcT[:, k * R:(k + 1) * R], actT_full(actT, k),
                    start=(k == 0), stop=(k == KC - 1)))
            m_mm = em.inc_last("te", "s_mm")
            em.wait("ve", "s_mm", m_mm)
            m_s2 = em.op("ve", lambda e: e.tensor_scalar(
                s2[0:R, :], ps_lo[0:R, 0:NT3], 2.0, None, op0=ALU.mult), inc="s_dv")
            em.wait("te", "s_dv", m_s2)
            for j in range(FC):
                jg = j // 2
                pz4 = j % 4
                pg = jg % 2
                if j % 2 == 0:
                    em.wait("te", "s_gd", wqf_mark[jg])
                if j % 4 == 0:
                    em.wait("te", "s_sc", em.cnt["s_sc"])
                pb = (ps_mm, 0) if pz4 == 0 else \
                     (ps_mm, 512) if pz4 == 1 else \
                     (ps_sc, 0) if pz4 == 2 else (ps_sc, 512)
                base_w = pg * 2 * KC * 128 + (j % 2) * KC * 128
                for k in range(KC):
                    em.op("te", lambda e, k=k, pb=pb, base_w=base_w: e.matmul(
                        pb[0][:, pb[1]:pb[1] + NT3],
                        wqf_sb[:, base_w + k * 128:base_w + (k + 1) * 128],
                        actT_full(actT, k), start=(k == 0), stop=False))
                m_mm = em.op("te", lambda e, j=j, pb=pb: e.matmul(
                    pb[0][:, pb[1]:pb[1] + NT3],
                    bfcT[0:R, j * 128:(j + 1) * 128], s2[0:R, :],
                    start=False, stop=True), inc="s_mm")
                if j % 2 == 1:
                    wqf_free[pg] = m_mm
                em.wait("sc", "s_mm", m_mm)
                em.op("sc", lambda e, j=j, pb=pb: e.activation(
                    h3[:, j * NT3:(j + 1) * NT3], pb[0][:, pb[1]:pb[1] + NT3],
                    AF.Gelu, bias=bfc_fm[:, j:j + 1]), inc="s_sc")
                if j % 2 == 1 and jg + 2 < FC // 2:
                    emit_wqf_load(jg + 2)
            emit_wqp_load(0)
            emit_wqp_load(1)
            em.wait("te", "s_sc", em.cnt["s_sc"])
            for j in range(FC):
                em.op("te", lambda e, j=j: e.matmul(
                    ps_lo[0:R, 0:NT3], aprT[:, j * R:(j + 1) * R],
                    h3[:, j * NT3:(j + 1) * NT3],
                    start=(j == 0), stop=(j == FC - 1)))
            m_mm = em.inc_last("te", "s_mm")
            em.wait("ve", "s_mm", m_mm)
            m_s2p = em.op("ve", lambda e: e.tensor_scalar(
                s2[0:R, :], ps_lo[0:R, 0:NT3], 2.0, None, op0=ALU.mult), inc="s_dv")
            m_proj = 0
            em.wait("te", "s_dv", m_s2p)
            ev_m = [0] * KC
            for m_ in range(KC):
                pz4 = m_ % 4
                em.wait("te", "s_gd", wqp_mark[m_])
                if m_ % 4 == 0:
                    em.wait("te", "s_sc", em.cnt["s_sc"])
                pb = (ps_mm, 0) if pz4 == 0 else \
                     (ps_mm, 512) if pz4 == 1 else \
                     (ps_sc, 0) if pz4 == 2 else (ps_sc, 512)
                pzb = m_ % 2
                for k in range(FC):
                    em.op("te", lambda e, m_=m_, k=k, pb=pb, pzb=pzb: e.matmul(
                        pb[0][:, pb[1]:pb[1] + NT3],
                        wqp_sb[:, pzb * FC * 128 + k * 128:pzb * FC * 128 + (k + 1) * 128],
                        h3[:, k * NT3:(k + 1) * NT3], start=(k == 0), stop=False))
                m_mm = em.op("te", lambda e, m_=m_, pb=pb: e.matmul(
                    pb[0][:, pb[1]:pb[1] + NT3],
                    bprT[0:R, m_ * 128:(m_ + 1) * 128], s2[0:R, :],
                    start=False, stop=True), inc="s_mm")
                wqp_free[pzb] = m_mm
                em.wait("sc", "s_mm", m_mm)
                ev_m[m_] = em.op("sc", lambda e, m_=m_, pb=pb: e.activation(
                    att_fm[:, m_ * NT3:(m_ + 1) * NT3], pb[0][:, pb[1]:pb[1] + NT3],
                    AF.Identity, scale=gfm[:, 16 + m_:17 + m_],
                    bias=gfm[:, 24 + m_:25 + m_]), inc="s_sc")
                if m_ + 2 < KC:
                    emit_wqp_load(m_ + 2)
            for s4 in range(NS3):
                pz = s4 % 2
                em.wait("te", "s_sc", ev_m[KC - 1])
                em.wait("te", pbf_free[pz][0], pbf_free[pz][1])
                for m_ in range(KC):
                    em.op("te", lambda e, m_=m_, s4=s4, pz=pz: e.transpose(
                        ps_bf[:, pz * 1024 + m_ * 128:pz * 1024 + (m_ + 1) * 128],
                        att_fm[:, m_ * NT3 + s4 * 128:m_ * NT3 + (s4 + 1) * 128], id_b[:]))
                mtr = em.inc_last("te", "s_tr")
                em.wait("ve", "s_tr", mtr)
                m_proj = em.op("ve", lambda e, s4=s4, pz=pz: e.tensor_copy(
                    att_tm[:, s4 * D:(s4 + 1) * D],
                    ps_bf[:, pz * 1024:(pz + 1) * 1024]), inc="s_dv", drain=True)
                pbf_free[pz] = ("s_dv", m_proj)
                mo = em.op("ve", lambda e, s4=s4: e.tensor_tensor(
                    x_sb[:, s4 * D:(s4 + 1) * D], x_sb[:, s4 * D:(s4 + 1) * D],
                    att_tm[:, s4 * D:(s4 + 1) * D], op=ALU.add), inc="s_dv")
                em.wait("sy", "s_dv", mo)
                em.dma_fence("sy", "s_out")
                out_mark[s4] = em.op("sy", lambda e, b=b, s4=s4, t0=t0: e.dma_start(
                    out_d.ap()[b, t0 + 128 * s4:t0 + 128 * (s4 + 1), :],
                    x_sb[:, s4 * D:(s4 + 1) * D]), inc="s_out", k=16)
    em.wait("sy", "s_out", em.cnt["s_out"])
    em.realize()
    for c in reversed(cm_p3):
        c.__exit__(None, None, None)
    return nc


def _slice_inputs(inputs, core):
    b0 = NB * core
    m = {
        "x": np.ascontiguousarray(inputs["L0_attn_x"][b0:b0 + NB]),
        "cvec": np.ascontiguousarray(inputs["L0_c"][b0:b0 + NB]),
        "wmod": inputs["L0_wmod"],
        "decay": inputs["L0_attn_decay"],
        "Wr": inputs["L0_attn_Wr"], "br": inputs["L0_attn_br"],
        "Wg": inputs["L0_attn_Wg"], "bg": inputs["L0_attn_bg"],
        "Wo": inputs["L0_attn_Wo"], "bo": inputs["L0_attn_bo"],
        "ln_w": inputs["L0_attn_ln_w"], "ln_b": inputs["L0_attn_ln_b"],
        "W_fc": inputs["L0_ffn_W_fc"], "b_fc": inputs["L0_ffn_b_fc"],
        "A_fc": inputs["L0_ffn_A_fc"], "B_fc": inputs["L0_ffn_B_fc"],
        "W_proj": inputs["L0_ffn_W_proj"], "b_proj": inputs["L0_ffn_b_proj"],
        "A_proj": inputs["L0_ffn_A_proj"], "B_proj": inputs["L0_ffn_B_proj"],
    }
    return {k: np.ascontiguousarray(v) for k, v in m.items()}


def kernel(**inputs):
    nc = build_program(L=2048)
    in_maps = [_slice_inputs(inputs, i) for i in range(8)]
    res = run_bass_kernel_spmd(nc, in_maps, core_ids=list(range(8)))
    return np.concatenate([np.asarray(r["out"]) for r in res.results], axis=0)


# version 22
# speedup vs baseline: 1.1600x; 1.0362x over previous
import numpy as np
import concourse.bass as bass
import concourse.mybir as mybir
from concourse import library_config
from concourse.bass_utils import run_bass_kernel_spmd

F32 = mybir.dt.float32
BF16 = mybir.dt.bfloat16
AF = mybir.ActivationFunctionType
ALU = mybir.AluOpType
AX = mybir.AxisListType

D = 1024
FF = 4096
NB = 2
KC = D // 128
FC = FF // 128
R = 16
SCL = 2.0
EPS = 1e-5
ENGS = ("sy", "gp", "ve", "sc", "te")


class EM:

    def __init__(self, nc, sems):
        self.nc = nc
        self.sems = sems
        self.cnt = {k: 0 for k in sems}
        self.lists = {e: [] for e in ENGS}

    def op(self, eng, fn, inc=None, k=1, drain=None):
        incs = []
        ret = None
        if inc is not None:
            self.cnt[inc] += k
            incs.append((self.sems[inc], k))
            ret = self.cnt[inc]
        self.lists[eng].append(("op", fn, incs))
        if drain is None:
            drain = eng == "ve"
        if drain:
            self.lists[eng].append(("drain",))
        return ret

    def drain(self, eng):
        self.lists[eng].append(("drain",))

    def inc_last(self, eng, inc, k=1):
        for entry in reversed(self.lists[eng]):
            if entry[0] == "op":
                if not entry[2]:
                    self.cnt[inc] += k
                    entry[2].append((self.sems[inc], k))
                    return self.cnt[inc]
                self.lists[eng].append(("drain",))
                return self.op(eng, lambda e: e.nop(), inc=inc, k=k)
        return self.op(eng, lambda e: e.nop(), inc=inc, k=k)

    def dma_fence(self, eng, sem_name):
        self.wait(eng, sem_name, self.cnt[sem_name])

    def wait(self, eng, sem_name, val):
        if val < 0:
            return
        self.lists[eng].append(("wait", self.sems[sem_name], val))

    def barrier(self):
        for e in ENGS:
            if e in ("sy", "gp"):
                self.op(e, lambda x: x.nop(), inc="s_ph")
            else:
                self.inc_last(e, "s_ph")
        tgt = self.cnt["s_ph"]
        for e in ENGS:
            self.wait(e, "s_ph", tgt)

    @staticmethod
    def _replay(eng, entries):
        for entry in entries:
            if entry[0] == "op":
                instr = entry[1](eng)
                for sem, k in entry[2]:
                    instr.then_inc(sem, k)
            elif entry[0] == "wait":
                eng.wait_ge(entry[1], entry[2])
            else:
                eng.drain()

    def realize(self):
        with self.nc.Block() as blk:
            for e, meth in (("sy", blk.sync), ("gp", blk.gpsimd), ("ve", blk.vector),
                            ("sc", blk.scalar), ("te", blk.tensor)):
                entries = self.lists[e]
                if entries:
                    meth(lambda eng, entries=entries: EM._replay(eng, entries))
        self.lists = {e: [] for e in ENGS}



def _close_truncated(nc, em):
    with nc.Block() as blk:
        def _w(eng):
            for name in ("s_ld", "s_gd", "s_scd", "s_uv", "s_ydma", "s_out"):
                if em.cnt[name] > 0:
                    eng.wait_ge(em.sems[name], em.cnt[name])
        blk.sync(_w)
    return nc

def build_program(L=2048, max_phase=4):
    NCH = L // 128
    NT1 = 256
    NT3 = min(512, L)
    NS3 = NT3 // 128
    NTIL1 = L // NT1
    NTIL3 = L // NT3
    nc = bass.Bass("TRN2", target_bir_lowering=False)

    x_d = nc.dram_tensor("x", [NB, L, D], F32, kind="ExternalInput")
    c_d = nc.dram_tensor("cvec", [NB, D], F32, kind="ExternalInput")
    wmod_d = nc.dram_tensor("wmod", [D, 6 * D], F32, kind="ExternalInput")
    decay_d = nc.dram_tensor("decay", [D], F32, kind="ExternalInput")
    wr_d = nc.dram_tensor("Wr", [D, D], F32, kind="ExternalInput")
    br_d = nc.dram_tensor("br", [D], F32, kind="ExternalInput")
    wg_d = nc.dram_tensor("Wg", [D, D], F32, kind="ExternalInput")
    bg_d = nc.dram_tensor("bg", [D], F32, kind="ExternalInput")
    wo_d = nc.dram_tensor("Wo", [D, D], F32, kind="ExternalInput")
    bo_d = nc.dram_tensor("bo", [D], F32, kind="ExternalInput")
    lnw_d = nc.dram_tensor("ln_w", [D], F32, kind="ExternalInput")
    lnb_d = nc.dram_tensor("ln_b", [D], F32, kind="ExternalInput")
    wfc_d = nc.dram_tensor("W_fc", [FF, D], F32, kind="ExternalInput")
    bfc_d = nc.dram_tensor("b_fc", [FF], F32, kind="ExternalInput")
    afc_d = nc.dram_tensor("A_fc", [R, D], F32, kind="ExternalInput")
    Bfc_d = nc.dram_tensor("B_fc", [FF, R], F32, kind="ExternalInput")
    wpr_d = nc.dram_tensor("W_proj", [D, FF], F32, kind="ExternalInput")
    bpr_d = nc.dram_tensor("b_proj", [D], F32, kind="ExternalInput")
    apr_d = nc.dram_tensor("A_proj", [R, FF], F32, kind="ExternalInput")
    Bpr_d = nc.dram_tensor("B_proj", [D, R], F32, kind="ExternalInput")
    out_d = nc.dram_tensor("out", [NB, L, D], F32, kind="ExternalOutput")
    ynT_h = nc.dram_tensor("ynT_h", [NB, KC, 128, L], BF16)
    wqfc_h = nc.dram_tensor("wqfc_h", [FC, 128, KC * 128], BF16)
    wqpr_h = nc.dram_tensor("wqpr_h", [KC, 128, FC * 128], BF16)

    A = nc.alloc_sbuf_tensor
    woT = A("woT", [128, KC * D], BF16)
    mod_sb = A("mod_sb", [64, 6 * D], F32)
    rep = A("rep", [128, 4 * D], BF16)
    tm_bf = A("tm_bf", [128, D], BF16)
    vecs = A("vecs", [128, 64], F32)
    dec_bc = A("dec_bc", [128, 16], F32)
    lnb_b = A("lnb_b", [128, 8], BF16)
    gfm = A("gfm", [128, 32], F32)
    bfc_fm = A("bfc_fm", [128, FC], F32)
    id_b = A("id_b", [128, 128], BF16)
    id_f = A("id_f", [128, 128], F32)
    ones1 = A("ones1", [64, 128], F32)
    afcT = A("afcT", [128, KC * R], BF16)
    aprT = A("aprT", [128, FC * R], BF16)
    bfcT = A("bfcT", [R, FF], BF16)
    bprT = A("bprT", [R, D], BF16)
    x_sb = A("x_sb", [128, 4 * D], F32)
    tm_a = A("tm_a", [128, D], F32)
    h_tmb = A("h_tmb", [128, 2 * D], BF16)
    stat = A("stat", [128, 16], F32)
    sq_dump = A("sq_dump", [128, D], F32)
    mx = A("mx", [128, 8], F32)
    siluT = A("siluT", [128, 16], F32)
    scan_tmp = A("scan_tmp", [128, 64], F32)
    s2 = A("s2", [R, NT3], BF16)

    ps_mm = nc.alloc_psum_tensor("ps_mm", [128, 1024], F32)
    ps_sc = nc.alloc_psum_tensor("ps_sc", [128, 1024], F32)
    ps_bf = nc.alloc_psum_tensor("ps_bf", [128, 2048], BF16)
    ps_lo = nc.alloc_psum_tensor("ps_lo", [128, 512], F32)
    ps_x = nc.alloc_psum_tensor("ps_x", [128, 1024], BF16)

    sems = {}
    for name in ["s_ld", "s_gd", "s_scd", "s_mm", "s_dv", "s_sc", "s_gp",
                 "s_uv", "s_p1", "s_ydma", "s_ev", "s_fl", "s_tr", "s_q", "s_qt", "s_out", "s_ph"]:
        sems[name] = nc.alloc_semaphore(name)
    em = EM(nc, sems)

    Z = vecs[:, 52:53]
    ONE = vecs[:, 53:54]

    cm_wrT = nc.sbuf_tensor("wrT", [128, KC * D], BF16)
    wrT = cm_wrT.__enter__()
    cm_wgT = nc.sbuf_tensor("wgT", [128, KC * D], BF16)
    wgT = cm_wgT.__enter__()
    cm_p0 = [nc.sbuf_tensor("wstage", [128, 2 * D], F32),
             nc.sbuf_tensor("wmodf", [128, 6 * D], F32),
             nc.sbuf_tensor("csb", [64, D], F32)]
    wstage, wmodf, csb = [c.__enter__() for c in cm_p0]
    lora_st = wstage

    em.op("ve", lambda e: e.memset(vecs[:], 0.0))
    mz = em.op("ve", lambda e: e.memset(ones1[:], 1.0), inc="s_dv")
    em.wait("gp", "s_dv", mz)
    em.op("gp", lambda e: e.iota(vecs[:, 54:55], pattern=[[0, 1]], base=0,
                                 channel_multiplier=1, allow_small_or_imprecise_dtypes=True))
    mgp = em.op("gp", lambda e: e.iota(id_f[:], pattern=[[1, 128]], base=0, channel_multiplier=0,
                                       allow_small_or_imprecise_dtypes=True), inc="s_gp")
    em.wait("ve", "s_gp", mgp)
    em.op("ve", lambda e: e.tensor_scalar(id_f[:], id_f[:], vecs[:, 54:55], None,
                                          op0=ALU.is_equal), inc="s_dv")
    em.op("ve", lambda e: e.tensor_copy(id_b[:], id_f[:]), inc="s_dv")
    mID = em.op("ve", lambda e: e.memset(vecs[:, 53:54], 1.0), inc="s_dv")
    em.wait("te", "s_dv", mID)
    em.wait("sc", "s_dv", mID)

    def vec_to_fm(rows, dsts):
        mld = 0
        for r, dr in enumerate(rows):
            mld = em.op("sy", lambda e, r=r, dr=dr: e.dma_start(
                csb[32 * r:32 * r + 1, :], dr.ap().rearrange("(o d) -> o d", o=1)),
                inc="s_ld", k=16)
        em.wait("te", "s_ld", mld)
        for r in range(len(rows)):
            for k in range(KC):
                em.op("te", lambda e, r=r, k=k: e.transpose(
                    ps_sc[:, r * 512 + k:r * 512 + k + 1],
                    csb[32 * r:32 * r + 1, k * 128:(k + 1) * 128],
                    ones1[32 * r:32 * r + 1, 0:1]))
        mtr = em.inc_last("te", "s_tr")
        em.wait("ve", "s_tr", mtr)
        for r, dcol in enumerate(dsts):
            em.op("ve", lambda e, r=r, dcol=dcol: e.tensor_copy(
                vecs[:, dcol:dcol + 8], ps_sc[:, r * 512:r * 512 + 8]))
        mdv = em.inc_last("ve", "s_dv")
        em.wait("sy", "s_dv", mdv)
        em.wait("te", "s_dv", mdv)
        return mdv

    mv1 = vec_to_fm([decay_d, br_d], [0, 16])
    vec_to_fm([bg_d, bo_d], [24, 32])
    vec_to_fm([bpr_d], [40])

    em.wait("sc", "s_dv", mv1)
    em.op("sc", lambda e: e.activation(vecs[:, 0:8], vecs[:, 0:8], AF.Sigmoid, bias=Z))
    msc = em.inc_last("sc", "s_sc")
    em.wait("ve", "s_sc", msc)
    em.op("ve", lambda e: e.tensor_scalar(vecs[:, 8:16], vecs[:, 0:8], -1.0, 1.0,
                                          op0=ALU.mult, op1=ALU.add))
    em.op("ve", lambda e: e.tensor_copy(dec_bc[:, 0:16:2], vecs[:, 0:8]))
    em.op("ve", lambda e: e.tensor_copy(dec_bc[:, 1:16:2], vecs[:, 0:8]), inc="s_dv")

    for p in range(4):
        em.dma_fence("sy", "s_ld")
        mld = em.op("sy", lambda e, p=p: e.dma_start(
            csb[0:1, :], bfc_d.ap().rearrange("(o d) -> o d", o=4)[p:p + 1, :]), inc="s_ld", k=16)
        em.wait("te", "s_ld", mld)
        for k in range(KC):
            em.op("te", lambda e, k=k: e.transpose(
                ps_sc[:, k:k + 1], csb[0:1, k * 128:(k + 1) * 128], id_f[0:1, 0:1]))
        mtr = em.inc_last("te", "s_tr")
        em.wait("ve", "s_tr", mtr)
        mdv = em.op("ve", lambda e, p=p: e.tensor_copy(
            bfc_fm[:, p * 8:(p + 1) * 8], ps_sc[:, 0:8]), inc="s_dv")
        em.wait("sy", "s_dv", mdv)
        em.wait("te", "s_dv", mdv)

    vec_to_fm([lnw_d], [55])
    mld = em.op("sy", lambda e: e.dma_start(csb[0:1, :], lnb_d.ap().rearrange("(o d) -> o d", o=1)), inc="s_ld", k=16)
    em.wait("te", "s_ld", mld)
    for k in range(KC):
        em.op("te", lambda e, k=k: e.transpose(
            ps_sc[:, k:k + 1], csb[0:1, k * 128:(k + 1) * 128], ones1[0:1, 0:1]))
    mtr = em.inc_last("te", "s_tr")
    em.wait("ve", "s_tr", mtr)
    em.op("ve", lambda e: e.tensor_copy(stat[:, 0:8], ps_sc[:, 0:8]))
    em.op("ve", lambda e: e.tensor_copy(lnb_b[:], ps_sc[:, 0:8]))
    mdv = em.inc_last("ve", "s_dv")
    em.wait("sy", "s_dv", mdv)
    em.wait("te", "s_dv", mdv)

    prev_ev = 0
    for wd, dstT in ((wg_d, wgT), (wr_d, wrT), (wo_d, woT)):
        for r in range(KC):
            par = r % 2
            em.wait("sy", "s_dv", prev_ev)
            em.dma_fence("sy", "s_ld")
            mld = em.op("sy", lambda e, wd=wd, par=par, r=r: e.dma_start(
                wstage[:, par * D:(par + 1) * D], wd.ap()[128 * r:128 * (r + 1), :]),
                inc="s_ld", k=16)
            em.wait("te", "s_ld", mld)
            em.wait("te", "s_dv", prev_ev)
            for k in range(KC):
                em.op("te", lambda e, par=par, k=k: e.transpose(
                    ps_sc[:, (k % 4) * 128 + (k // 4) * 512:(k % 4) * 128 + (k // 4) * 512 + 128],
                    wstage[:, par * D + k * 128:par * D + (k + 1) * 128], id_f[:]))
            mtr = em.inc_last("te", "s_tr")
            em.wait("ve", "s_tr", mtr)
            for half in range(2):
                dst3 = dstT[:].rearrange("p (k e) -> p k e", k=KC)[:, half * 4:(half + 1) * 4,
                                                                  r * 128:(r + 1) * 128]
                src3 = ps_sc[:, half * 512:(half + 1) * 512].rearrange("p (k i) -> p k i", k=4)
                em.op("ve", lambda e, dst3=dst3, src3=src3: e.tensor_copy(dst3, src3))
            prev_ev = em.inc_last("ve", "s_dv")

    em.wait("te", "s_dv", prev_ev)
    for hh in range(2):
        for k in range(KC):
            em.op("te", lambda e, hh=hh, k=k: e.matmul(
                ps_sc[0:1, hh * 512:(hh + 1) * 512], lnb_b[:, k:k + 1],
                woT[:, k * D + hh * 512:k * D + (hh + 1) * 512],
                start=(k == 0), stop=(k == KC - 1)))
    mq = em.inc_last("te", "s_mm")
    em.wait("ve", "s_mm", mq)
    em.op("ve", lambda e: e.tensor_copy(csb[0:1, 0:512], ps_sc[0:1, 0:512]))
    mdv = em.op("ve", lambda e: e.tensor_copy(csb[0:1, 512:1024], ps_sc[0:1, 512:1024]),
                inc="s_dv")
    em.wait("te", "s_dv", mdv)
    for k in range(KC):
        em.op("te", lambda e, k=k: e.transpose(
            ps_sc[:, k:k + 1], csb[0:1, k * 128:(k + 1) * 128], ones1[0:1, 0:1]))
    mtr = em.inc_last("te", "s_tr")
    em.wait("ve", "s_tr", mtr)
    em.op("ve", lambda e: e.tensor_tensor(vecs[:, 32:40], vecs[:, 32:40], ps_sc[:, 0:8],
                                          op=ALU.add))
    for k in range(KC):
        em.op("ve", lambda e, k=k: e.tensor_scalar(
            woT[:, k * D:(k + 1) * D], woT[:, k * D:(k + 1) * D],
            vecs[:, 55 + k:56 + k], None, op0=ALU.mult), drain=False)
    m_wofold = em.inc_last("ve", "s_dv")
    em.wait("te", "s_dv", m_wofold)
    em.wait("sy", "s_dv", m_wofold)

    em.dma_fence("sy", "s_ld")
    mld = em.op("sy", lambda e: e.dma_start(csb[0:2, :], c_d.ap()), inc="s_ld", k=16)
    em.wait("sc", "s_ld", mld)
    msc = em.op("sc", lambda e: e.activation(tm_a[0:2, :], csb[0:2, :], AF.Silu, bias=vecs[0:2, 52:53]), inc="s_sc")
    em.wait("te", "s_sc", msc)
    em.wait("te", "s_dv", prev_ev)
    for k in range(KC):
        em.op("te", lambda e, k=k: e.transpose(
            ps_sc[:, 2 * k:2 * k + 2], tm_a[0:2, k * 128:(k + 1) * 128], id_f[0:2, 0:2]))
    mtr = em.inc_last("te", "s_tr")
    em.wait("ve", "s_tr", mtr)
    msil = em.op("ve", lambda e: e.tensor_copy(siluT[:], ps_sc[:, 0:16]), inc="s_dv")

    ev_n = {}
    mq_prev = 0
    for kk in range(KC):
        em.wait("gp", "s_mm", mq_prev)
        em.dma_fence("gp", "s_gd")
        mgd = em.op("gp", lambda e, kk=kk: e.dma_start(
            wmodf[:], wmod_d.ap()[128 * kk:128 * (kk + 1), :]), inc="s_gd", k=16)
        em.wait("te", "s_gd", mgd)
        if kk == 0:
            em.wait("te", "s_dv", msil)
        for n in range(12):
            for b in range(2):
                par = (2 * n + b) % 2
                em.wait("te", "s_dv", ev_n.get((kk, 2 * n + b - 2), 0))
                em.op("te", lambda e, par=par, kk=kk, n=n, b=b: e.matmul(
                    ps_sc[0:1, par * 512:(par + 1) * 512],
                    siluT[:, 2 * kk + b:2 * kk + b + 1],
                    wmodf[:, n * 512:(n + 1) * 512],
                    start=True, stop=True))
                mmm = em.inc_last("te", "s_mm")
                mq_prev = mmm
                em.wait("ve", "s_mm", mmm)
                brow = 32 * b
                if kk == 0:
                    ev_n[(kk, 2 * n + b)] = em.op(
                        "ve", lambda e, par=par, n=n, brow=brow: e.tensor_copy(
                            mod_sb[brow:brow + 1, n * 512:(n + 1) * 512],
                            ps_sc[0:1, par * 512:(par + 1) * 512]), inc="s_dv")
                else:
                    ev_n[(kk, 2 * n + b)] = em.op(
                        "ve", lambda e, par=par, n=n, brow=brow: e.tensor_tensor(
                            mod_sb[brow:brow + 1, n * 512:(n + 1) * 512],
                            mod_sb[brow:brow + 1, n * 512:(n + 1) * 512],
                            ps_sc[0:1, par * 512:(par + 1) * 512], op=ALU.add), inc="s_dv")
        ev_n[(kk + 1, -2)] = ev_n[(kk, 22)]
        ev_n[(kk + 1, -1)] = ev_n[(kk, 23)]

    for off in (2 * D, 5 * D):
        for brow in (0, 32):
            em.op("ve", lambda e, off=off, brow=brow: e.tensor_scalar(
                tm_a[brow:brow + 1, :], mod_sb[brow:brow + 1, off:off + D], 0.0, 2.0,
                op0=ALU.is_ge, op1=ALU.mult))
            mdv = em.op("ve", lambda e, brow=brow: e.tensor_scalar(
                tm_a[brow:brow + 1, :], tm_a[brow:brow + 1, :], -1.0, None,
                op0=ALU.add), inc="s_dv")
            em.wait("sc", "s_dv", mdv)
            msc = em.op("sc", lambda e, off=off, brow=brow: e.activation(
                sq_dump[brow:brow + 1, :], mod_sb[brow:brow + 1, off:off + D], AF.Abs,
                bias=vecs[brow:brow + 1, 52:53]), inc="s_sc")
            em.wait("ve", "s_sc", msc)
            em.op("ve", lambda e, brow=brow: e.tensor_scalar(
                sq_dump[brow:brow + 1, :], sq_dump[brow:brow + 1, :], 0.25, None, op0=ALU.max))
            em.op("ve", lambda e, off=off, brow=brow: e.tensor_tensor(
                mod_sb[brow:brow + 1, off:off + D], tm_a[brow:brow + 1, :],
                sq_dump[brow:brow + 1, :], op=ALU.mult), inc="s_dv")

    em.wait("sy", "s_dv", prev_ev)
    em.dma_fence("sy", "s_ld")
    mld = em.op("sy", lambda e: e.dma_start(lora_st[0:R, 0:D], afc_d.ap()), inc="s_ld", k=16)
    em.wait("te", "s_dv", em.cnt["s_dv"])
    em.wait("te", "s_ld", mld)
    for k in range(KC):
        em.op("te", lambda e, k=k: e.transpose(
            ps_sc[:, k * 16:(k + 1) * 16], lora_st[0:R, k * 128:(k + 1) * 128], id_f[0:R, 0:R]))
    mtr = em.inc_last("te", "s_tr")
    em.wait("ve", "s_tr", mtr)
    mdv = em.op("ve", lambda e: e.tensor_copy(afcT[:], ps_sc[:, 0:KC * R]), inc="s_dv")
    em.wait("sy", "s_dv", mdv)
    em.wait("te", "s_dv", mdv)
    for hh in range(2):
        em.dma_fence("sy", "s_ld")
        mld = em.op("sy", lambda e, hh=hh: e.dma_start(
            lora_st[0:R, 0:2 * D], apr_d.ap()[:, hh * 2 * D:(hh + 1) * 2 * D]),
            inc="s_ld", k=16)
        em.wait("te", "s_ld", mld)
        for k in range(16):
            em.op("te", lambda e, k=k: e.transpose(
                ps_sc[:, k * 16:(k + 1) * 16], lora_st[0:R, k * 128:(k + 1) * 128],
                id_f[0:R, 0:R]))
        mtr = em.inc_last("te", "s_tr")
        em.wait("ve", "s_tr", mtr)
        mdv = em.op("ve", lambda e, hh=hh: e.tensor_copy(
            aprT[:, hh * 16 * R:(hh + 1) * 16 * R], ps_sc[:, 0:16 * R]), inc="s_dv")
        em.wait("te", "s_dv", mdv)
        em.wait("sy", "s_dv", mdv)
    em.wait("gp", "s_dv", mdv)
    for src_d, dstT, nt in ((Bfc_d, bfcT, FC), (Bpr_d, bprT, KC)):
        for g4 in range(nt // 4):
            em.dma_fence("gp", "s_gd")
            em.wait("gp", "s_dv", em.cnt["s_dv"])
            mgd = 0
            for jj in range(4):
                j = g4 * 4 + jj
                mgd = em.op("gp", lambda e, src_d=src_d, j=j, jj=jj: e.dma_start(
                    wstage[:, jj * 16:(jj + 1) * 16],
                    src_d.ap()[j * 128:(j + 1) * 128, :]), inc="s_gd", k=16)
            em.wait("te", "s_gd", mgd)
            for jj in range(4):
                em.op("te", lambda e, jj=jj: e.transpose(
                    ps_lo[0:128, jj * 128:(jj + 1) * 128][0:R, :],
                    wstage[:, jj * 16:(jj + 1) * 16], id_f[:]))
            mtr = em.inc_last("te", "s_tr")
            em.wait("ve", "s_tr", mtr)
            mdv = em.op("ve", lambda e, dstT=dstT, g4=g4: e.tensor_copy(
                dstT[0:R, g4 * 512:(g4 + 1) * 512], ps_lo[0:R, :]), inc="s_dv")
            em.wait("te", "s_dv", mdv)

    em.barrier()
    em.realize()
    for c in reversed(cm_p0):
        c.__exit__(None, None, None)
    if max_phase <= 0:
        return _close_truncated(nc, em)


    def actT_full(t, k):
        return t[:, k * NT3:(k + 1) * NT3]

    def build_rep(slot, b, off, bias_col, prev_mm):
        em.wait("te", "s_sc", prev_mm)
        for hh in range(2):
            mmm = em.op("te", lambda e, b=b, off=off, hh=hh: e.matmul(
                ps_mm[:, hh * 512:(hh + 1) * 512], ones1[32 * b:32 * b + 1, :],
                mod_sb[32 * b:32 * b + 1, off + hh * 512:off + (hh + 1) * 512],
                start=True, stop=True), inc="s_mm")
        em.wait("sc", "s_mm", mmm)
        for hh in range(2):
            msc = em.op("sc", lambda e, slot=slot, hh=hh, bias_col=bias_col: e.activation(
                rep[:, slot * D + hh * 512:slot * D + (hh + 1) * 512],
                ps_mm[:, hh * 512:(hh + 1) * 512], AF.Identity, bias=bias_col), inc="s_sc")
        return msc

    pbf_free = {0: ("s_dv", 0), 1: ("s_dv", 0)}
    x_free = [(0, 0)] * 4

    def emit_ln_modulate(src_ap, par_h, rep_mul, rep_add, mul2, add2,
                         m_data, data_sem):
        em.wait("ve", data_sem, m_data)
        em.op("ve", lambda e: e.bn_stats(stat[:, 0:6], src_ap[:, 0:512]))
        em.op("ve", lambda e: e.bn_stats(stat[:, 6:12], src_ap[:, 512:1024]))
        em.op("ve", lambda e: e.bn_aggr(stat[:, 12:14], stat[:, 0:12]))
        mdv = em.op("ve", lambda e: e.tensor_scalar(
            stat[:, 14:15], stat[:, 13:14], EPS, None, op0=ALU.add), inc="s_dv")
        em.wait("sc", "s_dv", mdv)
        msq = em.op("sc", lambda e: e.activation(
            stat[:, 7:8], stat[:, 14:15], AF.Sqrt, bias=Z), inc="s_sc")
        em.wait("ve", "s_sc", msq)
        em.op("ve", lambda e: e.reciprocal(stat[:, 15:16], stat[:, 7:8]))
        mnm = em.op("ve", lambda e: e.tensor_scalar(
            stat[:, 6:7], stat[:, 12:13], stat[:, 15:16], -1.0,
            op0=ALU.mult, op1=ALU.mult), inc="s_dv")
        em.wait("sc", "s_dv", mnm)
        mcons = em.op("sc", lambda e: e.activation(
            tm_bf[:], src_ap, AF.Identity, scale=stat[:, 15:16], bias=stat[:, 6:7]),
            inc="s_sc")
        em.wait("ve", "s_sc", mcons)
        em.op("ve", lambda e: e.tensor_tensor(tm_bf[:], tm_bf[:], rep_mul, op=ALU.mult))
        mh = em.op("ve", lambda e, par_h=par_h: e.tensor_tensor(
            h_tmb[:, par_h * D:(par_h + 1) * D], tm_bf[:], rep_add, op=ALU.add), inc="s_dv")
        return (mnm, mcons), mh

    def emit_act_transpose(par_h, mh, dstT, col0):
        em.wait("te", "s_dv", mh)
        em.wait("te", pbf_free[par_h][0], pbf_free[par_h][1])
        for k in range(KC):
            em.op("te", lambda e, par_h=par_h, k=k: e.transpose(
                ps_bf[:, par_h * 1024 + k * 128:par_h * 1024 + (k + 1) * 128],
                h_tmb[:, par_h * D + k * 128:par_h * D + (k + 1) * 128], id_b[:]))
        mtr = em.inc_last("te", "s_tr")
        em.wait("ve", "s_tr", mtr)
        dst3 = dstT[:].rearrange("p (k t) -> p k t", k=KC)[:, :, col0:col0 + 128]
        src3 = ps_bf[:, par_h * 1024:(par_h + 1) * 1024].rearrange("p (k i) -> p k i", k=KC)
        mev = em.op("ve", lambda e, dst3=dst3, src3=src3: e.tensor_copy(dst3, src3), inc="s_dv")
        pbf_free[par_h] = ("s_dv", mev)
        return mev

    cm_p2 = [nc.sbuf_tensor("y_fm", [128, 2 * 128 * 16], BF16),
             nc.sbuf_tensor("u_st", [128, 4 * 128 * 16], BF16),
             nc.sbuf_tensor("v_st", [128, 4 * 128 * 16], BF16),
             nc.sbuf_tensor("y_tm", [128, NB * D], BF16),
             nc.sbuf_tensor("ystage", [128, D], BF16),
             nc.sbuf_tensor("actT1", [128, KC * NT1], BF16),
             nc.sbuf_tensor("qload", [128, 2 * D], F32),
             nc.sbuf_tensor("qa", [128, D], F32),
             nc.sbuf_tensor("qr", [128, D], F32),
             nc.sbuf_tensor("qt", [128, D], F32),
             nc.sbuf_tensor("qsgn", [128, 2 * D], BF16),
             nc.sbuf_tensor("qb", [128, 2 * D], BF16),
             nc.sbuf_tensor("qstage", [128, 2 * D], BF16)]
    (y_fm, u_st, v_st, y_tm, ystage, actT1, qload, qa, qr, qt, qsgn, qb, qstage) = \
        [c.__enter__() for c in cm_p2]

    def ring_slice(chunk, c, b):
        base = (chunk % 4) * 2048 + c * 2 + b
        return base

    mrep = 0
    for b in range(NB):
        mrep = build_rep(2 * b + 0, b, D, ONE, mrep)
        mrep = build_rep(2 * b + 1, b, 0, Z, mrep)
    em.wait("ve", "s_sc", mrep)

    p1m = {}
    zev = {0: mrep, 1: mrep}
    zuse = {}

    def build_p1_tile(t1):
        items = []
        t0 = t1 * NT1

        def ln_items(b, s):
            slot = 2 * b + s
            src_ap = x_sb[:, slot * D:(slot + 1) * D]
            par_h = slot % 2

            def i_load(em, b=b, s=s, slot=slot, t0=t0, src_ap=src_ap):
                em.wait("sy", "s_p1", x_free[slot][0])
                em.wait("sy", "s_sc", x_free[slot][1])
                em.dma_fence("sy", "s_ld")
                p1m[('xl', slot)] = em.op("sy", lambda e: e.dma_start(
                    src_ap, x_d.ap()[b, t0 + 128 * s:t0 + 128 * (s + 1), :]),
                    inc="s_ld", k=16)

            def i_st1(em, slot=slot, src_ap=src_ap):
                em.wait("ve", "s_ld", p1m[('xl', slot)])
                em.op("ve", lambda e: e.reduce_sum(stat[:, 0:1], src_ap, axis=AX.X))
                em.wait("sc", "s_ld", p1m[('xl', slot)])
                em.drain("sc")
                p1m['sq'] = em.op("sc", lambda e: e.activation(
                    sq_dump[:], src_ap, AF.Square, bias=Z, accum_out=stat[:, 1:2]),
                    inc="s_sc")

            def i_st2(em):
                em.op("ve", lambda e: e.tensor_scalar(stat[:, 2:3], stat[:, 0:1],
                                                      1.0 / D, None, op0=ALU.mult))
                em.wait("ve", "s_sc", p1m['sq'])
                em.op("ve", lambda e: e.tensor_scalar(stat[:, 4:5], stat[:, 1:2],
                                                      1.0 / D, EPS, op0=ALU.mult, op1=ALU.add))
                em.op("ve", lambda e: e.tensor_tensor(stat[:, 5:6], stat[:, 2:3],
                                                      stat[:, 2:3], op=ALU.mult))
                p1m['var'] = em.op("ve", lambda e: e.tensor_tensor(
                    stat[:, 4:5], stat[:, 4:5], stat[:, 5:6], op=ALU.subtract), inc="s_p1")

            def i_st3(em):
                em.wait("sc", "s_p1", p1m['var'])
                p1m['sd'] = em.op("sc", lambda e: e.activation(
                    stat[:, 5:6], stat[:, 4:5], AF.Sqrt, bias=Z), inc="s_sc")

            def i_st4(em, slot=slot):
                em.wait("ve", "s_sc", p1m['sd'])
                em.op("ve", lambda e: e.reciprocal(stat[:, 3:4], stat[:, 5:6]))
                p1m['nm'] = em.op("ve", lambda e: e.tensor_scalar(
                    stat[:, 6:7], stat[:, 2:3], stat[:, 3:4], -1.0,
                    op0=ALU.mult, op1=ALU.mult), inc="s_p1")

            def i_xn(em, slot=slot, src_ap=src_ap):
                em.wait("sc", "s_p1", p1m['nm'])
                p1m[('xn', slot)] = em.op("sc", lambda e: e.activation(
                    tm_bf[:], src_ap, AF.Identity, scale=stat[:, 3:4],
                    bias=stat[:, 6:7]), inc="s_sc")

            def i_mod(em, b=b, slot=slot, par_h=par_h):
                em.wait("ve", "s_sc", p1m[('xn', slot)])
                em.op("ve", lambda e: e.tensor_tensor(
                    tm_bf[:], tm_bf[:], rep[:, (2 * b) * D:(2 * b + 1) * D], op=ALU.mult))
                mh = em.op("ve", lambda e: e.tensor_tensor(
                    h_tmb[:, par_h * D:(par_h + 1) * D], tm_bf[:],
                    rep[:, (2 * b + 1) * D:(2 * b + 2) * D], op=ALU.add), inc="s_p1")
                p1m[('h', slot)] = mh
                x_free[slot] = (p1m['nm'], p1m[('xn', slot)])

            def i_tr(em, slot=slot, par_h=par_h, s=s):
                em.wait("te", "s_p1", p1m[('h', slot)])
                em.wait("te", pbf_free[par_h][0], pbf_free[par_h][1])
                for k in range(KC):
                    em.op("te", lambda e, par_h=par_h, k=k: e.transpose(
                        ps_bf[:, par_h * 1024 + k * 128:par_h * 1024 + (k + 1) * 128],
                        h_tmb[:, par_h * D + k * 128:par_h * D + (k + 1) * 128], id_b[:]))
                p1m[('tr', slot)] = em.inc_last("te", "s_tr")

            def i_ev(em, slot=slot, par_h=par_h, s=s):
                em.wait("ve", "s_tr", p1m[('tr', slot)])
                dst3 = actT1[:].rearrange("p (k t) -> p k t", k=KC)[:, :, s * 128:(s + 1) * 128]
                src3 = ps_bf[:, par_h * 1024:(par_h + 1) * 1024].rearrange(
                    "p (k i) -> p k i", k=KC)
                mev = em.op("ve", lambda e, dst3=dst3, src3=src3: e.tensor_copy(dst3, src3),
                            inc="s_p1")
                pbf_free[par_h] = ("s_p1", mev)
                p1m[('ev', slot)] = mev
            return [i_load, i_st1, i_st2, i_st3, i_st4, i_xn, i_mod, i_tr, i_ev]

        def z_items(b, c, t1=t1):
            pz = c % 2
            chunk0 = 2 * t1

            def zi_mm(em, b=b, c=c, pz=pz):
                em.wait("te", "s_p1", p1m[('ev', 2 * b)])
                em.wait("te", "s_p1", p1m[('ev', 2 * b + 1)])
                em.wait("te", "s_sc", zev.get(pz, 0))
                for k in range(KC):
                    em.op("te", lambda e, c=c, k=k, pz=pz: e.matmul(
                        ps_mm[:, pz * 512:pz * 512 + NT1],
                        wgT[:, k * D + c * 128:k * D + (c + 1) * 128],
                        actT1[:, k * NT1:(k + 1) * NT1],
                        start=(k == 0), stop=(k == KC - 1)))
                p1m[('zmm', pz)] = em.inc_last("te", "s_mm")

            def zi_ev(em, c=c, pz=pz):
                em.wait("sc", "s_mm", p1m[('zmm', pz)])
                em.wait("sc", "s_p1", zuse.get(pz, 0))
                zev[pz] = em.op("sc", lambda e, pz=pz, c=c: e.activation(
                    tm_a[:, pz * 512:pz * 512 + NT1], ps_mm[:, pz * 512:pz * 512 + NT1],
                    AF.Sigmoid, bias=vecs[:, 24 + c:25 + c]), inc="s_sc")

            def zi_u(em, b=b, c=c, pz=pz, chunk0=chunk0):
                em.wait("ve", "s_sc", zev[pz])
                b0 = (chunk0 % 4) * 2048 + c * 2 + b
                b1 = ((chunk0 + 1) % 4) * 2048 + c * 2 + b
                em.op("ve", lambda e, b0=b0, pz=pz: e.tensor_scalar(
                    u_st[:, b0:b0 + 16 * 127 + 1:16], tm_a[:, pz * 512:pz * 512 + 128],
                    vecs[:, 8 + c:9 + c], None, op0=ALU.mult), drain=False)
                em.op("ve", lambda e, b1=b1, pz=pz: e.tensor_scalar(
                    u_st[:, b1:b1 + 16 * 127 + 1:16],
                    tm_a[:, pz * 512 + 128:pz * 512 + 256],
                    vecs[:, 8 + c:9 + c], None, op0=ALU.mult))

            def zi_v(em, b=b, c=c, pz=pz, chunk0=chunk0):
                b0 = (chunk0 % 4) * 2048 + c * 2 + b
                b1 = ((chunk0 + 1) % 4) * 2048 + c * 2 + b
                em.op("ve", lambda e, b0=b0, c=c: e.scalar_tensor_tensor(
                    v_st[:, b0:b0 + 16 * 127 + 1:16],
                    actT1[:, c * NT1:c * NT1 + 128],
                    vecs[:, 16 + c:17 + c],
                    u_st[:, b0:b0 + 16 * 127 + 1:16],
                    op0=ALU.subtract, op1=ALU.mult), drain=False)
                muv = em.op("ve", lambda e, b1=b1, c=c: e.scalar_tensor_tensor(
                    v_st[:, b1:b1 + 16 * 127 + 1:16],
                    actT1[:, c * NT1 + 128:(c + 1) * NT1],
                    vecs[:, 16 + c:17 + c],
                    u_st[:, b1:b1 + 16 * 127 + 1:16],
                    op0=ALU.subtract, op1=ALU.mult), inc="s_p1")
                zuse[pz] = muv
            return [zi_mm, zi_ev, zi_u, zi_v]

        for b in range(NB):
            for s in range(2):
                items += ln_items(b, s)
            for c in range(KC):
                items += z_items(b, c)
        return items

    for t1 in range(min(2, NTIL1)):
        for it in build_p1_tile(t1):
            it(em)
    p1q = []
    p1_need = {}
    for t1 in range(2, NTIL1):
        w = 256 * t1 - 255
        for it in build_p1_tile(t1):
            p1q.append((w, it))
        p1_need[2 * t1] = len(p1q)
    p1pos = [0]

    def drain_p1(t, n):
        while n > 0 and p1pos[0] < len(p1q) and p1q[p1pos[0]][0] <= t:
            p1q[p1pos[0]][1](em)
            p1pos[0] += 1
            n -= 1

    def gran_src(g):
        if g < FC:
            return wfc_d.ap()[g * 128:(g + 1) * 128, :]
        gg = g - FC
        r, qf = gg // 4, gg % 4
        return wpr_d.ap()[r * 128:(r + 1) * 128, qf * D:(qf + 1) * D]

    qitems = []
    qm = {}

    def qi_init(em):
        qm[('red', -1)] = em.op("ve", lambda e: e.memset(mx[:, 0:2], 0.0), inc="s_q")
    qitems.append(qi_init)

    for g in range(64):
        def qi_mload(em, g=g):
            if g >= 2:
                em.wait("gp", "s_q", qm[('red', g - 2)])
            em.dma_fence("gp", "s_gd")
            qm[('mld', g)] = em.op("gp", lambda e, g=g: e.dma_start(
                qload[:, (g % 2) * D:(g % 2 + 1) * D], gran_src(g)), inc="s_gd", k=16)

        def qi_mred(em, g=g):
            em.wait("ve", "s_gd", qm[('mld', g)])
            em.op("ve", lambda e, g=g: e.tensor_reduce(
                mx[:, 2:3], qload[:, (g % 2) * D:(g % 2 + 1) * D], axis=AX.X,
                op=ALU.max, apply_absolute_value=True))
            col = 0 if g < FC else 1
            qm[('red', g)] = em.op("ve", lambda e, col=col: e.tensor_tensor(
                mx[:, col:col + 1], mx[:, col:col + 1], mx[:, 2:3], op=ALU.max), inc="s_q")
        qitems += [qi_mload, qi_mred]

    def qi_fin(em):
        em.wait("te", "s_q", qm[('red', 63)])
        em.op("te", lambda e: e.transpose(ps_lo[:, 0:128][0:1, :], mx[:, 0:1], id_f[:]))
        qm['fint'] = em.op("te", lambda e: e.transpose(
            ps_lo[:, 128:256][0:1, :], mx[:, 1:2], id_f[:]), inc="s_qt")
        em.wait("ve", "s_qt", qm['fint'])
        em.op("ve", lambda e: e.tensor_reduce(mx[0:1, 4:5], ps_lo[0:1, 0:128], axis=AX.X, op=ALU.max))
        em.op("ve", lambda e: e.tensor_reduce(mx[0:1, 6:7], ps_lo[0:1, 128:256], axis=AX.X, op=ALU.max))
        em.op("ve", lambda e: e.reciprocal(mx[0:1, 5:6], mx[0:1, 4:5]))
        em.op("ve", lambda e: e.reciprocal(mx[0:1, 7:8], mx[0:1, 6:7]))
        em.op("ve", lambda e: e.tensor_scalar(mx[0:1, 4:5], mx[0:1, 4:5],
                                              1.0 / 127.0, None, op0=ALU.mult))
        em.op("ve", lambda e: e.tensor_scalar(mx[0:1, 6:7], mx[0:1, 6:7],
                                              1.0 / 127.0, None, op0=ALU.mult))
        em.op("ve", lambda e: e.tensor_scalar(mx[0:1, 5:6], mx[0:1, 5:6],
                                              127.0, None, op0=ALU.mult))
        qm['finv'] = em.op("ve", lambda e: e.tensor_scalar(
            mx[0:1, 7:8], mx[0:1, 7:8], 127.0, None, op0=ALU.mult), inc="s_q")
        em.wait("te", "s_q", qm['finv'])
        for i, mx_col in enumerate((4, 5, 6, 7)):
            em.op("te", lambda e, i=i, mx_col=mx_col: e.matmul(
                ps_lo[:, 256 + i:257 + i], ones1[0:1, :], mx[0:1, mx_col:mx_col + 1],
                start=True, stop=True))
        mtr = em.inc_last("te", "s_qt")
        em.wait("ve", "s_qt", mtr)
        for i, sc_col in enumerate((48, 49, 50, 51)):
            em.op("ve", lambda e, i=i, sc_col=sc_col: e.tensor_copy(
                vecs[:, sc_col:sc_col + 1], ps_lo[:, 256 + i:257 + i]))
        qm['fbc'] = em.inc_last("ve", "s_q")
    qitems.append(qi_fin)

    for g in range(64):
        iscale = vecs[:, 49:50] if g < FC else vecs[:, 51:52]
        mscale = vecs[:, 48:49] if g < FC else vecs[:, 50:51]

        def qi_load(em, g=g):
            em.wait("gp", "s_q",
                    qm[('wq', g - 2)] if g >= 2 else qm[('red', {0: 62, 1: 63}[g % 2])])
            em.dma_fence("gp", "s_gd")
            qm[('ld', g)] = em.op("gp", lambda e, g=g: e.dma_start(
                qload[:, (g % 2) * D:(g % 2 + 1) * D], gran_src(g)), inc="s_gd", k=16)

        def qi_q(em, g=g, iscale=iscale):
            em.wait("sc", "s_gd", qm[('ld', g)])
            if g == 0:
                em.wait("sc", "s_q", qm['fbc'])
            else:
                em.wait("sc", "s_q", qm[('wq', g - 1)])
            em.op("sc", lambda e, g=g, iscale=iscale: e.activation(
                qr[:], qload[:, (g % 2) * D:(g % 2 + 1) * D], AF.Identity,
                scale=iscale, bias=Z), drain=True)

        def qi_abs(em, g=g):
            em.op("sc", lambda e: e.activation(qa[:], qr[:], AF.Abs, bias=Z))
            qm[('sgn', g)] = em.op("sc", lambda e, g=g: e.activation(
                qsgn[:, (g % 2) * D:(g % 2 + 1) * D], qr[:], AF.Sign, bias=Z), inc="s_sc")

        def qi_b0(em, g=g):
            em.wait("ve", "s_sc", qm[('sgn', g)])
            em.op("ve", lambda e: e.tensor_scalar(qt[:], qa[:], 63.5, 64.0,
                                                  op0=ALU.is_ge, op1=ALU.mult))

        def qi_b0b(em, g=g):
            em.op("ve", lambda e: e.tensor_tensor(qr[:], qa[:], qt[:], op=ALU.subtract))

        def mk_bit(j):
            def qi_bj(em, j=j):
                em.op("ve", lambda e, j=j: e.tensor_scalar(
                    qt[:], qr[:], j - 0.5, float(j), op0=ALU.is_ge, op1=ALU.mult))
            def qi_bjs(em):
                em.op("ve", lambda e: e.tensor_tensor(qr[:], qr[:], qt[:], op=ALU.subtract))
            return [qi_bj, qi_bjs]

        def qi_rnd(em, g=g):
            qm[('rnd', g)] = em.op("ve", lambda e: e.tensor_tensor(
                qt[:], qa[:], qr[:], op=ALU.subtract), inc="s_q")

        def qi_sca(em, g=g, mscale=mscale):
            em.wait("sc", "s_q", qm[('rnd', g)])
            qm[('sca', g)] = em.op("sc", lambda e, mscale=mscale: e.activation(
                qr[:], qt[:], AF.Identity, scale=mscale, bias=Z), inc="s_sc",
                drain=True)

        def qi_sgm(em, g=g):
            em.wait("ve", "s_sc", qm[('sca', g)])
            qm[('wq', g)] = em.op("ve", lambda e, g=g: e.tensor_tensor(
                qb[:, (g % 2) * D:(g % 2 + 1) * D], qr[:],
                qsgn[:, (g % 2) * D:(g % 2 + 1) * D], op=ALU.mult), inc="s_q")

        def qi_tr(em, g=g):
            em.wait("te", "s_q", qm[('wq', g)])
            em.wait("te", "s_sc", qm.get(('ev', g - 1), 0))
            for k in range(KC):
                em.op("te", lambda e, g=g, k=k: e.transpose(
                    ps_x[:, k * 128:(k + 1) * 128],
                    qb[:, (g % 2) * D + k * 128:(g % 2) * D + (k + 1) * 128], id_b[:]))
            qm[('tr', g)] = em.inc_last("te", "s_qt")

        def qi_ev(em, g=g):
            em.wait("sc", "s_qt", qm[('tr', g)])
            if g >= 2:
                em.wait("sc", "s_gd", qm[('out', g - 2)])
            qm[('ev', g)] = em.op("sc", lambda e, g=g: e.activation(
                qstage[:, (g % 2) * D:(g % 2 + 1) * D], ps_x[:], AF.Identity,
                bias=Z), inc="s_sc")

        def qi_out(em, g=g):
            em.wait("gp", "s_sc", qm[('ev', g)])
            em.dma_fence("gp", "s_gd")
            if g < FC:
                dst = wqfc_h.ap()[g, :, :]
            else:
                gg = g - FC
                r, qf = gg // 4, gg % 4
                dst = wqpr_h.ap()[r, :, qf * D:(qf + 1) * D]
            qm[('out', g)] = em.op("gp", lambda e, g=g, dst=dst: e.dma_start(
                dst, qstage[:, (g % 2) * D:(g % 2 + 1) * D]), inc="s_gd", k=16)

        bit_items = []
        for j in (32, 16, 8, 4, 2, 1):
            bit_items += mk_bit(j)
        qitems += ([qi_load, qi_q, qi_abs, qi_b0, qi_b0b] + bit_items +
                   [qi_rnd, qi_sca, qi_sgm, qi_tr, qi_ev, qi_out])

    qpos = [0]

    def drain_quant(n):
        while n > 0 and qpos[0] < len(qitems):
            qitems[qpos[0]](em)
            qpos[0] += 1
            n -= 1

    base_dv = em.cnt["s_dv"]
    em.op("ve", lambda e: e.tensor_copy(y_fm[:, 0:8], v_st[:, 0:8]), inc="s_dv")
    em.op("ve", lambda e: e.tensor_copy(y_fm[:, 8:16], v_st[:, 8:16]), inc="s_dv")

    flush_ev = {}
    ydma_mark = {}
    flushq = []
    flpos = [0]

    def queue_flush(B):
        buf = B % 2
        fm = {}

        def mk(b0):
            bank = b0
            ytm = y_tm[:, b0 * 1024:(b0 + 1) * 1024]

            def f_tr1(em, b0=b0, bank=bank):
                em.wait("te", "s_dv", base_dv + 2 * (B + 1) * 128)
                em.wait("te", "s_fl", flush_ev.get(B - 1, 0))
                em.wait("te", pbf_free[bank][0], pbf_free[bank][1])
                for c in range(KC):
                    st = buf * 2048 + c * 2 + b0
                    em.op("te", lambda e, bank=bank, c=c, st=st: e.transpose(
                        ps_bf[:, bank * 1024 + c * 128:bank * 1024 + (c + 1) * 128],
                        y_fm[:, st:st + 16 * 127 + 1:16], id_b[:]))
                fm['tr1'] = em.inc_last("te", "s_tr")

            def f_ev1(em, b0=b0, bank=bank):
                em.wait("sc", "s_tr", fm['tr1'])
                em.wait("sc", "s_ydma", ydma_mark.get(B - 1, 0))
                fm['mev'] = em.op("sc", lambda e, b0=b0, bank=bank: e.copy(
                    y_tm[:, b0 * 1024:(b0 + 1) * 1024],
                    ps_bf[:, bank * 1024:(bank + 1) * 1024]), inc="s_ev", drain=True)

            def f_st1(em, ytm=ytm):
                em.wait("ve", "s_ev", fm['mev'])
                em.op("ve", lambda e, ytm=ytm: e.reduce_sum(stat[:, 8:9], ytm, axis=AX.X))
                em.drain("sc")
                fm['msq'] = em.op("sc", lambda e, ytm=ytm: e.activation(
                    sq_dump[:], ytm, AF.Square, bias=Z, accum_out=stat[:, 9:10]),
                    inc="s_sc")
                em.op("ve", lambda e: e.tensor_scalar(stat[:, 10:11], stat[:, 8:9],
                                                      1.0 / D, None, op0=ALU.mult),
                      drain=False)

            def f_st2(em):
                em.wait("ve", "s_sc", fm['msq'])
                em.op("ve", lambda e: e.tensor_scalar(stat[:, 12:13], stat[:, 9:10],
                                                      1.0 / D, EPS, op0=ALU.mult,
                                                      op1=ALU.add))
                em.op("ve", lambda e: e.tensor_tensor(stat[:, 13:14], stat[:, 10:11],
                                                      stat[:, 10:11], op=ALU.mult))
                fm['mdv'] = em.op("ve", lambda e: e.tensor_tensor(
                    stat[:, 12:13], stat[:, 12:13], stat[:, 13:14], op=ALU.subtract),
                    inc="s_fl", drain=True)

            def f_sd(em):
                em.wait("sc", "s_fl", fm['mdv'])
                fm['msd'] = em.op("sc", lambda e: e.activation(
                    stat[:, 13:14], stat[:, 12:13], AF.Sqrt, bias=Z), inc="s_sc")

            def f_rc(em):
                em.wait("ve", "s_sc", fm['msd'])
                em.op("ve", lambda e: e.reciprocal(stat[:, 11:12], stat[:, 13:14]))
                fm['mnm'] = em.op("ve", lambda e: e.tensor_scalar(
                    stat[:, 14:15], stat[:, 10:11], stat[:, 11:12], -1.0,
                    op0=ALU.mult, op1=ALU.mult), inc="s_fl")

            def f_no(em, b0=b0, ytm=ytm):
                em.wait("sc", "s_fl", fm['mnm'])
                fm['mno'] = em.op("sc", lambda e, b0=b0, ytm=ytm: e.activation(
                    y_tm[:, b0 * 1024:(b0 + 1) * 1024], ytm, AF.Identity,
                    scale=stat[:, 11:12], bias=stat[:, 14:15]), inc="s_sc",
                    drain=False)

            def f_tr2(em, b0=b0, bank=bank):
                em.wait("te", "s_sc", fm['mno'])
                for k in range(KC):
                    em.op("te", lambda e, bank=bank, b0=b0, k=k: e.transpose(
                        ps_bf[:, bank * 1024 + k * 128:bank * 1024 + (k + 1) * 128],
                        y_tm[:, b0 * 1024 + k * 128:b0 * 1024 + (k + 1) * 128],
                        id_b[:]))
                fm['tr2'] = em.inc_last("te", "s_tr")

            def f_ev2(em, b0=b0, bank=bank):
                em.wait("ve", "s_tr", fm['tr2'])
                em.wait("ve", "s_ydma", em.cnt["s_ydma"])
                mev = em.op("ve", lambda e, bank=bank: e.tensor_copy(
                    ystage[:], ps_bf[:, bank * 1024:(bank + 1) * 1024]),
                    inc="s_fl", drain=False)
                pbf_free[bank] = ("s_fl", mev)
                fm['mev2'] = mev
                if b0 == NB - 1:
                    flush_ev[B] = mev

            def f_dma(em, b0=b0):
                em.wait("sy", "s_fl", fm['mev2'])
                em.dma_fence("sy", "s_ydma")
                dstv = ynT_h.ap()[b0, :, :, B * 128:(B + 1) * 128].rearrange(
                    "k p i -> p k i")
                srcv = ystage[:].rearrange("p (k i) -> p k i", k=KC)
                em.op("sy", lambda e, dstv=dstv, srcv=srcv: e.dma_start(dstv, srcv),
                      inc="s_ydma", k=16)
                if b0 == NB - 1:
                    ydma_mark[B] = em.cnt["s_ydma"]
            return [f_tr1, f_ev1, f_st1, f_st2, f_sd, f_rc, f_no, f_tr2, f_ev2,
                    f_dma]

        for b0 in range(NB):
            flushq.extend(mk(b0))

    def drain_flush(n):
        while n > 0 and flpos[0] < len(flushq):
            flushq[flpos[0]](em)
            flpos[0] += 1
            n -= 1

    m_mm_h = {0: 0, 1: 0}
    for t in range(1, L):
        B = t // 128
        buf = B % 2
        pbuf = ((t - 1) // 128) % 2
        par = t % 2
        po = t % 128
        ppo = (t - 1) % 128
        if po == 0:
            queue_flush(B - 1)
            assert p1pos[0] >= p1_need.get(B, 0), (B, p1pos[0])
        prev0 = pbuf * 2048 + ppo * 16

        def mm_fn(e_, k, start, stop, prev0=prev0):
            hb = (e_ // 4) * 512
            return lambda e, e_=e_, k=k, hb=hb, prev0=prev0, st=start, sp=stop: \
                e.matmul(ps_sc[:, hb + (e_ % 4) * 2:hb + (e_ % 4) * 2 + 2],
                         wrT[:, k * D + e_ * 128:k * D + (e_ + 1) * 128],
                         y_fm[:, prev0 + k * 2:prev0 + k * 2 + 2],
                         start=st, stop=sp)
        em.wait("te", "s_dv", base_dv + 2 * t - 1)
        for e_ in range(4):
            for k in range(4):
                em.op("te", mm_fn(e_, k, e_ == 0 and k == 0, False))
        em.wait("te", "s_dv", base_dv + 2 * t)
        for e_ in range(4):
            for k in range(4, KC):
                fn = mm_fn(e_, k, False, e_ == 3 and k == KC - 1)
                if e_ == 3 and k == KC - 1:
                    m_mm_h[0] = em.op("te", fn, inc="s_mm")
                else:
                    em.op("te", fn)
        for e_ in range(4, KC):
            for k in range(KC):
                fn = mm_fn(e_, k, e_ == 4 and k == 0, e_ == KC - 1 and k == KC - 1)
                if e_ == KC - 1 and k == KC - 1:
                    m_mm_h[1] = em.op("te", fn, inc="s_mm")
                else:
                    em.op("te", fn)
        if po == 0:
            em.wait("ve", "s_fl", flush_ev.get(B - 2, 0))
        cur0 = buf * 2048 + po * 16
        uv0 = (B % 4) * 2048 + po * 16
        if t == 1:
            em.op("ve", lambda e, prev0=prev0: e.tensor_tensor(
                scan_tmp[:, 32:48], y_fm[:, prev0:prev0 + 16], dec_bc[:], op=ALU.mult))
            em.op("ve", lambda e, uv0=uv0: e.tensor_tensor(
                scan_tmp[:, 48:64], scan_tmp[:, 32:48], v_st[:, uv0:uv0 + 16],
                op=ALU.add))
        em.wait("ve", "s_mm", m_mm_h[0])
        em.op("ve", lambda e, uv0=uv0: e.tensor_tensor(
            scan_tmp[:, 0:8], ps_sc[:, 0:8],
            u_st[:, uv0:uv0 + 8], op=ALU.mult))
        em.op("ve", lambda e, cur0=cur0: e.tensor_tensor(
            y_fm[:, cur0:cur0 + 8], scan_tmp[:, 48:56], scan_tmp[:, 0:8],
            op=ALU.subtract), inc="s_dv", drain=False)
        em.wait("ve", "s_mm", m_mm_h[1])
        em.op("ve", lambda e, uv0=uv0: e.tensor_tensor(
            scan_tmp[:, 8:16], ps_sc[:, 512:520],
            u_st[:, uv0 + 8:uv0 + 16], op=ALU.mult))
        em.op("ve", lambda e, cur0=cur0: e.tensor_tensor(
            y_fm[:, cur0 + 8:cur0 + 16], scan_tmp[:, 56:64], scan_tmp[:, 8:16],
            op=ALU.subtract), inc="s_dv")
        if t < L - 1:
            nt = t + 1
            nuv0 = ((nt // 128) % 4) * 2048 + (nt % 128) * 16
            em.op("ve", lambda e, cur0=cur0: e.tensor_tensor(
                scan_tmp[:, 32:48], y_fm[:, cur0:cur0 + 16], dec_bc[:], op=ALU.mult))
            em.op("ve", lambda e, nuv0=nuv0: e.tensor_tensor(
                scan_tmp[:, 48:64], scan_tmp[:, 32:48], v_st[:, nuv0:nuv0 + 16],
                op=ALU.add), drain=False)
        drain_flush(1)
        if t >= 2:
            drain_quant(1)
        if t % 2 == 1:
            drain_p1(t, 1)
    queue_flush(NCH - 1)
    drain_flush(10 ** 9)
    drain_quant(10 ** 9)
    m_wq_all = em.cnt["s_gd"]
    m_y_all = em.cnt["s_ydma"]

    em.barrier()
    em.realize()
    for c in reversed(cm_p2):
        c.__exit__(None, None, None)
    cm_wgT.__exit__(None, None, None)
    cm_wrT.__exit__(None, None, None)
    if max_phase <= 2:
        return _close_truncated(nc, em)

    cm_p3 = [nc.sbuf_tensor("actT", [128, KC * NT3], BF16),
             nc.sbuf_tensor("ynT", [128, 2 * KC * NT3], BF16),
             nc.sbuf_tensor("att_tm", [128, 4 * D], BF16),
             nc.sbuf_tensor("att_fm", [128, KC * NT3], BF16),
             nc.sbuf_tensor("h3", [128, FC * NT3], BF16),
             nc.sbuf_tensor("wqf_sb", [128, 2 * 2 * KC * 128], BF16),
             nc.sbuf_tensor("wqp_sb", [128, 2 * FC * 128], BF16)]
    (actT, ynT, att_tm, att_fm, h3, wqf_sb, wqp_sb) = [c.__enter__() for c in cm_p3]

    out_mark = [0, 0, 0, 0]
    yin_mark = [0]
    mm_ev = {0: 0, 1: 0}
    wqf_mark = {}
    wqp_mark = {}
    wqf_free = {0: 0, 1: 0}
    wqp_free = {0: 0, 1: 0}
    first_wq = [True]

    def emit_wqf_load(jg):
        pz = jg % 2
        if first_wq[0]:
            em.wait("gp", "s_gd", m_wq_all)
            em.wait("gp", "s_ydma", m_y_all)
            first_wq[0] = False
        em.wait("gp", "s_mm", wqf_free[pz])
        em.dma_fence("gp", "s_gd")
        dstv = wqf_sb[:, pz * 2 * KC * 128:(pz + 1) * 2 * KC * 128].rearrange(
            "p (j c) -> p j c", j=2)
        srcv = wqfc_h.ap()[jg * 2:(jg + 1) * 2, :, :].rearrange("j p c -> p j c")
        wqf_mark[jg] = em.op("gp", lambda e, dstv=dstv, srcv=srcv: e.dma_start(
            dstv, srcv), inc="s_gd", k=16)

    def emit_wqp_load(m):
        pz = m % 2
        em.wait("gp", "s_mm", wqp_free[pz])
        em.dma_fence("gp", "s_gd")
        wqp_mark[m] = em.op("gp", lambda e, m=m, pz=pz: e.dma_start(
            wqp_sb[:, pz * FC * 128:(pz + 1) * FC * 128], wqpr_h.ap()[m, :, :]),
            inc="s_gd", k=16)

    em.wait("sc", "s_ydma", m_y_all)
    yn_mark = {}

    def emit_ynT_load(bb, tt0, par):
        em.wait("sc", "s_mm", yin_mark[0])
        em.dma_fence("sc", "s_scd")
        dstv = ynT[:, par * KC * NT3:(par + 1) * KC * NT3].rearrange(
            "p (k t) -> p k t", k=KC)
        srcv = ynT_h.ap()[bb, :, :, tt0:tt0 + NT3].rearrange("k p t -> p k t")
        return em.op("sc", lambda e, dstv=dstv, srcv=srcv: e.dma_start(dstv, srcv),
                     inc="s_scd", k=16)

    for b in range(NB):
        em.wait("sc", "s_dv", em.cnt["s_dv"])
        em.wait("te", "s_dv", em.cnt["s_dv"])
        mrep3 = em.cnt["s_sc"]
        mrep3 = build_rep(0, b, 4 * D, ONE, mrep3)
        mrep3 = build_rep(1, b, 3 * D, Z, mrep3)
        em.wait("ve", "s_sc", mrep3)
        em.wait("te", "s_sc", mrep3)
        for gi, goff in ((0, 2 * D), (2, 5 * D)):
            for k in range(KC):
                em.op("te", lambda e, b=b, goff=goff, k=k: e.transpose(
                    ps_sc[:, k:k + 1],
                    mod_sb[32 * b:32 * b + 1, goff + k * 128:goff + (k + 1) * 128],
                    ones1[32 * b:32 * b + 1, 0:1]))
            mtr = em.inc_last("te", "s_tr")
            em.wait("ve", "s_tr", mtr)
            em.op("ve", lambda e, gi=gi: e.tensor_copy(
                gfm[:, gi * 8:(gi + 1) * 8], ps_sc[:, 0:8]))
            bcol = vecs[:, 32:40] if gi == 0 else vecs[:, 40:48]
            mgf = em.op("ve", lambda e, gi=gi, bcol=bcol: e.tensor_tensor(
                gfm[:, (gi + 1) * 8:(gi + 2) * 8], gfm[:, gi * 8:(gi + 1) * 8],
                bcol, op=ALU.mult), inc="s_dv")
            em.wait("te", "s_dv", mgf)
            em.wait("sc", "s_dv", mgf)
        for t3 in range(NTIL3):
            t0 = t3 * NT3
            mxl = [0] * NS3
            myl = [0] * NS3
            for s in range(NS3):
                em.wait("sy", "s_out", out_mark[s])
                em.dma_fence("sy", "s_ld")
                mxl[s] = em.op("sy", lambda e, b=b, s=s, t0=t0: e.dma_start(
                    x_sb[:, s * D:(s + 1) * D],
                    x_d.ap()[b, t0 + 128 * s:t0 + 128 * (s + 1), :]), inc="s_ld", k=16)
            ti = b * NTIL3 + t3
            pyn = ti % 2
            if ti == 0:
                yn_mark[0] = emit_ynT_load(0, 0, 0)
            if ti + 1 < NB * NTIL3:
                bn, tn = divmod(ti + 1, NTIL3)
                yn_mark[(ti + 1) % 2] = emit_ynT_load(bn, tn * NT3, (ti + 1) % 2)
            myn = yn_mark[pyn]
            emit_wqf_load(0)
            emit_wqf_load(1)
            m_att = 0
            em.wait("te", "s_scd", myn)
            ev_c = [0] * KC
            for c in range(KC):
                if c % 4 == 0:
                    em.wait("te", "s_sc", em.cnt["s_sc"])
                pb = (ps_sc, 0) if c % 4 == 0 else \
                     (ps_sc, 512) if c % 4 == 1 else \
                     (ps_mm, 0) if c % 4 == 2 else (ps_mm, 512)
                for k in range(KC):
                    em.op("te", lambda e, c=c, k=k, pb=pb, pyn=pyn: e.matmul(
                        pb[0][:, pb[1]:pb[1] + NT3],
                        woT[:, k * D + c * 128:k * D + (c + 1) * 128],
                        ynT[:, pyn * KC * NT3 + k * NT3:pyn * KC * NT3 + (k + 1) * NT3],
                        start=(k == 0), stop=(k == KC - 1)))
                m_mm = em.inc_last("te", "s_mm")
                em.wait("sc", "s_mm", m_mm)
                ev_c[c] = em.op("sc", lambda e, c=c, pb=pb: e.activation(
                    att_fm[:, c * NT3:(c + 1) * NT3], pb[0][:, pb[1]:pb[1] + NT3],
                    AF.Identity, scale=gfm[:, c:c + 1], bias=gfm[:, 8 + c:9 + c]),
                    inc="s_sc")
            yn_used = em.cnt["s_mm"]
            mxp = [0] * NS3
            m_act2 = 0
            for s4 in range(NS3):
                pz = s4 % 2
                em.wait("te", "s_sc", ev_c[KC - 1])
                em.wait("te", pbf_free[pz][0], pbf_free[pz][1])
                for c in range(KC):
                    em.op("te", lambda e, c=c, s4=s4, pz=pz: e.transpose(
                        ps_bf[:, pz * 1024 + c * 128:pz * 1024 + (c + 1) * 128],
                        att_fm[:, c * NT3 + s4 * 128:c * NT3 + (s4 + 1) * 128], id_b[:]))
                mtr = em.inc_last("te", "s_tr")
                em.wait("ve", "s_tr", mtr)
                m_att = em.op("ve", lambda e, s4=s4, pz=pz: e.tensor_copy(
                    att_tm[:, s4 * D:(s4 + 1) * D],
                    ps_bf[:, pz * 1024:(pz + 1) * 1024]), inc="s_dv", drain=True)
                pbf_free[pz] = ("s_dv", m_att)
                em.wait("ve", "s_ld", mxl[s4])
                mxp[s4] = em.op("ve", lambda e, s4=s4: e.tensor_tensor(
                    x_sb[:, s4 * D:(s4 + 1) * D], x_sb[:, s4 * D:(s4 + 1) * D],
                    att_tm[:, s4 * D:(s4 + 1) * D], op=ALU.add), inc="s_dv")
                _, mh = emit_ln_modulate(
                    x_sb[:, s4 * D:(s4 + 1) * D], s4 % 2, rep[:, 0:D], rep[:, D:2 * D],
                    None, None, mxp[s4], "s_dv")
                m_act2 = emit_act_transpose(s4 % 2, mh, actT, s4 * 128)
            yin_mark[0] = yn_used
            em.wait("te", "s_dv", m_act2)
            for k in range(KC):
                em.op("te", lambda e, k=k: e.matmul(
                    ps_lo[0:R, 0:NT3], afcT[:, k * R:(k + 1) * R], actT_full(actT, k),
                    start=(k == 0), stop=(k == KC - 1)))
            m_mm = em.inc_last("te", "s_mm")
            em.wait("ve", "s_mm", m_mm)
            m_s2 = em.op("ve", lambda e: e.tensor_scalar(
                s2[0:R, :], ps_lo[0:R, 0:NT3], 2.0, None, op0=ALU.mult), inc="s_dv")
            em.wait("te", "s_dv", m_s2)
            for j in range(FC):
                jg = j // 2
                pz4 = j % 4
                pg = jg % 2
                if j % 2 == 0:
                    em.wait("te", "s_gd", wqf_mark[jg])
                if j % 4 == 0:
                    em.wait("te", "s_sc", em.cnt["s_sc"])
                pb = (ps_mm, 0) if pz4 == 0 else \
                     (ps_mm, 512) if pz4 == 1 else \
                     (ps_sc, 0) if pz4 == 2 else (ps_sc, 512)
                base_w = pg * 2 * KC * 128 + (j % 2) * KC * 128
                for k in range(KC):
                    em.op("te", lambda e, k=k, pb=pb, base_w=base_w: e.matmul(
                        pb[0][:, pb[1]:pb[1] + NT3],
                        wqf_sb[:, base_w + k * 128:base_w + (k + 1) * 128],
                        actT_full(actT, k), start=(k == 0), stop=False))
                m_mm = em.op("te", lambda e, j=j, pb=pb: e.matmul(
                    pb[0][:, pb[1]:pb[1] + NT3],
                    bfcT[0:R, j * 128:(j + 1) * 128], s2[0:R, :],
                    start=False, stop=True), inc="s_mm")
                if j % 2 == 1:
                    wqf_free[pg] = m_mm
                em.wait("sc", "s_mm", m_mm)
                em.op("sc", lambda e, j=j, pb=pb: e.activation(
                    h3[:, j * NT3:(j + 1) * NT3], pb[0][:, pb[1]:pb[1] + NT3],
                    AF.Gelu, bias=bfc_fm[:, j:j + 1]), inc="s_sc")
                if j % 2 == 1 and jg + 2 < FC // 2:
                    emit_wqf_load(jg + 2)
            emit_wqp_load(0)
            emit_wqp_load(1)
            em.wait("te", "s_sc", em.cnt["s_sc"])
            for j in range(FC):
                em.op("te", lambda e, j=j: e.matmul(
                    ps_lo[0:R, 0:NT3], aprT[:, j * R:(j + 1) * R],
                    h3[:, j * NT3:(j + 1) * NT3],
                    start=(j == 0), stop=(j == FC - 1)))
            m_mm = em.inc_last("te", "s_mm")
            em.wait("ve", "s_mm", m_mm)
            m_s2p = em.op("ve", lambda e: e.tensor_scalar(
                s2[0:R, :], ps_lo[0:R, 0:NT3], 2.0, None, op0=ALU.mult), inc="s_dv")
            m_proj = 0
            em.wait("te", "s_dv", m_s2p)
            ev_m = [0] * KC
            for m_ in range(KC):
                pz4 = m_ % 4
                em.wait("te", "s_gd", wqp_mark[m_])
                if m_ % 4 == 0:
                    em.wait("te", "s_sc", em.cnt["s_sc"])
                pb = (ps_mm, 0) if pz4 == 0 else \
                     (ps_mm, 512) if pz4 == 1 else \
                     (ps_sc, 0) if pz4 == 2 else (ps_sc, 512)
                pzb = m_ % 2
                for k in range(FC):
                    em.op("te", lambda e, m_=m_, k=k, pb=pb, pzb=pzb: e.matmul(
                        pb[0][:, pb[1]:pb[1] + NT3],
                        wqp_sb[:, pzb * FC * 128 + k * 128:pzb * FC * 128 + (k + 1) * 128],
                        h3[:, k * NT3:(k + 1) * NT3], start=(k == 0), stop=False))
                m_mm = em.op("te", lambda e, m_=m_, pb=pb: e.matmul(
                    pb[0][:, pb[1]:pb[1] + NT3],
                    bprT[0:R, m_ * 128:(m_ + 1) * 128], s2[0:R, :],
                    start=False, stop=True), inc="s_mm")
                wqp_free[pzb] = m_mm
                em.wait("sc", "s_mm", m_mm)
                ev_m[m_] = em.op("sc", lambda e, m_=m_, pb=pb: e.activation(
                    att_fm[:, m_ * NT3:(m_ + 1) * NT3], pb[0][:, pb[1]:pb[1] + NT3],
                    AF.Identity, scale=gfm[:, 16 + m_:17 + m_],
                    bias=gfm[:, 24 + m_:25 + m_]), inc="s_sc")
                if m_ + 2 < KC:
                    emit_wqp_load(m_ + 2)
            for s4 in range(NS3):
                pz = s4 % 2
                em.wait("te", "s_sc", ev_m[KC - 1])
                em.wait("te", pbf_free[pz][0], pbf_free[pz][1])
                for m_ in range(KC):
                    em.op("te", lambda e, m_=m_, s4=s4, pz=pz: e.transpose(
                        ps_bf[:, pz * 1024 + m_ * 128:pz * 1024 + (m_ + 1) * 128],
                        att_fm[:, m_ * NT3 + s4 * 128:m_ * NT3 + (s4 + 1) * 128], id_b[:]))
                mtr = em.inc_last("te", "s_tr")
                em.wait("ve", "s_tr", mtr)
                m_proj = em.op("ve", lambda e, s4=s4, pz=pz: e.tensor_copy(
                    att_tm[:, s4 * D:(s4 + 1) * D],
                    ps_bf[:, pz * 1024:(pz + 1) * 1024]), inc="s_dv", drain=True)
                pbf_free[pz] = ("s_dv", m_proj)
                mo = em.op("ve", lambda e, s4=s4: e.tensor_tensor(
                    x_sb[:, s4 * D:(s4 + 1) * D], x_sb[:, s4 * D:(s4 + 1) * D],
                    att_tm[:, s4 * D:(s4 + 1) * D], op=ALU.add), inc="s_dv")
                em.wait("sy", "s_dv", mo)
                em.dma_fence("sy", "s_out")
                out_mark[s4] = em.op("sy", lambda e, b=b, s4=s4, t0=t0: e.dma_start(
                    out_d.ap()[b, t0 + 128 * s4:t0 + 128 * (s4 + 1), :],
                    x_sb[:, s4 * D:(s4 + 1) * D]), inc="s_out", k=16)
    em.wait("sy", "s_out", em.cnt["s_out"])
    em.realize()
    for c in reversed(cm_p3):
        c.__exit__(None, None, None)
    return nc


def _slice_inputs(inputs, core):
    b0 = NB * core
    m = {
        "x": np.ascontiguousarray(inputs["L0_attn_x"][b0:b0 + NB]),
        "cvec": np.ascontiguousarray(inputs["L0_c"][b0:b0 + NB]),
        "wmod": inputs["L0_wmod"],
        "decay": inputs["L0_attn_decay"],
        "Wr": inputs["L0_attn_Wr"], "br": inputs["L0_attn_br"],
        "Wg": inputs["L0_attn_Wg"], "bg": inputs["L0_attn_bg"],
        "Wo": inputs["L0_attn_Wo"], "bo": inputs["L0_attn_bo"],
        "ln_w": inputs["L0_attn_ln_w"], "ln_b": inputs["L0_attn_ln_b"],
        "W_fc": inputs["L0_ffn_W_fc"], "b_fc": inputs["L0_ffn_b_fc"],
        "A_fc": inputs["L0_ffn_A_fc"], "B_fc": inputs["L0_ffn_B_fc"],
        "W_proj": inputs["L0_ffn_W_proj"], "b_proj": inputs["L0_ffn_b_proj"],
        "A_proj": inputs["L0_ffn_A_proj"], "B_proj": inputs["L0_ffn_B_proj"],
    }
    return {k: np.ascontiguousarray(v) for k, v in m.items()}


def kernel(**inputs):
    nc = build_program(L=2048)
    in_maps = [_slice_inputs(inputs, i) for i in range(8)]
    res = run_bass_kernel_spmd(nc, in_maps, core_ids=list(range(8)))
    return np.concatenate([np.asarray(r["out"]) for r in res.results], axis=0)
